# version 02
# baseline (speedup 1.0000x reference)
import numpy as np
import ml_dtypes
from contextlib import ExitStack

import concourse.bass as bass
import concourse.mybir as mybir
from concourse.bass_utils import run_bass_kernel_spmd

F32 = mybir.dt.float32
BF16 = mybir.dt.bfloat16
AF = mybir.ActivationFunctionType
ALU = mybir.AluOpType
AX = mybir.AxisListType
BF = ml_dtypes.bfloat16

D = 1024
NCTX = 4096
NQ = 2048
NKVT = 32
NQT = 16
HP = 8
SCALE = 0.125
EPS = 1e-5
NE = 16

ENGS = ("tensor", "vector", "scalar", "gpsimd", "sync")


class Prog:

    def __init__(self, nc, sems):
        self.nc = nc
        self.sems = sems
        self.cnt = {k: 0 for k in sems}
        self.q = {e: [] for e in ENGS}
        self.lastw = {e: {} for e in ENGS}

    def do(self, eng, fn, waits=(), inc=None):
        real_waits = []
        lw = self.lastw[eng]
        for s, v in waits:
            if s is None or v <= 0:
                continue
            if lw.get(s, -1) < v:
                real_waits.append((s, v))
                lw[s] = v
        if inc is not None:
            self.cnt[inc[0]] += inc[1]
        self.q[eng].append((fn, tuple(real_waits), inc))
        return self.cnt[inc[0]] if inc is not None else None

    def barrier(self):
        self.cnt["bar"] += 5
        tgt = self.cnt["bar"]
        for e in ENGS:
            def fn(eng, _t=tgt):
                eng.sem_inc(self.sems["bar"], 1)
                eng.wait_ge(self.sems["bar"], _t)
                return None
            self.q[e].append((fn, (), None))
            self.lastw[e]["bar"] = tgt

    def flush(self):
        with self.nc.Block() as blk:
            for e in ENGS:
                ops = self.q[e]
                self.q[e] = []
                if not ops:
                    continue

                def body(eng, _ops=tuple(ops), _sems=self.sems):
                    for fn, waits, inc in _ops:
                        for s, v in waits:
                            eng.wait_ge(_sems[s], v)
                        r = fn(eng)
                        if inc is not None and r is not None:
                            r.then_inc(_sems[inc[0]], inc[1])
                getattr(blk, e)(body)


def build_nc(stop_after='all'):
    nc = bass.Bass("TRN2", target_bir_lowering=False, detect_race_conditions=False)

    x_ctx = nc.dram_tensor("x_ctx", [NCTX, D], F32, kind="ExternalInput")
    x_q = nc.dram_tensor("x_q", [NQ, D], F32, kind="ExternalInput")
    wq = nc.dram_tensor("wq", [D, D], F32, kind="ExternalInput")
    wk = nc.dram_tensor("wk", [D, D], F32, kind="ExternalInput")
    wv = nc.dram_tensor("wv", [D, D], F32, kind="ExternalInput")
    wo = nc.dram_tensor("wo", [D, D], F32, kind="ExternalInput")
    wrt = nc.dram_tensor("wrt", [NE, D, D], F32, kind="ExternalInput")
    wst = nc.dram_tensor("wst", [2, D, D], F32, kind="ExternalInput")
    cent_t = nc.dram_tensor("cent_t", [D, 16], F32, kind="ExternalInput")
    eb = nc.dram_tensor("eb", [16, 1], F32, kind="ExternalInput")
    br = nc.dram_tensor("br", [16, D], F32, kind="ExternalInput")
    bs = nc.dram_tensor("bs", [2, D], F32, kind="ExternalInput")
    mask_in = nc.dram_tensor("mask", [128, 64], BF16, kind="ExternalInput")
    ident_b_in = nc.dram_tensor("ident_b", [128, 128], BF16, kind="ExternalInput")
    ident_f_in = nc.dram_tensor("ident_f", [128, 128], F32, kind="ExternalInput")
    out_d = nc.dram_tensor("out", [NQ, D], F32, kind="ExternalOutput")
    av_d = nc.dram_tensor("av_spill", [HP, 128, NQ], BF16, kind="ExternalOutput")

    es = ExitStack()
    sems = {}
    for s in ("qW", "qX", "tPE", "tSC", "tDV", "tGP", "bar"):
        sems[s] = es.enter_context(nc.semaphore(s))
    P = Prog(nc, sems)

    ident_b = es.enter_context(nc.sbuf_tensor("ident_b_s", [128, 128], BF16))
    mask_s = es.enter_context(nc.sbuf_tensor("mask_s", [128, 64], BF16))
    ones1 = es.enter_context(nc.sbuf_tensor("ones1", [128, 1], BF16))
    ones2 = es.enter_context(nc.sbuf_tensor("ones2", [2, 128], BF16))
    negb8 = es.enter_context(nc.sbuf_tensor("negb8", [128, 8], F32))
    stats = es.enter_context(nc.sbuf_tensor("stats", [128, 24], F32))

    c_idb = P.do("sync", lambda e: e.dma_start(ident_b[:], ident_b_in[:, :]), inc=("qX", 16))
    c_msk = P.do("sync", lambda e: e.dma_start(mask_s[:], mask_in[:, :]), inc=("qX", 16))
    P.do("gpsimd", lambda e: e.memset(ones1[:], 1.0))
    c_ones = P.do("gpsimd", lambda e: e.memset(ones2[:], 1.0), inc=("tGP", 1))

    def wcast_load(dst, dst_col, src, e_off):
        dst_stride = dst.shape[1]

        def fn(eng):
            return eng.dma_start(
                bass.AP(dst, dst_col, [[dst_stride, 128], [1024, 8], [1, 1024]]),
                bass.AP(src, e_off, [[1024, 128], [131072, 8], [1, 1024]]),
            )
        return fn

    kvq = ExitStack()
    k_all = kvq.enter_context(nc.sbuf_tensor("k_all", [128, HP * NCTX], BF16))
    v_all = kvq.enter_context(nc.sbuf_tensor("v_all", [128, NKVT * D], BF16))
    q_all = kvq.enter_context(nc.sbuf_tensor("q_all", [128, HP * NQ], BF16))
    xt = kvq.enter_context(nc.sbuf_tensor("xt", [128, 2 * 1024], F32))
    xlt = kvq.enter_context(nc.sbuf_tensor("xlt", [128, 1024], BF16))
    xlf = kvq.enter_context(nc.sbuf_tensor("xlf", [128, 8 * 256], BF16))

    lnst = {
        "xt_free": [[], []],
        "xlt_free": [],
        "cr_prev": [0, 0],
        "g": 0,
    }

    def ln_tile(src_dram, row0, dst_fm_col, tp_ps, qx_extra=()):
        g = lnst["g"]
        lnst["g"] += 1
        b = g % 2
        sb = b * 12
        c_x = P.do("sync", lambda e, _b=b, _r=row0, _s=src_dram: e.dma_start(
            xt[:, 1024 * _b:1024 * _b + 1024], _s[_r:_r + 128, :]),
            waits=list(lnst["xt_free"][b]) + list(qx_extra), inc=("qX", 16))
        xv = xt[:, 1024 * b:1024 * b + 1024]
        c_rs = P.do("vector", lambda e, _x=xv, _c=sb: e.reduce_sum(
            stats[:, _c:_c + 1], _x, axis=AX.X),
            waits=[("qX", c_x), ("tSC", lnst["cr_prev"][b])], inc=("tDV", 1))
        P.do("scalar", lambda e, _x=xv, _c=sb: e.activation(
            xlt[:, :], _x, AF.Square,
            accum_out=stats[:, _c + 1:_c + 2]),
            waits=[("qX", c_x), ("tSC", lnst["cr_prev"][b])] + list(lnst["xlt_free"]))
        c_sq = P.do("scalar", lambda e, _c=sb: e.copy(
            stats[:, _c + 9:_c + 10], stats[:, _c + 1:_c + 2]), inc=("tSC", 1))
        c1 = P.do("vector", lambda e, _c=sb: e.tensor_scalar(
            stats[:, _c + 2:_c + 3], stats[:, _c:_c + 1], 1.0 / D, None,
            ALU.mult, ALU.bypass), waits=[("tDV", c_rs)], inc=("tDV", 1))
        c2 = P.do("vector", lambda e, _c=sb: e.tensor_tensor(
            stats[:, _c + 3:_c + 4], stats[:, _c + 2:_c + 3], stats[:, _c + 2:_c + 3],
            ALU.mult), waits=[("tDV", c1)], inc=("tDV", 1))
        c3 = P.do("vector", lambda e, _c=sb: e.tensor_scalar(
            stats[:, _c + 4:_c + 5], stats[:, _c + 9:_c + 10], 1.0 / D, EPS,
            ALU.mult, ALU.add), waits=[("tSC", c_sq)], inc=("tDV", 1))
        c4 = P.do("vector", lambda e, _c=sb: e.tensor_tensor(
            stats[:, _c + 5:_c + 6], stats[:, _c + 4:_c + 5], stats[:, _c + 3:_c + 4],
            ALU.subtract), waits=[("tDV", c3)], inc=("tDV", 1))
        c_vr = P.do("vector", lambda e, _c=sb: e.reciprocal(
            stats[:, _c + 6:_c + 7], stats[:, _c + 5:_c + 6]),
            waits=[("tDV", c4)], inc=("tDV", 1))
        c_r = P.do("scalar", lambda e, _c=sb: e.activation(
            stats[:, _c + 7:_c + 8], stats[:, _c + 6:_c + 7], AF.Sqrt),
            waits=[("tDV", c_vr)], inc=("tSC", 1))
        lnst["cr_prev"][b] = c_r
        c5 = P.do("vector", lambda e, _c=sb: e.tensor_scalar(
            stats[:, _c + 8:_c + 9], stats[:, _c + 2:_c + 3], stats[:, _c + 7:_c + 8],
            -1.0, ALU.mult, ALU.mult), waits=[("tSC", c_r)], inc=("tDV", 1))
        c_ap = P.do("vector", lambda e, _x=xv, _c=sb: e.tensor_scalar(
            xlt[:, :], _x, stats[:, _c + 7:_c + 8],
            stats[:, _c + 8:_c + 9], ALU.mult, ALU.add),
            waits=[("tDV", c5)], inc=("tDV", 1))
        lnst["xt_free"][b] = [("tDV", c_ap), ("tSC", c_sq)]
        c_ev = None
        c_tp = None
        for dc in range(8):
            pb = dc % 2
            c_tp = P.do("tensor", lambda e, _dc=dc, _pb=pb, _tp=tp_ps: e.transpose(
                _tp[_pb][:, :], xlt[:, 128 * _dc:128 * _dc + 128],
                ident_b[:]),
                waits=[("tDV", c_ap), ("tSC", P.cnt["tSC"]), ("qX", c_idb)],
                inc=("tPE", 1))
            c_ev = P.do("scalar", lambda e, _dc=dc, _pb=pb, _col=dst_fm_col, _tp=tp_ps:
                        e.activation(
                            xlf[:, _col + 256 * _dc:_col + 256 * _dc + 128],
                            _tp[_pb][:, :], AF.Copy),
                        waits=[("tPE", c_tp)], inc=("tSC", 1))
        lnst["xlt_free"] = [("tPE", c_tp)]
        return c_ev

    p0 = ExitStack()
    wk_b = p0.enter_context(nc.sbuf_tensor("wk_b", [128, 8192], BF16))
    wv_b = p0.enter_context(nc.sbuf_tensor("wv_b", [128, 8192], BF16))
    tp_ps_a = [p0.enter_context(nc.psum_tensor(f"tp_ps{i}", [128, 128], BF16)) for i in range(2)]
    kq_ps_a = [p0.enter_context(nc.psum_tensor(f"kq_ps{i}", [128, 256], F32)) for i in range(2)]
    v_ps = [p0.enter_context(nc.psum_tensor(f"v_ps{i}", [128, 1024], F32)) for i in range(2)]

    c_wk = P.do("gpsimd", wcast_load(wk_b, 0, wk, 0), inc=("qW", 16))
    c_wv = P.do("gpsimd", wcast_load(wv_b, 0, wv, 0), inc=("qW", 16))

    kq_free = [[], []]
    v_free = [[], []]
    for kb in range(NCTX // 256):
        c_ev = 0
        for t2 in range(2):
            c_ev = ln_tile(x_ctx, 256 * kb + 128 * t2, 128 * t2, tp_ps_a)
            if stop_after == 'ln0' and kb == 0 and t2 == 0:
                P.do("sync", lambda e: e.dma_start(out_d[0:128, 0:24], stats[:, :]),
                     waits=[("tSC", P.cnt["tSC"]), ("tDV", P.cnt["tDV"])], inc=("qX", 16))
                P.do("sync", lambda e: e.dma_start(
                    bass.AP(av_d, 0, [[2048, 128], [1, 1024]]), xlt[:, :]), inc=("qX", 16))
                P.do("sync", lambda e: e.dma_start(
                    bass.AP(av_d, 128 * 2048, [[2048, 128], [1, 2048]]), xlf[:, :]), inc=("qX", 16))
                P.flush()
                return nc
        for hp in range(HP):
            pb = hp % 2
            w0 = [("tSC", c_ev), ("qW", c_wk)] + kq_free[pb]
            c_mm = None
            for dc in range(8):
                c_mm = P.do("tensor", lambda e, _hp=hp, _dc=dc, _pb=pb: e.matmul(
                    kq_ps_a[_pb][:, :],
                    wk_b[:, 1024 * _dc + 128 * _hp:1024 * _dc + 128 * _hp + 128],
                    xlf[:, 256 * _dc:256 * _dc + 256],
                    start=(_dc == 0), stop=(_dc == 7), skip_group_check=True),
                    waits=w0 if dc == 0 else [], inc=("tPE", 1))
            c_kev = P.do("scalar", lambda e, _hp=hp, _kb=kb, _pb=pb: e.activation(
                k_all[:, NCTX * _hp + 256 * _kb:NCTX * _hp + 256 * _kb + 256],
                kq_ps_a[_pb][:, :], AF.Copy), waits=[("tPE", c_mm)], inc=("tSC", 1))
            kq_free[pb] = [("tSC", c_kev)]
        for t2 in range(2):
            j = 2 * kb + t2
            pb = j % 2
            w0 = [("tSC", c_ev), ("qW", c_wv)] + v_free[pb]
            c_mm = None
            for dc in range(8):
                for nh in range(2):
                    c_mm = P.do("tensor", lambda e, _t2=t2, _dc=dc, _nh=nh, _pb=pb: e.matmul(
                        v_ps[_pb][:, 512 * _nh:512 * _nh + 512],
                        xlf[:, 256 * _dc + 128 * _t2:256 * _dc + 128 * _t2 + 128],
                        wv_b[:, 1024 * _dc + 512 * _nh:1024 * _dc + 512 * _nh + 512],
                        start=(_dc == 0), stop=(_dc == 7), skip_group_check=True),
                        waits=w0 if (dc == 0 and nh == 0) else [], inc=("tPE", 1))
            c_vev = P.do("scalar", lambda e, _j=j, _pb=pb: e.activation(
                v_all[:, 1024 * _j:1024 * _j + 1024], v_ps[_pb][:, :], AF.Copy),
                waits=[("tPE", c_mm)], inc=("tSC", 1))
            v_free[pb] = [("tSC", c_vev)]
    c_p0a_pe = P.cnt["tPE"]

    P.flush()
    if stop_after == 'p0a':
        return nc
    p0.close()

    p0b = ExitStack()
    wq_b = p0b.enter_context(nc.sbuf_tensor("wq_b", [128, 8192], BF16))
    tp_ps_b = [p0b.enter_context(nc.psum_tensor(f"tpb_ps{i}", [128, 128], BF16)) for i in range(2)]
    kq_ps_b = [p0b.enter_context(nc.psum_tensor(f"kqb_ps{i}", [128, 256], F32)) for i in range(2)]

    c_wq = P.do("gpsimd", wcast_load(wq_b, 0, wq, 0),
                waits=[("tPE", c_p0a_pe)], inc=("qW", 16))
    kq_free = [[], []]
    for qb in range(NQ // 256):
        c_ev = 0
        for t2 in range(2):
            c_ev = ln_tile(x_q, 256 * qb + 128 * t2, 128 * t2, tp_ps_b)
        for hp in range(HP):
            pb = hp % 2
            w0 = [("tSC", c_ev), ("qW", c_wq)] + kq_free[pb]
            c_mm = None
            for dc in range(8):
                c_mm = P.do("tensor", lambda e, _hp=hp, _dc=dc, _pb=pb: e.matmul(
                    kq_ps_b[_pb][:, :],
                    wq_b[:, 1024 * _dc + 128 * _hp:1024 * _dc + 128 * _hp + 128],
                    xlf[:, 256 * _dc:256 * _dc + 256],
                    start=(_dc == 0), stop=(_dc == 7), skip_group_check=True),
                    waits=w0 if dc == 0 else [], inc=("tPE", 1))
            c_qev = P.do("scalar", lambda e, _hp=hp, _qb=qb, _pb=pb: e.activation(
                q_all[:, NQ * _hp + 256 * _qb:NQ * _hp + 256 * _qb + 256],
                kq_ps_b[_pb][:, :], AF.Copy), waits=[("tPE", c_mm)], inc=("tSC", 1))
            kq_free[pb] = [("tSC", c_qev)]
    c_p0b_pe = P.cnt["tPE"]
    c_p0b_sc = P.cnt["tSC"]

    if stop_after == 'p0b':
        P.do("sync", lambda e: e.dma_start(
            bass.AP(av_d, 0, [[2048, 128], [1, 2048]]),
            k_all[:, 0:2048]), waits=[("tSC", P.cnt["tSC"])], inc=("qX", 16))
        P.do("sync", lambda e: e.dma_start(
            bass.AP(av_d, 128 * 2048, [[2048, 128], [1, 2048]]),
            k_all[:, 7 * 4096:7 * 4096 + 2048]), inc=("qX", 16))
        P.do("sync", lambda e: e.dma_start(
            bass.AP(av_d, 2 * 128 * 2048, [[2048, 128], [1, 2048]]),
            q_all[:, 7 * 2048:7 * 2048 + 2048]), inc=("qX", 16))
        P.do("sync", lambda e: e.dma_start(
            bass.AP(av_d, 3 * 128 * 2048, [[2048, 128], [1, 2048]]),
            v_all[:, 0:2048]), inc=("qX", 16))
        P.flush()
        return nc
    P.flush()
    p0b.close()

    aex = ExitStack()
    ksq = aex.enter_context(nc.sbuf_tensor("ksq", [128, 4096], BF16))
    pt = aex.enter_context(nc.sbuf_tensor("pt", [128, 2048], BF16))
    s64 = aex.enter_context(nc.sbuf_tensor("s64", [65, 1024], F32))
    hl64 = aex.enter_context(nc.sbuf_tensor("hl64", [65, 2048], BF16))
    rechl2 = aex.enter_context(nc.sbuf_tensor("rechl2", [2, 1024], BF16))
    av_sb = aex.enter_context(nc.sbuf_tensor("av_sb", [64, 1024], BF16))
    av_spl = aex.enter_context(nc.sbuf_tensor("av_spl", [64, 4096], BF16))
    km = aex.enter_context(nc.sbuf_tensor("km", [1, 16], F32))
    bsc = aex.enter_context(nc.sbuf_tensor("bsc", [1, 4], F32))
    negvb = aex.enter_context(nc.sbuf_tensor("negvb", [1, 1], BF16))
    av_ps = [aex.enter_context(nc.psum_tensor(f"av_ps{i}", [65, 1024], F32)) for i in range(2)]
    d_ps = [aex.enter_context(nc.psum_tensor(f"d_ps{i}", [128, 512], F32)) for i in range(4)]

    for hp in range(HP):
        c_k2 = P.do("vector", lambda e, _hp=hp: e.tensor_tensor(
            ksq[:, :], k_all[:, NCTX * _hp:NCTX * _hp + 4096],
            k_all[:, NCTX * _hp:NCTX * _hp + 4096], ALU.mult),
            waits=[("tPE", c_p0b_pe), ("tSC", c_p0b_sc)], inc=("tDV", 1))
        c_rm = 0
        for c in range(8):
            pb = c % 2
            c_n2 = P.do("tensor", lambda e, _c=c, _pb=pb: e.matmul(
                d_ps[_pb][0:1, :], ones1[:], ksq[:, 512 * _c:512 * _c + 512],
                start=True, stop=True, skip_group_check=True),
                waits=[("tDV", c_k2), ("tGP", c_ones)] if c < 2 else [("tDV", c_rm)],
                inc=("tPE", 1))
            c_rm = P.do("vector", lambda e, _c=c, _pb=pb: e.reduce_max(
                km[0:1, _c:_c + 1], d_ps[_pb][0:1, :], axis=AX.X),
                waits=[("tPE", c_n2)], inc=("tDV", 1))
        c_km = P.do("vector", lambda e: e.reduce_max(
            bsc[0:1, 0:1], km[0:1, 0:8], axis=AX.X),
            waits=[("tDV", c_rm)], inc=("tDV", 1))
        c_q2 = P.do("vector", lambda e, _hp=hp: e.tensor_tensor(
            ksq[:, 0:2048], q_all[:, NQ * _hp:NQ * _hp + 2048],
            q_all[:, NQ * _hp:NQ * _hp + 2048], ALU.mult),
            waits=[("tPE", P.cnt["tPE"])], inc=("tDV", 1))
        for c in range(4):
            pb = c % 2
            c_n2 = P.do("tensor", lambda e, _c=c, _pb=pb: e.matmul(
                d_ps[_pb][0:1, :], ones1[:], ksq[:, 512 * _c:512 * _c + 512],
                start=True, stop=True, skip_group_check=True),
                waits=[("tDV", c_q2)] if c < 2 else [("tDV", c_rm)], inc=("tPE", 1))
            c_rm = P.do("vector", lambda e, _c=c, _pb=pb: e.reduce_max(
                km[0:1, 8 + _c:9 + _c], d_ps[_pb][0:1, :], axis=AX.X),
                waits=[("tPE", c_n2)], inc=("tDV", 1))
        c_qm = P.do("vector", lambda e: e.reduce_max(
            bsc[0:1, 1:2], km[0:1, 8:12], axis=AX.X),
            waits=[("tDV", c_rm)], inc=("tDV", 1))
        c_pr = P.do("vector", lambda e: e.tensor_tensor(
            bsc[0:1, 2:3], bsc[0:1, 0:1], bsc[0:1, 1:2], ALU.mult),
            waits=[("tDV", c_qm)], inc=("tDV", 1))
        c_sq2 = P.do("scalar", lambda e: e.activation(
            bsc[0:1, 3:4], bsc[0:1, 2:3], AF.Sqrt), waits=[("tDV", c_pr)], inc=("tSC", 1))
        c_nv = P.do("vector", lambda e: e.tensor_scalar(
            negvb[0:1, 0:1], bsc[0:1, 3:4], -SCALE, None, ALU.mult, ALU.bypass),
            waits=[("tSC", c_sq2)], inc=("tDV", 1))
        c_bc = P.do("tensor", lambda e: e.matmul(
            d_ps[0][:, 0:1], ones2[0:1, :], negvb[0:1, 0:1], start=True, stop=True,
            skip_group_check=True), waits=[("tDV", c_nv)], inc=("tPE", 1))
        P.do("vector", lambda e, _hp=hp: e.tensor_copy(
            negb8[:, _hp:_hp + 1], d_ps[0][:, 0:1]), waits=[("tPE", c_bc)],
            inc=("tDV", 1))
    c_bounds_dv = P.cnt["tDV"]
    c_bounds_sc = P.cnt["tSC"]
    if stop_after == 'bounds':
        P.do("sync", lambda e: e.dma_start(out_d[0:128, 0:8], negb8[:, :]),
             waits=[("tDV", c_bounds_dv)], inc=("qX", 16))
        P.do("sync", lambda e: e.dma_start(out_d[128:129, 0:16], km[0:1, :]),
             inc=("qX", 16))
        P.do("sync", lambda e: e.dma_start(out_d[129:130, 0:4], bsc[0:1, :]),
             inc=("qX", 16))
        P.flush()
        return nc

    pt_cols = {0: 0, 1: 512}
    gs = 0
    exp_done = {}
    av_done = {}
    dps_guard = [("tDV", c_bounds_dv)]
    c_spill = {}
    for hp in range(HP):
        for h in range(2):
            jmax = 32 if h == 0 else 16
            slots = []
            for j in range(jmax):
                w = min(1024, 64 * (32 - j)) if h == 0 else 64 * (16 - j)
                for c0 in range(0, w, 512):
                    cw = min(512, w - c0)
                    masked = (((h == 0 and j >= 16) or h == 1)
                              and c0 <= w - 64 < c0 + cw)
                    jlast = (31 - c0 // 64) if h == 0 else (15 - c0 // 64)
                    slots.append((j, c0, cw, w, masked, j == 0, j == jlast))
            for si, (j, c0, cw, w, masked, st, sp) in enumerate(slots):
                b = gs % 2
                wt = list(dps_guard)
                dps_guard = []
                if gs >= 2:
                    wt.append(("tSC", exp_done[gs - 2]))
                kcol = NCTX * hp + 128 * j
                qcol = NQ * hp + 1024 * h + c0
                P.do("tensor", lambda e, _b=b, _k=kcol, _q=qcol, _cw=cw: e.matmul(
                    d_ps[_b][:, 0:_cw], k_all[0:64, _k:_k + 128],
                    q_all[0:64, _q:_q + _cw], start=True, stop=True,
                    tile_position=(0, 0), skip_group_check=True),
                    waits=wt, inc=("tPE", 1))
                c_dots = P.do("tensor", lambda e, _b=b, _k=kcol, _q=qcol, _cw=cw: e.matmul(
                    d_ps[2 + _b][:, 0:_cw], k_all[64:128, _k:_k + 128],
                    q_all[64:128, _q:_q + _cw], start=True, stop=True,
                    tile_position=(64, 0), skip_group_check=True), inc=("tPE", 1))
                ws = [("tPE", c_dots), ("tDV", c_bounds_dv)]
                if gs >= 2:
                    ws.append(("tPE", av_done[gs - 2]))
                P.do("scalar", lambda e, _b=b, _cw=cw, _hp=hp: e.activation(
                    pt[:, pt_cols[_b]:pt_cols[_b] + _cw], d_ps[_b][:, 0:_cw], AF.Exp,
                    bias=negb8[:, _hp:_hp + 1], scale=SCALE), waits=ws, inc=("tSC", 1))
                exp_done[gs] = P.do("scalar", lambda e, _b=b, _cw=cw, _hp=hp: e.activation(
                    pt[:, 1024 + pt_cols[_b]:1024 + pt_cols[_b] + _cw],
                    d_ps[2 + _b][:, 0:_cw], AF.Exp,
                    bias=negb8[:, _hp:_hp + 1], scale=SCALE), inc=("tSC", 1))
                if masked:
                    moff = w - 64 - c0
                    P.do("vector", lambda e, _b=b, _m=moff: e.tensor_tensor(
                        pt[:, pt_cols[_b] + _m:pt_cols[_b] + _m + 64],
                        pt[:, pt_cols[_b] + _m:pt_cols[_b] + _m + 64],
                        mask_s[:], ALU.mult),
                        waits=[("tSC", exp_done[gs]), ("qX", c_msk)], inc=("tDV", 1))
                    c_mask = P.do("vector", lambda e, _b=b, _m=moff: e.tensor_tensor(
                        pt[:, 1024 + pt_cols[_b] + _m:1024 + pt_cols[_b] + _m + 64],
                        pt[:, 1024 + pt_cols[_b] + _m:1024 + pt_cols[_b] + _m + 64],
                        mask_s[:], ALU.mult), inc=("tDV", 1))
                    wa = [("tDV", c_mask)]
                else:
                    wa = [("tSC", exp_done[gs])]
                vcol = 1024 * j + 128 * hp
                P.do("tensor", lambda e, _b=b, _v=vcol, _c0=c0, _cw=cw, _st=st, _sp=sp:
                     e.matmul(av_ps[0][0:64, _c0:_c0 + _cw], v_all[:, _v:_v + 64],
                              pt[:, pt_cols[_b]:pt_cols[_b] + _cw], start=_st, stop=_sp,
                              tile_position=(0, 0), skip_group_check=True),
                     waits=wa, inc=("tPE", 1))
                P.do("tensor", lambda e, _b=b, _c0=c0, _cw=cw, _st=st, _sp=sp:
                     e.matmul(av_ps[0][64:65, _c0:_c0 + _cw], ones1[:],
                              pt[:, pt_cols[_b]:pt_cols[_b] + _cw], start=_st, stop=_sp,
                              tile_position=(0, 64), skip_group_check=True),
                     inc=("tPE", 1))
                P.do("tensor", lambda e, _b=b, _v=vcol, _c0=c0, _cw=cw, _st=st, _sp=sp:
                     e.matmul(av_ps[1][0:64, _c0:_c0 + _cw], v_all[:, _v + 64:_v + 128],
                              pt[:, 1024 + pt_cols[_b]:1024 + pt_cols[_b] + _cw],
                              start=_st, stop=_sp,
                              tile_position=(0, 0), skip_group_check=True),
                     inc=("tPE", 1))
                av_done[gs] = P.do("tensor", lambda e, _b=b, _c0=c0, _cw=cw, _st=st, _sp=sp:
                                   e.matmul(av_ps[1][64:65, _c0:_c0 + _cw], ones1[:],
                                            pt[:, 1024 + pt_cols[_b]:1024 + pt_cols[_b] + _cw],
                                            start=_st, stop=_sp,
                                            tile_position=(0, 64), skip_group_check=True),
                                   inc=("tPE", 1))
                gs += 1
                if stop_after == 'slot' + str(gs):
                    P.flush()
                    return nc
            c_av = av_done[gs - 1]
            wsp = [("qX", c_spill[hp - 1])] if (hp - 1 in c_spill and h == 0) else []
            c_norm = 0
            for head in range(2):
                c_rc = P.do("vector", lambda e, _hd=head: e.reciprocal(
                    s64[64:65, :], av_ps[_hd][64:65, :]),
                    waits=[("tPE", c_av), ("qX", P.cnt["qX"])], inc=("tDV", 1))
                if stop_after == 'e1':
                    P.flush()
                    return nc
                c_aev = P.do("scalar", lambda e, _hd=head: e.activation(
                    av_sb[0:64, :], av_ps[_hd][0:64, :], AF.Copy),
                    waits=[("tPE", c_av), ("tDV", P.cnt["tDV"])], inc=("tSC", 1))
                if stop_after == 'e2':
                    P.flush()
                    return nc
                c_cph = P.do("vector", lambda e: e.tensor_copy(
                    hl64[64:65, 0:1024], s64[64:65, :]),
                    waits=[("tDV", c_rc)], inc=("tDV", 1))
                c_hl = P.do("vector", lambda e: e.tensor_tensor(
                    hl64[64:65, 1024:2048], s64[64:65, :], hl64[64:65, 0:1024],
                    ALU.subtract), waits=[("tDV", c_cph)], inc=("tDV", 1))
                if stop_after == 'e3':
                    P.flush()
                    return nc
                c_sh = P.do("sync", lambda e: e.dma_start(
                    bass.AP(rechl2, 0, [[1024, 2], [1, 1024]]),
                    bass.AP(hl64, 64 * 2048, [[2048, 1], [1, 2048]])),
                    waits=[("tDV", c_hl), ("tPE", P.cnt["tPE"])], inc=("qX", 16))
                if stop_after == 'e4':
                    P.flush()
                    return nc
                for c in range(2):
                    wbc = [("qX", c_sh), ("tSC", P.cnt["tSC"])]
                    if c_norm:
                        wbc.append(("tDV", c_norm))
                    c_bc = P.do("tensor", lambda e, _c=c: e.matmul(
                        d_ps[0][:, :], ones2[:], rechl2[:, 512 * _c:512 * _c + 512],
                        start=True, stop=True, skip_group_check=True),
                        waits=wbc, inc=("tPE", 1))
                    c_norm = P.do("vector", lambda e, _c=c, _hd=head, _h=h: e.tensor_tensor(
                        av_spl[0:64, 2048 * _hd + 1024 * _h + 512 * _c:
                               2048 * _hd + 1024 * _h + 512 * _c + 512],
                        av_sb[0:64, 512 * _c:512 * _c + 512],
                        d_ps[0][0:64, :], ALU.mult),
                        waits=[("tPE", c_bc), ("tSC", c_aev)] + wsp, inc=("tDV", 1))
                if stop_after == 'e5' and head == 0:
                    P.flush()
                    return nc
            dps_guard = [("tDV", c_norm)]
            if stop_after == 'epi%d_%d' % (hp, h):
                P.flush()
                return nc
        for head in range(2):
            c_spill[hp] = P.do("sync", lambda e, _hp=hp, _hd=head: e.dma_start(
                bass.AP(av_d, (_hp * 128 + 64 * _hd) * NQ, [[NQ, 64], [1, NQ]]),
                av_spl[0:64, 2048 * _hd:2048 * _hd + 2048]),
                waits=[("tDV", P.cnt["tDV"])], inc=("qX", 16))
        if stop_after == 'hp' + str(hp):
            P.flush()
            return nc
    c_spill_last = c_spill[HP - 1]

    P.flush()
    if stop_after == 'attn':
        return nc
    aex.close()
    kvq.close()

    P.barrier()
    om = ExitStack()
    u_fm = om.enter_context(nc.sbuf_tensor("u_fm", [128, 8 * NQ], BF16))
    facc = om.enter_context(nc.sbuf_tensor("facc", [128, NQT * 1024], F32))
    g_all = om.enter_context(nc.sbuf_tensor("g_all", [128, NQT * 16], F32))
    g2t = om.enter_context(nc.sbuf_tensor("g2t", [16, NQ], BF16))
    br_b = om.enter_context(nc.sbuf_tensor("br_b", [16, 1024], BF16))

    oex = ExitStack()
    av_all = oex.enter_context(nc.sbuf_tensor("av_all", [128, HP * NQ], BF16))
    wo_b = oex.enter_context(nc.sbuf_tensor("wo_b", [128, 8192], BF16))
    ident_f = oex.enter_context(nc.sbuf_tensor("ident_f_s", [128, 128], F32))
    xq_t = oex.enter_context(nc.sbuf_tensor("xq_t", [128, 2048], F32))
    res1 = oex.enter_context(nc.sbuf_tensor("res1", [128, 1024], F32))
    u_t = oex.enter_context(nc.sbuf_tensor("u_t", [128, 1024], F32))
    u_bf = oex.enter_context(nc.sbuf_tensor("u_bf", [128, 2048], BF16))
    cent_b = oex.enter_context(nc.sbuf_tensor("cent_b", [128, 128], BF16))
    sc_em = oex.enter_context(nc.sbuf_tensor("sc_em", [16, NQ], F32))
    eb_s = oex.enter_context(nc.sbuf_tensor("eb_s", [16, 1], F32))
    bs_b = oex.enter_context(nc.sbuf_tensor("bs_b", [2, 1024], BF16))
    rt = oex.enter_context(nc.sbuf_tensor("rt", [128, 80], F32))
    gg_t = oex.enter_context(nc.sbuf_tensor("gg_t", [128, 16], BF16))
    o_ps = [oex.enter_context(nc.psum_tensor(f"o_ps{i}", [128, 1024], F32)) for i in range(2)]
    tp2_ps = [oex.enter_context(nc.psum_tensor(f"tp2_ps{i}", [128, 128], BF16)) for i in range(2)]
    sc_ps = oex.enter_context(nc.psum_tensor("sc_ps", [16, 512], F32))
    misc_ps = oex.enter_context(nc.psum_tensor("misc_ps", [128, 16], F32))

    c_av_ld = P.do("sync", lambda e: e.dma_start(
        bass.AP(av_all, 0, [[HP * NQ, 128], [NQ, HP], [1, NQ]]),
        bass.AP(av_d, 0, [[NQ, 128], [128 * NQ, HP], [1, NQ]])),
        waits=[("qX", c_spill_last)], inc=("qX", 16))
    c_idf = P.do("sync", lambda e: e.dma_start(ident_f[:], ident_f_in[:, :]), inc=("qX", 16))
    c_wo = P.do("gpsimd", wcast_load(wo_b, 0, wo, 0), inc=("qW", 16))
    c_cent = P.do("gpsimd", lambda e: e.dma_start(
        bass.AP(cent_b, 0, [[128, 128], [16, 8], [1, 16]]),
        bass.AP(cent_t, 0, [[16, 128], [2048, 8], [1, 16]])), inc=("qW", 16))
    c_eb = P.do("gpsimd", lambda e: e.dma_start(eb_s[:], eb[:, :]), inc=("qW", 16))
    c_br = P.do("gpsimd", lambda e: e.dma_start(br_b[0:16, :], br[:, :]), inc=("qW", 16))
    c_bs = P.do("gpsimd", lambda e: e.dma_start(bs_b[:], bs[:, :]), inc=("qW", 16))

    xq_free = [[], []]
    ops_free = [[], []]
    ubf_free = [[], []]
    c_sq_prev = 0
    c_r_prev = 0
    for qt in range(NQT):
        b = qt % 2
        c_xq = P.do("sync", lambda e, _qt=qt, _b=b: e.dma_start(
            xq_t[:, 1024 * _b:1024 * _b + 1024], x_q[128 * _qt:128 * _qt + 128, :]),
            waits=list(xq_free[b]), inc=("qX", 16))
        wmm = [("qX", c_av_ld), ("qW", c_wo)] + ops_free[b]
        c_mm = None
        for ic in range(8):
            for nh in range(2):
                c_mm = P.do("tensor", lambda e, _qt=qt, _ic=ic, _nh=nh, _b=b: e.matmul(
                    o_ps[_b][:, 512 * _nh:512 * _nh + 512],
                    av_all[:, NQ * _ic + 128 * _qt:NQ * _ic + 128 * _qt + 128],
                    wo_b[:, 1024 * _ic + 512 * _nh:1024 * _ic + 512 * _nh + 512],
                    start=(_ic == 0), stop=(_ic == 7), skip_group_check=True),
                    waits=wmm if (ic == 0 and nh == 0) else [], inc=("tPE", 1))
        c_r1 = P.do("vector", lambda e, _b=b: e.tensor_tensor(
            res1[:, :], o_ps[_b][:, :], xq_t[:, 1024 * _b:1024 * _b + 1024], ALU.add),
            waits=[("tPE", c_mm), ("qX", c_xq), ("tSC", c_sq_prev)], inc=("tDV", 1))
        ops_free[b] = [("tDV", c_r1)]
        xq_free[b] = [("tDV", c_r1)]
        c_rs1 = P.do("vector", lambda e: e.reduce_sum(
            rt[:, 0:1], res1[:, :], axis=AX.X),
            waits=[("tSC", c_r_prev)], inc=("tDV", 1))
        wub = list(ubf_free[b])
        P.do("scalar", lambda e, _b=b: e.activation(
            u_bf[:, 1024 * _b:1024 * _b + 1024], res1[:, :], AF.Square,
            accum_out=rt[:, 1:2]), waits=[("tDV", c_r1)] + wub)
        c_sq = P.do("scalar", lambda e: e.copy(rt[:, 14:15], rt[:, 1:2]),
                    inc=("tSC", 1))
        c_sq_prev = c_sq
        d1 = P.do("vector", lambda e: e.tensor_scalar(
            rt[:, 2:3], rt[:, 0:1], 1.0 / D, None, ALU.mult, ALU.bypass),
            waits=[("tDV", c_rs1)], inc=("tDV", 1))
        d2 = P.do("vector", lambda e: e.tensor_tensor(
            rt[:, 3:4], rt[:, 2:3], rt[:, 2:3], ALU.mult),
            waits=[("tDV", d1)], inc=("tDV", 1))
        d3 = P.do("vector", lambda e: e.tensor_scalar(
            rt[:, 4:5], rt[:, 14:15], 1.0 / D, EPS, ALU.mult, ALU.add),
            waits=[("tSC", c_sq)], inc=("tDV", 1))
        d4 = P.do("vector", lambda e: e.tensor_tensor(
            rt[:, 5:6], rt[:, 4:5], rt[:, 3:4], ALU.subtract),
            waits=[("tDV", d3)], inc=("tDV", 1))
        c_vr = P.do("vector", lambda e: e.reciprocal(rt[:, 6:7], rt[:, 5:6]),
                    waits=[("tDV", d4)], inc=("tDV", 1))
        c_r = P.do("scalar", lambda e: e.activation(rt[:, 7:8], rt[:, 6:7], AF.Sqrt),
                   waits=[("tDV", c_vr)], inc=("tSC", 1))
        c_r_prev = c_r
        d5 = P.do("vector", lambda e: e.tensor_scalar(
            rt[:, 8:9], rt[:, 2:3], rt[:, 7:8], -1.0, ALU.mult, ALU.mult),
            waits=[("tSC", c_r)], inc=("tDV", 1))
        d6 = P.do("vector", lambda e: e.tensor_scalar(
            u_t[:, :], res1[:, :], rt[:, 7:8], rt[:, 8:9], ALU.mult, ALU.add),
            waits=[("tDV", d5)], inc=("tDV", 1))
        P.do("vector", lambda e, _qt=qt: e.tensor_tensor(
            facc[:, 1024 * _qt:1024 * _qt + 1024], res1[:, :], u_t[:, :], ALU.add),
            waits=[("tDV", d6)])
        c_ub = P.do("vector", lambda e, _b=b: e.tensor_copy(
            u_bf[:, 1024 * _b:1024 * _b + 1024], u_t[:, :]), inc=("tDV", 1))
        c_ev = None
        for dc in range(8):
            pb = dc % 2
            c_tp = P.do("tensor", lambda e, _b=b, _dc=dc, _pb=pb: e.transpose(
                tp2_ps[_pb][:, :], u_bf[:, 1024 * _b + 128 * _dc:1024 * _b + 128 * _dc + 128],
                ident_b[:]), waits=[("tDV", c_ub), ("tSC", P.cnt["tSC"])],
                inc=("tPE", 1))
            c_ev = P.do("scalar", lambda e, _dc=dc, _pb=pb, _qt=qt: e.activation(
                u_fm[:, NQ * _dc + 128 * _qt:NQ * _dc + 128 * _qt + 128],
                tp2_ps[_pb][:, :], AF.Copy), waits=[("tPE", c_tp)], inc=("tSC", 1))
        ubf_free[b] = [("tSC", c_ev)]
    c_o1_sc = P.cnt["tSC"]

    c_scev = {}
    for c in range(4):
        c_mm = None
        for dc in range(8):
            w0 = [("qW", c_cent), ("tSC", P.cnt["tSC"])]
            if c > 0:
                w0.append(("tDV", c_scev[c - 1]))
            c_mm = P.do("tensor", lambda e, _c=c, _dc=dc: e.matmul(
                sc_ps[:, :], cent_b[:, 16 * _dc:16 * _dc + 16],
                u_fm[:, NQ * _dc + 512 * _c:NQ * _dc + 512 * _c + 512],
                start=(_dc == 0), stop=(_dc == 7), skip_group_check=True),
                waits=w0 if dc == 0 else [], inc=("tPE", 1))
        c_scev[c] = P.do("vector", lambda e, _c=c: e.tensor_scalar(
            sc_em[:, 512 * _c:512 * _c + 512], sc_ps[:, :], eb_s[:, 0:1], None,
            ALU.add, ALU.bypass), waits=[("tPE", c_mm), ("qW", c_eb)], inc=("tDV", 1))
    c_es_prev = 0
    c_m1_prev = 0
    c_gt_prev = 0
    for qt in range(NQT):
        c_tr = P.do("tensor", lambda e, _qt=qt: e.transpose(
            misc_ps[:, :], sc_em[0:16, 128 * _qt:128 * _qt + 128], ident_f[0:16, 0:16]),
            waits=[("tDV", c_scev[qt // 4]), ("qX", c_idf),
                   ("tSC", c_es_prev), ("tDV", c_m1_prev)], inc=("tPE", 1))
        c_m1 = P.do("vector", lambda e: e.reduce_max(
            rt[:, 9:10], misc_ps[:, :], axis=AX.X, negate=True),
            waits=[("tPE", c_tr)], inc=("tDV", 1))
        c_m1_prev = c_m1
        c_es = P.do("scalar", lambda e: e.activation(
            rt[:, 32:48], misc_ps[:, :], AF.Exp, bias=rt[:, 9:10], scale=1.0),
            waits=[("tDV", c_m1)], inc=("tSC", 1))
        c_es_prev = c_es
        c_rsm = P.do("vector", lambda e: e.reduce_sum(
            rt[:, 10:11], rt[:, 32:48], axis=AX.X), waits=[("tSC", c_es)],
            inc=("tDV", 1))
        e1 = P.do("vector", lambda e: e.reciprocal(rt[:, 11:12], rt[:, 10:11]),
             waits=[("tDV", c_rsm)], inc=("tDV", 1))
        e2 = P.do("vector", lambda e: e.tensor_scalar(
            rt[:, 48:64], rt[:, 32:48], rt[:, 11:12], None, ALU.mult, ALU.bypass),
            waits=[("tDV", e1)], inc=("tDV", 1))
        c_rm1 = P.do("vector", lambda e: e.reduce_max(
            rt[:, 12:13], rt[:, 48:64], axis=AX.X),
            waits=[("tDV", e2)], inc=("tDV", 1))
        e3 = P.do("vector", lambda e: e.tensor_scalar(
            rt[:, 64:80], rt[:, 48:64], rt[:, 12:13], None, ALU.is_ge, ALU.bypass),
            waits=[("tDV", c_rm1)], inc=("tDV", 1))
        e4 = P.do("vector", lambda e: e.scalar_tensor_tensor(
            rt[:, 16:32], rt[:, 64:80], -1e4, rt[:, 48:64], ALU.mult, ALU.add),
            waits=[("tDV", e3)], inc=("tDV", 1))
        c_rm2 = P.do("vector", lambda e: e.reduce_max(
            rt[:, 13:14], rt[:, 16:32], axis=AX.X),
            waits=[("tDV", e4)], inc=("tDV", 1))
        e5 = P.do("vector", lambda e: e.tensor_scalar(
            rt[:, 64:80], rt[:, 48:64], rt[:, 13:14], None, ALU.is_ge, ALU.bypass),
            waits=[("tDV", c_rm2)], inc=("tDV", 1))
        c_g = P.do("vector", lambda e, _qt=qt: e.tensor_tensor(
            g_all[:, 16 * _qt:16 * _qt + 16], rt[:, 48:64], rt[:, 64:80], ALU.mult),
            waits=[("tDV", e5)], inc=("tDV", 1))
        c_gg = P.do("vector", lambda e, _qt=qt: e.tensor_scalar(
            gg_t[:, :], g_all[:, 16 * _qt:16 * _qt + 16], 2.0, None,
            ALU.mult, ALU.bypass), waits=[("tDV", c_g), ("tPE", c_gt_prev)],
            inc=("tDV", 1))
        c_gt = P.do("tensor", lambda e: e.transpose(
            tp2_ps[0][0:16, :], gg_t[:, :], ident_b[:]),
            waits=[("tDV", c_gg), ("tSC", P.cnt["tSC"])], inc=("tPE", 1))
        c_gt_prev = c_gt
        P.do("scalar", lambda e, _qt=qt: e.activation(
            g2t[0:16, 128 * _qt:128 * _qt + 128], tp2_ps[0][0:16, :], AF.Copy),
            waits=[("tPE", c_gt)], inc=("tSC", 1))
    c_o_done_sc = P.cnt["tSC"]
    c_o_done_dv = P.cnt["tDV"]

    if stop_after == 'o':
        P.do("sync", lambda e: e.dma_start(
            bass.AP(out_d, 0, [[1024, 128], [131072, 16], [1, 1024]]),
            bass.AP(facc, 0, [[16384, 128], [1024, 16], [1, 1024]])),
            waits=[("tDV", P.cnt["tDV"]), ("tSC", P.cnt["tSC"])], inc=("qX", 16))
        P.do("gpsimd", lambda e: e.dma_start(
            bass.AP(av_d, 0, [[256, 128], [1, 256]]),
            bass.AP(g_all, 0, [[256, 128], [1, 256]])),
            waits=[("tDV", P.cnt["tDV"])], inc=("qW", 16))
        P.do("sync", lambda e: e.dma_start(
            bass.AP(av_d, 128 * 256, [[2048, 16], [1, 2048]]),
            bass.AP(g2t, 0, [[2048, 16], [1, 2048]])), inc=("qX", 16))
        P.do("sync", lambda e: e.dma_start(
            bass.AP(av_d, 2 * 128 * 256, [[2048, 128], [1, 2048]]),
            u_fm[:, 0:2048]), inc=("qX", 16))
        P.flush()
        return nc
    P.flush()
    oex.close()

    P.barrier()
    mex = ExitStack()
    wr_b = mex.enter_context(nc.sbuf_tensor("wr_b", [128, 2 * 8192], BF16))
    ws_tmp = mex.enter_context(nc.sbuf_tensor("ws_tmp", [128, 8192], BF16))
    e_ps = [mex.enter_context(nc.psum_tensor(f"e_ps{i}", [128, 1024], F32)) for i in range(3)]

    c_ws0 = P.do("gpsimd", wcast_load(wr_b, 0, wst, 0), inc=("qW", 16))
    c_ws1 = P.do("gpsimd", wcast_load(ws_tmp, 0, wst, D * D), inc=("qW", 16))
    c_wsum = P.do("vector", lambda e: e.tensor_tensor(
        wr_b[:, 0:8192], wr_b[:, 0:8192], ws_tmp[:, 0:8192], ALU.add),
        waits=[("qW", c_ws1)], inc=("tDV", 1))
    w_ready = {0: ("tDV", c_wsum)}
    wbuf_last_mm = {0: 0, 1: 0}
    mm_done = {}
    stt_done = {}
    psum_guard = {0: None, 1: None, 2: None}
    for t in range(NE + 1):
        if t + 1 <= NE:
            ee = t
            half = (t + 1) % 2
            w_ready[t + 1] = ("qW", P.do(
                "gpsimd", wcast_load(wr_b, 8192 * half, wrt, ee * D * D),
                waits=[("tPE", wbuf_last_mm[half])] if wbuf_last_mm[half] else [],
                inc=("qW", 16)))
        half = t % 2
        for qt in range(NQT):
            r = (t * NQT + qt) % 3
            w0 = [w_ready[t]]
            if psum_guard[r] is not None:
                w0.append(psum_guard[r])
            c_mm = None
            for dc in range(8):
                for nh in range(2):
                    c_mm = P.do("tensor", lambda e, _t=t, _qt=qt, _dc=dc, _nh=nh,
                                _r=r, _hf=half: e.matmul(
                        e_ps[_r][:, 512 * _nh:512 * _nh + 512],
                        u_fm[:, NQ * _dc + 128 * _qt:NQ * _dc + 128 * _qt + 128],
                        wr_b[:, 8192 * _hf + 1024 * _dc + 512 * _nh:
                             8192 * _hf + 1024 * _dc + 512 * _nh + 512],
                        start=(_dc == 0), stop=(_dc == 7 and _t > 0),
                        skip_group_check=True),
                        waits=w0 if (dc == 0 and nh == 0) else [], inc=("tPE", 1))
            if t == 0:
                for nh in range(2):
                    P.do("tensor", lambda e, _qt=qt, _nh=nh, _r=r: e.matmul(
                        e_ps[_r][:, 512 * _nh:512 * _nh + 512],
                        g2t[0:16, 128 * _qt:128 * _qt + 128],
                        br_b[0:16, 512 * _nh:512 * _nh + 512],
                        start=False, stop=False, skip_group_check=True),
                        waits=[("tSC", c_o_done_sc), ("qW", c_br), ("qW", c_bs)] if (qt == 0 and nh == 0) else [],
                        inc=("tPE", 1))
                    c_mm = P.do("tensor", lambda e, _nh=nh, _r=r: e.matmul(
                        e_ps[_r][:, 512 * _nh:512 * _nh + 512],
                        ones2[:, 0:128],
                        bs_b[:, 512 * _nh:512 * _nh + 512],
                        start=False, stop=True, skip_group_check=True),
                        inc=("tPE", 1))
            mm_done[(t, qt)] = c_mm
            wbuf_last_mm[half] = c_mm
            c_stt = None
            w_prev_stt = [("tDV", stt_done[(t - 1, qt)])] if t > 0 else [("tDV", c_o_done_dv)]
            for nh in range(2):
                if t == 0:
                    c_stt = P.do("vector", lambda e, _qt=qt, _nh=nh, _r=r:
                                 e.scalar_tensor_tensor(
                                     facc[:, 1024 * _qt + 512 * _nh:1024 * _qt + 512 * _nh + 512],
                                     e_ps[_r][:, 512 * _nh:512 * _nh + 512], 0.5,
                                     facc[:, 1024 * _qt + 512 * _nh:1024 * _qt + 512 * _nh + 512],
                                     ALU.mult, ALU.add),
                                 waits=([("tPE", c_mm)] + w_prev_stt) if nh == 0 else [],
                                 inc=("tDV", 1))
                else:
                    c_stt = P.do("vector", lambda e, _t=t, _qt=qt, _nh=nh, _r=r:
                                 e.scalar_tensor_tensor(
                                     facc[:, 1024 * _qt + 512 * _nh:1024 * _qt + 512 * _nh + 512],
                                     e_ps[_r][:, 512 * _nh:512 * _nh + 512],
                                     g_all[:, 16 * _qt + (_t - 1):16 * _qt + _t],
                                     facc[:, 1024 * _qt + 512 * _nh:1024 * _qt + 512 * _nh + 512],
                                     ALU.mult, ALU.add),
                                 waits=([("tPE", c_mm)] + w_prev_stt) if nh == 0 else [],
                                 inc=("tDV", 1))
            stt_done[(t, qt)] = c_stt
            psum_guard[r] = ("tDV", c_stt)

    c_store = None
    for qt in range(NQT):
        c_store = P.do("sync", lambda e, _qt=qt: e.dma_start(
            out_d[128 * _qt:128 * _qt + 128, :], facc[:, 1024 * _qt:1024 * _qt + 1024]),
            waits=[("tDV", stt_done[(NE, qt)])], inc=("qX", 16))
    P.do("sync", lambda e: (e.wait_ge(sems["qX"], c_store), None)[1])
    P.flush()
    mex.close()
    om.close()
    es.close()
    return nc


_NC_CACHE = None


def _get_nc():
    global _NC_CACHE
    if _NC_CACHE is None:
        _NC_CACHE = build_nc()
    return _NC_CACHE


def kernel(L0_attn_x, L0_attn_Wq, L0_attn_Wk, L0_attn_Wv, L0_attn_Wo,
           L0_ffn_centroids, L0_ffn_expert_biases, L0_ffn_Wr, L0_ffn_br,
           L0_ffn_Ws, L0_ffn_bs):
    x = np.ascontiguousarray(np.asarray(L0_attn_x, dtype=np.float32))
    wq = np.ascontiguousarray(np.asarray(L0_attn_Wq, dtype=np.float32))
    wk = np.ascontiguousarray(np.asarray(L0_attn_Wk, dtype=np.float32))
    wv = np.ascontiguousarray(np.asarray(L0_attn_Wv, dtype=np.float32))
    wo = np.ascontiguousarray(np.asarray(L0_attn_Wo, dtype=np.float32))
    wrt = np.ascontiguousarray(np.asarray(L0_ffn_Wr, dtype=np.float32).transpose(0, 2, 1))
    wst = np.ascontiguousarray(np.asarray(L0_ffn_Ws, dtype=np.float32).transpose(0, 2, 1))
    cent_t = np.ascontiguousarray(np.asarray(L0_ffn_centroids, dtype=np.float32).T)
    ebv = np.ascontiguousarray(np.asarray(L0_ffn_expert_biases, dtype=np.float32).reshape(16, 1))
    brv = np.ascontiguousarray(np.asarray(L0_ffn_br, dtype=np.float32))
    bsv = np.ascontiguousarray(np.asarray(L0_ffn_bs, dtype=np.float32))
    ident_b = np.eye(128, dtype=BF)
    ident_f = np.eye(128, dtype=np.float32)

    perms = []
    for par in range(2):
        idx = np.concatenate([
            np.arange(128 * (31 - k) + 64 * par, 128 * (31 - k) + 64 * par + 64)
            for k in range(32)])
        perms.append(idx)
    masks = []
    for par in range(2):
        jj = np.arange(128)[:, None]
        ii = np.arange(64)[None, :]
        masks.append((jj <= 64 * par + ii).astype(BF))

    in_maps = []
    for c in range(8):
        b, par = c // 2, c % 2
        in_maps.append({
            "x_ctx": x[b],
            "x_q": np.ascontiguousarray(x[b][perms[par]]),
            "wq": wq, "wk": wk, "wv": wv, "wo": wo,
            "wrt": wrt, "wst": wst, "cent_t": cent_t, "eb": ebv,
            "br": brv, "bs": bsv, "mask": masks[par],
            "ident_b": ident_b, "ident_f": ident_f,
        })

    nc = _get_nc()
    res = run_bass_kernel_spmd(nc, in_maps, core_ids=list(range(8)))
    out = np.empty((4, 4096, 1024), dtype=np.float32)
    for c in range(8):
        b, par = c // 2, c % 2
        out[b][perms[par]] = res.results[c]["out"]
    return out


# version 03
# speedup vs baseline: 1.3803x; 1.3803x over previous
import numpy as np
import ml_dtypes
from contextlib import ExitStack

import concourse.bass as bass
import concourse.mybir as mybir
from concourse.bass_utils import run_bass_kernel_spmd

F32 = mybir.dt.float32
BF16 = mybir.dt.bfloat16
AF = mybir.ActivationFunctionType
ALU = mybir.AluOpType
AX = mybir.AxisListType
BF = ml_dtypes.bfloat16

D = 1024
NCTX = 4096
NQ = 2048
NKVT = 32
NQT = 16
HP = 8
SCALE = 0.125
EPS = 1e-5
NE = 16

ENGS = ("tensor", "vector", "scalar", "gpsimd", "sync")


class Prog:

    def __init__(self, nc, sems):
        self.nc = nc
        self.sems = sems
        self.cnt = {k: 0 for k in sems}
        self.q = {e: [] for e in ENGS}
        self.lastw = {e: {} for e in ENGS}

    def do(self, eng, fn, waits=(), inc=None):
        real_waits = []
        lw = self.lastw[eng]
        for s, v in waits:
            if s is None or v <= 0:
                continue
            if lw.get(s, -1) < v:
                real_waits.append((s, v))
                lw[s] = v
        if inc is not None:
            self.cnt[inc[0]] += inc[1]
        self.q[eng].append((fn, tuple(real_waits), inc))
        return self.cnt[inc[0]] if inc is not None else None

    def barrier(self):
        self.cnt["bar"] += 5
        tgt = self.cnt["bar"]
        for e in ENGS:
            def fn(eng, _t=tgt):
                eng.sem_inc(self.sems["bar"], 1)
                eng.wait_ge(self.sems["bar"], _t)
                return None
            self.q[e].append((fn, (), None))
            self.lastw[e]["bar"] = tgt

    def flush(self):
        with self.nc.Block() as blk:
            for e in ENGS:
                ops = self.q[e]
                self.q[e] = []
                if not ops:
                    continue

                def body(eng, _ops=tuple(ops), _sems=self.sems):
                    for fn, waits, inc in _ops:
                        for s, v in waits:
                            eng.wait_ge(_sems[s], v)
                        r = fn(eng)
                        if inc is not None and r is not None:
                            r.then_inc(_sems[inc[0]], inc[1])
                getattr(blk, e)(body)


def build_nc(stop_after='all'):
    nc = bass.Bass("TRN2", target_bir_lowering=False, detect_race_conditions=False)

    x_ctx = nc.dram_tensor("x_ctx", [NCTX, D], F32, kind="ExternalInput")
    x_q = nc.dram_tensor("x_q", [NQ, D], F32, kind="ExternalInput")
    wq = nc.dram_tensor("wq", [D, D], F32, kind="ExternalInput")
    wk = nc.dram_tensor("wk", [D, D], F32, kind="ExternalInput")
    wv = nc.dram_tensor("wv", [D, D], F32, kind="ExternalInput")
    wo = nc.dram_tensor("wo", [D, D], F32, kind="ExternalInput")
    wrt = nc.dram_tensor("wrt", [NE, D, D], F32, kind="ExternalInput")
    wst = nc.dram_tensor("wst", [2, D, D], F32, kind="ExternalInput")
    cent_t = nc.dram_tensor("cent_t", [D, 16], F32, kind="ExternalInput")
    eb = nc.dram_tensor("eb", [16, 1], F32, kind="ExternalInput")
    br = nc.dram_tensor("br", [16, D], F32, kind="ExternalInput")
    bs = nc.dram_tensor("bs", [2, D], F32, kind="ExternalInput")
    mask_in = nc.dram_tensor("mask", [128, 64], BF16, kind="ExternalInput")
    ident_b_in = nc.dram_tensor("ident_b", [128, 128], BF16, kind="ExternalInput")
    ident_f_in = nc.dram_tensor("ident_f", [128, 128], F32, kind="ExternalInput")
    out_d = nc.dram_tensor("out", [NQ, D], F32, kind="ExternalOutput")
    av_d = nc.dram_tensor("av_spill", [HP, 128, NQ], BF16, kind="ExternalOutput")

    es = ExitStack()
    sems = {}
    for s in ("qW", "qX", "tPE", "tSC", "tDV", "tGP", "bar"):
        sems[s] = es.enter_context(nc.semaphore(s))
    P = Prog(nc, sems)

    ident_b = es.enter_context(nc.sbuf_tensor("ident_b_s", [128, 128], BF16))
    mask_s = es.enter_context(nc.sbuf_tensor("mask_s", [128, 64], BF16))
    ones1 = es.enter_context(nc.sbuf_tensor("ones1", [128, 1], BF16))
    ones2 = es.enter_context(nc.sbuf_tensor("ones2", [2, 128], BF16))
    negb8 = es.enter_context(nc.sbuf_tensor("negb8", [128, 8], F32))
    stats = es.enter_context(nc.sbuf_tensor("stats", [128, 24], F32))

    c_idb = P.do("sync", lambda e: e.dma_start(ident_b[:], ident_b_in[:, :]), inc=("qX", 16))
    c_msk = P.do("sync", lambda e: e.dma_start(mask_s[:], mask_in[:, :]), inc=("qX", 16))
    P.do("gpsimd", lambda e: e.memset(ones1[:], 1.0))
    c_ones = P.do("gpsimd", lambda e: e.memset(ones2[:], 1.0), inc=("tGP", 1))

    def wcast_load(dst, dst_col, src, e_off):
        dst_stride = dst.shape[1]

        def fn(eng):
            return eng.dma_start(
                bass.AP(dst, dst_col, [[dst_stride, 128], [1024, 8], [1, 1024]]),
                bass.AP(src, e_off, [[1024, 128], [131072, 8], [1, 1024]]),
            )
        return fn

    kvq = ExitStack()
    k_all = kvq.enter_context(nc.sbuf_tensor("k_all", [128, HP * NCTX], BF16))
    v_all = kvq.enter_context(nc.sbuf_tensor("v_all", [128, NKVT * D], BF16))
    q_all = kvq.enter_context(nc.sbuf_tensor("q_all", [128, HP * NQ], BF16))
    xt = kvq.enter_context(nc.sbuf_tensor("xt", [128, 2 * 1024], F32))
    xlt = kvq.enter_context(nc.sbuf_tensor("xlt", [128, 1024], BF16))
    xlf = kvq.enter_context(nc.sbuf_tensor("xlf", [128, 8 * 256], BF16))

    lnst = {
        "xt_free": [[], []],
        "xlt_free": [],
        "cr_prev": [0, 0],
        "g": 0,
    }

    def ln_tile(src_dram, row0, dst_fm_col, tp_ps, qx_extra=()):
        g = lnst["g"]
        lnst["g"] += 1
        b = g % 2
        sb = b * 12
        c_x = P.do("sync", lambda e, _b=b, _r=row0, _s=src_dram: e.dma_start(
            xt[:, 1024 * _b:1024 * _b + 1024], _s[_r:_r + 128, :]),
            waits=list(lnst["xt_free"][b]) + list(qx_extra), inc=("qX", 16))
        xv = xt[:, 1024 * b:1024 * b + 1024]
        c_rs = P.do("vector", lambda e, _x=xv, _c=sb: e.reduce_sum(
            stats[:, _c:_c + 1], _x, axis=AX.X),
            waits=[("qX", c_x), ("tSC", lnst["cr_prev"][b])], inc=("tDV", 1))
        P.do("scalar", lambda e, _x=xv, _c=sb: e.activation(
            xlt[:, :], _x, AF.Square,
            accum_out=stats[:, _c + 1:_c + 2]),
            waits=[("qX", c_x), ("tSC", lnst["cr_prev"][b])] + list(lnst["xlt_free"]))
        c_sq = P.do("scalar", lambda e, _c=sb: e.copy(
            stats[:, _c + 9:_c + 10], stats[:, _c + 1:_c + 2]), inc=("tSC", 1))
        c1 = P.do("vector", lambda e, _c=sb: e.tensor_scalar(
            stats[:, _c + 2:_c + 3], stats[:, _c:_c + 1], 1.0 / D, None,
            ALU.mult, ALU.bypass), waits=[("tDV", c_rs)], inc=("tDV", 1))
        c2 = P.do("vector", lambda e, _c=sb: e.tensor_tensor(
            stats[:, _c + 3:_c + 4], stats[:, _c + 2:_c + 3], stats[:, _c + 2:_c + 3],
            ALU.mult), waits=[("tDV", c1)], inc=("tDV", 1))
        c3 = P.do("vector", lambda e, _c=sb: e.tensor_scalar(
            stats[:, _c + 4:_c + 5], stats[:, _c + 9:_c + 10], 1.0 / D, EPS,
            ALU.mult, ALU.add), waits=[("tSC", c_sq)], inc=("tDV", 1))
        c4 = P.do("vector", lambda e, _c=sb: e.tensor_tensor(
            stats[:, _c + 5:_c + 6], stats[:, _c + 4:_c + 5], stats[:, _c + 3:_c + 4],
            ALU.subtract), waits=[("tDV", c3)], inc=("tDV", 1))
        c_vr = P.do("vector", lambda e, _c=sb: e.reciprocal(
            stats[:, _c + 6:_c + 7], stats[:, _c + 5:_c + 6]),
            waits=[("tDV", c4)], inc=("tDV", 1))
        c_r = P.do("scalar", lambda e, _c=sb: e.activation(
            stats[:, _c + 7:_c + 8], stats[:, _c + 6:_c + 7], AF.Sqrt),
            waits=[("tDV", c_vr)], inc=("tSC", 1))
        lnst["cr_prev"][b] = c_r
        c5 = P.do("vector", lambda e, _c=sb: e.tensor_scalar(
            stats[:, _c + 8:_c + 9], stats[:, _c + 2:_c + 3], stats[:, _c + 7:_c + 8],
            -1.0, ALU.mult, ALU.mult), waits=[("tSC", c_r)], inc=("tDV", 1))
        c_ap = P.do("vector", lambda e, _x=xv, _c=sb: e.tensor_scalar(
            xlt[:, :], _x, stats[:, _c + 7:_c + 8],
            stats[:, _c + 8:_c + 9], ALU.mult, ALU.add),
            waits=[("tDV", c5)], inc=("tDV", 1))
        lnst["xt_free"][b] = [("tDV", c_ap), ("tSC", c_sq)]
        c_ev = None
        c_tp = None
        for dc in range(8):
            pb = dc % 2
            c_tp = P.do("tensor", lambda e, _dc=dc, _pb=pb, _tp=tp_ps: e.transpose(
                _tp[_pb][:, :], xlt[:, 128 * _dc:128 * _dc + 128],
                ident_b[:]),
                waits=[("tDV", c_ap), ("tSC", P.cnt["tSC"]), ("qX", c_idb)],
                inc=("tPE", 1))
            c_ev = P.do("scalar", lambda e, _dc=dc, _pb=pb, _col=dst_fm_col, _tp=tp_ps:
                        e.activation(
                            xlf[:, _col + 256 * _dc:_col + 256 * _dc + 128],
                            _tp[_pb][:, :], AF.Copy),
                        waits=[("tPE", c_tp)], inc=("tSC", 1))
        lnst["xlt_free"] = [("tPE", c_tp)]
        return c_ev

    p0 = ExitStack()
    wk_b = p0.enter_context(nc.sbuf_tensor("wk_b", [128, 8192], BF16))
    wv_b = p0.enter_context(nc.sbuf_tensor("wv_b", [128, 8192], BF16))
    tp_ps_a = [p0.enter_context(nc.psum_tensor(f"tp_ps{i}", [128, 128], BF16)) for i in range(2)]
    kq_ps_a = [p0.enter_context(nc.psum_tensor(f"kq_ps{i}", [128, 256], F32)) for i in range(2)]
    v_ps = [p0.enter_context(nc.psum_tensor(f"v_ps{i}", [128, 1024], F32)) for i in range(2)]

    c_wk = P.do("gpsimd", wcast_load(wk_b, 0, wk, 0), inc=("qW", 16))
    c_wv = P.do("gpsimd", wcast_load(wv_b, 0, wv, 0), inc=("qW", 16))

    kq_free = [[], []]
    v_free = [[], []]
    for kb in range(NCTX // 256):
        c_ev = 0
        for t2 in range(2):
            c_ev = ln_tile(x_ctx, 256 * kb + 128 * t2, 128 * t2, tp_ps_a)
            if stop_after == 'ln0' and kb == 0 and t2 == 0:
                P.do("sync", lambda e: e.dma_start(out_d[0:128, 0:24], stats[:, :]),
                     waits=[("tSC", P.cnt["tSC"]), ("tDV", P.cnt["tDV"])], inc=("qX", 16))
                P.do("sync", lambda e: e.dma_start(
                    bass.AP(av_d, 0, [[2048, 128], [1, 1024]]), xlt[:, :]), inc=("qX", 16))
                P.do("sync", lambda e: e.dma_start(
                    bass.AP(av_d, 128 * 2048, [[2048, 128], [1, 2048]]), xlf[:, :]), inc=("qX", 16))
                P.flush()
                return nc
        for hp in range(HP):
            pb = hp % 2
            w0 = [("tSC", c_ev), ("qW", c_wk)] + kq_free[pb]
            c_mm = None
            for dc in range(8):
                c_mm = P.do("tensor", lambda e, _hp=hp, _dc=dc, _pb=pb: e.matmul(
                    kq_ps_a[_pb][:, :],
                    wk_b[:, 1024 * _dc + 128 * _hp:1024 * _dc + 128 * _hp + 128],
                    xlf[:, 256 * _dc:256 * _dc + 256],
                    start=(_dc == 0), stop=(_dc == 7), skip_group_check=True),
                    waits=w0 if dc == 0 else [], inc=("tPE", 1))
            c_kev = P.do("scalar", lambda e, _hp=hp, _kb=kb, _pb=pb: e.activation(
                k_all[:, NCTX * _hp + 256 * _kb:NCTX * _hp + 256 * _kb + 256],
                kq_ps_a[_pb][:, :], AF.Copy), waits=[("tPE", c_mm)], inc=("tSC", 1))
            kq_free[pb] = [("tSC", c_kev)]
        for t2 in range(2):
            j = 2 * kb + t2
            pb = j % 2
            w0 = [("tSC", c_ev), ("qW", c_wv)] + v_free[pb]
            c_mm = None
            for dc in range(8):
                for nh in range(2):
                    c_mm = P.do("tensor", lambda e, _t2=t2, _dc=dc, _nh=nh, _pb=pb: e.matmul(
                        v_ps[_pb][:, 512 * _nh:512 * _nh + 512],
                        xlf[:, 256 * _dc + 128 * _t2:256 * _dc + 128 * _t2 + 128],
                        wv_b[:, 1024 * _dc + 512 * _nh:1024 * _dc + 512 * _nh + 512],
                        start=(_dc == 0), stop=(_dc == 7), skip_group_check=True),
                        waits=w0 if (dc == 0 and nh == 0) else [], inc=("tPE", 1))
            c_vev = P.do("scalar", lambda e, _j=j, _pb=pb: e.activation(
                v_all[:, 1024 * _j:1024 * _j + 1024], v_ps[_pb][:, :], AF.Copy),
                waits=[("tPE", c_mm)], inc=("tSC", 1))
            v_free[pb] = [("tSC", c_vev)]
    c_p0a_pe = P.cnt["tPE"]

    P.flush()
    if stop_after == 'p0a':
        return nc
    p0.close()

    p0b = ExitStack()
    wq_b = p0b.enter_context(nc.sbuf_tensor("wq_b", [128, 8192], BF16))
    tp_ps_b = [p0b.enter_context(nc.psum_tensor(f"tpb_ps{i}", [128, 128], BF16)) for i in range(2)]
    kq_ps_b = [p0b.enter_context(nc.psum_tensor(f"kqb_ps{i}", [128, 256], F32)) for i in range(2)]

    c_wq = P.do("gpsimd", wcast_load(wq_b, 0, wq, 0),
                waits=[("tPE", c_p0a_pe)], inc=("qW", 16))
    kq_free = [[], []]
    for qb in range(NQ // 256):
        c_ev = 0
        for t2 in range(2):
            c_ev = ln_tile(x_q, 256 * qb + 128 * t2, 128 * t2, tp_ps_b)
        for hp in range(HP):
            pb = hp % 2
            w0 = [("tSC", c_ev), ("qW", c_wq)] + kq_free[pb]
            c_mm = None
            for dc in range(8):
                c_mm = P.do("tensor", lambda e, _hp=hp, _dc=dc, _pb=pb: e.matmul(
                    kq_ps_b[_pb][:, :],
                    wq_b[:, 1024 * _dc + 128 * _hp:1024 * _dc + 128 * _hp + 128],
                    xlf[:, 256 * _dc:256 * _dc + 256],
                    start=(_dc == 0), stop=(_dc == 7), skip_group_check=True),
                    waits=w0 if dc == 0 else [], inc=("tPE", 1))
            c_qev = P.do("scalar", lambda e, _hp=hp, _qb=qb, _pb=pb: e.activation(
                q_all[:, NQ * _hp + 256 * _qb:NQ * _hp + 256 * _qb + 256],
                kq_ps_b[_pb][:, :], AF.Copy), waits=[("tPE", c_mm)], inc=("tSC", 1))
            kq_free[pb] = [("tSC", c_qev)]
    c_p0b_pe = P.cnt["tPE"]
    c_p0b_sc = P.cnt["tSC"]

    if stop_after == 'p0b':
        P.do("sync", lambda e: e.dma_start(
            bass.AP(av_d, 0, [[2048, 128], [1, 2048]]),
            k_all[:, 0:2048]), waits=[("tSC", P.cnt["tSC"])], inc=("qX", 16))
        P.do("sync", lambda e: e.dma_start(
            bass.AP(av_d, 128 * 2048, [[2048, 128], [1, 2048]]),
            k_all[:, 7 * 4096:7 * 4096 + 2048]), inc=("qX", 16))
        P.do("sync", lambda e: e.dma_start(
            bass.AP(av_d, 2 * 128 * 2048, [[2048, 128], [1, 2048]]),
            q_all[:, 7 * 2048:7 * 2048 + 2048]), inc=("qX", 16))
        P.do("sync", lambda e: e.dma_start(
            bass.AP(av_d, 3 * 128 * 2048, [[2048, 128], [1, 2048]]),
            v_all[:, 0:2048]), inc=("qX", 16))
        P.flush()
        return nc
    P.flush()
    p0b.close()

    aex = ExitStack()
    pt = aex.enter_context(nc.sbuf_tensor("pt", [128, 4096], BF16))
    ksq = pt
    s64 = aex.enter_context(nc.sbuf_tensor("s64", [65, 1024], F32))
    hl64 = aex.enter_context(nc.sbuf_tensor("hl64", [65, 2048], BF16))
    rechl2 = aex.enter_context(nc.sbuf_tensor("rechl2", [2, 1024], BF16))
    av_sb = aex.enter_context(nc.sbuf_tensor("av_sb", [64, 1024], BF16))
    av_spl = aex.enter_context(nc.sbuf_tensor("av_spl", [64, 4096], BF16))
    km = aex.enter_context(nc.sbuf_tensor("km", [1, 16], F32))
    bsc = aex.enter_context(nc.sbuf_tensor("bsc", [1, 4], F32))
    negvb = aex.enter_context(nc.sbuf_tensor("negvb", [1, 1], BF16))
    av_ps = [aex.enter_context(nc.psum_tensor(f"av_ps{i}", [65, 1024], F32)) for i in range(2)]
    d_psA = aex.enter_context(nc.psum_tensor("d_psA", [128, 1024], F32))
    d_psB = aex.enter_context(nc.psum_tensor("d_psB", [128, 1024], F32))

    for hp in range(HP):
        c_k2 = P.do("vector", lambda e, _hp=hp: e.tensor_tensor(
            ksq[:, :], k_all[:, NCTX * _hp:NCTX * _hp + 4096],
            k_all[:, NCTX * _hp:NCTX * _hp + 4096], ALU.mult),
            waits=[("tPE", c_p0b_pe), ("tSC", c_p0b_sc)], inc=("tDV", 1))
        c_rm = 0
        for c in range(8):
            pb = c % 2
            c_n2 = P.do("tensor", lambda e, _c=c, _pb=pb: e.matmul(
                d_psA[0:1, 512 * _pb:512 * _pb + 512], ones1[:],
                ksq[:, 512 * _c:512 * _c + 512],
                start=True, stop=True, skip_group_check=True),
                waits=[("tDV", c_k2), ("tGP", c_ones)] if c < 2 else [("tDV", c_rm)],
                inc=("tPE", 1))
            c_rm = P.do("vector", lambda e, _c=c, _pb=pb: e.reduce_max(
                km[0:1, _c:_c + 1], d_psA[0:1, 512 * _pb:512 * _pb + 512], axis=AX.X),
                waits=[("tPE", c_n2)], inc=("tDV", 1))
        c_km = P.do("vector", lambda e: e.reduce_max(
            bsc[0:1, 0:1], km[0:1, 0:8], axis=AX.X),
            waits=[("tDV", c_rm)], inc=("tDV", 1))
        c_q2 = P.do("vector", lambda e, _hp=hp: e.tensor_tensor(
            ksq[:, 0:2048], q_all[:, NQ * _hp:NQ * _hp + 2048],
            q_all[:, NQ * _hp:NQ * _hp + 2048], ALU.mult),
            waits=[("tPE", P.cnt["tPE"])], inc=("tDV", 1))
        for c in range(4):
            pb = c % 2
            c_n2 = P.do("tensor", lambda e, _c=c, _pb=pb: e.matmul(
                d_psA[0:1, 512 * _pb:512 * _pb + 512], ones1[:],
                ksq[:, 512 * _c:512 * _c + 512],
                start=True, stop=True, skip_group_check=True),
                waits=[("tDV", c_q2)] if c < 2 else [("tDV", c_rm)], inc=("tPE", 1))
            c_rm = P.do("vector", lambda e, _c=c, _pb=pb: e.reduce_max(
                km[0:1, 8 + _c:9 + _c], d_psA[0:1, 512 * _pb:512 * _pb + 512], axis=AX.X),
                waits=[("tPE", c_n2)], inc=("tDV", 1))
        c_qm = P.do("vector", lambda e: e.reduce_max(
            bsc[0:1, 1:2], km[0:1, 8:12], axis=AX.X),
            waits=[("tDV", c_rm)], inc=("tDV", 1))
        c_pr = P.do("vector", lambda e: e.tensor_tensor(
            bsc[0:1, 2:3], bsc[0:1, 0:1], bsc[0:1, 1:2], ALU.mult),
            waits=[("tDV", c_qm)], inc=("tDV", 1))
        c_sq2 = P.do("scalar", lambda e: e.activation(
            bsc[0:1, 3:4], bsc[0:1, 2:3], AF.Sqrt), waits=[("tDV", c_pr)], inc=("tSC", 1))
        c_nv = P.do("vector", lambda e: e.tensor_scalar(
            negvb[0:1, 0:1], bsc[0:1, 3:4], -SCALE, None, ALU.mult, ALU.bypass),
            waits=[("tSC", c_sq2)], inc=("tDV", 1))
        c_bc = P.do("tensor", lambda e: e.matmul(
            d_psA[:, 0:1], ones2[0:1, :], negvb[0:1, 0:1], start=True, stop=True,
            skip_group_check=True), waits=[("tDV", c_nv)], inc=("tPE", 1))
        P.do("vector", lambda e, _hp=hp: e.tensor_copy(
            negb8[:, _hp:_hp + 1], d_psA[:, 0:1]), waits=[("tPE", c_bc)],
            inc=("tDV", 1))
    c_bounds_dv = P.cnt["tDV"]
    c_bounds_sc = P.cnt["tSC"]

    slot_list = []
    for hp in range(HP):
        for h in range(2):
            jmax = 32 if h == 0 else 16
            for j in range(jmax):
                w = min(1024, 64 * (32 - j)) if h == 0 else 64 * (16 - j)
                masked = (h == 0 and j >= 16) or h == 1
                jlast0 = 31 if h == 0 else 15
                jlast1 = (31 - 8) if h == 0 else (15 - 8)
                slot_list.append((hp, h, j, w, masked,
                                  j == 0, (j == jlast0, j == jlast1),
                                  j == jmax - 1))
    exp_done = {}
    av_done = {}
    c_spill = {}
    dps_guard = [("tDV", c_bounds_dv)]

    def emit_av(s):
        hp, h, j, w, masked, st, sps, last = slot_list[s]
        p = s % 2
        wa = [("tDV", mask_done[s])] if s in mask_done else [("tSC", exp_done[s])]
        vcol = 1024 * j + 128 * hp
        r = None
        for c0 in range(0, w, 512):
            cw = min(512, w - c0)
            sp = sps[c0 // 512]
            for hd in range(2):
                ptc = 2048 * hd + 1024 * p
                P.do("tensor", lambda e, _hd=hd, _v=vcol, _c0=c0, _cw=cw, _st=st,
                     _sp=sp, _pc=ptc: e.matmul(
                         av_ps[_hd][0:64, _c0:_c0 + _cw],
                         v_all[:, _v + 64 * _hd:_v + 64 * _hd + 64],
                         pt[:, _pc + _c0:_pc + _c0 + _cw], start=_st, stop=_sp,
                         tile_position=(0, 0), skip_group_check=True),
                     waits=wa if r is None else [], inc=("tPE", 1))
                r = P.do("tensor", lambda e, _hd=hd, _c0=c0, _cw=cw, _st=st,
                         _sp=sp, _pc=ptc: e.matmul(
                             av_ps[_hd][64:65, _c0:_c0 + _cw], ones1[:],
                             pt[:, _pc + _c0:_pc + _c0 + _cw], start=_st, stop=_sp,
                             tile_position=(0, 64), skip_group_check=True),
                         inc=("tPE", 1))
        av_done[s] = r

    mask_done = {}
    ns = len(slot_list)
    for s in range(ns):
        hp, h, j, w, masked, st, sps, last = slot_list[s]
        p = s % 2
        kcol = NCTX * hp + 128 * j
        qcol = NQ * hp + 1024 * h
        wt = list(dps_guard)
        dps_guard = []
        if s >= 1:
            wt.append(("tSC", exp_done[s - 1]))
        for c0 in range(0, w, 512):
            cw = min(512, w - c0)
            P.do("tensor", lambda e, _k=kcol, _q=qcol + c0, _c0=c0, _cw=cw: e.matmul(
                d_psA[:, _c0:_c0 + _cw], k_all[0:64, _k:_k + 128],
                q_all[0:64, _q:_q + _cw], start=True, stop=True,
                tile_position=(0, 0), skip_group_check=True),
                waits=wt if c0 == 0 else [], inc=("tPE", 1))
            c_dots = P.do("tensor", lambda e, _k=kcol, _q=qcol + c0, _c0=c0, _cw=cw: e.matmul(
                d_psB[:, _c0:_c0 + _cw], k_all[64:128, _k:_k + 128],
                q_all[64:128, _q:_q + _cw], start=True, stop=True,
                tile_position=(64, 0), skip_group_check=True), inc=("tPE", 1))
        if s >= 1:
            emit_av(s - 1)
        ws = [("tPE", c_dots), ("tDV", c_bounds_dv)]
        if s >= 2:
            ws.append(("tPE", av_done[s - 2]))
        P.do("scalar", lambda e, _w=w, _hp=hp, _pc=1024 * p: e.activation(
            pt[:, _pc:_pc + _w], d_psA[:, 0:_w], AF.Exp,
            bias=negb8[:, _hp:_hp + 1], scale=SCALE), waits=ws, inc=("tSC", 1))
        exp_done[s] = P.do("scalar", lambda e, _w=w, _hp=hp, _pc=2048 + 1024 * p: e.activation(
            pt[:, _pc:_pc + _w], d_psB[:, 0:_w], AF.Exp,
            bias=negb8[:, _hp:_hp + 1], scale=SCALE), inc=("tSC", 1))
        if masked:
            moff = w - 64
            P.do("vector", lambda e, _m=moff, _pc=1024 * p: e.tensor_tensor(
                pt[:, _pc + _m:_pc + _m + 64], pt[:, _pc + _m:_pc + _m + 64],
                mask_s[:], ALU.mult),
                waits=[("tSC", exp_done[s]), ("qX", c_msk)], inc=("tDV", 1))
            mask_done[s] = P.do("vector", lambda e, _m=moff, _pc=2048 + 1024 * p: e.tensor_tensor(
                pt[:, _pc + _m:_pc + _m + 64], pt[:, _pc + _m:_pc + _m + 64],
                mask_s[:], ALU.mult), inc=("tDV", 1))
        if last:
            emit_av(s)
            c_av = av_done[s]
            wsp = [("qX", c_spill[hp - 1])] if (hp - 1 in c_spill and h == 0) else []
            c_norm = 0
            for head in range(2):
                c_rc = P.do("vector", lambda e, _hd=head: e.reciprocal(
                    s64[64:65, :], av_ps[_hd][64:65, :]),
                    waits=[("tPE", c_av), ("qX", P.cnt["qX"])], inc=("tDV", 1))
                c_aev = P.do("scalar", lambda e, _hd=head: e.activation(
                    av_sb[0:64, :], av_ps[_hd][0:64, :], AF.Copy),
                    waits=[("tPE", c_av), ("tDV", P.cnt["tDV"])], inc=("tSC", 1))
                c_cph = P.do("vector", lambda e: e.tensor_copy(
                    hl64[64:65, 0:1024], s64[64:65, :]),
                    waits=[("tDV", c_rc)], inc=("tDV", 1))
                c_hl = P.do("vector", lambda e: e.tensor_tensor(
                    hl64[64:65, 1024:2048], s64[64:65, :], hl64[64:65, 0:1024],
                    ALU.subtract), waits=[("tDV", c_cph)], inc=("tDV", 1))
                c_sh = P.do("sync", lambda e: e.dma_start(
                    bass.AP(rechl2, 0, [[1024, 2], [1, 1024]]),
                    bass.AP(hl64, 64 * 2048, [[2048, 1], [1, 2048]])),
                    waits=[("tDV", c_hl), ("tPE", P.cnt["tPE"])], inc=("qX", 16))
                for c in range(2):
                    wbc = [("qX", c_sh), ("tSC", P.cnt["tSC"])]
                    if c_norm:
                        wbc.append(("tDV", c_norm))
                    c_bc = P.do("tensor", lambda e, _c=c: e.matmul(
                        d_psA[:, 0:512], ones2[:], rechl2[:, 512 * _c:512 * _c + 512],
                        start=True, stop=True, skip_group_check=True),
                        waits=wbc, inc=("tPE", 1))
                    c_norm = P.do("vector", lambda e, _c=c, _hd=head, _h=h: e.tensor_tensor(
                        av_spl[0:64, 2048 * _hd + 1024 * _h + 512 * _c:
                               2048 * _hd + 1024 * _h + 512 * _c + 512],
                        av_sb[0:64, 512 * _c:512 * _c + 512],
                        d_psA[0:64, 0:512], ALU.mult),
                        waits=[("tPE", c_bc), ("tSC", c_aev)] + wsp, inc=("tDV", 1))
            dps_guard = [("tDV", c_norm)]
            if h == 1:
                for head in range(2):
                    c_spill[hp] = P.do("sync", lambda e, _hp=hp, _hd=head: e.dma_start(
                        bass.AP(av_d, (_hp * 128 + 64 * _hd) * NQ, [[NQ, 64], [1, NQ]]),
                        av_spl[0:64, 2048 * _hd:2048 * _hd + 2048]),
                        waits=[("tDV", P.cnt["tDV"])], inc=("qX", 16))
    c_spill_last = c_spill[HP - 1]

    P.flush()
    if stop_after == 'attn':
        return nc
    aex.close()
    kvq.close()

    P.barrier()
    om = ExitStack()
    u_fm = om.enter_context(nc.sbuf_tensor("u_fm", [128, 8 * NQ], BF16))
    facc = om.enter_context(nc.sbuf_tensor("facc", [128, NQT * 1024], F32))
    g_all = om.enter_context(nc.sbuf_tensor("g_all", [128, NQT * 16], F32))
    g2t = om.enter_context(nc.sbuf_tensor("g2t", [16, NQ], BF16))
    br_b = om.enter_context(nc.sbuf_tensor("br_b", [16, 1024], BF16))

    oex = ExitStack()
    av_all = oex.enter_context(nc.sbuf_tensor("av_all", [128, HP * NQ], BF16))
    wo_b = oex.enter_context(nc.sbuf_tensor("wo_b", [128, 8192], BF16))
    ident_f = oex.enter_context(nc.sbuf_tensor("ident_f_s", [128, 128], F32))
    xq_t = oex.enter_context(nc.sbuf_tensor("xq_t", [128, 2048], F32))
    res1 = oex.enter_context(nc.sbuf_tensor("res1", [128, 1024], F32))
    u_t = oex.enter_context(nc.sbuf_tensor("u_t", [128, 1024], F32))
    u_bf = oex.enter_context(nc.sbuf_tensor("u_bf", [128, 2048], BF16))
    cent_b = oex.enter_context(nc.sbuf_tensor("cent_b", [128, 128], BF16))
    sc_em = oex.enter_context(nc.sbuf_tensor("sc_em", [16, NQ], F32))
    eb_s = oex.enter_context(nc.sbuf_tensor("eb_s", [16, 1], F32))
    bs_b = oex.enter_context(nc.sbuf_tensor("bs_b", [2, 1024], BF16))
    rt = oex.enter_context(nc.sbuf_tensor("rt", [128, 80], F32))
    gg_t = oex.enter_context(nc.sbuf_tensor("gg_t", [128, 16], BF16))
    o_ps = [oex.enter_context(nc.psum_tensor(f"o_ps{i}", [128, 1024], F32)) for i in range(2)]
    tp2_ps = [oex.enter_context(nc.psum_tensor(f"tp2_ps{i}", [128, 128], BF16)) for i in range(2)]
    sc_ps = oex.enter_context(nc.psum_tensor("sc_ps", [16, 512], F32))
    misc_ps = oex.enter_context(nc.psum_tensor("misc_ps", [128, 16], F32))

    c_av_ld = P.do("sync", lambda e: e.dma_start(
        bass.AP(av_all, 0, [[HP * NQ, 128], [NQ, HP], [1, NQ]]),
        bass.AP(av_d, 0, [[NQ, 128], [128 * NQ, HP], [1, NQ]])),
        waits=[("qX", c_spill_last)], inc=("qX", 16))
    c_idf = P.do("sync", lambda e: e.dma_start(ident_f[:], ident_f_in[:, :]), inc=("qX", 16))
    c_wo = P.do("gpsimd", wcast_load(wo_b, 0, wo, 0), inc=("qW", 16))
    c_cent = P.do("gpsimd", lambda e: e.dma_start(
        bass.AP(cent_b, 0, [[128, 128], [16, 8], [1, 16]]),
        bass.AP(cent_t, 0, [[16, 128], [2048, 8], [1, 16]])), inc=("qW", 16))
    c_eb = P.do("gpsimd", lambda e: e.dma_start(eb_s[:], eb[:, :]), inc=("qW", 16))
    c_br = P.do("gpsimd", lambda e: e.dma_start(br_b[0:16, :], br[:, :]), inc=("qW", 16))
    c_bs = P.do("gpsimd", lambda e: e.dma_start(bs_b[:], bs[:, :]), inc=("qW", 16))

    xq_free = [[], []]
    ops_free = [[], []]
    ubf_free = [[], []]
    c_sq_prev = 0
    c_r_prev = 0
    for qt in range(NQT):
        b = qt % 2
        c_xq = P.do("sync", lambda e, _qt=qt, _b=b: e.dma_start(
            xq_t[:, 1024 * _b:1024 * _b + 1024], x_q[128 * _qt:128 * _qt + 128, :]),
            waits=list(xq_free[b]), inc=("qX", 16))
        wmm = [("qX", c_av_ld), ("qW", c_wo)] + ops_free[b]
        c_mm = None
        for ic in range(8):
            for nh in range(2):
                c_mm = P.do("tensor", lambda e, _qt=qt, _ic=ic, _nh=nh, _b=b: e.matmul(
                    o_ps[_b][:, 512 * _nh:512 * _nh + 512],
                    av_all[:, NQ * _ic + 128 * _qt:NQ * _ic + 128 * _qt + 128],
                    wo_b[:, 1024 * _ic + 512 * _nh:1024 * _ic + 512 * _nh + 512],
                    start=(_ic == 0), stop=(_ic == 7), skip_group_check=True),
                    waits=wmm if (ic == 0 and nh == 0) else [], inc=("tPE", 1))
        c_r1 = P.do("vector", lambda e, _b=b: e.tensor_tensor(
            res1[:, :], o_ps[_b][:, :], xq_t[:, 1024 * _b:1024 * _b + 1024], ALU.add),
            waits=[("tPE", c_mm), ("qX", c_xq), ("tSC", c_sq_prev)], inc=("tDV", 1))
        ops_free[b] = [("tDV", c_r1)]
        xq_free[b] = [("tDV", c_r1)]
        c_rs1 = P.do("vector", lambda e: e.reduce_sum(
            rt[:, 0:1], res1[:, :], axis=AX.X),
            waits=[("tSC", c_r_prev)], inc=("tDV", 1))
        wub = list(ubf_free[b])
        P.do("scalar", lambda e, _b=b: e.activation(
            u_bf[:, 1024 * _b:1024 * _b + 1024], res1[:, :], AF.Square,
            accum_out=rt[:, 1:2]), waits=[("tDV", c_r1)] + wub)
        c_sq = P.do("scalar", lambda e: e.copy(rt[:, 14:15], rt[:, 1:2]),
                    inc=("tSC", 1))
        c_sq_prev = c_sq
        d1 = P.do("vector", lambda e: e.tensor_scalar(
            rt[:, 2:3], rt[:, 0:1], 1.0 / D, None, ALU.mult, ALU.bypass),
            waits=[("tDV", c_rs1)], inc=("tDV", 1))
        d2 = P.do("vector", lambda e: e.tensor_tensor(
            rt[:, 3:4], rt[:, 2:3], rt[:, 2:3], ALU.mult),
            waits=[("tDV", d1)], inc=("tDV", 1))
        d3 = P.do("vector", lambda e: e.tensor_scalar(
            rt[:, 4:5], rt[:, 14:15], 1.0 / D, EPS, ALU.mult, ALU.add),
            waits=[("tSC", c_sq)], inc=("tDV", 1))
        d4 = P.do("vector", lambda e: e.tensor_tensor(
            rt[:, 5:6], rt[:, 4:5], rt[:, 3:4], ALU.subtract),
            waits=[("tDV", d3)], inc=("tDV", 1))
        c_vr = P.do("vector", lambda e: e.reciprocal(rt[:, 6:7], rt[:, 5:6]),
                    waits=[("tDV", d4)], inc=("tDV", 1))
        c_r = P.do("scalar", lambda e: e.activation(rt[:, 7:8], rt[:, 6:7], AF.Sqrt),
                   waits=[("tDV", c_vr)], inc=("tSC", 1))
        c_r_prev = c_r
        d5 = P.do("vector", lambda e: e.tensor_scalar(
            rt[:, 8:9], rt[:, 2:3], rt[:, 7:8], -1.0, ALU.mult, ALU.mult),
            waits=[("tSC", c_r)], inc=("tDV", 1))
        d6 = P.do("vector", lambda e: e.tensor_scalar(
            u_t[:, :], res1[:, :], rt[:, 7:8], rt[:, 8:9], ALU.mult, ALU.add),
            waits=[("tDV", d5)], inc=("tDV", 1))
        P.do("vector", lambda e, _qt=qt: e.tensor_tensor(
            facc[:, 1024 * _qt:1024 * _qt + 1024], res1[:, :], u_t[:, :], ALU.add),
            waits=[("tDV", d6)])
        c_ub = P.do("vector", lambda e, _b=b: e.tensor_copy(
            u_bf[:, 1024 * _b:1024 * _b + 1024], u_t[:, :]), inc=("tDV", 1))
        c_ev = None
        for dc in range(8):
            pb = dc % 2
            c_tp = P.do("tensor", lambda e, _b=b, _dc=dc, _pb=pb: e.transpose(
                tp2_ps[_pb][:, :], u_bf[:, 1024 * _b + 128 * _dc:1024 * _b + 128 * _dc + 128],
                ident_b[:]), waits=[("tDV", c_ub), ("tSC", P.cnt["tSC"])],
                inc=("tPE", 1))
            c_ev = P.do("scalar", lambda e, _dc=dc, _pb=pb, _qt=qt: e.activation(
                u_fm[:, NQ * _dc + 128 * _qt:NQ * _dc + 128 * _qt + 128],
                tp2_ps[_pb][:, :], AF.Copy), waits=[("tPE", c_tp)], inc=("tSC", 1))
        ubf_free[b] = [("tSC", c_ev)]
    c_o1_sc = P.cnt["tSC"]

    c_scev = {}
    for c in range(4):
        c_mm = None
        for dc in range(8):
            w0 = [("qW", c_cent), ("tSC", P.cnt["tSC"])]
            if c > 0:
                w0.append(("tDV", c_scev[c - 1]))
            c_mm = P.do("tensor", lambda e, _c=c, _dc=dc: e.matmul(
                sc_ps[:, :], cent_b[:, 16 * _dc:16 * _dc + 16],
                u_fm[:, NQ * _dc + 512 * _c:NQ * _dc + 512 * _c + 512],
                start=(_dc == 0), stop=(_dc == 7), skip_group_check=True),
                waits=w0 if dc == 0 else [], inc=("tPE", 1))
        c_scev[c] = P.do("vector", lambda e, _c=c: e.tensor_scalar(
            sc_em[:, 512 * _c:512 * _c + 512], sc_ps[:, :], eb_s[:, 0:1], None,
            ALU.add, ALU.bypass), waits=[("tPE", c_mm), ("qW", c_eb)], inc=("tDV", 1))
    c_es_prev = 0
    c_m1_prev = 0
    c_gt_prev = 0
    for qt in range(NQT):
        c_tr = P.do("tensor", lambda e, _qt=qt: e.transpose(
            misc_ps[:, :], sc_em[0:16, 128 * _qt:128 * _qt + 128], ident_f[0:16, 0:16]),
            waits=[("tDV", c_scev[qt // 4]), ("qX", c_idf),
                   ("tSC", c_es_prev), ("tDV", c_m1_prev)], inc=("tPE", 1))
        c_m1 = P.do("vector", lambda e: e.reduce_max(
            rt[:, 9:10], misc_ps[:, :], axis=AX.X, negate=True),
            waits=[("tPE", c_tr)], inc=("tDV", 1))
        c_m1_prev = c_m1
        c_es = P.do("scalar", lambda e: e.activation(
            rt[:, 32:48], misc_ps[:, :], AF.Exp, bias=rt[:, 9:10], scale=1.0),
            waits=[("tDV", c_m1)], inc=("tSC", 1))
        c_es_prev = c_es
        c_rsm = P.do("vector", lambda e: e.reduce_sum(
            rt[:, 10:11], rt[:, 32:48], axis=AX.X), waits=[("tSC", c_es)],
            inc=("tDV", 1))
        e1 = P.do("vector", lambda e: e.reciprocal(rt[:, 11:12], rt[:, 10:11]),
             waits=[("tDV", c_rsm)], inc=("tDV", 1))
        e2 = P.do("vector", lambda e: e.tensor_scalar(
            rt[:, 48:64], rt[:, 32:48], rt[:, 11:12], None, ALU.mult, ALU.bypass),
            waits=[("tDV", e1)], inc=("tDV", 1))
        c_rm1 = P.do("vector", lambda e: e.reduce_max(
            rt[:, 12:13], rt[:, 48:64], axis=AX.X),
            waits=[("tDV", e2)], inc=("tDV", 1))
        e3 = P.do("vector", lambda e: e.tensor_scalar(
            rt[:, 64:80], rt[:, 48:64], rt[:, 12:13], None, ALU.is_ge, ALU.bypass),
            waits=[("tDV", c_rm1)], inc=("tDV", 1))
        e4 = P.do("vector", lambda e: e.scalar_tensor_tensor(
            rt[:, 16:32], rt[:, 64:80], -1e4, rt[:, 48:64], ALU.mult, ALU.add),
            waits=[("tDV", e3)], inc=("tDV", 1))
        c_rm2 = P.do("vector", lambda e: e.reduce_max(
            rt[:, 13:14], rt[:, 16:32], axis=AX.X),
            waits=[("tDV", e4)], inc=("tDV", 1))
        e5 = P.do("vector", lambda e: e.tensor_scalar(
            rt[:, 64:80], rt[:, 48:64], rt[:, 13:14], None, ALU.is_ge, ALU.bypass),
            waits=[("tDV", c_rm2)], inc=("tDV", 1))
        c_g = P.do("vector", lambda e, _qt=qt: e.tensor_tensor(
            g_all[:, 16 * _qt:16 * _qt + 16], rt[:, 48:64], rt[:, 64:80], ALU.mult),
            waits=[("tDV", e5)], inc=("tDV", 1))
        c_gg = P.do("vector", lambda e, _qt=qt: e.tensor_scalar(
            gg_t[:, :], g_all[:, 16 * _qt:16 * _qt + 16], 2.0, None,
            ALU.mult, ALU.bypass), waits=[("tDV", c_g), ("tPE", c_gt_prev)],
            inc=("tDV", 1))
        c_gt = P.do("tensor", lambda e: e.transpose(
            tp2_ps[0][0:16, :], gg_t[:, :], ident_b[:]),
            waits=[("tDV", c_gg), ("tSC", P.cnt["tSC"])], inc=("tPE", 1))
        c_gt_prev = c_gt
        P.do("scalar", lambda e, _qt=qt: e.activation(
            g2t[0:16, 128 * _qt:128 * _qt + 128], tp2_ps[0][0:16, :], AF.Copy),
            waits=[("tPE", c_gt)], inc=("tSC", 1))
    c_o_done_sc = P.cnt["tSC"]
    c_o_done_dv = P.cnt["tDV"]

    if stop_after == 'o':
        P.do("sync", lambda e: e.dma_start(
            bass.AP(out_d, 0, [[1024, 128], [131072, 16], [1, 1024]]),
            bass.AP(facc, 0, [[16384, 128], [1024, 16], [1, 1024]])),
            waits=[("tDV", P.cnt["tDV"]), ("tSC", P.cnt["tSC"])], inc=("qX", 16))
        P.do("gpsimd", lambda e: e.dma_start(
            bass.AP(av_d, 0, [[256, 128], [1, 256]]),
            bass.AP(g_all, 0, [[256, 128], [1, 256]])),
            waits=[("tDV", P.cnt["tDV"])], inc=("qW", 16))
        P.do("sync", lambda e: e.dma_start(
            bass.AP(av_d, 128 * 256, [[2048, 16], [1, 2048]]),
            bass.AP(g2t, 0, [[2048, 16], [1, 2048]])), inc=("qX", 16))
        P.do("sync", lambda e: e.dma_start(
            bass.AP(av_d, 2 * 128 * 256, [[2048, 128], [1, 2048]]),
            u_fm[:, 0:2048]), inc=("qX", 16))
        P.flush()
        return nc
    P.flush()
    oex.close()

    P.barrier()
    mex = ExitStack()
    wr_b = mex.enter_context(nc.sbuf_tensor("wr_b", [128, 2 * 8192], BF16))
    ws_tmp = mex.enter_context(nc.sbuf_tensor("ws_tmp", [128, 8192], BF16))
    e_ps = [mex.enter_context(nc.psum_tensor(f"e_ps{i}", [128, 1024], F32)) for i in range(3)]

    c_ws0 = P.do("gpsimd", wcast_load(wr_b, 0, wst, 0), inc=("qW", 16))
    c_ws1 = P.do("gpsimd", wcast_load(ws_tmp, 0, wst, D * D), inc=("qW", 16))
    c_wsum = P.do("vector", lambda e: e.tensor_tensor(
        wr_b[:, 0:8192], wr_b[:, 0:8192], ws_tmp[:, 0:8192], ALU.add),
        waits=[("qW", c_ws1)], inc=("tDV", 1))
    w_ready = {0: ("tDV", c_wsum)}
    wbuf_last_mm = {0: 0, 1: 0}
    mm_done = {}
    stt_done = {}
    psum_guard = {0: None, 1: None, 2: None}
    for t in range(NE + 1):
        if t + 1 <= NE:
            ee = t
            half = (t + 1) % 2
            w_ready[t + 1] = ("qW", P.do(
                "gpsimd", wcast_load(wr_b, 8192 * half, wrt, ee * D * D),
                waits=[("tPE", wbuf_last_mm[half])] if wbuf_last_mm[half] else [],
                inc=("qW", 16)))
        half = t % 2
        for qt in range(NQT):
            r = (t * NQT + qt) % 3
            w0 = [w_ready[t]]
            if psum_guard[r] is not None:
                w0.append(psum_guard[r])
            c_mm = None
            for dc in range(8):
                for nh in range(2):
                    c_mm = P.do("tensor", lambda e, _t=t, _qt=qt, _dc=dc, _nh=nh,
                                _r=r, _hf=half: e.matmul(
                        e_ps[_r][:, 512 * _nh:512 * _nh + 512],
                        u_fm[:, NQ * _dc + 128 * _qt:NQ * _dc + 128 * _qt + 128],
                        wr_b[:, 8192 * _hf + 1024 * _dc + 512 * _nh:
                             8192 * _hf + 1024 * _dc + 512 * _nh + 512],
                        start=(_dc == 0), stop=(_dc == 7 and _t > 0),
                        skip_group_check=True),
                        waits=w0 if (dc == 0 and nh == 0) else [], inc=("tPE", 1))
            if t == 0:
                for nh in range(2):
                    P.do("tensor", lambda e, _qt=qt, _nh=nh, _r=r: e.matmul(
                        e_ps[_r][:, 512 * _nh:512 * _nh + 512],
                        g2t[0:16, 128 * _qt:128 * _qt + 128],
                        br_b[0:16, 512 * _nh:512 * _nh + 512],
                        start=False, stop=False, skip_group_check=True),
                        waits=[("tSC", c_o_done_sc), ("qW", c_br), ("qW", c_bs)] if (qt == 0 and nh == 0) else [],
                        inc=("tPE", 1))
                    c_mm = P.do("tensor", lambda e, _nh=nh, _r=r: e.matmul(
                        e_ps[_r][:, 512 * _nh:512 * _nh + 512],
                        ones2[:, 0:128],
                        bs_b[:, 512 * _nh:512 * _nh + 512],
                        start=False, stop=True, skip_group_check=True),
                        inc=("tPE", 1))
            mm_done[(t, qt)] = c_mm
            wbuf_last_mm[half] = c_mm
            c_stt = None
            w_prev_stt = [("tDV", stt_done[(t - 1, qt)])] if t > 0 else [("tDV", c_o_done_dv)]
            for nh in range(2):
                if t == 0:
                    c_stt = P.do("vector", lambda e, _qt=qt, _nh=nh, _r=r:
                                 e.scalar_tensor_tensor(
                                     facc[:, 1024 * _qt + 512 * _nh:1024 * _qt + 512 * _nh + 512],
                                     e_ps[_r][:, 512 * _nh:512 * _nh + 512], 0.5,
                                     facc[:, 1024 * _qt + 512 * _nh:1024 * _qt + 512 * _nh + 512],
                                     ALU.mult, ALU.add),
                                 waits=([("tPE", c_mm)] + w_prev_stt) if nh == 0 else [],
                                 inc=("tDV", 1))
                else:
                    c_stt = P.do("vector", lambda e, _t=t, _qt=qt, _nh=nh, _r=r:
                                 e.scalar_tensor_tensor(
                                     facc[:, 1024 * _qt + 512 * _nh:1024 * _qt + 512 * _nh + 512],
                                     e_ps[_r][:, 512 * _nh:512 * _nh + 512],
                                     g_all[:, 16 * _qt + (_t - 1):16 * _qt + _t],
                                     facc[:, 1024 * _qt + 512 * _nh:1024 * _qt + 512 * _nh + 512],
                                     ALU.mult, ALU.add),
                                 waits=([("tPE", c_mm)] + w_prev_stt) if nh == 0 else [],
                                 inc=("tDV", 1))
            stt_done[(t, qt)] = c_stt
            psum_guard[r] = ("tDV", c_stt)

    c_store = None
    for qt in range(NQT):
        c_store = P.do("sync", lambda e, _qt=qt: e.dma_start(
            out_d[128 * _qt:128 * _qt + 128, :], facc[:, 1024 * _qt:1024 * _qt + 1024]),
            waits=[("tDV", stt_done[(NE, qt)])], inc=("qX", 16))
    P.do("sync", lambda e: (e.wait_ge(sems["qX"], c_store), None)[1])
    P.flush()
    mex.close()
    om.close()
    es.close()
    return nc


_NC_CACHE = None


def _get_nc():
    global _NC_CACHE
    if _NC_CACHE is None:
        _NC_CACHE = build_nc()
    return _NC_CACHE


def kernel(L0_attn_x, L0_attn_Wq, L0_attn_Wk, L0_attn_Wv, L0_attn_Wo,
           L0_ffn_centroids, L0_ffn_expert_biases, L0_ffn_Wr, L0_ffn_br,
           L0_ffn_Ws, L0_ffn_bs):
    x = np.ascontiguousarray(np.asarray(L0_attn_x, dtype=np.float32))
    wq = np.ascontiguousarray(np.asarray(L0_attn_Wq, dtype=np.float32))
    wk = np.ascontiguousarray(np.asarray(L0_attn_Wk, dtype=np.float32))
    wv = np.ascontiguousarray(np.asarray(L0_attn_Wv, dtype=np.float32))
    wo = np.ascontiguousarray(np.asarray(L0_attn_Wo, dtype=np.float32))
    wrt = np.ascontiguousarray(np.asarray(L0_ffn_Wr, dtype=np.float32).transpose(0, 2, 1))
    wst = np.ascontiguousarray(np.asarray(L0_ffn_Ws, dtype=np.float32).transpose(0, 2, 1))
    cent_t = np.ascontiguousarray(np.asarray(L0_ffn_centroids, dtype=np.float32).T)
    ebv = np.ascontiguousarray(np.asarray(L0_ffn_expert_biases, dtype=np.float32).reshape(16, 1))
    brv = np.ascontiguousarray(np.asarray(L0_ffn_br, dtype=np.float32))
    bsv = np.ascontiguousarray(np.asarray(L0_ffn_bs, dtype=np.float32))
    ident_b = np.eye(128, dtype=BF)
    ident_f = np.eye(128, dtype=np.float32)

    perms = []
    for par in range(2):
        idx = np.concatenate([
            np.arange(128 * (31 - k) + 64 * par, 128 * (31 - k) + 64 * par + 64)
            for k in range(32)])
        perms.append(idx)
    masks = []
    for par in range(2):
        jj = np.arange(128)[:, None]
        ii = np.arange(64)[None, :]
        masks.append((jj <= 64 * par + ii).astype(BF))

    in_maps = []
    for c in range(8):
        b, par = c // 2, c % 2
        in_maps.append({
            "x_ctx": x[b],
            "x_q": np.ascontiguousarray(x[b][perms[par]]),
            "wq": wq, "wk": wk, "wv": wv, "wo": wo,
            "wrt": wrt, "wst": wst, "cent_t": cent_t, "eb": ebv,
            "br": brv, "bs": bsv, "mask": masks[par],
            "ident_b": ident_b, "ident_f": ident_f,
        })

    nc = _get_nc()
    res = run_bass_kernel_spmd(nc, in_maps, core_ids=list(range(8)))
    out = np.empty((4, 4096, 1024), dtype=np.float32)
    for c in range(8):
        b, par = c // 2, c % 2
        out[b][perms[par]] = res.results[c]["out"]
    return out


# version 04
# speedup vs baseline: 1.4794x; 1.0718x over previous
import numpy as np
import ml_dtypes
from contextlib import ExitStack

import concourse.bass as bass
import concourse.mybir as mybir
from concourse.bass_utils import run_bass_kernel_spmd

F32 = mybir.dt.float32
BF16 = mybir.dt.bfloat16
AF = mybir.ActivationFunctionType
ALU = mybir.AluOpType
AX = mybir.AxisListType
BF = ml_dtypes.bfloat16

D = 1024
NCTX = 4096
NQ = 2048
NKVT = 32
NQT = 16
HP = 8
SCALE = 0.125
EPS = 1e-5
NE = 16

ENGS = ("tensor", "vector", "scalar", "gpsimd", "sync")


class Prog:

    def __init__(self, nc, sems):
        self.nc = nc
        self.sems = sems
        self.cnt = {k: 0 for k in sems}
        self.q = {e: [] for e in ENGS}
        self.lastw = {e: {} for e in ENGS}

    def do(self, eng, fn, waits=(), inc=None):
        real_waits = []
        lw = self.lastw[eng]
        for s, v in waits:
            if s is None or v <= 0:
                continue
            if lw.get(s, -1) < v:
                real_waits.append((s, v))
                lw[s] = v
        if inc is not None:
            self.cnt[inc[0]] += inc[1]
        self.q[eng].append((fn, tuple(real_waits), inc))
        return self.cnt[inc[0]] if inc is not None else None

    def barrier(self):
        self.cnt["bar"] += 5
        tgt = self.cnt["bar"]
        for e in ENGS:
            def fn(eng, _t=tgt):
                eng.sem_inc(self.sems["bar"], 1)
                eng.wait_ge(self.sems["bar"], _t)
                return None
            self.q[e].append((fn, (), None))
            self.lastw[e]["bar"] = tgt

    def flush(self):
        with self.nc.Block() as blk:
            for e in ENGS:
                ops = self.q[e]
                self.q[e] = []
                if not ops:
                    continue

                def body(eng, _ops=tuple(ops), _sems=self.sems):
                    for fn, waits, inc in _ops:
                        for s, v in waits:
                            eng.wait_ge(_sems[s], v)
                        r = fn(eng)
                        if inc is not None and r is not None:
                            r.then_inc(_sems[inc[0]], inc[1])
                getattr(blk, e)(body)


def build_nc(stop_after='all'):
    nc = bass.Bass("TRN2", target_bir_lowering=False, detect_race_conditions=False)

    x_ctx = nc.dram_tensor("x_ctx", [NCTX, D], F32, kind="ExternalInput")
    x_q = nc.dram_tensor("x_q", [NQ, D], F32, kind="ExternalInput")
    wq = nc.dram_tensor("wq", [D, D], F32, kind="ExternalInput")
    wk = nc.dram_tensor("wk", [D, D], F32, kind="ExternalInput")
    wv = nc.dram_tensor("wv", [D, D], F32, kind="ExternalInput")
    wo = nc.dram_tensor("wo", [D, D], F32, kind="ExternalInput")
    wrt = nc.dram_tensor("wrt", [NE, D, D], F32, kind="ExternalInput")
    wst = nc.dram_tensor("wst", [2, D, D], F32, kind="ExternalInput")
    cent_t = nc.dram_tensor("cent_t", [D, 16], F32, kind="ExternalInput")
    eb = nc.dram_tensor("eb", [16, 1], F32, kind="ExternalInput")
    br = nc.dram_tensor("br", [16, D], F32, kind="ExternalInput")
    bs = nc.dram_tensor("bs", [2, D], F32, kind="ExternalInput")
    mask_in = nc.dram_tensor("mask", [128, 64], BF16, kind="ExternalInput")
    ident_b_in = nc.dram_tensor("ident_b", [128, 128], BF16, kind="ExternalInput")
    ident_f_in = nc.dram_tensor("ident_f", [128, 128], F32, kind="ExternalInput")
    out_d = nc.dram_tensor("out", [NQ, D], F32, kind="ExternalOutput")
    av_d = nc.dram_tensor("av_spill", [HP, 128, NQ], BF16, kind="ExternalOutput")

    es = ExitStack()
    sems = {}
    for s in ("qW", "qX", "tPE", "tSC", "tDV", "tGP", "bar"):
        sems[s] = es.enter_context(nc.semaphore(s))
    P = Prog(nc, sems)

    ident_b = es.enter_context(nc.sbuf_tensor("ident_b_s", [128, 128], BF16))
    mask_s = es.enter_context(nc.sbuf_tensor("mask_s", [128, 64], BF16))
    ones1 = es.enter_context(nc.sbuf_tensor("ones1", [128, 1], BF16))
    ones2 = es.enter_context(nc.sbuf_tensor("ones2", [2, 128], BF16))
    negb8 = es.enter_context(nc.sbuf_tensor("negb8", [128, 8], F32))
    stats = es.enter_context(nc.sbuf_tensor("stats", [128, 24], F32))

    c_idb = P.do("sync", lambda e: e.dma_start(ident_b[:], ident_b_in[:, :]), inc=("qX", 16))
    c_msk = P.do("sync", lambda e: e.dma_start(mask_s[:], mask_in[:, :]), inc=("qX", 16))
    P.do("gpsimd", lambda e: e.memset(ones1[:], 1.0))
    c_ones = P.do("gpsimd", lambda e: e.memset(ones2[:], 1.0), inc=("tGP", 1))

    def wcast_load(dst, dst_col, src, e_off):
        dst_stride = dst.shape[1]

        def fn(eng):
            return eng.dma_start(
                bass.AP(dst, dst_col, [[dst_stride, 128], [1024, 8], [1, 1024]]),
                bass.AP(src, e_off, [[1024, 128], [131072, 8], [1, 1024]]),
            )
        return fn

    kvq = ExitStack()
    k_all = kvq.enter_context(nc.sbuf_tensor("k_all", [128, HP * NCTX], BF16))
    v_all = kvq.enter_context(nc.sbuf_tensor("v_all", [128, NKVT * 1040], BF16))
    q_all = kvq.enter_context(nc.sbuf_tensor("q_all", [128, HP * NQ], BF16))
    xt = kvq.enter_context(nc.sbuf_tensor("xt", [128, 2 * 1024], F32))
    xlt = kvq.enter_context(nc.sbuf_tensor("xlt", [128, 1024], BF16))
    xlf = kvq.enter_context(nc.sbuf_tensor("xlf", [128, 8 * 256], BF16))

    lnst = {
        "xt_free": [[], []],
        "xlt_free": [],
        "cr_prev": [0, 0],
        "g": 0,
    }

    def ln_tile(src_dram, row0, dst_fm_col, tp_ps, qx_extra=()):
        g = lnst["g"]
        lnst["g"] += 1
        b = g % 2
        sb = b * 12
        c_x = P.do("sync", lambda e, _b=b, _r=row0, _s=src_dram: e.dma_start(
            xt[:, 1024 * _b:1024 * _b + 1024], _s[_r:_r + 128, :]),
            waits=list(lnst["xt_free"][b]) + list(qx_extra), inc=("qX", 16))
        xv = xt[:, 1024 * b:1024 * b + 1024]
        c_rs = P.do("vector", lambda e, _x=xv, _c=sb: e.reduce_sum(
            stats[:, _c:_c + 1], _x, axis=AX.X),
            waits=[("qX", c_x), ("tSC", lnst["cr_prev"][b])], inc=("tDV", 1))
        P.do("scalar", lambda e, _x=xv, _c=sb: e.activation(
            xlt[:, :], _x, AF.Square,
            accum_out=stats[:, _c + 1:_c + 2]),
            waits=[("qX", c_x), ("tSC", lnst["cr_prev"][b])] + list(lnst["xlt_free"]))
        c_sq = P.do("scalar", lambda e, _c=sb: e.copy(
            stats[:, _c + 9:_c + 10], stats[:, _c + 1:_c + 2]), inc=("tSC", 1))
        c1 = P.do("vector", lambda e, _c=sb: e.tensor_scalar(
            stats[:, _c + 2:_c + 3], stats[:, _c:_c + 1], 1.0 / D, None,
            ALU.mult, ALU.bypass), waits=[("tDV", c_rs)], inc=("tDV", 1))
        c2 = P.do("vector", lambda e, _c=sb: e.tensor_tensor(
            stats[:, _c + 3:_c + 4], stats[:, _c + 2:_c + 3], stats[:, _c + 2:_c + 3],
            ALU.mult), waits=[("tDV", c1)], inc=("tDV", 1))
        c3 = P.do("vector", lambda e, _c=sb: e.tensor_scalar(
            stats[:, _c + 4:_c + 5], stats[:, _c + 9:_c + 10], 1.0 / D, EPS,
            ALU.mult, ALU.add), waits=[("tSC", c_sq)], inc=("tDV", 1))
        c4 = P.do("vector", lambda e, _c=sb: e.tensor_tensor(
            stats[:, _c + 5:_c + 6], stats[:, _c + 4:_c + 5], stats[:, _c + 3:_c + 4],
            ALU.subtract), waits=[("tDV", c3)], inc=("tDV", 1))
        c_vr = P.do("vector", lambda e, _c=sb: e.reciprocal(
            stats[:, _c + 6:_c + 7], stats[:, _c + 5:_c + 6]),
            waits=[("tDV", c4)], inc=("tDV", 1))
        c_r = P.do("scalar", lambda e, _c=sb: e.activation(
            stats[:, _c + 7:_c + 8], stats[:, _c + 6:_c + 7], AF.Sqrt),
            waits=[("tDV", c_vr)], inc=("tSC", 1))
        lnst["cr_prev"][b] = c_r
        c5 = P.do("vector", lambda e, _c=sb: e.tensor_scalar(
            stats[:, _c + 8:_c + 9], stats[:, _c + 2:_c + 3], stats[:, _c + 7:_c + 8],
            -1.0, ALU.mult, ALU.mult), waits=[("tSC", c_r)], inc=("tDV", 1))
        c_ap = P.do("vector", lambda e, _x=xv, _c=sb: e.tensor_scalar(
            xlt[:, :], _x, stats[:, _c + 7:_c + 8],
            stats[:, _c + 8:_c + 9], ALU.mult, ALU.add),
            waits=[("tDV", c5)], inc=("tDV", 1))
        lnst["xt_free"][b] = [("tDV", c_ap), ("tSC", c_sq)]
        c_ev = None
        c_tp = None
        for dc in range(8):
            pb = dc % 2
            c_tp = P.do("tensor", lambda e, _dc=dc, _pb=pb, _tp=tp_ps: e.transpose(
                _tp[_pb][:, :], xlt[:, 128 * _dc:128 * _dc + 128],
                ident_b[:]),
                waits=[("tDV", c_ap), ("tSC", P.cnt["tSC"]), ("qX", c_idb)],
                inc=("tPE", 1))
            c_ev = P.do("scalar", lambda e, _dc=dc, _pb=pb, _col=dst_fm_col, _tp=tp_ps:
                        e.activation(
                            xlf[:, _col + 256 * _dc:_col + 256 * _dc + 128],
                            _tp[_pb][:, :], AF.Copy),
                        waits=[("tPE", c_tp)], inc=("tSC", 1))
        lnst["xlt_free"] = [("tPE", c_tp)]
        return c_ev

    p0 = ExitStack()
    wk_b = p0.enter_context(nc.sbuf_tensor("wk_b", [128, 8192], BF16))
    wv_b = p0.enter_context(nc.sbuf_tensor("wv_b", [128, 8192], BF16))
    tp_ps_a = [p0.enter_context(nc.psum_tensor(f"tp_ps{i}", [128, 128], BF16)) for i in range(2)]
    kq_ps_a = [p0.enter_context(nc.psum_tensor(f"kq_ps{i}", [128, 256], F32)) for i in range(2)]
    v_ps = [p0.enter_context(nc.psum_tensor(f"v_ps{i}", [128, 1024], F32)) for i in range(2)]

    c_wk = P.do("gpsimd", wcast_load(wk_b, 0, wk, 0), inc=("qW", 16))
    c_wv = P.do("gpsimd", wcast_load(wv_b, 0, wv, 0), inc=("qW", 16))
    c_vones = P.do("gpsimd", lambda e: e.memset(v_all[:, :], 1.0), inc=("tGP", 1))

    kq_free = [[], []]
    v_free = [[], []]
    for kb in range(NCTX // 256):
        c_ev = 0
        for t2 in range(2):
            c_ev = ln_tile(x_ctx, 256 * kb + 128 * t2, 128 * t2, tp_ps_a)
            if stop_after == 'ln0' and kb == 0 and t2 == 0:
                P.do("sync", lambda e: e.dma_start(out_d[0:128, 0:24], stats[:, :]),
                     waits=[("tSC", P.cnt["tSC"]), ("tDV", P.cnt["tDV"])], inc=("qX", 16))
                P.do("sync", lambda e: e.dma_start(
                    bass.AP(av_d, 0, [[2048, 128], [1, 1024]]), xlt[:, :]), inc=("qX", 16))
                P.do("sync", lambda e: e.dma_start(
                    bass.AP(av_d, 128 * 2048, [[2048, 128], [1, 2048]]), xlf[:, :]), inc=("qX", 16))
                P.flush()
                return nc
        for hp in range(HP):
            pb = hp % 2
            w0 = [("tSC", c_ev), ("qW", c_wk)] + kq_free[pb]
            c_mm = None
            for dc in range(8):
                c_mm = P.do("tensor", lambda e, _hp=hp, _dc=dc, _pb=pb: e.matmul(
                    kq_ps_a[_pb][:, :],
                    wk_b[:, 1024 * _dc + 128 * _hp:1024 * _dc + 128 * _hp + 128],
                    xlf[:, 256 * _dc:256 * _dc + 256],
                    start=(_dc == 0), stop=(_dc == 7), skip_group_check=True),
                    waits=w0 if dc == 0 else [], inc=("tPE", 1))
            c_kev = P.do("scalar", lambda e, _hp=hp, _kb=kb, _pb=pb: e.activation(
                k_all[:, NCTX * _hp + 256 * _kb:NCTX * _hp + 256 * _kb + 256],
                kq_ps_a[_pb][:, :], AF.Copy), waits=[("tPE", c_mm)], inc=("tSC", 1))
            kq_free[pb] = [("tSC", c_kev)]
        for t2 in range(2):
            j = 2 * kb + t2
            pb = j % 2
            w0 = [("tSC", c_ev), ("qW", c_wv)] + v_free[pb]
            c_mm = None
            for dc in range(8):
                for nh in range(2):
                    c_mm = P.do("tensor", lambda e, _t2=t2, _dc=dc, _nh=nh, _pb=pb: e.matmul(
                        v_ps[_pb][:, 512 * _nh:512 * _nh + 512],
                        xlf[:, 256 * _dc + 128 * _t2:256 * _dc + 128 * _t2 + 128],
                        wv_b[:, 1024 * _dc + 512 * _nh:1024 * _dc + 512 * _nh + 512],
                        start=(_dc == 0), stop=(_dc == 7), skip_group_check=True),
                        waits=w0 if (dc == 0 and nh == 0) else [], inc=("tPE", 1))
            c_vev = P.do("scalar", lambda e, _j=j, _pb=pb: e.activation(
                bass.AP(v_all, 1040 * _j, [[NKVT * 1040, 128], [130, 8], [65, 2], [1, 64]]),
                bass.AP(v_ps[_pb], 0, [[1024, 128], [128, 8], [64, 2], [1, 64]]),
                AF.Copy),
                waits=[("tPE", c_mm), ("tGP", c_vones)], inc=("tSC", 1))
            v_free[pb] = [("tSC", c_vev)]
    c_p0a_pe = P.cnt["tPE"]

    P.flush()
    if stop_after == 'p0a':
        return nc
    p0.close()

    p0b = ExitStack()
    wq_b = p0b.enter_context(nc.sbuf_tensor("wq_b", [128, 8192], BF16))
    tp_ps_b = [p0b.enter_context(nc.psum_tensor(f"tpb_ps{i}", [128, 128], BF16)) for i in range(2)]
    kq_ps_b = [p0b.enter_context(nc.psum_tensor(f"kqb_ps{i}", [128, 256], F32)) for i in range(2)]

    c_wq = P.do("gpsimd", wcast_load(wq_b, 0, wq, 0),
                waits=[("tPE", c_p0a_pe)], inc=("qW", 16))
    kq_free = [[], []]
    for qb in range(NQ // 256):
        c_ev = 0
        for t2 in range(2):
            c_ev = ln_tile(x_q, 256 * qb + 128 * t2, 128 * t2, tp_ps_b)
        for hp in range(HP):
            pb = hp % 2
            w0 = [("tSC", c_ev), ("qW", c_wq)] + kq_free[pb]
            c_mm = None
            for dc in range(8):
                c_mm = P.do("tensor", lambda e, _hp=hp, _dc=dc, _pb=pb: e.matmul(
                    kq_ps_b[_pb][:, :],
                    wq_b[:, 1024 * _dc + 128 * _hp:1024 * _dc + 128 * _hp + 128],
                    xlf[:, 256 * _dc:256 * _dc + 256],
                    start=(_dc == 0), stop=(_dc == 7), skip_group_check=True),
                    waits=w0 if dc == 0 else [], inc=("tPE", 1))
            c_qev = P.do("scalar", lambda e, _hp=hp, _qb=qb, _pb=pb: e.activation(
                q_all[:, NQ * _hp + 256 * _qb:NQ * _hp + 256 * _qb + 256],
                kq_ps_b[_pb][:, :], AF.Copy), waits=[("tPE", c_mm)], inc=("tSC", 1))
            kq_free[pb] = [("tSC", c_qev)]
    c_p0b_pe = P.cnt["tPE"]
    c_p0b_sc = P.cnt["tSC"]

    if stop_after == 'p0b':
        P.do("sync", lambda e: e.dma_start(
            bass.AP(av_d, 0, [[2048, 128], [1, 2048]]),
            k_all[:, 0:2048]), waits=[("tSC", P.cnt["tSC"])], inc=("qX", 16))
        P.do("sync", lambda e: e.dma_start(
            bass.AP(av_d, 128 * 2048, [[2048, 128], [1, 2048]]),
            k_all[:, 7 * 4096:7 * 4096 + 2048]), inc=("qX", 16))
        P.do("sync", lambda e: e.dma_start(
            bass.AP(av_d, 2 * 128 * 2048, [[2048, 128], [1, 2048]]),
            q_all[:, 7 * 2048:7 * 2048 + 2048]), inc=("qX", 16))
        P.do("sync", lambda e: e.dma_start(
            bass.AP(av_d, 3 * 128 * 2048, [[2048, 128], [1, 2048]]),
            v_all[:, 0:2048]), inc=("qX", 16))
        P.flush()
        return nc
    P.flush()
    p0b.close()

    aex = ExitStack()
    pt = aex.enter_context(nc.sbuf_tensor("pt", [128, 4096], BF16))
    ksq = pt
    s64 = aex.enter_context(nc.sbuf_tensor("s64", [65, 1024], F32))
    hl64 = aex.enter_context(nc.sbuf_tensor("hl64", [65, 2048], BF16))
    rechl2 = aex.enter_context(nc.sbuf_tensor("rechl2", [2, 1024], BF16))
    av_sb = aex.enter_context(nc.sbuf_tensor("av_sb", [64, 1024], BF16))
    av_spl = aex.enter_context(nc.sbuf_tensor("av_spl", [64, 4096], BF16))
    km = aex.enter_context(nc.sbuf_tensor("km", [1, 16], F32))
    bsc = aex.enter_context(nc.sbuf_tensor("bsc", [1, 4], F32))
    negvb = aex.enter_context(nc.sbuf_tensor("negvb", [1, 1], BF16))
    av_ps = [aex.enter_context(nc.psum_tensor(f"av_ps{i}", [65, 1024], F32)) for i in range(2)]
    d_psA = aex.enter_context(nc.psum_tensor("d_psA", [128, 1024], F32))
    d_psB = aex.enter_context(nc.psum_tensor("d_psB", [128, 1024], F32))

    for hp in range(HP):
        c_k2 = P.do("vector", lambda e, _hp=hp: e.tensor_tensor(
            ksq[:, :], k_all[:, NCTX * _hp:NCTX * _hp + 4096],
            k_all[:, NCTX * _hp:NCTX * _hp + 4096], ALU.mult),
            waits=[("tPE", c_p0b_pe), ("tSC", c_p0b_sc)], inc=("tDV", 1))
        c_rm = 0
        for c in range(8):
            pb = c % 2
            c_n2 = P.do("tensor", lambda e, _c=c, _pb=pb: e.matmul(
                d_psA[0:1, 512 * _pb:512 * _pb + 512], ones1[:],
                ksq[:, 512 * _c:512 * _c + 512],
                start=True, stop=True, skip_group_check=True),
                waits=[("tDV", c_k2), ("tGP", c_ones)] if c < 2 else [("tDV", c_rm)],
                inc=("tPE", 1))
            c_rm = P.do("vector", lambda e, _c=c, _pb=pb: e.reduce_max(
                km[0:1, _c:_c + 1], d_psA[0:1, 512 * _pb:512 * _pb + 512], axis=AX.X),
                waits=[("tPE", c_n2)], inc=("tDV", 1))
        c_km = P.do("vector", lambda e: e.reduce_max(
            bsc[0:1, 0:1], km[0:1, 0:8], axis=AX.X),
            waits=[("tDV", c_rm)], inc=("tDV", 1))
        c_q2 = P.do("vector", lambda e, _hp=hp: e.tensor_tensor(
            ksq[:, 0:2048], q_all[:, NQ * _hp:NQ * _hp + 2048],
            q_all[:, NQ * _hp:NQ * _hp + 2048], ALU.mult),
            waits=[("tPE", P.cnt["tPE"])], inc=("tDV", 1))
        for c in range(4):
            pb = c % 2
            c_n2 = P.do("tensor", lambda e, _c=c, _pb=pb: e.matmul(
                d_psA[0:1, 512 * _pb:512 * _pb + 512], ones1[:],
                ksq[:, 512 * _c:512 * _c + 512],
                start=True, stop=True, skip_group_check=True),
                waits=[("tDV", c_q2)] if c < 2 else [("tDV", c_rm)], inc=("tPE", 1))
            c_rm = P.do("vector", lambda e, _c=c, _pb=pb: e.reduce_max(
                km[0:1, 8 + _c:9 + _c], d_psA[0:1, 512 * _pb:512 * _pb + 512], axis=AX.X),
                waits=[("tPE", c_n2)], inc=("tDV", 1))
        c_qm = P.do("vector", lambda e: e.reduce_max(
            bsc[0:1, 1:2], km[0:1, 8:12], axis=AX.X),
            waits=[("tDV", c_rm)], inc=("tDV", 1))
        c_pr = P.do("vector", lambda e: e.tensor_tensor(
            bsc[0:1, 2:3], bsc[0:1, 0:1], bsc[0:1, 1:2], ALU.mult),
            waits=[("tDV", c_qm)], inc=("tDV", 1))
        c_sq2 = P.do("scalar", lambda e: e.activation(
            bsc[0:1, 3:4], bsc[0:1, 2:3], AF.Sqrt), waits=[("tDV", c_pr)], inc=("tSC", 1))
        c_nv = P.do("vector", lambda e: e.tensor_scalar(
            negvb[0:1, 0:1], bsc[0:1, 3:4], -SCALE, None, ALU.mult, ALU.bypass),
            waits=[("tSC", c_sq2)], inc=("tDV", 1))
        c_bc = P.do("tensor", lambda e: e.matmul(
            d_psA[:, 0:1], ones2[0:1, :], negvb[0:1, 0:1], start=True, stop=True,
            skip_group_check=True), waits=[("tDV", c_nv)], inc=("tPE", 1))
        P.do("vector", lambda e, _hp=hp: e.tensor_copy(
            negb8[:, _hp:_hp + 1], d_psA[:, 0:1]), waits=[("tPE", c_bc)],
            inc=("tDV", 1))
    c_bounds_dv = P.cnt["tDV"]
    c_bounds_sc = P.cnt["tSC"]

    slot_list = []
    for hp in range(HP):
        for h in range(2):
            jmax = 32 if h == 0 else 16
            for j in range(jmax):
                w = min(1024, 64 * (32 - j)) if h == 0 else 64 * (16 - j)
                masked = (h == 0 and j >= 16) or h == 1
                jlast0 = 31 if h == 0 else 15
                jlast1 = (31 - 8) if h == 0 else (15 - 8)
                slot_list.append((hp, h, j, w, masked,
                                  j == 0, (j == jlast0, j == jlast1),
                                  j == jmax - 1))
    exp_done = {}
    av_done = {}
    c_spill = {}
    dps_guard = [("tDV", c_bounds_dv)]

    def emit_av(s):
        hp, h, j, w, masked, st, sps, last = slot_list[s]
        p = s % 2
        vcol = 1040 * j + 130 * hp
        r = {}
        for hd in range(2):
            ptc = 2048 * hd + 1024 * p
            if s in mask_done:
                wa = [("tDV", mask_done[s][hd])]
            else:
                wa = [("tSC", exp_done[s][hd]), ("tGP", c_vones)]
            first = True
            for c0 in range(0, w, 512):
                cw = min(512, w - c0)
                sp = sps[c0 // 512]
                r[hd] = P.do("tensor", lambda e, _hd=hd, _v=vcol, _c0=c0, _cw=cw,
                             _st=st, _sp=sp, _pc=ptc: e.matmul(
                                 av_ps[_hd][0:65, _c0:_c0 + _cw],
                                 v_all[:, _v + 65 * _hd:_v + 65 * _hd + 65],
                                 pt[:, _pc + _c0:_pc + _c0 + _cw], start=_st, stop=_sp,
                                 skip_group_check=True),
                             waits=wa if first else [], inc=("tPE", 1))
                first = False
        av_done[s] = r

    mask_done = {}
    ns = len(slot_list)
    for s in range(ns):
        hp, h, j, w, masked, st, sps, last = slot_list[s]
        p = s % 2
        kcol = NCTX * hp + 128 * j
        qcol = NQ * hp + 1024 * h
        wtA = list(dps_guard)
        wtB = []
        dps_guard = []
        if s >= 1:
            wtA.append(("tSC", exp_done[s - 1][0]))
            wtB.append(("tSC", exp_done[s - 1][1]))
        dots_done = {}
        for c0 in range(0, w, 512):
            cw = min(512, w - c0)
            dots_done[0] = P.do("tensor", lambda e, _k=kcol, _q=qcol + c0, _c0=c0,
                                _cw=cw: e.matmul(
                d_psA[:, _c0:_c0 + _cw], k_all[0:64, _k:_k + 128],
                q_all[0:64, _q:_q + _cw], start=True, stop=True,
                tile_position=(0, 0), skip_group_check=True),
                waits=wtA if c0 == 0 else [], inc=("tPE", 1))
        for c0 in range(0, w, 512):
            cw = min(512, w - c0)
            dots_done[1] = P.do("tensor", lambda e, _k=kcol, _q=qcol + c0, _c0=c0,
                                _cw=cw: e.matmul(
                d_psB[:, _c0:_c0 + _cw], k_all[64:128, _k:_k + 128],
                q_all[64:128, _q:_q + _cw], start=True, stop=True,
                tile_position=(64, 0), skip_group_check=True),
                waits=wtB if c0 == 0 else [], inc=("tPE", 1))
        if s >= 1:
            emit_av(s - 1)
        eA = {}
        for hd in range(2):
            ws = [("tPE", dots_done[hd]), ("tDV", c_bounds_dv)]
            if s >= 2:
                ws.append(("tPE", av_done[s - 2][hd]))
            dps = d_psA if hd == 0 else d_psB
            eA[hd] = P.do("scalar", lambda e, _w=w, _hp=hp, _pc=2048 * hd + 1024 * p,
                          _d=dps: e.activation(
                pt[:, _pc:_pc + _w], _d[:, 0:_w], AF.Exp,
                bias=negb8[:, _hp:_hp + 1], scale=SCALE), waits=ws, inc=("tSC", 1))
        exp_done[s] = eA
        if masked:
            moff = w - 64
            mk = {}
            for hd in range(2):
                mk[hd] = P.do("vector", lambda e, _m=moff, _pc=2048 * hd + 1024 * p:
                              e.tensor_tensor(
                    pt[:, _pc + _m:_pc + _m + 64], pt[:, _pc + _m:_pc + _m + 64],
                    mask_s[:], ALU.mult),
                    waits=[("tSC", exp_done[s][hd]), ("qX", c_msk)], inc=("tDV", 1))
            mask_done[s] = mk
        if last:
            emit_av(s)
            c_av = av_done[s][1]
            wsp = [("qX", c_spill[hp - 1])] if (hp - 1 in c_spill and h == 0) else []
            c_norm = 0
            for head in range(2):
                c_rc = P.do("vector", lambda e, _hd=head: e.reciprocal(
                    s64[64:65, :], av_ps[_hd][64:65, :]),
                    waits=[("tPE", c_av), ("qX", P.cnt["qX"])], inc=("tDV", 1))
                c_aev = P.do("scalar", lambda e, _hd=head: e.activation(
                    av_sb[0:64, :], av_ps[_hd][0:64, :], AF.Copy),
                    waits=[("tPE", c_av), ("tDV", P.cnt["tDV"])], inc=("tSC", 1))
                c_cph = P.do("vector", lambda e: e.tensor_copy(
                    hl64[64:65, 0:1024], s64[64:65, :]),
                    waits=[("tDV", c_rc)], inc=("tDV", 1))
                c_hl = P.do("vector", lambda e: e.tensor_tensor(
                    hl64[64:65, 1024:2048], s64[64:65, :], hl64[64:65, 0:1024],
                    ALU.subtract), waits=[("tDV", c_cph)], inc=("tDV", 1))
                c_sh = P.do("sync", lambda e: e.dma_start(
                    bass.AP(rechl2, 0, [[1024, 2], [1, 1024]]),
                    bass.AP(hl64, 64 * 2048, [[2048, 1], [1, 2048]])),
                    waits=[("tDV", c_hl), ("tPE", P.cnt["tPE"])], inc=("qX", 16))
                for c in range(2):
                    wbc = [("qX", c_sh), ("tSC", P.cnt["tSC"])]
                    if c_norm:
                        wbc.append(("tDV", c_norm))
                    c_bc = P.do("tensor", lambda e, _c=c: e.matmul(
                        d_psA[:, 0:512], ones2[:], rechl2[:, 512 * _c:512 * _c + 512],
                        start=True, stop=True, skip_group_check=True),
                        waits=wbc, inc=("tPE", 1))
                    c_norm = P.do("vector", lambda e, _c=c, _hd=head, _h=h: e.tensor_tensor(
                        av_spl[0:64, 2048 * _hd + 1024 * _h + 512 * _c:
                               2048 * _hd + 1024 * _h + 512 * _c + 512],
                        av_sb[0:64, 512 * _c:512 * _c + 512],
                        d_psA[0:64, 0:512], ALU.mult),
                        waits=[("tPE", c_bc), ("tSC", c_aev)] + wsp, inc=("tDV", 1))
            dps_guard = [("tDV", c_norm)]
            if h == 1:
                for head in range(2):
                    c_spill[hp] = P.do("sync", lambda e, _hp=hp, _hd=head: e.dma_start(
                        bass.AP(av_d, (_hp * 128 + 64 * _hd) * NQ, [[NQ, 64], [1, NQ]]),
                        av_spl[0:64, 2048 * _hd:2048 * _hd + 2048]),
                        waits=[("tDV", P.cnt["tDV"])], inc=("qX", 16))
    c_spill_last = c_spill[HP - 1]

    P.flush()
    if stop_after == 'attn':
        return nc
    aex.close()
    kvq.close()

    P.barrier()
    om = ExitStack()
    u_fm = om.enter_context(nc.sbuf_tensor("u_fm", [128, 8 * NQ], BF16))
    facc = om.enter_context(nc.sbuf_tensor("facc", [128, NQT * 1024], F32))
    g_all = om.enter_context(nc.sbuf_tensor("g_all", [128, NQT * 16], F32))
    g2t = om.enter_context(nc.sbuf_tensor("g2t", [16, NQ], BF16))
    br_b = om.enter_context(nc.sbuf_tensor("br_b", [16, 1024], BF16))

    oex = ExitStack()
    av_all = oex.enter_context(nc.sbuf_tensor("av_all", [128, HP * NQ], BF16))
    wo_b = oex.enter_context(nc.sbuf_tensor("wo_b", [128, 8192], BF16))
    ident_f = oex.enter_context(nc.sbuf_tensor("ident_f_s", [128, 128], F32))
    xq_t = oex.enter_context(nc.sbuf_tensor("xq_t", [128, 2048], F32))
    res1 = oex.enter_context(nc.sbuf_tensor("res1", [128, 1024], F32))
    u_t = oex.enter_context(nc.sbuf_tensor("u_t", [128, 1024], F32))
    u_bf = oex.enter_context(nc.sbuf_tensor("u_bf", [128, 2048], BF16))
    cent_b = oex.enter_context(nc.sbuf_tensor("cent_b", [128, 128], BF16))
    sc_em = oex.enter_context(nc.sbuf_tensor("sc_em", [16, NQ], F32))
    eb_s = oex.enter_context(nc.sbuf_tensor("eb_s", [16, 1], F32))
    bs_b = oex.enter_context(nc.sbuf_tensor("bs_b", [2, 1024], BF16))
    rt = oex.enter_context(nc.sbuf_tensor("rt", [128, 80], F32))
    gg_t = oex.enter_context(nc.sbuf_tensor("gg_t", [128, 16], BF16))
    o_ps = [oex.enter_context(nc.psum_tensor(f"o_ps{i}", [128, 1024], F32)) for i in range(2)]
    tp2_ps = [oex.enter_context(nc.psum_tensor(f"tp2_ps{i}", [128, 128], BF16)) for i in range(2)]
    sc_ps = oex.enter_context(nc.psum_tensor("sc_ps", [16, 512], F32))
    misc_ps = oex.enter_context(nc.psum_tensor("misc_ps", [128, 16], F32))

    c_av_ld = P.do("sync", lambda e: e.dma_start(
        bass.AP(av_all, 0, [[HP * NQ, 128], [NQ, HP], [1, NQ]]),
        bass.AP(av_d, 0, [[NQ, 128], [128 * NQ, HP], [1, NQ]])),
        waits=[("qX", c_spill_last)], inc=("qX", 16))
    c_idf = P.do("sync", lambda e: e.dma_start(ident_f[:], ident_f_in[:, :]), inc=("qX", 16))
    c_wo = P.do("gpsimd", wcast_load(wo_b, 0, wo, 0), inc=("qW", 16))
    c_cent = P.do("gpsimd", lambda e: e.dma_start(
        bass.AP(cent_b, 0, [[128, 128], [16, 8], [1, 16]]),
        bass.AP(cent_t, 0, [[16, 128], [2048, 8], [1, 16]])), inc=("qW", 16))
    c_eb = P.do("gpsimd", lambda e: e.dma_start(eb_s[:], eb[:, :]), inc=("qW", 16))
    c_br = P.do("gpsimd", lambda e: e.dma_start(br_b[0:16, :], br[:, :]), inc=("qW", 16))
    c_bs = P.do("gpsimd", lambda e: e.dma_start(bs_b[:], bs[:, :]), inc=("qW", 16))

    xq_free = [[], []]
    ops_free = [[], []]
    ubf_free = [[], []]
    c_sq_prev = 0
    c_r_prev = 0
    for qt in range(NQT):
        b = qt % 2
        c_xq = P.do("sync", lambda e, _qt=qt, _b=b: e.dma_start(
            xq_t[:, 1024 * _b:1024 * _b + 1024], x_q[128 * _qt:128 * _qt + 128, :]),
            waits=list(xq_free[b]), inc=("qX", 16))
        wmm = [("qX", c_av_ld), ("qW", c_wo)] + ops_free[b]
        c_mm = None
        for ic in range(8):
            for nh in range(2):
                c_mm = P.do("tensor", lambda e, _qt=qt, _ic=ic, _nh=nh, _b=b: e.matmul(
                    o_ps[_b][:, 512 * _nh:512 * _nh + 512],
                    av_all[:, NQ * _ic + 128 * _qt:NQ * _ic + 128 * _qt + 128],
                    wo_b[:, 1024 * _ic + 512 * _nh:1024 * _ic + 512 * _nh + 512],
                    start=(_ic == 0), stop=(_ic == 7), skip_group_check=True),
                    waits=wmm if (ic == 0 and nh == 0) else [], inc=("tPE", 1))
        c_r1 = P.do("vector", lambda e, _b=b: e.tensor_tensor(
            res1[:, :], o_ps[_b][:, :], xq_t[:, 1024 * _b:1024 * _b + 1024], ALU.add),
            waits=[("tPE", c_mm), ("qX", c_xq), ("tSC", c_sq_prev)], inc=("tDV", 1))
        ops_free[b] = [("tDV", c_r1)]
        xq_free[b] = [("tDV", c_r1)]
        c_rs1 = P.do("vector", lambda e: e.reduce_sum(
            rt[:, 0:1], res1[:, :], axis=AX.X),
            waits=[("tSC", c_r_prev)], inc=("tDV", 1))
        wub = list(ubf_free[b])
        P.do("scalar", lambda e, _b=b: e.activation(
            u_bf[:, 1024 * _b:1024 * _b + 1024], res1[:, :], AF.Square,
            accum_out=rt[:, 1:2]), waits=[("tDV", c_r1)] + wub)
        c_sq = P.do("scalar", lambda e: e.copy(rt[:, 14:15], rt[:, 1:2]),
                    inc=("tSC", 1))
        c_sq_prev = c_sq
        d1 = P.do("vector", lambda e: e.tensor_scalar(
            rt[:, 2:3], rt[:, 0:1], 1.0 / D, None, ALU.mult, ALU.bypass),
            waits=[("tDV", c_rs1)], inc=("tDV", 1))
        d2 = P.do("vector", lambda e: e.tensor_tensor(
            rt[:, 3:4], rt[:, 2:3], rt[:, 2:3], ALU.mult),
            waits=[("tDV", d1)], inc=("tDV", 1))
        d3 = P.do("vector", lambda e: e.tensor_scalar(
            rt[:, 4:5], rt[:, 14:15], 1.0 / D, EPS, ALU.mult, ALU.add),
            waits=[("tSC", c_sq)], inc=("tDV", 1))
        d4 = P.do("vector", lambda e: e.tensor_tensor(
            rt[:, 5:6], rt[:, 4:5], rt[:, 3:4], ALU.subtract),
            waits=[("tDV", d3)], inc=("tDV", 1))
        c_vr = P.do("vector", lambda e: e.reciprocal(rt[:, 6:7], rt[:, 5:6]),
                    waits=[("tDV", d4)], inc=("tDV", 1))
        c_r = P.do("scalar", lambda e: e.activation(rt[:, 7:8], rt[:, 6:7], AF.Sqrt),
                   waits=[("tDV", c_vr)], inc=("tSC", 1))
        c_r_prev = c_r
        d5 = P.do("vector", lambda e: e.tensor_scalar(
            rt[:, 8:9], rt[:, 2:3], rt[:, 7:8], -1.0, ALU.mult, ALU.mult),
            waits=[("tSC", c_r)], inc=("tDV", 1))
        d6 = P.do("vector", lambda e: e.tensor_scalar(
            u_t[:, :], res1[:, :], rt[:, 7:8], rt[:, 8:9], ALU.mult, ALU.add),
            waits=[("tDV", d5)], inc=("tDV", 1))
        P.do("vector", lambda e, _qt=qt: e.tensor_tensor(
            facc[:, 1024 * _qt:1024 * _qt + 1024], res1[:, :], u_t[:, :], ALU.add),
            waits=[("tDV", d6)])
        c_ub = P.do("vector", lambda e, _b=b: e.tensor_copy(
            u_bf[:, 1024 * _b:1024 * _b + 1024], u_t[:, :]), inc=("tDV", 1))
        c_ev = None
        for dc in range(8):
            pb = dc % 2
            c_tp = P.do("tensor", lambda e, _b=b, _dc=dc, _pb=pb: e.transpose(
                tp2_ps[_pb][:, :], u_bf[:, 1024 * _b + 128 * _dc:1024 * _b + 128 * _dc + 128],
                ident_b[:]), waits=[("tDV", c_ub), ("tSC", P.cnt["tSC"])],
                inc=("tPE", 1))
            c_ev = P.do("scalar", lambda e, _dc=dc, _pb=pb, _qt=qt: e.activation(
                u_fm[:, NQ * _dc + 128 * _qt:NQ * _dc + 128 * _qt + 128],
                tp2_ps[_pb][:, :], AF.Copy), waits=[("tPE", c_tp)], inc=("tSC", 1))
        ubf_free[b] = [("tSC", c_ev)]
    c_o1_sc = P.cnt["tSC"]

    c_scev = {}
    for c in range(4):
        c_mm = None
        for dc in range(8):
            w0 = [("qW", c_cent), ("tSC", P.cnt["tSC"])]
            if c > 0:
                w0.append(("tDV", c_scev[c - 1]))
            c_mm = P.do("tensor", lambda e, _c=c, _dc=dc: e.matmul(
                sc_ps[:, :], cent_b[:, 16 * _dc:16 * _dc + 16],
                u_fm[:, NQ * _dc + 512 * _c:NQ * _dc + 512 * _c + 512],
                start=(_dc == 0), stop=(_dc == 7), skip_group_check=True),
                waits=w0 if dc == 0 else [], inc=("tPE", 1))
        c_scev[c] = P.do("vector", lambda e, _c=c: e.tensor_scalar(
            sc_em[:, 512 * _c:512 * _c + 512], sc_ps[:, :], eb_s[:, 0:1], None,
            ALU.add, ALU.bypass), waits=[("tPE", c_mm), ("qW", c_eb)], inc=("tDV", 1))
    c_es_prev = 0
    c_m1_prev = 0
    c_gt_prev = 0
    for qt in range(NQT):
        c_tr = P.do("tensor", lambda e, _qt=qt: e.transpose(
            misc_ps[:, :], sc_em[0:16, 128 * _qt:128 * _qt + 128], ident_f[0:16, 0:16]),
            waits=[("tDV", c_scev[qt // 4]), ("qX", c_idf),
                   ("tSC", c_es_prev), ("tDV", c_m1_prev)], inc=("tPE", 1))
        c_m1 = P.do("vector", lambda e: e.reduce_max(
            rt[:, 9:10], misc_ps[:, :], axis=AX.X, negate=True),
            waits=[("tPE", c_tr)], inc=("tDV", 1))
        c_m1_prev = c_m1
        c_es = P.do("scalar", lambda e: e.activation(
            rt[:, 32:48], misc_ps[:, :], AF.Exp, bias=rt[:, 9:10], scale=1.0),
            waits=[("tDV", c_m1)], inc=("tSC", 1))
        c_es_prev = c_es
        c_rsm = P.do("vector", lambda e: e.reduce_sum(
            rt[:, 10:11], rt[:, 32:48], axis=AX.X), waits=[("tSC", c_es)],
            inc=("tDV", 1))
        e1 = P.do("vector", lambda e: e.reciprocal(rt[:, 11:12], rt[:, 10:11]),
             waits=[("tDV", c_rsm)], inc=("tDV", 1))
        e2 = P.do("vector", lambda e: e.tensor_scalar(
            rt[:, 48:64], rt[:, 32:48], rt[:, 11:12], None, ALU.mult, ALU.bypass),
            waits=[("tDV", e1)], inc=("tDV", 1))
        c_rm1 = P.do("vector", lambda e: e.reduce_max(
            rt[:, 12:13], rt[:, 48:64], axis=AX.X),
            waits=[("tDV", e2)], inc=("tDV", 1))
        e3 = P.do("vector", lambda e: e.tensor_scalar(
            rt[:, 64:80], rt[:, 48:64], rt[:, 12:13], None, ALU.is_ge, ALU.bypass),
            waits=[("tDV", c_rm1)], inc=("tDV", 1))
        e4 = P.do("vector", lambda e: e.scalar_tensor_tensor(
            rt[:, 16:32], rt[:, 64:80], -1e4, rt[:, 48:64], ALU.mult, ALU.add),
            waits=[("tDV", e3)], inc=("tDV", 1))
        c_rm2 = P.do("vector", lambda e: e.reduce_max(
            rt[:, 13:14], rt[:, 16:32], axis=AX.X),
            waits=[("tDV", e4)], inc=("tDV", 1))
        e5 = P.do("vector", lambda e: e.tensor_scalar(
            rt[:, 64:80], rt[:, 48:64], rt[:, 13:14], None, ALU.is_ge, ALU.bypass),
            waits=[("tDV", c_rm2)], inc=("tDV", 1))
        c_g = P.do("vector", lambda e, _qt=qt: e.tensor_tensor(
            g_all[:, 16 * _qt:16 * _qt + 16], rt[:, 48:64], rt[:, 64:80], ALU.mult),
            waits=[("tDV", e5)], inc=("tDV", 1))
        c_gg = P.do("vector", lambda e, _qt=qt: e.tensor_scalar(
            gg_t[:, :], g_all[:, 16 * _qt:16 * _qt + 16], 2.0, None,
            ALU.mult, ALU.bypass), waits=[("tDV", c_g), ("tPE", c_gt_prev)],
            inc=("tDV", 1))
        c_gt = P.do("tensor", lambda e: e.transpose(
            tp2_ps[0][0:16, :], gg_t[:, :], ident_b[:]),
            waits=[("tDV", c_gg), ("tSC", P.cnt["tSC"])], inc=("tPE", 1))
        c_gt_prev = c_gt
        P.do("scalar", lambda e, _qt=qt: e.activation(
            g2t[0:16, 128 * _qt:128 * _qt + 128], tp2_ps[0][0:16, :], AF.Copy),
            waits=[("tPE", c_gt)], inc=("tSC", 1))
    c_o_done_sc = P.cnt["tSC"]
    c_o_done_dv = P.cnt["tDV"]

    if stop_after == 'o':
        P.do("sync", lambda e: e.dma_start(
            bass.AP(out_d, 0, [[1024, 128], [131072, 16], [1, 1024]]),
            bass.AP(facc, 0, [[16384, 128], [1024, 16], [1, 1024]])),
            waits=[("tDV", P.cnt["tDV"]), ("tSC", P.cnt["tSC"])], inc=("qX", 16))
        P.do("gpsimd", lambda e: e.dma_start(
            bass.AP(av_d, 0, [[256, 128], [1, 256]]),
            bass.AP(g_all, 0, [[256, 128], [1, 256]])),
            waits=[("tDV", P.cnt["tDV"])], inc=("qW", 16))
        P.do("sync", lambda e: e.dma_start(
            bass.AP(av_d, 128 * 256, [[2048, 16], [1, 2048]]),
            bass.AP(g2t, 0, [[2048, 16], [1, 2048]])), inc=("qX", 16))
        P.do("sync", lambda e: e.dma_start(
            bass.AP(av_d, 2 * 128 * 256, [[2048, 128], [1, 2048]]),
            u_fm[:, 0:2048]), inc=("qX", 16))
        P.flush()
        return nc
    P.flush()
    oex.close()

    P.barrier()
    mex = ExitStack()
    wr_b = mex.enter_context(nc.sbuf_tensor("wr_b", [128, 2 * 8192], BF16))
    ws_tmp = mex.enter_context(nc.sbuf_tensor("ws_tmp", [128, 8192], BF16))
    e_ps = [mex.enter_context(nc.psum_tensor(f"e_ps{i}", [128, 1024], F32)) for i in range(3)]

    c_ws0 = P.do("gpsimd", wcast_load(wr_b, 0, wst, 0), inc=("qW", 16))
    c_ws1 = P.do("gpsimd", wcast_load(ws_tmp, 0, wst, D * D), inc=("qW", 16))
    c_wsum = P.do("vector", lambda e: e.tensor_tensor(
        wr_b[:, 0:8192], wr_b[:, 0:8192], ws_tmp[:, 0:8192], ALU.add),
        waits=[("qW", c_ws1)], inc=("tDV", 1))
    w_ready = {0: ("tDV", c_wsum)}
    wbuf_last_mm = {0: 0, 1: 0}
    mm_done = {}
    stt_done = {}
    psum_guard = {0: None, 1: None, 2: None}
    for t in range(NE + 1):
        if t + 1 <= NE:
            ee = t
            half = (t + 1) % 2
            w_ready[t + 1] = ("qW", P.do(
                "gpsimd", wcast_load(wr_b, 8192 * half, wrt, ee * D * D),
                waits=[("tPE", wbuf_last_mm[half])] if wbuf_last_mm[half] else [],
                inc=("qW", 16)))
        half = t % 2
        for qt in range(NQT):
            r = (t * NQT + qt) % 3
            w0 = [w_ready[t]]
            if psum_guard[r] is not None:
                w0.append(psum_guard[r])
            c_mm = None
            for dc in range(8):
                for nh in range(2):
                    c_mm = P.do("tensor", lambda e, _t=t, _qt=qt, _dc=dc, _nh=nh,
                                _r=r, _hf=half: e.matmul(
                        e_ps[_r][:, 512 * _nh:512 * _nh + 512],
                        u_fm[:, NQ * _dc + 128 * _qt:NQ * _dc + 128 * _qt + 128],
                        wr_b[:, 8192 * _hf + 1024 * _dc + 512 * _nh:
                             8192 * _hf + 1024 * _dc + 512 * _nh + 512],
                        start=(_dc == 0), stop=(_dc == 7 and _t > 0),
                        skip_group_check=True),
                        waits=w0 if (dc == 0 and nh == 0) else [], inc=("tPE", 1))
            if t == 0:
                for nh in range(2):
                    P.do("tensor", lambda e, _qt=qt, _nh=nh, _r=r: e.matmul(
                        e_ps[_r][:, 512 * _nh:512 * _nh + 512],
                        g2t[0:16, 128 * _qt:128 * _qt + 128],
                        br_b[0:16, 512 * _nh:512 * _nh + 512],
                        start=False, stop=False, skip_group_check=True),
                        waits=[("tSC", c_o_done_sc), ("qW", c_br), ("qW", c_bs)] if (qt == 0 and nh == 0) else [],
                        inc=("tPE", 1))
                    c_mm = P.do("tensor", lambda e, _nh=nh, _r=r: e.matmul(
                        e_ps[_r][:, 512 * _nh:512 * _nh + 512],
                        ones2[:, 0:128],
                        bs_b[:, 512 * _nh:512 * _nh + 512],
                        start=False, stop=True, skip_group_check=True),
                        inc=("tPE", 1))
            mm_done[(t, qt)] = c_mm
            wbuf_last_mm[half] = c_mm
            c_stt = None
            w_prev_stt = [("tDV", stt_done[(t - 1, qt)])] if t > 0 else [("tDV", c_o_done_dv)]
            for nh in range(2):
                if t == 0:
                    c_stt = P.do("vector", lambda e, _qt=qt, _nh=nh, _r=r:
                                 e.scalar_tensor_tensor(
                                     facc[:, 1024 * _qt + 512 * _nh:1024 * _qt + 512 * _nh + 512],
                                     e_ps[_r][:, 512 * _nh:512 * _nh + 512], 0.5,
                                     facc[:, 1024 * _qt + 512 * _nh:1024 * _qt + 512 * _nh + 512],
                                     ALU.mult, ALU.add),
                                 waits=([("tPE", c_mm)] + w_prev_stt) if nh == 0 else [],
                                 inc=("tDV", 1))
                else:
                    c_stt = P.do("vector", lambda e, _t=t, _qt=qt, _nh=nh, _r=r:
                                 e.scalar_tensor_tensor(
                                     facc[:, 1024 * _qt + 512 * _nh:1024 * _qt + 512 * _nh + 512],
                                     e_ps[_r][:, 512 * _nh:512 * _nh + 512],
                                     g_all[:, 16 * _qt + (_t - 1):16 * _qt + _t],
                                     facc[:, 1024 * _qt + 512 * _nh:1024 * _qt + 512 * _nh + 512],
                                     ALU.mult, ALU.add),
                                 waits=([("tPE", c_mm)] + w_prev_stt) if nh == 0 else [],
                                 inc=("tDV", 1))
            stt_done[(t, qt)] = c_stt
            psum_guard[r] = ("tDV", c_stt)

    c_store = None
    for qt in range(NQT):
        c_store = P.do("sync", lambda e, _qt=qt: e.dma_start(
            out_d[128 * _qt:128 * _qt + 128, :], facc[:, 1024 * _qt:1024 * _qt + 1024]),
            waits=[("tDV", stt_done[(NE, qt)])], inc=("qX", 16))
    P.do("sync", lambda e: (e.wait_ge(sems["qX"], c_store), None)[1])
    P.flush()
    mex.close()
    om.close()
    es.close()
    return nc


_NC_CACHE = None


def _get_nc():
    global _NC_CACHE
    if _NC_CACHE is None:
        _NC_CACHE = build_nc()
    return _NC_CACHE


def kernel(L0_attn_x, L0_attn_Wq, L0_attn_Wk, L0_attn_Wv, L0_attn_Wo,
           L0_ffn_centroids, L0_ffn_expert_biases, L0_ffn_Wr, L0_ffn_br,
           L0_ffn_Ws, L0_ffn_bs):
    x = np.ascontiguousarray(np.asarray(L0_attn_x, dtype=np.float32))
    wq = np.ascontiguousarray(np.asarray(L0_attn_Wq, dtype=np.float32))
    wk = np.ascontiguousarray(np.asarray(L0_attn_Wk, dtype=np.float32))
    wv = np.ascontiguousarray(np.asarray(L0_attn_Wv, dtype=np.float32))
    wo = np.ascontiguousarray(np.asarray(L0_attn_Wo, dtype=np.float32))
    wrt = np.ascontiguousarray(np.asarray(L0_ffn_Wr, dtype=np.float32).transpose(0, 2, 1))
    wst = np.ascontiguousarray(np.asarray(L0_ffn_Ws, dtype=np.float32).transpose(0, 2, 1))
    cent_t = np.ascontiguousarray(np.asarray(L0_ffn_centroids, dtype=np.float32).T)
    ebv = np.ascontiguousarray(np.asarray(L0_ffn_expert_biases, dtype=np.float32).reshape(16, 1))
    brv = np.ascontiguousarray(np.asarray(L0_ffn_br, dtype=np.float32))
    bsv = np.ascontiguousarray(np.asarray(L0_ffn_bs, dtype=np.float32))
    ident_b = np.eye(128, dtype=BF)
    ident_f = np.eye(128, dtype=np.float32)

    perms = []
    for par in range(2):
        idx = np.concatenate([
            np.arange(128 * (31 - k) + 64 * par, 128 * (31 - k) + 64 * par + 64)
            for k in range(32)])
        perms.append(idx)
    masks = []
    for par in range(2):
        jj = np.arange(128)[:, None]
        ii = np.arange(64)[None, :]
        masks.append((jj <= 64 * par + ii).astype(BF))

    in_maps = []
    for c in range(8):
        b, par = c // 2, c % 2
        in_maps.append({
            "x_ctx": x[b],
            "x_q": np.ascontiguousarray(x[b][perms[par]]),
            "wq": wq, "wk": wk, "wv": wv, "wo": wo,
            "wrt": wrt, "wst": wst, "cent_t": cent_t, "eb": ebv,
            "br": brv, "bs": bsv, "mask": masks[par],
            "ident_b": ident_b, "ident_f": ident_f,
        })

    nc = _get_nc()
    res = run_bass_kernel_spmd(nc, in_maps, core_ids=list(range(8)))
    out = np.empty((4, 4096, 1024), dtype=np.float32)
    for c in range(8):
        b, par = c // 2, c % 2
        out[b][perms[par]] = res.results[c]["out"]
    return out


# version 06
# speedup vs baseline: 1.5670x; 1.0592x over previous
import numpy as np
import ml_dtypes
from contextlib import ExitStack

import concourse.bass as bass
import concourse.mybir as mybir
from concourse.bass_utils import run_bass_kernel_spmd

F32 = mybir.dt.float32
BF16 = mybir.dt.bfloat16
AF = mybir.ActivationFunctionType
ALU = mybir.AluOpType
AX = mybir.AxisListType
BF = ml_dtypes.bfloat16

D = 1024
NCTX = 4096
NQ = 2048
NKVT = 32
NQT = 16
HP = 8
SCALE = 0.125
EPS = 1e-5
NE = 16

ENGS = ("tensor", "vector", "scalar", "gpsimd", "sync")


class Prog:

    def __init__(self, nc, sems):
        self.nc = nc
        self.sems = sems
        self.cnt = {k: 0 for k in sems}
        self.q = {e: [] for e in ENGS}
        self.lastw = {e: {} for e in ENGS}

    def do(self, eng, fn, waits=(), inc=None):
        real_waits = []
        lw = self.lastw[eng]
        for s, v in waits:
            if s is None or v <= 0:
                continue
            if lw.get(s, -1) < v:
                real_waits.append((s, v))
                lw[s] = v
        if inc is not None:
            self.cnt[inc[0]] += inc[1]
        self.q[eng].append((fn, tuple(real_waits), inc))
        return self.cnt[inc[0]] if inc is not None else None

    def barrier(self):
        self.cnt["bar"] += 5
        tgt = self.cnt["bar"]
        for e in ENGS:
            def fn(eng, _t=tgt):
                eng.sem_inc(self.sems["bar"], 1)
                eng.wait_ge(self.sems["bar"], _t)
                return None
            self.q[e].append((fn, (), None))
            self.lastw[e]["bar"] = tgt

    def flush(self):
        with self.nc.Block() as blk:
            for e in ENGS:
                ops = self.q[e]
                self.q[e] = []
                if not ops:
                    continue

                def body(eng, _ops=tuple(ops), _sems=self.sems):
                    for fn, waits, inc in _ops:
                        for s, v in waits:
                            eng.wait_ge(_sems[s], v)
                        r = fn(eng)
                        if inc is not None and r is not None:
                            r.then_inc(_sems[inc[0]], inc[1])
                getattr(blk, e)(body)


def build_nc(stop_after='all'):
    nc = bass.Bass("TRN2", target_bir_lowering=False, detect_race_conditions=False)

    x_ctx = nc.dram_tensor("x_ctx", [NCTX, D], F32, kind="ExternalInput")
    x_q = nc.dram_tensor("x_q", [NQ, D], F32, kind="ExternalInput")
    wq = nc.dram_tensor("wq", [D, D], F32, kind="ExternalInput")
    wk = nc.dram_tensor("wk", [D, D], F32, kind="ExternalInput")
    wv = nc.dram_tensor("wv", [D, D], F32, kind="ExternalInput")
    wo = nc.dram_tensor("wo", [D, D], F32, kind="ExternalInput")
    wrt = nc.dram_tensor("wrt", [NE, D, D], F32, kind="ExternalInput")
    wst = nc.dram_tensor("wst", [2, D, D], F32, kind="ExternalInput")
    cent_t = nc.dram_tensor("cent_t", [D, 16], F32, kind="ExternalInput")
    eb = nc.dram_tensor("eb", [16, 1], F32, kind="ExternalInput")
    br = nc.dram_tensor("br", [16, D], F32, kind="ExternalInput")
    bs = nc.dram_tensor("bs", [2, D], F32, kind="ExternalInput")
    mask_in = nc.dram_tensor("mask", [128, 64], BF16, kind="ExternalInput")
    ident_b_in = nc.dram_tensor("ident_b", [128, 128], BF16, kind="ExternalInput")
    ident_f_in = nc.dram_tensor("ident_f", [128, 128], F32, kind="ExternalInput")
    sel2_in = nc.dram_tensor("sel2", [16, 1024], BF16, kind="ExternalInput")
    out_d = nc.dram_tensor("out", [NQ, D], F32, kind="ExternalOutput")
    av_d = nc.dram_tensor("av_spill", [HP, 128, NQ], BF16, kind="ExternalOutput")
    sums_d = nc.dram_tensor("sums_spill", [HP, 2, NQ], F32)

    es = ExitStack()
    sems = {}
    for s in ("qW", "qX", "tPE", "tSC", "tDV", "tGP", "bar"):
        sems[s] = es.enter_context(nc.semaphore(s))
    P = Prog(nc, sems)

    ident_b = es.enter_context(nc.sbuf_tensor("ident_b_s", [128, 128], BF16))
    mask_s = es.enter_context(nc.sbuf_tensor("mask_s", [128, 64], BF16))
    ones1 = es.enter_context(nc.sbuf_tensor("ones1", [128, 1], BF16))
    ones2 = es.enter_context(nc.sbuf_tensor("ones2", [2, 128], BF16))
    negb8 = es.enter_context(nc.sbuf_tensor("negb8", [128, 8], F32))
    stats = es.enter_context(nc.sbuf_tensor("stats", [128, 24], F32))

    c_idb = P.do("sync", lambda e: e.dma_start(ident_b[:], ident_b_in[:, :]), inc=("qX", 16))
    c_msk = P.do("sync", lambda e: e.dma_start(mask_s[:], mask_in[:, :]), inc=("qX", 16))
    P.do("gpsimd", lambda e: e.memset(ones1[:], 1.0))
    c_ones = P.do("gpsimd", lambda e: e.memset(ones2[:], 1.0), inc=("tGP", 1))

    def wcast_load(dst, dst_col, src, e_off):
        dst_stride = dst.shape[1]

        def fn(eng):
            return eng.dma_start(
                bass.AP(dst, dst_col, [[dst_stride, 128], [1024, 8], [1, 1024]]),
                bass.AP(src, e_off, [[1024, 128], [131072, 8], [1, 1024]]),
            )
        return fn

    kvq = ExitStack()
    k_all = kvq.enter_context(nc.sbuf_tensor("k_all", [128, HP * NCTX], BF16))
    v_all = kvq.enter_context(nc.sbuf_tensor("v_all", [128, NKVT * 1040], BF16))
    q_all = kvq.enter_context(nc.sbuf_tensor("q_all", [128, HP * NQ], BF16))
    xt = kvq.enter_context(nc.sbuf_tensor("xt", [128, 2 * 1024], F32))
    xlt = kvq.enter_context(nc.sbuf_tensor("xlt", [128, 1024], BF16))
    xlf = kvq.enter_context(nc.sbuf_tensor("xlf", [128, 8 * 256], BF16))

    lnst = {
        "xt_free": [[], []],
        "xlt_free": [],
        "cr_prev": [0, 0],
        "g": 0,
    }

    def ln_tile(src_dram, row0, dst_fm_col, tp_ps, qx_extra=()):
        g = lnst["g"]
        lnst["g"] += 1
        b = g % 2
        sb = b * 12
        c_x = P.do("sync", lambda e, _b=b, _r=row0, _s=src_dram: e.dma_start(
            xt[:, 1024 * _b:1024 * _b + 1024], _s[_r:_r + 128, :]),
            waits=list(lnst["xt_free"][b]) + list(qx_extra), inc=("qX", 16))
        xv = xt[:, 1024 * b:1024 * b + 1024]
        c_rs = P.do("vector", lambda e, _x=xv, _c=sb: e.reduce_sum(
            stats[:, _c:_c + 1], _x, axis=AX.X),
            waits=[("qX", c_x), ("tSC", lnst["cr_prev"][b])], inc=("tDV", 1))
        P.do("scalar", lambda e, _x=xv, _c=sb: e.activation(
            xlt[:, :], _x, AF.Square,
            accum_out=stats[:, _c + 1:_c + 2]),
            waits=[("qX", c_x), ("tSC", lnst["cr_prev"][b])] + list(lnst["xlt_free"]))
        c_sq = P.do("scalar", lambda e, _c=sb: e.copy(
            stats[:, _c + 9:_c + 10], stats[:, _c + 1:_c + 2]), inc=("tSC", 1))
        c1 = P.do("vector", lambda e, _c=sb: e.tensor_scalar(
            stats[:, _c + 2:_c + 3], stats[:, _c:_c + 1], 1.0 / D, None,
            ALU.mult, ALU.bypass), waits=[("tDV", c_rs)], inc=("tDV", 1))
        c2 = P.do("vector", lambda e, _c=sb: e.tensor_tensor(
            stats[:, _c + 3:_c + 4], stats[:, _c + 2:_c + 3], stats[:, _c + 2:_c + 3],
            ALU.mult), waits=[("tDV", c1)], inc=("tDV", 1))
        c3 = P.do("vector", lambda e, _c=sb: e.tensor_scalar(
            stats[:, _c + 4:_c + 5], stats[:, _c + 9:_c + 10], 1.0 / D, EPS,
            ALU.mult, ALU.add), waits=[("tSC", c_sq)], inc=("tDV", 1))
        c4 = P.do("vector", lambda e, _c=sb: e.tensor_tensor(
            stats[:, _c + 5:_c + 6], stats[:, _c + 4:_c + 5], stats[:, _c + 3:_c + 4],
            ALU.subtract), waits=[("tDV", c3)], inc=("tDV", 1))
        c_vr = P.do("vector", lambda e, _c=sb: e.reciprocal(
            stats[:, _c + 6:_c + 7], stats[:, _c + 5:_c + 6]),
            waits=[("tDV", c4)], inc=("tDV", 1))
        c_r = P.do("scalar", lambda e, _c=sb: e.activation(
            stats[:, _c + 7:_c + 8], stats[:, _c + 6:_c + 7], AF.Sqrt),
            waits=[("tDV", c_vr)], inc=("tSC", 1))
        lnst["cr_prev"][b] = c_r
        c5 = P.do("vector", lambda e, _c=sb: e.tensor_scalar(
            stats[:, _c + 8:_c + 9], stats[:, _c + 2:_c + 3], stats[:, _c + 7:_c + 8],
            -1.0, ALU.mult, ALU.mult), waits=[("tSC", c_r)], inc=("tDV", 1))
        c_ap = P.do("vector", lambda e, _x=xv, _c=sb: e.tensor_scalar(
            xlt[:, :], _x, stats[:, _c + 7:_c + 8],
            stats[:, _c + 8:_c + 9], ALU.mult, ALU.add),
            waits=[("tDV", c5)], inc=("tDV", 1))
        lnst["xt_free"][b] = [("tDV", c_ap), ("tSC", c_sq)]
        c_ev = None
        c_tp = None
        for dc in range(8):
            pb = dc % 2
            c_tp = P.do("tensor", lambda e, _dc=dc, _pb=pb, _tp=tp_ps: e.transpose(
                _tp[_pb][:, :], xlt[:, 128 * _dc:128 * _dc + 128],
                ident_b[:]),
                waits=[("tDV", c_ap), ("tSC", P.cnt["tSC"]), ("qX", c_idb)],
                inc=("tPE", 1))
            c_ev = P.do("scalar", lambda e, _dc=dc, _pb=pb, _col=dst_fm_col, _tp=tp_ps:
                        e.activation(
                            xlf[:, _col + 256 * _dc:_col + 256 * _dc + 128],
                            _tp[_pb][:, :], AF.Copy),
                        waits=[("tPE", c_tp)], inc=("tSC", 1))
        lnst["xlt_free"] = [("tPE", c_tp)]
        return c_ev

    p0 = ExitStack()
    wk_b = p0.enter_context(nc.sbuf_tensor("wk_b", [128, 8192], BF16))
    wv_b = p0.enter_context(nc.sbuf_tensor("wv_b", [128, 8192], BF16))
    tp_ps_a = [p0.enter_context(nc.psum_tensor(f"tp_ps{i}", [128, 128], BF16)) for i in range(2)]
    kq_ps_a = [p0.enter_context(nc.psum_tensor(f"kq_ps{i}", [128, 256], F32)) for i in range(2)]
    v_ps = [p0.enter_context(nc.psum_tensor(f"v_ps{i}", [128, 1024], F32)) for i in range(2)]

    c_wk = P.do("gpsimd", wcast_load(wk_b, 0, wk, 0), inc=("qW", 16))
    c_wv = P.do("gpsimd", wcast_load(wv_b, 0, wv, 0), inc=("qW", 16))
    c_vones = P.do("gpsimd", lambda e: e.memset(v_all[:, :], 1.0), inc=("tGP", 1))

    kq_free = [[], []]
    v_free = [[], []]
    for kb in range(NCTX // 256):
        c_ev = 0
        for t2 in range(2):
            c_ev = ln_tile(x_ctx, 256 * kb + 128 * t2, 128 * t2, tp_ps_a)
            if stop_after == 'ln0' and kb == 0 and t2 == 0:
                P.do("sync", lambda e: e.dma_start(out_d[0:128, 0:24], stats[:, :]),
                     waits=[("tSC", P.cnt["tSC"]), ("tDV", P.cnt["tDV"])], inc=("qX", 16))
                P.do("sync", lambda e: e.dma_start(
                    bass.AP(av_d, 0, [[2048, 128], [1, 1024]]), xlt[:, :]), inc=("qX", 16))
                P.do("sync", lambda e: e.dma_start(
                    bass.AP(av_d, 128 * 2048, [[2048, 128], [1, 2048]]), xlf[:, :]), inc=("qX", 16))
                P.flush()
                return nc
        for hp in range(HP):
            pb = hp % 2
            w0 = [("tSC", c_ev), ("qW", c_wk)] + kq_free[pb]
            c_mm = None
            for dc in range(8):
                c_mm = P.do("tensor", lambda e, _hp=hp, _dc=dc, _pb=pb: e.matmul(
                    kq_ps_a[_pb][:, :],
                    wk_b[:, 1024 * _dc + 128 * _hp:1024 * _dc + 128 * _hp + 128],
                    xlf[:, 256 * _dc:256 * _dc + 256],
                    start=(_dc == 0), stop=(_dc == 7), skip_group_check=True),
                    waits=w0 if dc == 0 else [], inc=("tPE", 1))
            c_kev = P.do("scalar", lambda e, _hp=hp, _kb=kb, _pb=pb: e.activation(
                k_all[:, NCTX * _hp + 256 * _kb:NCTX * _hp + 256 * _kb + 256],
                kq_ps_a[_pb][:, :], AF.Copy), waits=[("tPE", c_mm)], inc=("tSC", 1))
            kq_free[pb] = [("tSC", c_kev)]
        for t2 in range(2):
            j = 2 * kb + t2
            pb = j % 2
            w0 = [("tSC", c_ev), ("qW", c_wv)] + v_free[pb]
            c_mm = None
            for dc in range(8):
                for nh in range(2):
                    c_mm = P.do("tensor", lambda e, _t2=t2, _dc=dc, _nh=nh, _pb=pb: e.matmul(
                        v_ps[_pb][:, 512 * _nh:512 * _nh + 512],
                        xlf[:, 256 * _dc + 128 * _t2:256 * _dc + 128 * _t2 + 128],
                        wv_b[:, 1024 * _dc + 512 * _nh:1024 * _dc + 512 * _nh + 512],
                        start=(_dc == 0), stop=(_dc == 7), skip_group_check=True),
                        waits=w0 if (dc == 0 and nh == 0) else [], inc=("tPE", 1))
            c_vev = P.do("scalar", lambda e, _j=j, _pb=pb: e.activation(
                bass.AP(v_all, 1040 * _j, [[NKVT * 1040, 128], [130, 8], [65, 2], [1, 64]]),
                bass.AP(v_ps[_pb], 0, [[1024, 128], [128, 8], [64, 2], [1, 64]]),
                AF.Copy),
                waits=[("tPE", c_mm), ("tGP", c_vones)], inc=("tSC", 1))
            v_free[pb] = [("tSC", c_vev)]
    c_p0a_pe = P.cnt["tPE"]

    P.flush()
    if stop_after == 'p0a':
        return nc
    p0.close()

    p0b = ExitStack()
    wq_b = p0b.enter_context(nc.sbuf_tensor("wq_b", [128, 8192], BF16))
    tp_ps_b = [p0b.enter_context(nc.psum_tensor(f"tpb_ps{i}", [128, 128], BF16)) for i in range(2)]
    kq_ps_b = [p0b.enter_context(nc.psum_tensor(f"kqb_ps{i}", [128, 256], F32)) for i in range(2)]

    c_wq = P.do("gpsimd", wcast_load(wq_b, 0, wq, 0),
                waits=[("tPE", c_p0a_pe)], inc=("qW", 16))
    kq_free = [[], []]
    for qb in range(NQ // 256):
        c_ev = 0
        for t2 in range(2):
            c_ev = ln_tile(x_q, 256 * qb + 128 * t2, 128 * t2, tp_ps_b)
        for hp in range(HP):
            pb = hp % 2
            w0 = [("tSC", c_ev), ("qW", c_wq)] + kq_free[pb]
            c_mm = None
            for dc in range(8):
                c_mm = P.do("tensor", lambda e, _hp=hp, _dc=dc, _pb=pb: e.matmul(
                    kq_ps_b[_pb][:, :],
                    wq_b[:, 1024 * _dc + 128 * _hp:1024 * _dc + 128 * _hp + 128],
                    xlf[:, 256 * _dc:256 * _dc + 256],
                    start=(_dc == 0), stop=(_dc == 7), skip_group_check=True),
                    waits=w0 if dc == 0 else [], inc=("tPE", 1))
            c_qev = P.do("scalar", lambda e, _hp=hp, _qb=qb, _pb=pb: e.activation(
                q_all[:, NQ * _hp + 256 * _qb:NQ * _hp + 256 * _qb + 256],
                kq_ps_b[_pb][:, :], AF.Copy), waits=[("tPE", c_mm)], inc=("tSC", 1))
            kq_free[pb] = [("tSC", c_qev)]
    c_p0b_pe = P.cnt["tPE"]
    c_p0b_sc = P.cnt["tSC"]

    if stop_after == 'p0b':
        P.do("sync", lambda e: e.dma_start(
            bass.AP(av_d, 0, [[2048, 128], [1, 2048]]),
            k_all[:, 0:2048]), waits=[("tSC", P.cnt["tSC"])], inc=("qX", 16))
        P.do("sync", lambda e: e.dma_start(
            bass.AP(av_d, 128 * 2048, [[2048, 128], [1, 2048]]),
            k_all[:, 7 * 4096:7 * 4096 + 2048]), inc=("qX", 16))
        P.do("sync", lambda e: e.dma_start(
            bass.AP(av_d, 2 * 128 * 2048, [[2048, 128], [1, 2048]]),
            q_all[:, 7 * 2048:7 * 2048 + 2048]), inc=("qX", 16))
        P.do("sync", lambda e: e.dma_start(
            bass.AP(av_d, 3 * 128 * 2048, [[2048, 128], [1, 2048]]),
            v_all[:, 0:2048]), inc=("qX", 16))
        P.flush()
        return nc
    P.flush()
    p0b.close()

    aex = ExitStack()
    pt = aex.enter_context(nc.sbuf_tensor("pt", [128, 4096], BF16))
    ksq = pt
    s64r = aex.enter_context(nc.sbuf_tensor("s64r", [65, 2048], F32))
    av_spl = aex.enter_context(nc.sbuf_tensor("av_spl", [64, 4096], BF16))
    km = aex.enter_context(nc.sbuf_tensor("km", [1, 16], F32))
    bsc = aex.enter_context(nc.sbuf_tensor("bsc", [1, 4], F32))
    negvb = aex.enter_context(nc.sbuf_tensor("negvb", [1, 1], BF16))
    av_ps = [aex.enter_context(nc.psum_tensor(f"av_ps{i}", [65, 1024], F32)) for i in range(2)]
    d_psA = aex.enter_context(nc.psum_tensor("d_psA", [128, 1024], F32))
    d_psB = aex.enter_context(nc.psum_tensor("d_psB", [128, 1024], F32))

    for hp in range(HP):
        c_k2 = P.do("vector", lambda e, _hp=hp: e.tensor_tensor(
            ksq[:, :], k_all[:, NCTX * _hp:NCTX * _hp + 4096],
            k_all[:, NCTX * _hp:NCTX * _hp + 4096], ALU.mult),
            waits=[("tPE", c_p0b_pe), ("tSC", c_p0b_sc)], inc=("tDV", 1))
        c_rm = 0
        for c in range(8):
            pb = c % 2
            c_n2 = P.do("tensor", lambda e, _c=c, _pb=pb: e.matmul(
                d_psA[0:1, 512 * _pb:512 * _pb + 512], ones1[:],
                ksq[:, 512 * _c:512 * _c + 512],
                start=True, stop=True, skip_group_check=True),
                waits=[("tDV", c_k2), ("tGP", c_ones)] if c < 2 else [("tDV", c_rm)],
                inc=("tPE", 1))
            c_rm = P.do("vector", lambda e, _c=c, _pb=pb: e.reduce_max(
                km[0:1, _c:_c + 1], d_psA[0:1, 512 * _pb:512 * _pb + 512], axis=AX.X),
                waits=[("tPE", c_n2)], inc=("tDV", 1))
        c_km = P.do("vector", lambda e: e.reduce_max(
            bsc[0:1, 0:1], km[0:1, 0:8], axis=AX.X),
            waits=[("tDV", c_rm)], inc=("tDV", 1))
        c_q2 = P.do("vector", lambda e, _hp=hp: e.tensor_tensor(
            ksq[:, 0:2048], q_all[:, NQ * _hp:NQ * _hp + 2048],
            q_all[:, NQ * _hp:NQ * _hp + 2048], ALU.mult),
            waits=[("tPE", P.cnt["tPE"])], inc=("tDV", 1))
        for c in range(4):
            pb = c % 2
            c_n2 = P.do("tensor", lambda e, _c=c, _pb=pb: e.matmul(
                d_psA[0:1, 512 * _pb:512 * _pb + 512], ones1[:],
                ksq[:, 512 * _c:512 * _c + 512],
                start=True, stop=True, skip_group_check=True),
                waits=[("tDV", c_q2)] if c < 2 else [("tDV", c_rm)], inc=("tPE", 1))
            c_rm = P.do("vector", lambda e, _c=c, _pb=pb: e.reduce_max(
                km[0:1, 8 + _c:9 + _c], d_psA[0:1, 512 * _pb:512 * _pb + 512], axis=AX.X),
                waits=[("tPE", c_n2)], inc=("tDV", 1))
        c_qm = P.do("vector", lambda e: e.reduce_max(
            bsc[0:1, 1:2], km[0:1, 8:12], axis=AX.X),
            waits=[("tDV", c_rm)], inc=("tDV", 1))
        c_pr = P.do("vector", lambda e: e.tensor_tensor(
            bsc[0:1, 2:3], bsc[0:1, 0:1], bsc[0:1, 1:2], ALU.mult),
            waits=[("tDV", c_qm)], inc=("tDV", 1))
        c_sq2 = P.do("scalar", lambda e: e.activation(
            bsc[0:1, 3:4], bsc[0:1, 2:3], AF.Sqrt), waits=[("tDV", c_pr)], inc=("tSC", 1))
        c_nv = P.do("vector", lambda e: e.tensor_scalar(
            negvb[0:1, 0:1], bsc[0:1, 3:4], -SCALE, None, ALU.mult, ALU.bypass),
            waits=[("tSC", c_sq2)], inc=("tDV", 1))
        c_bc = P.do("tensor", lambda e: e.matmul(
            d_psA[:, 0:1], ones2[0:1, :], negvb[0:1, 0:1], start=True, stop=True,
            skip_group_check=True), waits=[("tDV", c_nv)], inc=("tPE", 1))
        P.do("vector", lambda e, _hp=hp: e.tensor_copy(
            negb8[:, _hp:_hp + 1], d_psA[:, 0:1]), waits=[("tPE", c_bc)],
            inc=("tDV", 1))
    c_bounds_dv = P.cnt["tDV"]
    c_bounds_sc = P.cnt["tSC"]

    slot_list = []
    for hp in range(HP):
        for h in range(2):
            jmax = 32 if h == 0 else 16
            for j in range(jmax):
                w = min(1024, 64 * (32 - j)) if h == 0 else 64 * (16 - j)
                masked = (h == 0 and j >= 16) or h == 1
                jlast0 = 31 if h == 0 else 15
                jlast1 = (31 - 8) if h == 0 else (15 - 8)
                slot_list.append((hp, h, j, w, masked,
                                  j == 0, (j == jlast0, j == jlast1),
                                  j == jmax - 1))
    exp_done = {}
    av_done = {}
    c_spill = {}
    c_sums_dma = {}
    dps_guard = [("tDV", c_bounds_dv)]

    def emit_av(s):
        hp, h, j, w, masked, st, sps, last = slot_list[s]
        p = s % 2
        vcol = 1040 * j + 130 * hp
        r = {}
        for hd in range(2):
            ptc = 2048 * hd + 1024 * p
            if s in mask_done:
                wa = [("tDV", mask_done[s][hd])]
            else:
                wa = [("tSC", exp_done[s][hd]), ("tGP", c_vones)]
            first = True
            for c0 in range(0, w, 512):
                cw = min(512, w - c0)
                sp = sps[c0 // 512]
                r[hd] = P.do("tensor", lambda e, _hd=hd, _v=vcol, _c0=c0, _cw=cw,
                             _st=st, _sp=sp, _pc=ptc: e.matmul(
                                 av_ps[_hd][0:65, _c0:_c0 + _cw],
                                 v_all[:, _v + 65 * _hd:_v + 65 * _hd + 65],
                                 pt[:, _pc + _c0:_pc + _c0 + _cw], start=_st, stop=_sp,
                                 skip_group_check=True),
                             waits=wa if first else [], inc=("tPE", 1))
                first = False
        av_done[s] = r

    mask_done = {}
    ns = len(slot_list)
    for s in range(ns):
        hp, h, j, w, masked, st, sps, last = slot_list[s]
        p = s % 2
        kcol = NCTX * hp + 128 * j
        qcol = NQ * hp + 1024 * h
        wtA = list(dps_guard)
        wtB = []
        dps_guard = []
        if s >= 1:
            wtA.append(("tSC", exp_done[s - 1][0]))
            wtB.append(("tSC", exp_done[s - 1][1]))
        dots_done = {}
        for c0 in range(0, w, 512):
            cw = min(512, w - c0)
            dots_done[0] = P.do("tensor", lambda e, _k=kcol, _q=qcol + c0, _c0=c0,
                                _cw=cw: e.matmul(
                d_psA[:, _c0:_c0 + _cw], k_all[0:64, _k:_k + 128],
                q_all[0:64, _q:_q + _cw], start=True, stop=True,
                tile_position=(0, 0), skip_group_check=True),
                waits=wtA if c0 == 0 else [], inc=("tPE", 1))
        for c0 in range(0, w, 512):
            cw = min(512, w - c0)
            dots_done[1] = P.do("tensor", lambda e, _k=kcol, _q=qcol + c0, _c0=c0,
                                _cw=cw: e.matmul(
                d_psB[:, _c0:_c0 + _cw], k_all[64:128, _k:_k + 128],
                q_all[64:128, _q:_q + _cw], start=True, stop=True,
                tile_position=(64, 0), skip_group_check=True),
                waits=wtB if c0 == 0 else [], inc=("tPE", 1))
        if s >= 1:
            emit_av(s - 1)
        eA = {}
        for hd in range(2):
            ws = [("tPE", dots_done[hd]), ("tDV", c_bounds_dv)]
            if s >= 2:
                ws.append(("tPE", av_done[s - 2][hd]))
            dps = d_psA if hd == 0 else d_psB
            eA[hd] = P.do("scalar", lambda e, _w=w, _hp=hp, _pc=2048 * hd + 1024 * p,
                          _d=dps: e.activation(
                pt[:, _pc:_pc + _w], _d[:, 0:_w], AF.Exp,
                bias=negb8[:, _hp:_hp + 1], scale=SCALE), waits=ws, inc=("tSC", 1))
        exp_done[s] = eA
        if masked:
            moff = w - 64
            mk = {}
            for hd in range(2):
                mk[hd] = P.do("vector", lambda e, _m=moff, _pc=2048 * hd + 1024 * p:
                              e.tensor_tensor(
                    pt[:, _pc + _m:_pc + _m + 64], pt[:, _pc + _m:_pc + _m + 64],
                    mask_s[:], ALU.mult),
                    waits=[("tSC", exp_done[s][hd]), ("qX", c_msk)], inc=("tDV", 1))
            mask_done[s] = mk
        if last:
            emit_av(s)
            c_av = av_done[s][1]
            wsp = [("qX", c_spill[hp - 1])] if (hp - 1 in c_spill and h == 0) else []
            c_sumsdma = c_sums_dma.get((hp, h - 1)) or c_sums_dma.get((hp - 1, 1))
            c_evs = []
            for head in range(2):
                c_evs.append(P.do("scalar", lambda e, _hd=head, _h=h: e.activation(
                    av_spl[0:64, 2048 * _hd + 1024 * _h:2048 * _hd + 1024 * _h + 1024],
                    av_ps[_hd][0:64, :], AF.Copy),
                    waits=[("tPE", c_av)] + wsp, inc=("tSC", 1)))
                c_evs.append(P.do("scalar", lambda e, _hd=head: e.activation(
                    s64r[64:65, 1024 * _hd:1024 * _hd + 1024],
                    av_ps[_hd][64:65, :], AF.Copy),
                    waits=([("qX", c_sumsdma)] if c_sumsdma else []), inc=("tSC", 1)))
            c_ep = c_evs[-1]
            dps_guard = [("tSC", c_ep)]
            c_sums_dma[(hp, h)] = P.do("sync", lambda e, _hp=hp, _h=h: e.dma_start(
                bass.AP(sums_d, _hp * 2 * NQ + 1024 * _h, [[2 * NQ, 1], [NQ, 2], [1, 1024]]),
                bass.AP(s64r, 64 * 2048, [[2048, 1], [1024, 2], [1, 1024]])),
                waits=[("tSC", c_ep)], inc=("qX", 16))
            if h == 1:
                for head in range(2):
                    c_spill[hp] = P.do("sync", lambda e, _hp=hp, _hd=head: e.dma_start(
                        bass.AP(av_d, (_hp * 128 + 64 * _hd) * NQ, [[NQ, 64], [1, NQ]]),
                        av_spl[0:64, 2048 * _hd:2048 * _hd + 2048]),
                        waits=[("tSC", c_ep)], inc=("qX", 16))
    c_spill_last = c_spill[HP - 1]

    P.flush()
    if stop_after == 'attn':
        return nc
    aex.close()
    kvq.close()

    P.barrier()
    om = ExitStack()
    u_fm = om.enter_context(nc.sbuf_tensor("u_fm", [128, 8 * NQ], BF16))
    facc = om.enter_context(nc.sbuf_tensor("facc", [128, NQT * 1024], F32))
    g_all = om.enter_context(nc.sbuf_tensor("g_all", [128, NQT * 16], F32))
    g2t = om.enter_context(nc.sbuf_tensor("g2t", [16, NQ], BF16))
    br_b = om.enter_context(nc.sbuf_tensor("br_b", [16, 1024], BF16))

    oex = ExitStack()
    av_all = oex.enter_context(nc.sbuf_tensor("av_all", [128, HP * NQ], BF16))
    wo_b = oex.enter_context(nc.sbuf_tensor("wo_b", [128, 8192], BF16))
    ident_f = oex.enter_context(nc.sbuf_tensor("ident_f_s", [128, 128], F32))
    sel2 = oex.enter_context(nc.sbuf_tensor("sel2_s", [16, 1024], BF16))
    rec2 = oex.enter_context(nc.sbuf_tensor("rec2", [16, 2048], F32))
    rec2h = oex.enter_context(nc.sbuf_tensor("rec2h", [16, 2048], BF16))
    rec2l = oex.enter_context(nc.sbuf_tensor("rec2l", [16, 2048], BF16))
    av_nb = oex.enter_context(nc.sbuf_tensor("av_nb", [128, 256], BF16))
    xq_t = oex.enter_context(nc.sbuf_tensor("xq_t", [128, 2048], F32))
    res1 = oex.enter_context(nc.sbuf_tensor("res1", [128, 1024], F32))
    u_t = oex.enter_context(nc.sbuf_tensor("u_t", [128, 1024], F32))
    u_bf = oex.enter_context(nc.sbuf_tensor("u_bf", [128, 2048], BF16))
    cent_b = oex.enter_context(nc.sbuf_tensor("cent_b", [128, 128], BF16))
    sc_em = oex.enter_context(nc.sbuf_tensor("sc_em", [16, NQ], F32))
    eb_s = oex.enter_context(nc.sbuf_tensor("eb_s", [16, 1], F32))
    bs_b = oex.enter_context(nc.sbuf_tensor("bs_b", [2, 1024], BF16))
    rt = oex.enter_context(nc.sbuf_tensor("rt", [128, 80], F32))
    gg_t = oex.enter_context(nc.sbuf_tensor("gg_t", [128, 16], BF16))
    o_ps = [oex.enter_context(nc.psum_tensor(f"o_ps{i}", [128, 1024], F32)) for i in range(2)]
    tp2_ps = [oex.enter_context(nc.psum_tensor("tp2_ps0", [128, 128], BF16))]
    tp2_ps.append(tp2_ps[0])
    sc_ps = oex.enter_context(nc.psum_tensor("sc_ps", [16, 512], F32))
    rt_ps = [oex.enter_context(nc.psum_tensor(f"rt_ps{i}", [128, 128], F32)) for i in range(2)]
    misc_ps = rt_ps[0]

    c_av_ld = P.do("sync", lambda e: e.dma_start(
        bass.AP(av_all, 0, [[HP * NQ, 128], [NQ, HP], [1, NQ]]),
        bass.AP(av_d, 0, [[NQ, 128], [128 * NQ, HP], [1, NQ]])),
        waits=[("qX", c_spill_last)], inc=("qX", 16))
    c_sums_ld = P.do("sync", lambda e: e.dma_start(
        bass.AP(rec2, 0, [[2048, 16], [1, 2048]]),
        bass.AP(sums_d, 0, [[2048, 16], [1, 2048]])), inc=("qX", 16))
    c_sel2 = P.do("sync", lambda e: e.dma_start(sel2[:], sel2_in[:, :]), inc=("qX", 16))
    c_rc2 = P.do("vector", lambda e: e.reciprocal(rec2[:, :], rec2[:, :]),
                 waits=[("qX", c_sums_ld)], inc=("tDV", 1))
    c_rh = P.do("vector", lambda e: e.tensor_copy(rec2h[:, :], rec2[:, :]),
                waits=[("tDV", c_rc2)], inc=("tDV", 1))
    c_rl = P.do("vector", lambda e: e.tensor_tensor(
        rec2l[:, :], rec2[:, :], rec2h[:, :], ALU.subtract),
        waits=[("tDV", c_rh)], inc=("tDV", 1))
    c_idf = P.do("sync", lambda e: e.dma_start(ident_f[:], ident_f_in[:, :]), inc=("qX", 16))
    c_wo = P.do("gpsimd", wcast_load(wo_b, 0, wo, 0), inc=("qW", 16))
    c_cent = P.do("gpsimd", lambda e: e.dma_start(
        bass.AP(cent_b, 0, [[128, 128], [16, 8], [1, 16]]),
        bass.AP(cent_t, 0, [[16, 128], [2048, 8], [1, 16]])), inc=("qW", 16))
    c_eb = P.do("gpsimd", lambda e: e.dma_start(eb_s[:], eb[:, :]), inc=("qW", 16))
    c_br = P.do("gpsimd", lambda e: e.dma_start(br_b[0:16, :], br[:, :]), inc=("qW", 16))
    c_bs = P.do("gpsimd", lambda e: e.dma_start(bs_b[:], bs[:, :]), inc=("qW", 16))

    xq_free = [[], []]
    ops_free = [[], []]
    ubf_free = [[], []]
    c_sq_prev = 0
    c_r_prev = 0
    for qt in range(NQT):
        b = qt % 2
        c_xq = P.do("sync", lambda e, _qt=qt, _b=b: e.dma_start(
            xq_t[:, 1024 * _b:1024 * _b + 1024], x_q[128 * _qt:128 * _qt + 128, :]),
            waits=list(xq_free[b]), inc=("qX", 16))
        wmm = [("qX", c_av_ld), ("qW", c_wo), ("tDV", c_rl), ("qX", c_sel2)] + ops_free[b]
        c_mm = None
        nrm_done = {}
        ab_free = {}
        for ic in range(8):
            rp = ic % 2
            wn = list(wmm) if ic < 2 else [("tDV", nrm_done[ic - 2])]
            P.do("tensor", lambda e, _qt=qt, _ic=ic, _rp=rp: e.matmul(
                rt_ps[_rp][:, :], sel2[:, 128 * _ic:128 * _ic + 128],
                rec2h[:, 128 * _qt:128 * _qt + 128],
                start=True, stop=False, skip_group_check=True),
                waits=wn, inc=("tPE", 1))
            c_nm = P.do("tensor", lambda e, _qt=qt, _ic=ic, _rp=rp: e.matmul(
                rt_ps[_rp][:, :], sel2[:, 128 * _ic:128 * _ic + 128],
                rec2l[:, 128 * _qt:128 * _qt + 128],
                start=False, stop=True, skip_group_check=True), inc=("tPE", 1))
            wtv = [("tPE", c_nm)]
            if ic >= 2:
                wtv.append(ab_free[ic - 2])
            nrm_done[ic] = P.do("vector", lambda e, _qt=qt, _ic=ic, _rp=rp: e.tensor_tensor(
                av_nb[:, 128 * _rp:128 * _rp + 128],
                av_all[:, NQ * _ic + 128 * _qt:NQ * _ic + 128 * _qt + 128],
                rt_ps[_rp][:, :], ALU.mult), waits=wtv, inc=("tDV", 1))
            for nh in range(2):
                c_mm = P.do("tensor", lambda e, _ic=ic, _nh=nh, _b=b, _rp=rp: e.matmul(
                    o_ps[_b][:, 512 * _nh:512 * _nh + 512],
                    av_nb[:, 128 * _rp:128 * _rp + 128],
                    wo_b[:, 1024 * _ic + 512 * _nh:1024 * _ic + 512 * _nh + 512],
                    start=(_ic == 0), stop=(_ic == 7), skip_group_check=True),
                    waits=[("tDV", nrm_done[ic])] if nh == 0 else [], inc=("tPE", 1))
            ab_free[ic] = ("tPE", c_mm)
        c_r1 = P.do("vector", lambda e, _b=b: e.tensor_tensor(
            res1[:, :], o_ps[_b][:, :], xq_t[:, 1024 * _b:1024 * _b + 1024], ALU.add),
            waits=[("tPE", c_mm), ("qX", c_xq), ("tSC", c_sq_prev)], inc=("tDV", 1))
        ops_free[b] = [("tDV", c_r1)]
        xq_free[b] = [("tDV", c_r1)]
        c_rs1 = P.do("vector", lambda e: e.reduce_sum(
            rt[:, 0:1], res1[:, :], axis=AX.X),
            waits=[("tSC", c_r_prev)], inc=("tDV", 1))
        wub = list(ubf_free[b])
        P.do("scalar", lambda e, _b=b: e.activation(
            u_bf[:, 1024 * _b:1024 * _b + 1024], res1[:, :], AF.Square,
            accum_out=rt[:, 1:2]), waits=[("tDV", c_r1)] + wub)
        c_sq = P.do("scalar", lambda e: e.copy(rt[:, 14:15], rt[:, 1:2]),
                    inc=("tSC", 1))
        c_sq_prev = c_sq
        d1 = P.do("vector", lambda e: e.tensor_scalar(
            rt[:, 2:3], rt[:, 0:1], 1.0 / D, None, ALU.mult, ALU.bypass),
            waits=[("tDV", c_rs1)], inc=("tDV", 1))
        d2 = P.do("vector", lambda e: e.tensor_tensor(
            rt[:, 3:4], rt[:, 2:3], rt[:, 2:3], ALU.mult),
            waits=[("tDV", d1)], inc=("tDV", 1))
        d3 = P.do("vector", lambda e: e.tensor_scalar(
            rt[:, 4:5], rt[:, 14:15], 1.0 / D, EPS, ALU.mult, ALU.add),
            waits=[("tSC", c_sq)], inc=("tDV", 1))
        d4 = P.do("vector", lambda e: e.tensor_tensor(
            rt[:, 5:6], rt[:, 4:5], rt[:, 3:4], ALU.subtract),
            waits=[("tDV", d3)], inc=("tDV", 1))
        c_vr = P.do("vector", lambda e: e.reciprocal(rt[:, 6:7], rt[:, 5:6]),
                    waits=[("tDV", d4)], inc=("tDV", 1))
        c_r = P.do("scalar", lambda e: e.activation(rt[:, 7:8], rt[:, 6:7], AF.Sqrt),
                   waits=[("tDV", c_vr)], inc=("tSC", 1))
        c_r_prev = c_r
        d5 = P.do("vector", lambda e: e.tensor_scalar(
            rt[:, 8:9], rt[:, 2:3], rt[:, 7:8], -1.0, ALU.mult, ALU.mult),
            waits=[("tSC", c_r)], inc=("tDV", 1))
        d6 = P.do("vector", lambda e: e.tensor_scalar(
            u_t[:, :], res1[:, :], rt[:, 7:8], rt[:, 8:9], ALU.mult, ALU.add),
            waits=[("tDV", d5)], inc=("tDV", 1))
        P.do("vector", lambda e, _qt=qt: e.tensor_tensor(
            facc[:, 1024 * _qt:1024 * _qt + 1024], res1[:, :], u_t[:, :], ALU.add),
            waits=[("tDV", d6)])
        c_ub = P.do("vector", lambda e, _b=b: e.tensor_copy(
            u_bf[:, 1024 * _b:1024 * _b + 1024], u_t[:, :]), inc=("tDV", 1))
        c_ev = None
        for dc in range(8):
            pb = dc % 2
            c_tp = P.do("tensor", lambda e, _b=b, _dc=dc, _pb=pb: e.transpose(
                tp2_ps[_pb][:, :], u_bf[:, 1024 * _b + 128 * _dc:1024 * _b + 128 * _dc + 128],
                ident_b[:]), waits=[("tDV", c_ub), ("tSC", P.cnt["tSC"])],
                inc=("tPE", 1))
            c_ev = P.do("scalar", lambda e, _dc=dc, _pb=pb, _qt=qt: e.activation(
                u_fm[:, NQ * _dc + 128 * _qt:NQ * _dc + 128 * _qt + 128],
                tp2_ps[_pb][:, :], AF.Copy), waits=[("tPE", c_tp)], inc=("tSC", 1))
        ubf_free[b] = [("tSC", c_ev)]
    c_o1_sc = P.cnt["tSC"]

    c_scev = {}
    for c in range(4):
        c_mm = None
        for dc in range(8):
            w0 = [("qW", c_cent), ("tSC", P.cnt["tSC"])]
            if c > 0:
                w0.append(("tDV", c_scev[c - 1]))
            c_mm = P.do("tensor", lambda e, _c=c, _dc=dc: e.matmul(
                sc_ps[:, :], cent_b[:, 16 * _dc:16 * _dc + 16],
                u_fm[:, NQ * _dc + 512 * _c:NQ * _dc + 512 * _c + 512],
                start=(_dc == 0), stop=(_dc == 7), skip_group_check=True),
                waits=w0 if dc == 0 else [], inc=("tPE", 1))
        c_scev[c] = P.do("vector", lambda e, _c=c: e.tensor_scalar(
            sc_em[:, 512 * _c:512 * _c + 512], sc_ps[:, :], eb_s[:, 0:1], None,
            ALU.add, ALU.bypass), waits=[("tPE", c_mm), ("qW", c_eb)], inc=("tDV", 1))
    c_es_prev = 0
    c_m1_prev = 0
    c_gt_prev = 0
    for qt in range(NQT):
        c_tr = P.do("tensor", lambda e, _qt=qt: e.transpose(
            misc_ps[:, 0:16], sc_em[0:16, 128 * _qt:128 * _qt + 128], ident_f[0:16, 0:16]),
            waits=[("tDV", c_scev[qt // 4]), ("qX", c_idf),
                   ("tSC", c_es_prev), ("tDV", c_m1_prev)], inc=("tPE", 1))
        c_m1 = P.do("vector", lambda e: e.reduce_max(
            rt[:, 9:10], misc_ps[:, 0:16], axis=AX.X, negate=True),
            waits=[("tPE", c_tr)], inc=("tDV", 1))
        c_m1_prev = c_m1
        c_es = P.do("scalar", lambda e: e.activation(
            rt[:, 32:48], misc_ps[:, 0:16], AF.Exp, bias=rt[:, 9:10], scale=1.0),
            waits=[("tDV", c_m1)], inc=("tSC", 1))
        c_es_prev = c_es
        c_rsm = P.do("vector", lambda e: e.reduce_sum(
            rt[:, 10:11], rt[:, 32:48], axis=AX.X), waits=[("tSC", c_es)],
            inc=("tDV", 1))
        e1 = P.do("vector", lambda e: e.reciprocal(rt[:, 11:12], rt[:, 10:11]),
             waits=[("tDV", c_rsm)], inc=("tDV", 1))
        e2 = P.do("vector", lambda e: e.tensor_scalar(
            rt[:, 48:64], rt[:, 32:48], rt[:, 11:12], None, ALU.mult, ALU.bypass),
            waits=[("tDV", e1)], inc=("tDV", 1))
        c_rm1 = P.do("vector", lambda e: e.reduce_max(
            rt[:, 12:13], rt[:, 48:64], axis=AX.X),
            waits=[("tDV", e2)], inc=("tDV", 1))
        e3 = P.do("vector", lambda e: e.tensor_scalar(
            rt[:, 64:80], rt[:, 48:64], rt[:, 12:13], None, ALU.is_ge, ALU.bypass),
            waits=[("tDV", c_rm1)], inc=("tDV", 1))
        e4 = P.do("vector", lambda e: e.scalar_tensor_tensor(
            rt[:, 16:32], rt[:, 64:80], -1e4, rt[:, 48:64], ALU.mult, ALU.add),
            waits=[("tDV", e3)], inc=("tDV", 1))
        c_rm2 = P.do("vector", lambda e: e.reduce_max(
            rt[:, 13:14], rt[:, 16:32], axis=AX.X),
            waits=[("tDV", e4)], inc=("tDV", 1))
        e5 = P.do("vector", lambda e: e.tensor_scalar(
            rt[:, 64:80], rt[:, 48:64], rt[:, 13:14], None, ALU.is_ge, ALU.bypass),
            waits=[("tDV", c_rm2)], inc=("tDV", 1))
        c_g = P.do("vector", lambda e, _qt=qt: e.tensor_tensor(
            g_all[:, 16 * _qt:16 * _qt + 16], rt[:, 48:64], rt[:, 64:80], ALU.mult),
            waits=[("tDV", e5)], inc=("tDV", 1))
        c_gg = P.do("vector", lambda e, _qt=qt: e.tensor_scalar(
            gg_t[:, :], g_all[:, 16 * _qt:16 * _qt + 16], 2.0, None,
            ALU.mult, ALU.bypass), waits=[("tDV", c_g), ("tPE", c_gt_prev)],
            inc=("tDV", 1))
        c_gt = P.do("tensor", lambda e: e.transpose(
            tp2_ps[0][0:16, :], gg_t[:, :], ident_b[:]),
            waits=[("tDV", c_gg), ("tSC", P.cnt["tSC"])], inc=("tPE", 1))
        c_gt_prev = c_gt
        P.do("scalar", lambda e, _qt=qt: e.activation(
            g2t[0:16, 128 * _qt:128 * _qt + 128], tp2_ps[0][0:16, :], AF.Copy),
            waits=[("tPE", c_gt)], inc=("tSC", 1))
    c_o_done_sc = P.cnt["tSC"]
    c_o_done_dv = P.cnt["tDV"]

    if stop_after == 'o':
        P.do("sync", lambda e: e.dma_start(
            bass.AP(out_d, 0, [[1024, 128], [131072, 16], [1, 1024]]),
            bass.AP(facc, 0, [[16384, 128], [1024, 16], [1, 1024]])),
            waits=[("tDV", P.cnt["tDV"]), ("tSC", P.cnt["tSC"])], inc=("qX", 16))
        P.do("gpsimd", lambda e: e.dma_start(
            bass.AP(av_d, 0, [[256, 128], [1, 256]]),
            bass.AP(g_all, 0, [[256, 128], [1, 256]])),
            waits=[("tDV", P.cnt["tDV"])], inc=("qW", 16))
        P.do("sync", lambda e: e.dma_start(
            bass.AP(av_d, 128 * 256, [[2048, 16], [1, 2048]]),
            bass.AP(g2t, 0, [[2048, 16], [1, 2048]])), inc=("qX", 16))
        P.do("sync", lambda e: e.dma_start(
            bass.AP(av_d, 2 * 128 * 256, [[2048, 128], [1, 2048]]),
            u_fm[:, 0:2048]), inc=("qX", 16))
        P.flush()
        return nc
    P.flush()
    oex.close()

    P.barrier()
    mex = ExitStack()
    wr_b = mex.enter_context(nc.sbuf_tensor("wr_b", [128, 2 * 8192], BF16))
    ws_tmp = mex.enter_context(nc.sbuf_tensor("ws_tmp", [128, 8192], BF16))
    e_ps = [mex.enter_context(nc.psum_tensor(f"e_ps{i}", [128, 1024], F32)) for i in range(3)]

    c_ws0 = P.do("gpsimd", wcast_load(wr_b, 0, wst, 0), inc=("qW", 16))
    c_ws1 = P.do("gpsimd", wcast_load(ws_tmp, 0, wst, D * D), inc=("qW", 16))
    c_wsum = P.do("vector", lambda e: e.tensor_tensor(
        wr_b[:, 0:8192], wr_b[:, 0:8192], ws_tmp[:, 0:8192], ALU.add),
        waits=[("qW", c_ws1)], inc=("tDV", 1))
    w_ready = {0: ("tDV", c_wsum)}
    wbuf_last_mm = {0: 0, 1: 0}
    mm_done = {}
    stt_done = {}
    psum_guard = {0: None, 1: None, 2: None}
    for t in range(NE + 1):
        if t + 1 <= NE:
            ee = t
            half = (t + 1) % 2
            w_ready[t + 1] = ("qW", P.do(
                "gpsimd", wcast_load(wr_b, 8192 * half, wrt, ee * D * D),
                waits=[("tPE", wbuf_last_mm[half])] if wbuf_last_mm[half] else [],
                inc=("qW", 16)))
        half = t % 2
        for qt in range(NQT):
            r = (t * NQT + qt) % 3
            w0 = [w_ready[t]]
            if psum_guard[r] is not None:
                w0.append(psum_guard[r])
            c_mm = None
            for dc in range(8):
                for nh in range(2):
                    c_mm = P.do("tensor", lambda e, _t=t, _qt=qt, _dc=dc, _nh=nh,
                                _r=r, _hf=half: e.matmul(
                        e_ps[_r][:, 512 * _nh:512 * _nh + 512],
                        u_fm[:, NQ * _dc + 128 * _qt:NQ * _dc + 128 * _qt + 128],
                        wr_b[:, 8192 * _hf + 1024 * _dc + 512 * _nh:
                             8192 * _hf + 1024 * _dc + 512 * _nh + 512],
                        start=(_dc == 0), stop=(_dc == 7 and _t > 0),
                        skip_group_check=True),
                        waits=w0 if (dc == 0 and nh == 0) else [], inc=("tPE", 1))
            if t == 0:
                for nh in range(2):
                    P.do("tensor", lambda e, _qt=qt, _nh=nh, _r=r: e.matmul(
                        e_ps[_r][:, 512 * _nh:512 * _nh + 512],
                        g2t[0:16, 128 * _qt:128 * _qt + 128],
                        br_b[0:16, 512 * _nh:512 * _nh + 512],
                        start=False, stop=False, skip_group_check=True),
                        waits=[("tSC", c_o_done_sc), ("qW", c_br), ("qW", c_bs)] if (qt == 0 and nh == 0) else [],
                        inc=("tPE", 1))
                    c_mm = P.do("tensor", lambda e, _nh=nh, _r=r: e.matmul(
                        e_ps[_r][:, 512 * _nh:512 * _nh + 512],
                        ones2[:, 0:128],
                        bs_b[:, 512 * _nh:512 * _nh + 512],
                        start=False, stop=True, skip_group_check=True),
                        inc=("tPE", 1))
            mm_done[(t, qt)] = c_mm
            wbuf_last_mm[half] = c_mm
            c_stt = None
            w_prev_stt = [("tDV", stt_done[(t - 1, qt)])] if t > 0 else [("tDV", c_o_done_dv)]
            for nh in range(2):
                if t == 0:
                    c_stt = P.do("vector", lambda e, _qt=qt, _nh=nh, _r=r:
                                 e.scalar_tensor_tensor(
                                     facc[:, 1024 * _qt + 512 * _nh:1024 * _qt + 512 * _nh + 512],
                                     e_ps[_r][:, 512 * _nh:512 * _nh + 512], 0.5,
                                     facc[:, 1024 * _qt + 512 * _nh:1024 * _qt + 512 * _nh + 512],
                                     ALU.mult, ALU.add),
                                 waits=([("tPE", c_mm)] + w_prev_stt) if nh == 0 else [],
                                 inc=("tDV", 1))
                else:
                    c_stt = P.do("vector", lambda e, _t=t, _qt=qt, _nh=nh, _r=r:
                                 e.scalar_tensor_tensor(
                                     facc[:, 1024 * _qt + 512 * _nh:1024 * _qt + 512 * _nh + 512],
                                     e_ps[_r][:, 512 * _nh:512 * _nh + 512],
                                     g_all[:, 16 * _qt + (_t - 1):16 * _qt + _t],
                                     facc[:, 1024 * _qt + 512 * _nh:1024 * _qt + 512 * _nh + 512],
                                     ALU.mult, ALU.add),
                                 waits=([("tPE", c_mm)] + w_prev_stt) if nh == 0 else [],
                                 inc=("tDV", 1))
            stt_done[(t, qt)] = c_stt
            psum_guard[r] = ("tDV", c_stt)

    c_store = None
    for qt in range(NQT):
        c_store = P.do("sync", lambda e, _qt=qt: e.dma_start(
            out_d[128 * _qt:128 * _qt + 128, :], facc[:, 1024 * _qt:1024 * _qt + 1024]),
            waits=[("tDV", stt_done[(NE, qt)])], inc=("qX", 16))
    P.do("sync", lambda e: (e.wait_ge(sems["qX"], c_store), None)[1])
    P.flush()
    mex.close()
    om.close()
    es.close()
    return nc


_NC_CACHE = None


def _get_nc():
    global _NC_CACHE
    if _NC_CACHE is None:
        _NC_CACHE = build_nc()
    return _NC_CACHE


def kernel(L0_attn_x, L0_attn_Wq, L0_attn_Wk, L0_attn_Wv, L0_attn_Wo,
           L0_ffn_centroids, L0_ffn_expert_biases, L0_ffn_Wr, L0_ffn_br,
           L0_ffn_Ws, L0_ffn_bs):
    x = np.ascontiguousarray(np.asarray(L0_attn_x, dtype=np.float32))
    wq = np.ascontiguousarray(np.asarray(L0_attn_Wq, dtype=np.float32))
    wk = np.ascontiguousarray(np.asarray(L0_attn_Wk, dtype=np.float32))
    wv = np.ascontiguousarray(np.asarray(L0_attn_Wv, dtype=np.float32))
    wo = np.ascontiguousarray(np.asarray(L0_attn_Wo, dtype=np.float32))
    wrt = np.ascontiguousarray(np.asarray(L0_ffn_Wr, dtype=np.float32).transpose(0, 2, 1))
    wst = np.ascontiguousarray(np.asarray(L0_ffn_Ws, dtype=np.float32).transpose(0, 2, 1))
    cent_t = np.ascontiguousarray(np.asarray(L0_ffn_centroids, dtype=np.float32).T)
    ebv = np.ascontiguousarray(np.asarray(L0_ffn_expert_biases, dtype=np.float32).reshape(16, 1))
    brv = np.ascontiguousarray(np.asarray(L0_ffn_br, dtype=np.float32))
    bsv = np.ascontiguousarray(np.asarray(L0_ffn_bs, dtype=np.float32))
    ident_b = np.eye(128, dtype=BF)
    ident_f = np.eye(128, dtype=np.float32)
    sel16 = np.zeros((16, 1024), dtype=BF)
    for hp_ in range(8):
        sel16[2 * hp_, 128 * hp_:128 * hp_ + 64] = 1.0
        sel16[2 * hp_ + 1, 128 * hp_ + 64:128 * hp_ + 128] = 1.0

    perms = []
    for par in range(2):
        idx = np.concatenate([
            np.arange(128 * (31 - k) + 64 * par, 128 * (31 - k) + 64 * par + 64)
            for k in range(32)])
        perms.append(idx)
    masks = []
    for par in range(2):
        jj = np.arange(128)[:, None]
        ii = np.arange(64)[None, :]
        masks.append((jj <= 64 * par + ii).astype(BF))

    in_maps = []
    for c in range(8):
        b, par = c // 2, c % 2
        in_maps.append({
            "x_ctx": x[b],
            "x_q": np.ascontiguousarray(x[b][perms[par]]),
            "wq": wq, "wk": wk, "wv": wv, "wo": wo,
            "wrt": wrt, "wst": wst, "cent_t": cent_t, "eb": ebv,
            "br": brv, "bs": bsv, "mask": masks[par],
            "ident_b": ident_b, "ident_f": ident_f, "sel2": sel16,
        })

    nc = _get_nc()
    res = run_bass_kernel_spmd(nc, in_maps, core_ids=list(range(8)))
    out = np.empty((4, 4096, 1024), dtype=np.float32)
    for c in range(8):
        b, par = c // 2, c % 2
        out[b][perms[par]] = res.results[c]["out"]
    return out


# version 07
# speedup vs baseline: 1.6173x; 1.0321x over previous
import numpy as np
import ml_dtypes
from contextlib import ExitStack

import concourse.bass as bass
import concourse.mybir as mybir
from concourse.bass_utils import run_bass_kernel_spmd

F32 = mybir.dt.float32
BF16 = mybir.dt.bfloat16
AF = mybir.ActivationFunctionType
ALU = mybir.AluOpType
AX = mybir.AxisListType
BF = ml_dtypes.bfloat16

D = 1024
NCTX = 4096
NQ = 2048
NKVT = 32
NQT = 16
HP = 8
SCALE = 0.125
EPS = 1e-5
NE = 16

ENGS = ("tensor", "vector", "scalar", "gpsimd", "sync")


class Prog:

    def __init__(self, nc, sems):
        self.nc = nc
        self.sems = sems
        self.cnt = {k: 0 for k in sems}
        self.q = {e: [] for e in ENGS}
        self.lastw = {e: {} for e in ENGS}

    def do(self, eng, fn, waits=(), inc=None):
        real_waits = []
        lw = self.lastw[eng]
        for s, v in waits:
            if s is None or v <= 0:
                continue
            if lw.get(s, -1) < v:
                real_waits.append((s, v))
                lw[s] = v
        if inc is not None:
            self.cnt[inc[0]] += inc[1]
        self.q[eng].append((fn, tuple(real_waits), inc))
        return self.cnt[inc[0]] if inc is not None else None

    def barrier(self):
        self.cnt["bar"] += 5
        tgt = self.cnt["bar"]
        for e in ENGS:
            def fn(eng, _t=tgt):
                eng.sem_inc(self.sems["bar"], 1)
                eng.wait_ge(self.sems["bar"], _t)
                return None
            self.q[e].append((fn, (), None))
            self.lastw[e]["bar"] = tgt

    def flush(self):
        with self.nc.Block() as blk:
            for e in ENGS:
                ops = self.q[e]
                self.q[e] = []
                if not ops:
                    continue

                def body(eng, _ops=tuple(ops), _sems=self.sems):
                    for fn, waits, inc in _ops:
                        for s, v in waits:
                            eng.wait_ge(_sems[s], v)
                        r = fn(eng)
                        if inc is not None and r is not None:
                            r.then_inc(_sems[inc[0]], inc[1])
                getattr(blk, e)(body)


def build_nc(stop_after='all'):
    nc = bass.Bass("TRN2", target_bir_lowering=False, detect_race_conditions=False)

    x_ctx = nc.dram_tensor("x_ctx", [NCTX, D], F32, kind="ExternalInput")
    x_q = nc.dram_tensor("x_q", [NQ, D], F32, kind="ExternalInput")
    wq = nc.dram_tensor("wq", [D, D], F32, kind="ExternalInput")
    wk = nc.dram_tensor("wk", [D, D], F32, kind="ExternalInput")
    wv = nc.dram_tensor("wv", [D, D], F32, kind="ExternalInput")
    wo = nc.dram_tensor("wo", [D, D], F32, kind="ExternalInput")
    wrt = nc.dram_tensor("wrt", [NE, D, D], F32, kind="ExternalInput")
    wst = nc.dram_tensor("wst", [2, D, D], F32, kind="ExternalInput")
    cent_t = nc.dram_tensor("cent_t", [D, 16], F32, kind="ExternalInput")
    eb = nc.dram_tensor("eb", [16, 1], F32, kind="ExternalInput")
    br = nc.dram_tensor("br", [16, D], F32, kind="ExternalInput")
    bs = nc.dram_tensor("bs", [2, D], F32, kind="ExternalInput")
    mask_in = nc.dram_tensor("mask", [128, 64], BF16, kind="ExternalInput")
    ident_b_in = nc.dram_tensor("ident_b", [128, 128], BF16, kind="ExternalInput")
    ident_f_in = nc.dram_tensor("ident_f", [128, 128], F32, kind="ExternalInput")
    sel2_in = nc.dram_tensor("sel2", [16, 1024], BF16, kind="ExternalInput")
    out_d = nc.dram_tensor("out", [NQ, D], F32, kind="ExternalOutput")
    av_d = nc.dram_tensor("av_spill", [HP, 128, NQ], BF16, kind="ExternalOutput")
    sums_d = nc.dram_tensor("sums_spill", [HP, 2, NQ], F32)

    es = ExitStack()
    sems = {}
    for s in ("qW", "qX", "tPE", "tSC", "tDV", "tGP", "bar"):
        sems[s] = es.enter_context(nc.semaphore(s))
    P = Prog(nc, sems)

    ident_b = es.enter_context(nc.sbuf_tensor("ident_b_s", [128, 128], BF16))
    mask_s = es.enter_context(nc.sbuf_tensor("mask_s", [128, 64], BF16))
    ones1 = es.enter_context(nc.sbuf_tensor("ones1", [128, 1], BF16))
    ones2 = es.enter_context(nc.sbuf_tensor("ones2", [2, 128], BF16))
    negb8 = es.enter_context(nc.sbuf_tensor("negb8", [128, 8], F32))
    stats = es.enter_context(nc.sbuf_tensor("stats", [128, 24], F32))

    c_idb = P.do("sync", lambda e: e.dma_start(ident_b[:], ident_b_in[:, :]), inc=("qX", 16))
    c_msk = P.do("sync", lambda e: e.dma_start(mask_s[:], mask_in[:, :]), inc=("qX", 16))
    P.do("gpsimd", lambda e: e.memset(ones1[:], 1.0))
    c_ones = P.do("gpsimd", lambda e: e.memset(ones2[:], 1.0), inc=("tGP", 1))

    def wcast_load(dst, dst_col, src, e_off):
        dst_stride = dst.shape[1]

        def fn(eng):
            return eng.dma_start(
                bass.AP(dst, dst_col, [[dst_stride, 128], [1024, 8], [1, 1024]]),
                bass.AP(src, e_off, [[1024, 128], [131072, 8], [1, 1024]]),
            )
        return fn

    kvq = ExitStack()
    k_all = kvq.enter_context(nc.sbuf_tensor("k_all", [128, HP * NCTX], BF16))
    v_all = kvq.enter_context(nc.sbuf_tensor("v_all", [128, NKVT * 1040], BF16))
    q_all = kvq.enter_context(nc.sbuf_tensor("q_all", [128, HP * NQ], BF16))
    xt = kvq.enter_context(nc.sbuf_tensor("xt", [128, 2 * 1024], F32))
    xlt = kvq.enter_context(nc.sbuf_tensor("xlt", [128, 1024], BF16))
    xlf = kvq.enter_context(nc.sbuf_tensor("xlf", [128, 8 * 256], BF16))

    lnst = {
        "xt_free": [[], []],
        "xlt_free": [],
        "cr_prev": [0, 0],
        "g": 0,
    }

    def ln_tile(src_dram, row0, dst_fm_col, tp_ps, qx_extra=()):
        g = lnst["g"]
        lnst["g"] += 1
        b = g % 2
        sb = b * 12
        c_x = P.do("sync", lambda e, _b=b, _r=row0, _s=src_dram: e.dma_start(
            xt[:, 1024 * _b:1024 * _b + 1024], _s[_r:_r + 128, :]),
            waits=list(lnst["xt_free"][b]) + list(qx_extra), inc=("qX", 16))
        xv = xt[:, 1024 * b:1024 * b + 1024]
        c_rs = P.do("vector", lambda e, _x=xv, _c=sb: e.reduce_sum(
            stats[:, _c:_c + 1], _x, axis=AX.X),
            waits=[("qX", c_x), ("tSC", lnst["cr_prev"][b])], inc=("tDV", 1))
        P.do("scalar", lambda e, _x=xv, _c=sb: e.activation(
            xlt[:, :], _x, AF.Square,
            accum_out=stats[:, _c + 1:_c + 2]),
            waits=[("qX", c_x), ("tSC", lnst["cr_prev"][b])] + list(lnst["xlt_free"]))
        c_sq = P.do("scalar", lambda e, _c=sb: e.copy(
            stats[:, _c + 9:_c + 10], stats[:, _c + 1:_c + 2]), inc=("tSC", 1))
        c1 = P.do("vector", lambda e, _c=sb: e.tensor_scalar(
            stats[:, _c + 2:_c + 3], stats[:, _c:_c + 1], 1.0 / D, None,
            ALU.mult, ALU.bypass), waits=[("tDV", c_rs)], inc=("tDV", 1))
        c2 = P.do("vector", lambda e, _c=sb: e.tensor_tensor(
            stats[:, _c + 3:_c + 4], stats[:, _c + 2:_c + 3], stats[:, _c + 2:_c + 3],
            ALU.mult), waits=[("tDV", c1)], inc=("tDV", 1))
        c3 = P.do("vector", lambda e, _c=sb: e.tensor_scalar(
            stats[:, _c + 4:_c + 5], stats[:, _c + 9:_c + 10], 1.0 / D, EPS,
            ALU.mult, ALU.add), waits=[("tSC", c_sq)], inc=("tDV", 1))
        c4 = P.do("vector", lambda e, _c=sb: e.tensor_tensor(
            stats[:, _c + 5:_c + 6], stats[:, _c + 4:_c + 5], stats[:, _c + 3:_c + 4],
            ALU.subtract), waits=[("tDV", c3)], inc=("tDV", 1))
        c_vr = P.do("vector", lambda e, _c=sb: e.reciprocal(
            stats[:, _c + 6:_c + 7], stats[:, _c + 5:_c + 6]),
            waits=[("tDV", c4)], inc=("tDV", 1))
        c_r = P.do("scalar", lambda e, _c=sb: e.activation(
            stats[:, _c + 7:_c + 8], stats[:, _c + 6:_c + 7], AF.Sqrt),
            waits=[("tDV", c_vr)], inc=("tSC", 1))
        lnst["cr_prev"][b] = c_r
        c5 = P.do("vector", lambda e, _c=sb: e.tensor_scalar(
            stats[:, _c + 8:_c + 9], stats[:, _c + 2:_c + 3], stats[:, _c + 7:_c + 8],
            -1.0, ALU.mult, ALU.mult), waits=[("tSC", c_r)], inc=("tDV", 1))
        c_ap = P.do("vector", lambda e, _x=xv, _c=sb: e.tensor_scalar(
            xlt[:, :], _x, stats[:, _c + 7:_c + 8],
            stats[:, _c + 8:_c + 9], ALU.mult, ALU.add),
            waits=[("tDV", c5)], inc=("tDV", 1))
        lnst["xt_free"][b] = [("tDV", c_ap), ("tSC", c_sq)]
        c_ev = None
        c_tp = None
        for dc in range(8):
            pb = dc % 2
            c_tp = P.do("tensor", lambda e, _dc=dc, _pb=pb, _tp=tp_ps: e.transpose(
                _tp[_pb][:, :], xlt[:, 128 * _dc:128 * _dc + 128],
                ident_b[:]),
                waits=[("tDV", c_ap), ("tDV", P.cnt["tDV"]), ("qX", c_idb)],
                inc=("tPE", 1))
            c_ev = P.do("vector", lambda e, _dc=dc, _pb=pb, _col=dst_fm_col, _tp=tp_ps:
                        e.tensor_copy(
                            xlf[:, _col + 256 * _dc:_col + 256 * _dc + 128],
                            _tp[_pb][:, :]),
                        waits=[("tPE", c_tp)], inc=("tDV", 1))
        lnst["xlt_free"] = [("tPE", c_tp)]
        return c_ev

    p0 = ExitStack()
    wk_b = p0.enter_context(nc.sbuf_tensor("wk_b", [128, 8192], BF16))
    wv_b = p0.enter_context(nc.sbuf_tensor("wv_b", [128, 8192], BF16))
    tp_ps_a = [p0.enter_context(nc.psum_tensor(f"tp_ps{i}", [128, 128], BF16)) for i in range(2)]
    kq_ps_a = [p0.enter_context(nc.psum_tensor(f"kq_ps{i}", [128, 256], F32)) for i in range(2)]
    v_ps = [p0.enter_context(nc.psum_tensor(f"v_ps{i}", [128, 1024], F32)) for i in range(2)]

    c_wk = P.do("gpsimd", wcast_load(wk_b, 0, wk, 0), inc=("qW", 16))
    c_wv = P.do("gpsimd", wcast_load(wv_b, 0, wv, 0), inc=("qW", 16))
    c_vones = P.do("gpsimd", lambda e: e.memset(v_all[:, :], 1.0), inc=("tGP", 1))

    kq_free = [[], []]
    v_free = [[], []]
    for kb in range(NCTX // 256):
        c_ev = 0
        for t2 in range(2):
            c_ev = ln_tile(x_ctx, 256 * kb + 128 * t2, 128 * t2, tp_ps_a)
            if stop_after == 'ln0' and kb == 0 and t2 == 0:
                P.do("sync", lambda e: e.dma_start(out_d[0:128, 0:24], stats[:, :]),
                     waits=[("tSC", P.cnt["tSC"]), ("tDV", P.cnt["tDV"])], inc=("qX", 16))
                P.do("sync", lambda e: e.dma_start(
                    bass.AP(av_d, 0, [[2048, 128], [1, 1024]]), xlt[:, :]), inc=("qX", 16))
                P.do("sync", lambda e: e.dma_start(
                    bass.AP(av_d, 128 * 2048, [[2048, 128], [1, 2048]]), xlf[:, :]), inc=("qX", 16))
                P.flush()
                return nc
        for hp in range(HP):
            pb = hp % 2
            w0 = [("tDV", c_ev), ("qW", c_wk)] + kq_free[pb]
            c_mm = None
            for dc in range(8):
                c_mm = P.do("tensor", lambda e, _hp=hp, _dc=dc, _pb=pb: e.matmul(
                    kq_ps_a[_pb][:, :],
                    wk_b[:, 1024 * _dc + 128 * _hp:1024 * _dc + 128 * _hp + 128],
                    xlf[:, 256 * _dc:256 * _dc + 256],
                    start=(_dc == 0), stop=(_dc == 7), skip_group_check=True),
                    waits=w0 if dc == 0 else [], inc=("tPE", 1))
            c_kev = P.do("scalar", lambda e, _hp=hp, _kb=kb, _pb=pb: e.activation(
                k_all[:, NCTX * _hp + 256 * _kb:NCTX * _hp + 256 * _kb + 256],
                kq_ps_a[_pb][:, :], AF.Copy), waits=[("tPE", c_mm)], inc=("tSC", 1))
            kq_free[pb] = [("tSC", c_kev)]
        for t2 in range(2):
            j = 2 * kb + t2
            pb = j % 2
            w0 = [("tDV", c_ev), ("qW", c_wv)] + v_free[pb]
            c_mm = None
            for dc in range(8):
                for nh in range(2):
                    c_mm = P.do("tensor", lambda e, _t2=t2, _dc=dc, _nh=nh, _pb=pb: e.matmul(
                        v_ps[_pb][:, 512 * _nh:512 * _nh + 512],
                        xlf[:, 256 * _dc + 128 * _t2:256 * _dc + 128 * _t2 + 128],
                        wv_b[:, 1024 * _dc + 512 * _nh:1024 * _dc + 512 * _nh + 512],
                        start=(_dc == 0), stop=(_dc == 7), skip_group_check=True),
                        waits=w0 if (dc == 0 and nh == 0) else [], inc=("tPE", 1))
            c_vev = P.do("scalar", lambda e, _j=j, _pb=pb: e.activation(
                bass.AP(v_all, 1040 * _j, [[NKVT * 1040, 128], [130, 8], [65, 2], [1, 64]]),
                bass.AP(v_ps[_pb], 0, [[1024, 128], [128, 8], [64, 2], [1, 64]]),
                AF.Copy),
                waits=[("tPE", c_mm), ("tGP", c_vones)], inc=("tSC", 1))
            v_free[pb] = [("tSC", c_vev)]
    c_p0a_pe = P.cnt["tPE"]

    P.flush()
    if stop_after == 'p0a':
        return nc
    p0.close()

    p0b = ExitStack()
    wq_b = p0b.enter_context(nc.sbuf_tensor("wq_b", [128, 8192], BF16))
    tp_ps_b = [p0b.enter_context(nc.psum_tensor(f"tpb_ps{i}", [128, 128], BF16)) for i in range(2)]
    kq_ps_b = [p0b.enter_context(nc.psum_tensor(f"kqb_ps{i}", [128, 256], F32)) for i in range(2)]

    c_wq = P.do("gpsimd", wcast_load(wq_b, 0, wq, 0),
                waits=[("tPE", c_p0a_pe)], inc=("qW", 16))
    kq_free = [[], []]
    for qb in range(NQ // 256):
        c_ev = 0
        for t2 in range(2):
            c_ev = ln_tile(x_q, 256 * qb + 128 * t2, 128 * t2, tp_ps_b)
        for hp in range(HP):
            pb = hp % 2
            w0 = [("tDV", c_ev), ("qW", c_wq)] + kq_free[pb]
            c_mm = None
            for dc in range(8):
                c_mm = P.do("tensor", lambda e, _hp=hp, _dc=dc, _pb=pb: e.matmul(
                    kq_ps_b[_pb][:, :],
                    wq_b[:, 1024 * _dc + 128 * _hp:1024 * _dc + 128 * _hp + 128],
                    xlf[:, 256 * _dc:256 * _dc + 256],
                    start=(_dc == 0), stop=(_dc == 7), skip_group_check=True),
                    waits=w0 if dc == 0 else [], inc=("tPE", 1))
            c_qev = P.do("scalar", lambda e, _hp=hp, _qb=qb, _pb=pb: e.activation(
                q_all[:, NQ * _hp + 256 * _qb:NQ * _hp + 256 * _qb + 256],
                kq_ps_b[_pb][:, :], AF.Copy), waits=[("tPE", c_mm)], inc=("tSC", 1))
            kq_free[pb] = [("tSC", c_qev)]
    c_p0b_pe = P.cnt["tPE"]
    c_p0b_sc = P.cnt["tSC"]

    if stop_after == 'p0b':
        P.do("sync", lambda e: e.dma_start(
            bass.AP(av_d, 0, [[2048, 128], [1, 2048]]),
            k_all[:, 0:2048]), waits=[("tSC", P.cnt["tSC"])], inc=("qX", 16))
        P.do("sync", lambda e: e.dma_start(
            bass.AP(av_d, 128 * 2048, [[2048, 128], [1, 2048]]),
            k_all[:, 7 * 4096:7 * 4096 + 2048]), inc=("qX", 16))
        P.do("sync", lambda e: e.dma_start(
            bass.AP(av_d, 2 * 128 * 2048, [[2048, 128], [1, 2048]]),
            q_all[:, 7 * 2048:7 * 2048 + 2048]), inc=("qX", 16))
        P.do("sync", lambda e: e.dma_start(
            bass.AP(av_d, 3 * 128 * 2048, [[2048, 128], [1, 2048]]),
            v_all[:, 0:2048]), inc=("qX", 16))
        P.flush()
        return nc
    P.flush()
    p0b.close()

    aex = ExitStack()
    pt = aex.enter_context(nc.sbuf_tensor("pt", [128, 4096], BF16))
    ksq = pt
    s64r = aex.enter_context(nc.sbuf_tensor("s64r", [65, 2048], F32))
    av_spl = aex.enter_context(nc.sbuf_tensor("av_spl", [64, 4096], BF16))
    km = aex.enter_context(nc.sbuf_tensor("km", [1, 16], F32))
    bsc = aex.enter_context(nc.sbuf_tensor("bsc", [1, 4], F32))
    negvb = aex.enter_context(nc.sbuf_tensor("negvb", [1, 1], BF16))
    av_ps = [aex.enter_context(nc.psum_tensor(f"av_ps{i}", [65, 1024], F32)) for i in range(2)]
    d_psA = aex.enter_context(nc.psum_tensor("d_psA", [128, 1024], F32))
    d_psB = aex.enter_context(nc.psum_tensor("d_psB", [128, 1024], F32))

    for hp in range(HP):
        c_k2 = P.do("vector", lambda e, _hp=hp: e.tensor_tensor(
            ksq[:, :], k_all[:, NCTX * _hp:NCTX * _hp + 4096],
            k_all[:, NCTX * _hp:NCTX * _hp + 4096], ALU.mult),
            waits=[("tPE", c_p0b_pe), ("tSC", c_p0b_sc)], inc=("tDV", 1))
        c_rm = 0
        for c in range(8):
            pb = c % 2
            c_n2 = P.do("tensor", lambda e, _c=c, _pb=pb: e.matmul(
                d_psA[0:1, 512 * _pb:512 * _pb + 512], ones1[:],
                ksq[:, 512 * _c:512 * _c + 512],
                start=True, stop=True, skip_group_check=True),
                waits=[("tDV", c_k2), ("tGP", c_ones)] if c < 2 else [("tDV", c_rm)],
                inc=("tPE", 1))
            c_rm = P.do("vector", lambda e, _c=c, _pb=pb: e.reduce_max(
                km[0:1, _c:_c + 1], d_psA[0:1, 512 * _pb:512 * _pb + 512], axis=AX.X),
                waits=[("tPE", c_n2)], inc=("tDV", 1))
        c_km = P.do("vector", lambda e: e.reduce_max(
            bsc[0:1, 0:1], km[0:1, 0:8], axis=AX.X),
            waits=[("tDV", c_rm)], inc=("tDV", 1))
        c_q2 = P.do("vector", lambda e, _hp=hp: e.tensor_tensor(
            ksq[:, 0:2048], q_all[:, NQ * _hp:NQ * _hp + 2048],
            q_all[:, NQ * _hp:NQ * _hp + 2048], ALU.mult),
            waits=[("tPE", P.cnt["tPE"])], inc=("tDV", 1))
        for c in range(4):
            pb = c % 2
            c_n2 = P.do("tensor", lambda e, _c=c, _pb=pb: e.matmul(
                d_psA[0:1, 512 * _pb:512 * _pb + 512], ones1[:],
                ksq[:, 512 * _c:512 * _c + 512],
                start=True, stop=True, skip_group_check=True),
                waits=[("tDV", c_q2)] if c < 2 else [("tDV", c_rm)], inc=("tPE", 1))
            c_rm = P.do("vector", lambda e, _c=c, _pb=pb: e.reduce_max(
                km[0:1, 8 + _c:9 + _c], d_psA[0:1, 512 * _pb:512 * _pb + 512], axis=AX.X),
                waits=[("tPE", c_n2)], inc=("tDV", 1))
        c_qm = P.do("vector", lambda e: e.reduce_max(
            bsc[0:1, 1:2], km[0:1, 8:12], axis=AX.X),
            waits=[("tDV", c_rm)], inc=("tDV", 1))
        c_pr = P.do("vector", lambda e: e.tensor_tensor(
            bsc[0:1, 2:3], bsc[0:1, 0:1], bsc[0:1, 1:2], ALU.mult),
            waits=[("tDV", c_qm)], inc=("tDV", 1))
        c_sq2 = P.do("scalar", lambda e: e.activation(
            bsc[0:1, 3:4], bsc[0:1, 2:3], AF.Sqrt), waits=[("tDV", c_pr)], inc=("tSC", 1))
        c_nv = P.do("vector", lambda e: e.tensor_scalar(
            negvb[0:1, 0:1], bsc[0:1, 3:4], -SCALE, None, ALU.mult, ALU.bypass),
            waits=[("tSC", c_sq2)], inc=("tDV", 1))
        c_bc = P.do("tensor", lambda e: e.matmul(
            d_psA[:, 0:1], ones2[0:1, :], negvb[0:1, 0:1], start=True, stop=True,
            skip_group_check=True), waits=[("tDV", c_nv)], inc=("tPE", 1))
        P.do("vector", lambda e, _hp=hp: e.tensor_copy(
            negb8[:, _hp:_hp + 1], d_psA[:, 0:1]), waits=[("tPE", c_bc)],
            inc=("tDV", 1))
    c_bounds_dv = P.cnt["tDV"]
    c_bounds_sc = P.cnt["tSC"]

    slot_list = []
    for hp in range(HP):
        for h in range(2):
            jmax = 32 if h == 0 else 16
            for j in range(jmax):
                w = min(1024, 64 * (32 - j)) if h == 0 else 64 * (16 - j)
                masked = (h == 0 and j >= 16) or h == 1
                jlast0 = 31 if h == 0 else 15
                jlast1 = (31 - 8) if h == 0 else (15 - 8)
                slot_list.append((hp, h, j, w, masked,
                                  j == 0, (j == jlast0, j == jlast1),
                                  j == jmax - 1))
    exp_done = {}
    av_done = {}
    c_spill = {}
    c_sums_dma = {}
    dps_guard = [("tDV", c_bounds_dv)]

    def emit_av(s):
        hp, h, j, w, masked, st, sps, last = slot_list[s]
        p = s % 2
        vcol = 1040 * j + 130 * hp
        r = {}
        for hd in range(2):
            ptc = 2048 * hd + 1024 * p
            if s in mask_done:
                wa = [("tDV", mask_done[s][hd])]
            else:
                wa = [("tSC", exp_done[s][hd]), ("tGP", c_vones)]
            first = True
            for c0 in range(0, w, 512):
                cw = min(512, w - c0)
                sp = sps[c0 // 512]
                r[hd] = P.do("tensor", lambda e, _hd=hd, _v=vcol, _c0=c0, _cw=cw,
                             _st=st, _sp=sp, _pc=ptc: e.matmul(
                                 av_ps[_hd][0:65, _c0:_c0 + _cw],
                                 v_all[:, _v + 65 * _hd:_v + 65 * _hd + 65],
                                 pt[:, _pc + _c0:_pc + _c0 + _cw], start=_st, stop=_sp,
                                 skip_group_check=True),
                             waits=wa if first else [], inc=("tPE", 1))
                first = False
        av_done[s] = r

    mask_done = {}
    ns = len(slot_list)
    for s in range(ns):
        hp, h, j, w, masked, st, sps, last = slot_list[s]
        p = s % 2
        kcol = NCTX * hp + 128 * j
        qcol = NQ * hp + 1024 * h
        wtA = list(dps_guard)
        wtB = []
        dps_guard = []
        if s >= 1:
            wtA.append(("tSC", exp_done[s - 1][0]))
            wtB.append(("tSC", exp_done[s - 1][1]))
        dots_done = {}
        for c0 in range(0, w, 512):
            cw = min(512, w - c0)
            dots_done[0] = P.do("tensor", lambda e, _k=kcol, _q=qcol + c0, _c0=c0,
                                _cw=cw: e.matmul(
                d_psA[:, _c0:_c0 + _cw], k_all[0:64, _k:_k + 128],
                q_all[0:64, _q:_q + _cw], start=True, stop=True,
                tile_position=(0, 0), skip_group_check=True),
                waits=wtA if c0 == 0 else [], inc=("tPE", 1))
        for c0 in range(0, w, 512):
            cw = min(512, w - c0)
            dots_done[1] = P.do("tensor", lambda e, _k=kcol, _q=qcol + c0, _c0=c0,
                                _cw=cw: e.matmul(
                d_psB[:, _c0:_c0 + _cw], k_all[64:128, _k:_k + 128],
                q_all[64:128, _q:_q + _cw], start=True, stop=True,
                tile_position=(64, 0), skip_group_check=True),
                waits=wtB if c0 == 0 else [], inc=("tPE", 1))
        if s >= 1:
            emit_av(s - 1)
        eA = {}
        for hd in range(2):
            ws = [("tPE", dots_done[hd]), ("tDV", c_bounds_dv)]
            if s >= 2:
                ws.append(("tPE", av_done[s - 2][hd]))
            dps = d_psA if hd == 0 else d_psB
            eA[hd] = P.do("scalar", lambda e, _w=w, _hp=hp, _pc=2048 * hd + 1024 * p,
                          _d=dps: e.activation(
                pt[:, _pc:_pc + _w], _d[:, 0:_w], AF.Exp,
                bias=negb8[:, _hp:_hp + 1], scale=SCALE), waits=ws, inc=("tSC", 1))
        exp_done[s] = eA
        if masked:
            moff = w - 64
            mk = {}
            for hd in range(2):
                mk[hd] = P.do("vector", lambda e, _m=moff, _pc=2048 * hd + 1024 * p:
                              e.tensor_tensor(
                    pt[:, _pc + _m:_pc + _m + 64], pt[:, _pc + _m:_pc + _m + 64],
                    mask_s[:], ALU.mult),
                    waits=[("tSC", exp_done[s][hd]), ("qX", c_msk)], inc=("tDV", 1))
            mask_done[s] = mk
        if last:
            emit_av(s)
            c_av = av_done[s][1]
            wsp = [("qX", c_spill[hp - 1])] if (hp - 1 in c_spill and h == 0) else []
            c_sumsdma = c_sums_dma.get((hp, h - 1)) or c_sums_dma.get((hp - 1, 1))
            c_evs = []
            for head in range(2):
                c_evs.append(P.do("scalar", lambda e, _hd=head, _h=h: e.activation(
                    av_spl[0:64, 2048 * _hd + 1024 * _h:2048 * _hd + 1024 * _h + 1024],
                    av_ps[_hd][0:64, :], AF.Copy),
                    waits=[("tPE", c_av)] + wsp, inc=("tSC", 1)))
                c_evs.append(P.do("scalar", lambda e, _hd=head: e.activation(
                    s64r[64:65, 1024 * _hd:1024 * _hd + 1024],
                    av_ps[_hd][64:65, :], AF.Copy),
                    waits=([("qX", c_sumsdma)] if c_sumsdma else []), inc=("tSC", 1)))
            c_ep = c_evs[-1]
            dps_guard = [("tSC", c_ep)]
            c_sums_dma[(hp, h)] = P.do("sync", lambda e, _hp=hp, _h=h: e.dma_start(
                bass.AP(sums_d, _hp * 2 * NQ + 1024 * _h, [[2 * NQ, 1], [NQ, 2], [1, 1024]]),
                bass.AP(s64r, 64 * 2048, [[2048, 1], [1024, 2], [1, 1024]])),
                waits=[("tSC", c_ep)], inc=("qX", 16))
            if h == 1:
                for head in range(2):
                    c_spill[hp] = P.do("sync", lambda e, _hp=hp, _hd=head: e.dma_start(
                        bass.AP(av_d, (_hp * 128 + 64 * _hd) * NQ, [[NQ, 64], [1, NQ]]),
                        av_spl[0:64, 2048 * _hd:2048 * _hd + 2048]),
                        waits=[("tSC", c_ep)], inc=("qX", 16))
    c_spill_last = c_spill[HP - 1]

    P.flush()
    if stop_after == 'attn':
        return nc
    aex.close()
    kvq.close()

    P.barrier()
    om = ExitStack()
    u_fm = om.enter_context(nc.sbuf_tensor("u_fm", [128, 8 * NQ], BF16))
    facc = om.enter_context(nc.sbuf_tensor("facc", [128, NQT * 1024], F32))
    g_all = om.enter_context(nc.sbuf_tensor("g_all", [128, NQT * 16], F32))
    g2t = om.enter_context(nc.sbuf_tensor("g2t", [16, NQ], BF16))
    br_b = om.enter_context(nc.sbuf_tensor("br_b", [16, 1024], BF16))

    oex = ExitStack()
    av_all = oex.enter_context(nc.sbuf_tensor("av_all", [128, HP * NQ], BF16))
    wo_b = oex.enter_context(nc.sbuf_tensor("wo_b", [128, 8192], BF16))
    ident_f = oex.enter_context(nc.sbuf_tensor("ident_f_s", [128, 128], F32))
    sel2 = oex.enter_context(nc.sbuf_tensor("sel2_s", [16, 1024], BF16))
    rec2 = oex.enter_context(nc.sbuf_tensor("rec2", [16, 2048], F32))
    rec2h = oex.enter_context(nc.sbuf_tensor("rec2h", [16, 2048], BF16))
    rec2l = oex.enter_context(nc.sbuf_tensor("rec2l", [16, 2048], BF16))
    xq_t = oex.enter_context(nc.sbuf_tensor("xq_t", [128, 2048], F32))
    res1 = oex.enter_context(nc.sbuf_tensor("res1", [128, 1024], F32))
    u_t = oex.enter_context(nc.sbuf_tensor("u_t", [128, 1024], F32))
    u_bf = oex.enter_context(nc.sbuf_tensor("u_bf", [128, 2048], BF16))
    cent_b = oex.enter_context(nc.sbuf_tensor("cent_b", [128, 128], BF16))
    sc_em = oex.enter_context(nc.sbuf_tensor("sc_em", [16, NQ], F32))
    eb_s = oex.enter_context(nc.sbuf_tensor("eb_s", [16, 1], F32))
    bs_b = oex.enter_context(nc.sbuf_tensor("bs_b", [2, 1024], BF16))
    rt = oex.enter_context(nc.sbuf_tensor("rt", [128, 80], F32))
    gg_t = oex.enter_context(nc.sbuf_tensor("gg_t", [128, 16], BF16))
    o_ps = [oex.enter_context(nc.psum_tensor(f"o_ps{i}", [128, 1024], F32)) for i in range(2)]
    tp2_ps = [oex.enter_context(nc.psum_tensor("tp2_ps0", [128, 128], BF16))]
    tp2_ps.append(tp2_ps[0])
    sc_ps = oex.enter_context(nc.psum_tensor("sc_ps", [16, 512], F32))
    rt_ps = [oex.enter_context(nc.psum_tensor(f"rt_ps{i}", [128, 128], F32)) for i in range(2)]
    misc_ps = rt_ps[0]

    c_av_ld = P.do("sync", lambda e: e.dma_start(
        bass.AP(av_all, 0, [[HP * NQ, 128], [NQ, HP], [1, NQ]]),
        bass.AP(av_d, 0, [[NQ, 128], [128 * NQ, HP], [1, NQ]])),
        waits=[("qX", c_spill_last)], inc=("qX", 16))
    c_sums_ld = P.do("sync", lambda e: e.dma_start(
        bass.AP(rec2, 0, [[2048, 16], [1, 2048]]),
        bass.AP(sums_d, 0, [[2048, 16], [1, 2048]])), inc=("qX", 16))
    c_sel2 = P.do("sync", lambda e: e.dma_start(sel2[:], sel2_in[:, :]), inc=("qX", 16))
    c_rc2 = P.do("vector", lambda e: e.reciprocal(rec2[:, :], rec2[:, :]),
                 waits=[("qX", c_sums_ld)], inc=("tDV", 1))
    c_rh = P.do("vector", lambda e: e.tensor_copy(rec2h[:, :], rec2[:, :]),
                waits=[("tDV", c_rc2)], inc=("tDV", 1))
    c_rl = P.do("vector", lambda e: e.tensor_tensor(
        rec2l[:, :], rec2[:, :], rec2h[:, :], ALU.subtract),
        waits=[("tDV", c_rh)], inc=("tDV", 1))

    nrm_done = {}
    for hp in range(HP):
        for qt in range(NQT):
            i = hp * NQT + qt
            rp = i % 2
            wn = [("tDV", c_rl), ("qX", c_av_ld), ("qX", c_sel2)]
            if i >= 2:
                wn.append(("tDV", nrm_done[i - 2]))
            P.do("tensor", lambda e, _hp=hp, _qt=qt, _rp=rp: e.matmul(
                rt_ps[_rp][:, :], sel2[:, 128 * _hp:128 * _hp + 128],
                rec2h[:, 128 * _qt:128 * _qt + 128],
                start=True, stop=False, skip_group_check=True),
                waits=wn, inc=("tPE", 1))
            c_nm = P.do("tensor", lambda e, _hp=hp, _qt=qt, _rp=rp: e.matmul(
                rt_ps[_rp][:, :], sel2[:, 128 * _hp:128 * _hp + 128],
                rec2l[:, 128 * _qt:128 * _qt + 128],
                start=False, stop=True, skip_group_check=True), inc=("tPE", 1))
            nrm_done[i] = P.do("vector", lambda e, _hp=hp, _qt=qt, _rp=rp: e.tensor_tensor(
                av_all[:, NQ * _hp + 128 * _qt:NQ * _hp + 128 * _qt + 128],
                av_all[:, NQ * _hp + 128 * _qt:NQ * _hp + 128 * _qt + 128],
                rt_ps[_rp][:, :], ALU.mult), waits=[("tPE", c_nm)], inc=("tDV", 1))
    c_nrm_all = P.cnt["tDV"]
    c_idf = P.do("sync", lambda e: e.dma_start(ident_f[:], ident_f_in[:, :]), inc=("qX", 16))
    c_wo = P.do("gpsimd", wcast_load(wo_b, 0, wo, 0), inc=("qW", 16))
    c_cent = P.do("gpsimd", lambda e: e.dma_start(
        bass.AP(cent_b, 0, [[128, 128], [16, 8], [1, 16]]),
        bass.AP(cent_t, 0, [[16, 128], [2048, 8], [1, 16]])), inc=("qW", 16))
    c_eb = P.do("gpsimd", lambda e: e.dma_start(eb_s[:], eb[:, :]), inc=("qW", 16))
    c_br = P.do("gpsimd", lambda e: e.dma_start(br_b[0:16, :], br[:, :]), inc=("qW", 16))
    c_bs = P.do("gpsimd", lambda e: e.dma_start(bs_b[:], bs[:, :]), inc=("qW", 16))

    xq_free = [[], []]
    ops_free = [[], []]
    ubf_free = [[], []]
    c_sq_prev = 0
    c_r_prev = 0
    for qt in range(NQT):
        b = qt % 2
        c_xq = P.do("sync", lambda e, _qt=qt, _b=b: e.dma_start(
            xq_t[:, 1024 * _b:1024 * _b + 1024], x_q[128 * _qt:128 * _qt + 128, :]),
            waits=list(xq_free[b]), inc=("qX", 16))
        wmm = [("tDV", c_nrm_all), ("qW", c_wo)] + ops_free[b]
        c_mm = None
        for ic in range(8):
            for nh in range(2):
                c_mm = P.do("tensor", lambda e, _qt=qt, _ic=ic, _nh=nh, _b=b: e.matmul(
                    o_ps[_b][:, 512 * _nh:512 * _nh + 512],
                    av_all[:, NQ * _ic + 128 * _qt:NQ * _ic + 128 * _qt + 128],
                    wo_b[:, 1024 * _ic + 512 * _nh:1024 * _ic + 512 * _nh + 512],
                    start=(_ic == 0), stop=(_ic == 7), skip_group_check=True),
                    waits=wmm if (ic == 0 and nh == 0) else [], inc=("tPE", 1))
        c_r1 = P.do("vector", lambda e, _b=b: e.tensor_tensor(
            res1[:, :], o_ps[_b][:, :], xq_t[:, 1024 * _b:1024 * _b + 1024], ALU.add),
            waits=[("tPE", c_mm), ("qX", c_xq), ("tSC", c_sq_prev)], inc=("tDV", 1))
        ops_free[b] = [("tDV", c_r1)]
        xq_free[b] = [("tDV", c_r1)]
        c_rs1 = P.do("vector", lambda e: e.reduce_sum(
            rt[:, 0:1], res1[:, :], axis=AX.X),
            waits=[("tSC", c_r_prev)], inc=("tDV", 1))
        wub = list(ubf_free[b])
        P.do("scalar", lambda e, _b=b: e.activation(
            u_bf[:, 1024 * _b:1024 * _b + 1024], res1[:, :], AF.Square,
            accum_out=rt[:, 1:2]), waits=[("tDV", c_r1)] + wub)
        c_sq = P.do("scalar", lambda e: e.copy(rt[:, 14:15], rt[:, 1:2]),
                    inc=("tSC", 1))
        c_sq_prev = c_sq
        d1 = P.do("vector", lambda e: e.tensor_scalar(
            rt[:, 2:3], rt[:, 0:1], 1.0 / D, None, ALU.mult, ALU.bypass),
            waits=[("tDV", c_rs1)], inc=("tDV", 1))
        d2 = P.do("vector", lambda e: e.tensor_tensor(
            rt[:, 3:4], rt[:, 2:3], rt[:, 2:3], ALU.mult),
            waits=[("tDV", d1)], inc=("tDV", 1))
        d3 = P.do("vector", lambda e: e.tensor_scalar(
            rt[:, 4:5], rt[:, 14:15], 1.0 / D, EPS, ALU.mult, ALU.add),
            waits=[("tSC", c_sq)], inc=("tDV", 1))
        d4 = P.do("vector", lambda e: e.tensor_tensor(
            rt[:, 5:6], rt[:, 4:5], rt[:, 3:4], ALU.subtract),
            waits=[("tDV", d3)], inc=("tDV", 1))
        c_vr = P.do("vector", lambda e: e.reciprocal(rt[:, 6:7], rt[:, 5:6]),
                    waits=[("tDV", d4)], inc=("tDV", 1))
        c_r = P.do("scalar", lambda e: e.activation(rt[:, 7:8], rt[:, 6:7], AF.Sqrt),
                   waits=[("tDV", c_vr)], inc=("tSC", 1))
        c_r_prev = c_r
        d5 = P.do("vector", lambda e: e.tensor_scalar(
            rt[:, 8:9], rt[:, 2:3], rt[:, 7:8], -1.0, ALU.mult, ALU.mult),
            waits=[("tSC", c_r)], inc=("tDV", 1))
        d6 = P.do("vector", lambda e: e.tensor_scalar(
            u_t[:, :], res1[:, :], rt[:, 7:8], rt[:, 8:9], ALU.mult, ALU.add),
            waits=[("tDV", d5)], inc=("tDV", 1))
        P.do("vector", lambda e, _qt=qt: e.tensor_tensor(
            facc[:, 1024 * _qt:1024 * _qt + 1024], res1[:, :], u_t[:, :], ALU.add),
            waits=[("tDV", d6)])
        c_ub = P.do("vector", lambda e, _b=b: e.tensor_copy(
            u_bf[:, 1024 * _b:1024 * _b + 1024], u_t[:, :]), inc=("tDV", 1))
        c_ev = None
        for dc in range(8):
            pb = dc % 2
            c_tp = P.do("tensor", lambda e, _b=b, _dc=dc, _pb=pb: e.transpose(
                tp2_ps[_pb][:, :], u_bf[:, 1024 * _b + 128 * _dc:1024 * _b + 128 * _dc + 128],
                ident_b[:]), waits=[("tDV", c_ub), ("tSC", P.cnt["tSC"])],
                inc=("tPE", 1))
            c_ev = P.do("scalar", lambda e, _dc=dc, _pb=pb, _qt=qt: e.activation(
                u_fm[:, NQ * _dc + 128 * _qt:NQ * _dc + 128 * _qt + 128],
                tp2_ps[_pb][:, :], AF.Copy), waits=[("tPE", c_tp)], inc=("tSC", 1))
        ubf_free[b] = [("tSC", c_ev)]
    c_o1_sc = P.cnt["tSC"]

    c_scev = {}
    for c in range(4):
        c_mm = None
        for dc in range(8):
            w0 = [("qW", c_cent), ("tSC", P.cnt["tSC"])]
            if c > 0:
                w0.append(("tDV", c_scev[c - 1]))
            c_mm = P.do("tensor", lambda e, _c=c, _dc=dc: e.matmul(
                sc_ps[:, :], cent_b[:, 16 * _dc:16 * _dc + 16],
                u_fm[:, NQ * _dc + 512 * _c:NQ * _dc + 512 * _c + 512],
                start=(_dc == 0), stop=(_dc == 7), skip_group_check=True),
                waits=w0 if dc == 0 else [], inc=("tPE", 1))
        c_scev[c] = P.do("vector", lambda e, _c=c: e.tensor_scalar(
            sc_em[:, 512 * _c:512 * _c + 512], sc_ps[:, :], eb_s[:, 0:1], None,
            ALU.add, ALU.bypass), waits=[("tPE", c_mm), ("qW", c_eb)], inc=("tDV", 1))
    c_es_prev = 0
    c_m1_prev = 0
    c_gt_prev = 0
    for qt in range(NQT):
        c_tr = P.do("tensor", lambda e, _qt=qt: e.transpose(
            misc_ps[:, 0:16], sc_em[0:16, 128 * _qt:128 * _qt + 128], ident_f[0:16, 0:16]),
            waits=[("tDV", c_scev[qt // 4]), ("qX", c_idf),
                   ("tSC", c_es_prev), ("tDV", c_m1_prev)], inc=("tPE", 1))
        c_m1 = P.do("vector", lambda e: e.reduce_max(
            rt[:, 9:10], misc_ps[:, 0:16], axis=AX.X, negate=True),
            waits=[("tPE", c_tr)], inc=("tDV", 1))
        c_m1_prev = c_m1
        c_es = P.do("scalar", lambda e: e.activation(
            rt[:, 32:48], misc_ps[:, 0:16], AF.Exp, bias=rt[:, 9:10], scale=1.0),
            waits=[("tDV", c_m1)], inc=("tSC", 1))
        c_es_prev = c_es
        c_rsm = P.do("vector", lambda e: e.reduce_sum(
            rt[:, 10:11], rt[:, 32:48], axis=AX.X), waits=[("tSC", c_es)],
            inc=("tDV", 1))
        e1 = P.do("vector", lambda e: e.reciprocal(rt[:, 11:12], rt[:, 10:11]),
             waits=[("tDV", c_rsm)], inc=("tDV", 1))
        e2 = P.do("vector", lambda e: e.tensor_scalar(
            rt[:, 48:64], rt[:, 32:48], rt[:, 11:12], None, ALU.mult, ALU.bypass),
            waits=[("tDV", e1)], inc=("tDV", 1))
        c_rm1 = P.do("vector", lambda e: e.reduce_max(
            rt[:, 12:13], rt[:, 48:64], axis=AX.X),
            waits=[("tDV", e2)], inc=("tDV", 1))
        e3 = P.do("vector", lambda e: e.tensor_scalar(
            rt[:, 64:80], rt[:, 48:64], rt[:, 12:13], None, ALU.is_ge, ALU.bypass),
            waits=[("tDV", c_rm1)], inc=("tDV", 1))
        e4 = P.do("vector", lambda e: e.scalar_tensor_tensor(
            rt[:, 16:32], rt[:, 64:80], -1e4, rt[:, 48:64], ALU.mult, ALU.add),
            waits=[("tDV", e3)], inc=("tDV", 1))
        c_rm2 = P.do("vector", lambda e: e.reduce_max(
            rt[:, 13:14], rt[:, 16:32], axis=AX.X),
            waits=[("tDV", e4)], inc=("tDV", 1))
        e5 = P.do("vector", lambda e: e.tensor_scalar(
            rt[:, 64:80], rt[:, 48:64], rt[:, 13:14], None, ALU.is_ge, ALU.bypass),
            waits=[("tDV", c_rm2)], inc=("tDV", 1))
        c_g = P.do("vector", lambda e, _qt=qt: e.tensor_tensor(
            g_all[:, 16 * _qt:16 * _qt + 16], rt[:, 48:64], rt[:, 64:80], ALU.mult),
            waits=[("tDV", e5)], inc=("tDV", 1))
        c_gg = P.do("vector", lambda e, _qt=qt: e.tensor_scalar(
            gg_t[:, :], g_all[:, 16 * _qt:16 * _qt + 16], 2.0, None,
            ALU.mult, ALU.bypass), waits=[("tDV", c_g), ("tPE", c_gt_prev)],
            inc=("tDV", 1))
        c_gt = P.do("tensor", lambda e: e.transpose(
            tp2_ps[0][0:16, :], gg_t[:, :], ident_b[:]),
            waits=[("tDV", c_gg), ("tSC", P.cnt["tSC"])], inc=("tPE", 1))
        c_gt_prev = c_gt
        P.do("scalar", lambda e, _qt=qt: e.activation(
            g2t[0:16, 128 * _qt:128 * _qt + 128], tp2_ps[0][0:16, :], AF.Copy),
            waits=[("tPE", c_gt)], inc=("tSC", 1))
    c_o_done_sc = P.cnt["tSC"]
    c_o_done_dv = P.cnt["tDV"]

    if stop_after == 'o':
        P.do("sync", lambda e: e.dma_start(
            bass.AP(out_d, 0, [[1024, 128], [131072, 16], [1, 1024]]),
            bass.AP(facc, 0, [[16384, 128], [1024, 16], [1, 1024]])),
            waits=[("tDV", P.cnt["tDV"]), ("tSC", P.cnt["tSC"])], inc=("qX", 16))
        P.do("gpsimd", lambda e: e.dma_start(
            bass.AP(av_d, 0, [[256, 128], [1, 256]]),
            bass.AP(g_all, 0, [[256, 128], [1, 256]])),
            waits=[("tDV", P.cnt["tDV"])], inc=("qW", 16))
        P.do("sync", lambda e: e.dma_start(
            bass.AP(av_d, 128 * 256, [[2048, 16], [1, 2048]]),
            bass.AP(g2t, 0, [[2048, 16], [1, 2048]])), inc=("qX", 16))
        P.do("sync", lambda e: e.dma_start(
            bass.AP(av_d, 2 * 128 * 256, [[2048, 128], [1, 2048]]),
            u_fm[:, 0:2048]), inc=("qX", 16))
        P.flush()
        return nc
    P.flush()
    oex.close()

    P.barrier()
    mex = ExitStack()
    wr_b = mex.enter_context(nc.sbuf_tensor("wr_b", [128, 2 * 8192], BF16))
    ws_tmp = mex.enter_context(nc.sbuf_tensor("ws_tmp", [128, 8192], BF16))
    e_ps = [mex.enter_context(nc.psum_tensor(f"e_ps{i}", [128, 1024], F32)) for i in range(3)]

    c_ws0 = P.do("gpsimd", wcast_load(wr_b, 0, wst, 0), inc=("qW", 16))
    c_ws1 = P.do("gpsimd", wcast_load(ws_tmp, 0, wst, D * D), inc=("qW", 16))
    c_wsum = P.do("vector", lambda e: e.tensor_tensor(
        wr_b[:, 0:8192], wr_b[:, 0:8192], ws_tmp[:, 0:8192], ALU.add),
        waits=[("qW", c_ws1)], inc=("tDV", 1))
    w_ready = {0: ("tDV", c_wsum)}
    wbuf_last_mm = {0: 0, 1: 0}
    mm_done = {}
    stt_done = {}
    psum_guard = {0: None, 1: None, 2: None}
    for t in range(NE + 1):
        if t + 1 <= NE:
            ee = t
            half = (t + 1) % 2
            w_ready[t + 1] = ("qW", P.do(
                "gpsimd", wcast_load(wr_b, 8192 * half, wrt, ee * D * D),
                waits=[("tPE", wbuf_last_mm[half])] if wbuf_last_mm[half] else [],
                inc=("qW", 16)))
        half = t % 2
        for qt in range(NQT):
            r = (t * NQT + qt) % 3
            w0 = [w_ready[t]]
            if psum_guard[r] is not None:
                w0.append(psum_guard[r])
            c_mm = None
            for dc in range(8):
                for nh in range(2):
                    c_mm = P.do("tensor", lambda e, _t=t, _qt=qt, _dc=dc, _nh=nh,
                                _r=r, _hf=half: e.matmul(
                        e_ps[_r][:, 512 * _nh:512 * _nh + 512],
                        u_fm[:, NQ * _dc + 128 * _qt:NQ * _dc + 128 * _qt + 128],
                        wr_b[:, 8192 * _hf + 1024 * _dc + 512 * _nh:
                             8192 * _hf + 1024 * _dc + 512 * _nh + 512],
                        start=(_dc == 0), stop=(_dc == 7 and _t > 0),
                        skip_group_check=True),
                        waits=w0 if (dc == 0 and nh == 0) else [], inc=("tPE", 1))
            if t == 0:
                for nh in range(2):
                    P.do("tensor", lambda e, _qt=qt, _nh=nh, _r=r: e.matmul(
                        e_ps[_r][:, 512 * _nh:512 * _nh + 512],
                        g2t[0:16, 128 * _qt:128 * _qt + 128],
                        br_b[0:16, 512 * _nh:512 * _nh + 512],
                        start=False, stop=False, skip_group_check=True),
                        waits=[("tSC", c_o_done_sc), ("qW", c_br), ("qW", c_bs)] if (qt == 0 and nh == 0) else [],
                        inc=("tPE", 1))
                    c_mm = P.do("tensor", lambda e, _nh=nh, _r=r: e.matmul(
                        e_ps[_r][:, 512 * _nh:512 * _nh + 512],
                        ones2[:, 0:128],
                        bs_b[:, 512 * _nh:512 * _nh + 512],
                        start=False, stop=True, skip_group_check=True),
                        inc=("tPE", 1))
            mm_done[(t, qt)] = c_mm
            wbuf_last_mm[half] = c_mm
            c_stt = None
            w_prev_stt = [("tDV", stt_done[(t - 1, qt)])] if t > 0 else [("tDV", c_o_done_dv)]
            for nh in range(2):
                if t == 0:
                    c_stt = P.do("vector", lambda e, _qt=qt, _nh=nh, _r=r:
                                 e.scalar_tensor_tensor(
                                     facc[:, 1024 * _qt + 512 * _nh:1024 * _qt + 512 * _nh + 512],
                                     e_ps[_r][:, 512 * _nh:512 * _nh + 512], 0.5,
                                     facc[:, 1024 * _qt + 512 * _nh:1024 * _qt + 512 * _nh + 512],
                                     ALU.mult, ALU.add),
                                 waits=([("tPE", c_mm)] + w_prev_stt) if nh == 0 else [],
                                 inc=("tDV", 1))
                else:
                    c_stt = P.do("vector", lambda e, _t=t, _qt=qt, _nh=nh, _r=r:
                                 e.scalar_tensor_tensor(
                                     facc[:, 1024 * _qt + 512 * _nh:1024 * _qt + 512 * _nh + 512],
                                     e_ps[_r][:, 512 * _nh:512 * _nh + 512],
                                     g_all[:, 16 * _qt + (_t - 1):16 * _qt + _t],
                                     facc[:, 1024 * _qt + 512 * _nh:1024 * _qt + 512 * _nh + 512],
                                     ALU.mult, ALU.add),
                                 waits=([("tPE", c_mm)] + w_prev_stt) if nh == 0 else [],
                                 inc=("tDV", 1))
            stt_done[(t, qt)] = c_stt
            psum_guard[r] = ("tDV", c_stt)

    c_store = None
    for qt in range(NQT):
        c_store = P.do("sync", lambda e, _qt=qt: e.dma_start(
            out_d[128 * _qt:128 * _qt + 128, :], facc[:, 1024 * _qt:1024 * _qt + 1024]),
            waits=[("tDV", stt_done[(NE, qt)])], inc=("qX", 16))
    P.do("sync", lambda e: (e.wait_ge(sems["qX"], c_store), None)[1])
    P.flush()
    mex.close()
    om.close()
    es.close()
    return nc


_NC_CACHE = None


def _get_nc():
    global _NC_CACHE
    if _NC_CACHE is None:
        _NC_CACHE = build_nc()
    return _NC_CACHE


def kernel(L0_attn_x, L0_attn_Wq, L0_attn_Wk, L0_attn_Wv, L0_attn_Wo,
           L0_ffn_centroids, L0_ffn_expert_biases, L0_ffn_Wr, L0_ffn_br,
           L0_ffn_Ws, L0_ffn_bs):
    x = np.ascontiguousarray(np.asarray(L0_attn_x, dtype=np.float32))
    wq = np.ascontiguousarray(np.asarray(L0_attn_Wq, dtype=np.float32))
    wk = np.ascontiguousarray(np.asarray(L0_attn_Wk, dtype=np.float32))
    wv = np.ascontiguousarray(np.asarray(L0_attn_Wv, dtype=np.float32))
    wo = np.ascontiguousarray(np.asarray(L0_attn_Wo, dtype=np.float32))
    wrt = np.ascontiguousarray(np.asarray(L0_ffn_Wr, dtype=np.float32).transpose(0, 2, 1))
    wst = np.ascontiguousarray(np.asarray(L0_ffn_Ws, dtype=np.float32).transpose(0, 2, 1))
    cent_t = np.ascontiguousarray(np.asarray(L0_ffn_centroids, dtype=np.float32).T)
    ebv = np.ascontiguousarray(np.asarray(L0_ffn_expert_biases, dtype=np.float32).reshape(16, 1))
    brv = np.ascontiguousarray(np.asarray(L0_ffn_br, dtype=np.float32))
    bsv = np.ascontiguousarray(np.asarray(L0_ffn_bs, dtype=np.float32))
    ident_b = np.eye(128, dtype=BF)
    ident_f = np.eye(128, dtype=np.float32)
    sel16 = np.zeros((16, 1024), dtype=BF)
    for hp_ in range(8):
        sel16[2 * hp_, 128 * hp_:128 * hp_ + 64] = 1.0
        sel16[2 * hp_ + 1, 128 * hp_ + 64:128 * hp_ + 128] = 1.0

    perms = []
    for par in range(2):
        idx = np.concatenate([
            np.arange(128 * (31 - k) + 64 * par, 128 * (31 - k) + 64 * par + 64)
            for k in range(32)])
        perms.append(idx)
    masks = []
    for par in range(2):
        jj = np.arange(128)[:, None]
        ii = np.arange(64)[None, :]
        masks.append((jj <= 64 * par + ii).astype(BF))

    in_maps = []
    for c in range(8):
        b, par = c // 2, c % 2
        in_maps.append({
            "x_ctx": x[b],
            "x_q": np.ascontiguousarray(x[b][perms[par]]),
            "wq": wq, "wk": wk, "wv": wv, "wo": wo,
            "wrt": wrt, "wst": wst, "cent_t": cent_t, "eb": ebv,
            "br": brv, "bs": bsv, "mask": masks[par],
            "ident_b": ident_b, "ident_f": ident_f, "sel2": sel16,
        })

    nc = _get_nc()
    res = run_bass_kernel_spmd(nc, in_maps, core_ids=list(range(8)))
    out = np.empty((4, 4096, 1024), dtype=np.float32)
    for c in range(8):
        b, par = c // 2, c % 2
        out[b][perms[par]] = res.results[c]["out"]
    return out


# version 08
# speedup vs baseline: 1.6240x; 1.0042x over previous
import numpy as np
import ml_dtypes
from contextlib import ExitStack

import concourse.bass as bass
import concourse.mybir as mybir
from concourse.bass_utils import run_bass_kernel_spmd

F32 = mybir.dt.float32
BF16 = mybir.dt.bfloat16
AF = mybir.ActivationFunctionType
ALU = mybir.AluOpType
AX = mybir.AxisListType
BF = ml_dtypes.bfloat16

D = 1024
NCTX = 4096
NQ = 2048
NKVT = 32
NQT = 16
HP = 8
SCALE = 0.125
EPS = 1e-5
NE = 16

ENGS = ("tensor", "vector", "scalar", "gpsimd", "sync")


class Prog:

    def __init__(self, nc, sems):
        self.nc = nc
        self.sems = sems
        self.cnt = {k: 0 for k in sems}
        self.q = {e: [] for e in ENGS}
        self.lastw = {e: {} for e in ENGS}

    def do(self, eng, fn, waits=(), inc=None):
        real_waits = []
        lw = self.lastw[eng]
        for s, v in waits:
            if s is None or v <= 0:
                continue
            if lw.get(s, -1) < v:
                real_waits.append((s, v))
                lw[s] = v
        if inc is not None:
            self.cnt[inc[0]] += inc[1]
        self.q[eng].append((fn, tuple(real_waits), inc))
        return self.cnt[inc[0]] if inc is not None else None

    def barrier(self):
        self.cnt["bar"] += 5
        tgt = self.cnt["bar"]
        for e in ENGS:
            def fn(eng, _t=tgt):
                eng.sem_inc(self.sems["bar"], 1)
                eng.wait_ge(self.sems["bar"], _t)
                return None
            self.q[e].append((fn, (), None))
            self.lastw[e]["bar"] = tgt

    def flush(self):
        with self.nc.Block() as blk:
            for e in ENGS:
                ops = self.q[e]
                self.q[e] = []
                if not ops:
                    continue

                def body(eng, _ops=tuple(ops), _sems=self.sems):
                    for fn, waits, inc in _ops:
                        for s, v in waits:
                            eng.wait_ge(_sems[s], v)
                        r = fn(eng)
                        if inc is not None and r is not None:
                            r.then_inc(_sems[inc[0]], inc[1])
                getattr(blk, e)(body)


def build_nc(stop_after='all'):
    nc = bass.Bass("TRN2", target_bir_lowering=False, detect_race_conditions=False)

    x_ctx = nc.dram_tensor("x_ctx", [NCTX, D], F32, kind="ExternalInput")
    x_q = nc.dram_tensor("x_q", [NQ, D], F32, kind="ExternalInput")
    wq = nc.dram_tensor("wq", [D, D], F32, kind="ExternalInput")
    wk = nc.dram_tensor("wk", [D, D], F32, kind="ExternalInput")
    wv = nc.dram_tensor("wv", [D, D], F32, kind="ExternalInput")
    wo = nc.dram_tensor("wo", [D, D], F32, kind="ExternalInput")
    wrt = nc.dram_tensor("wrt", [NE, D, D], F32, kind="ExternalInput")
    wst = nc.dram_tensor("wst", [2, D, D], F32, kind="ExternalInput")
    cent_t = nc.dram_tensor("cent_t", [D, 16], F32, kind="ExternalInput")
    eb = nc.dram_tensor("eb", [16, 1], F32, kind="ExternalInput")
    br = nc.dram_tensor("br", [16, D], F32, kind="ExternalInput")
    bs = nc.dram_tensor("bs", [2, D], F32, kind="ExternalInput")
    mask_in = nc.dram_tensor("mask", [128, 64], BF16, kind="ExternalInput")
    ident_b_in = nc.dram_tensor("ident_b", [128, 128], BF16, kind="ExternalInput")
    ident_f_in = nc.dram_tensor("ident_f", [128, 128], F32, kind="ExternalInput")
    sel2_in = nc.dram_tensor("sel2", [16, 1024], BF16, kind="ExternalInput")
    out_d = nc.dram_tensor("out", [NQ, D], F32, kind="ExternalOutput")
    av_d = nc.dram_tensor("av_spill", [HP, 128, NQ], BF16, kind="ExternalOutput")
    sums_d = nc.dram_tensor("sums_spill", [HP, 2, NQ], F32)

    es = ExitStack()
    sems = {}
    for s in ("qW", "qX", "tPE", "tSC", "tDV", "tGP", "bar"):
        sems[s] = es.enter_context(nc.semaphore(s))
    P = Prog(nc, sems)

    ident_b = es.enter_context(nc.sbuf_tensor("ident_b_s", [128, 128], BF16))
    mask_s = es.enter_context(nc.sbuf_tensor("mask_s", [128, 64], BF16))
    ones1 = es.enter_context(nc.sbuf_tensor("ones1", [128, 1], BF16))
    ones2 = es.enter_context(nc.sbuf_tensor("ones2", [2, 128], BF16))
    negb8 = es.enter_context(nc.sbuf_tensor("negb8", [128, 8], F32))
    stats = es.enter_context(nc.sbuf_tensor("stats", [128, 24], F32))

    c_idb = P.do("sync", lambda e: e.dma_start(ident_b[:], ident_b_in[:, :]), inc=("qX", 16))
    c_msk = P.do("sync", lambda e: e.dma_start(mask_s[:], mask_in[:, :]), inc=("qX", 16))
    P.do("gpsimd", lambda e: e.memset(ones1[:], 1.0))
    c_ones = P.do("gpsimd", lambda e: e.memset(ones2[:], 1.0), inc=("tGP", 1))

    def wcast_load(dst, dst_col, src, e_off):
        dst_stride = dst.shape[1]

        def fn(eng):
            return eng.dma_start(
                bass.AP(dst, dst_col, [[dst_stride, 128], [1024, 8], [1, 1024]]),
                bass.AP(src, e_off, [[1024, 128], [131072, 8], [1, 1024]]),
            )
        return fn

    kvq = ExitStack()
    k_all = kvq.enter_context(nc.sbuf_tensor("k_all", [128, HP * NCTX], BF16))
    v_all = kvq.enter_context(nc.sbuf_tensor("v_all", [128, NKVT * 1040], BF16))
    q_all = kvq.enter_context(nc.sbuf_tensor("q_all", [128, HP * NQ], BF16))
    xt = kvq.enter_context(nc.sbuf_tensor("xt", [128, 2 * 1024], F32))
    xlt = kvq.enter_context(nc.sbuf_tensor("xlt", [128, 1024], BF16))
    xlf = kvq.enter_context(nc.sbuf_tensor("xlf", [128, 8 * 256], BF16))

    lnst = {
        "xt_free": [[], []],
        "xlt_free": [],
        "cr_prev": [0, 0],
        "g": 0,
    }

    def ln_tile(src_dram, row0, dst_fm_col, tp_ps, dump=None, qx_extra=()):
        g = lnst["g"]
        lnst["g"] += 1
        b = g % 2
        sb = b * 12
        c_x = P.do("sync", lambda e, _b=b, _r=row0, _s=src_dram: e.dma_start(
            xt[:, 1024 * _b:1024 * _b + 1024], _s[_r:_r + 128, :]),
            waits=list(lnst["xt_free"][b]) + list(qx_extra), inc=("qX", 16))
        xv = xt[:, 1024 * b:1024 * b + 1024]
        c_rs = P.do("vector", lambda e, _x=xv, _c=sb: e.reduce_sum(
            stats[:, _c:_c + 1], _x, axis=AX.X),
            waits=[("qX", c_x), ("tSC", lnst["cr_prev"][b])], inc=("tDV", 1))
        dmp = dump if dump is not None else xlt[:, :]
        wsq = [("qX", c_x), ("tSC", lnst["cr_prev"][b])]
        if dump is None:
            wsq += list(lnst["xlt_free"])
        P.do("scalar", lambda e, _x=xv, _c=sb, _d=dmp: e.activation(
            _d, _x, AF.Square, accum_out=stats[:, _c + 1:_c + 2]), waits=wsq)
        c_sq = P.do("scalar", lambda e, _c=sb: e.copy(
            stats[:, _c + 9:_c + 10], stats[:, _c + 1:_c + 2]), inc=("tSC", 1))
        c1 = P.do("vector", lambda e, _c=sb: e.tensor_scalar(
            stats[:, _c + 2:_c + 3], stats[:, _c:_c + 1], 1.0 / D, None,
            ALU.mult, ALU.bypass), waits=[("tDV", c_rs)], inc=("tDV", 1))
        c2 = P.do("vector", lambda e, _c=sb: e.tensor_tensor(
            stats[:, _c + 3:_c + 4], stats[:, _c + 2:_c + 3], stats[:, _c + 2:_c + 3],
            ALU.mult), waits=[("tDV", c1)], inc=("tDV", 1))
        c3 = P.do("vector", lambda e, _c=sb: e.tensor_scalar(
            stats[:, _c + 4:_c + 5], stats[:, _c + 9:_c + 10], 1.0 / D, EPS,
            ALU.mult, ALU.add), waits=[("tSC", c_sq)], inc=("tDV", 1))
        c4 = P.do("vector", lambda e, _c=sb: e.tensor_tensor(
            stats[:, _c + 5:_c + 6], stats[:, _c + 4:_c + 5], stats[:, _c + 3:_c + 4],
            ALU.subtract), waits=[("tDV", c3)], inc=("tDV", 1))
        c_vr = P.do("vector", lambda e, _c=sb: e.reciprocal(
            stats[:, _c + 6:_c + 7], stats[:, _c + 5:_c + 6]),
            waits=[("tDV", c4)], inc=("tDV", 1))
        c_r = P.do("scalar", lambda e, _c=sb: e.activation(
            stats[:, _c + 7:_c + 8], stats[:, _c + 6:_c + 7], AF.Sqrt),
            waits=[("tDV", c_vr)], inc=("tSC", 1))
        lnst["cr_prev"][b] = c_r
        c5 = P.do("vector", lambda e, _c=sb: e.tensor_scalar(
            stats[:, _c + 8:_c + 9], stats[:, _c + 2:_c + 3], stats[:, _c + 7:_c + 8],
            -1.0, ALU.mult, ALU.mult), waits=[("tSC", c_r)], inc=("tDV", 1))
        c_ap = P.do("vector", lambda e, _x=xv, _c=sb: e.tensor_scalar(
            xlt[:, :], _x, stats[:, _c + 7:_c + 8],
            stats[:, _c + 8:_c + 9], ALU.mult, ALU.add),
            waits=[("tDV", c5)] + list(lnst["xlt_free"]), inc=("tDV", 1))
        lnst["xt_free"][b] = [("tDV", c_ap), ("tSC", c_sq)]
        c_ev = None
        c_tp = None
        for dc in range(8):
            pb = dc % 2
            c_tp = P.do("tensor", lambda e, _dc=dc, _pb=pb, _tp=tp_ps: e.transpose(
                _tp[_pb][:, :], xlt[:, 128 * _dc:128 * _dc + 128],
                ident_b[:]),
                waits=[("tDV", c_ap), ("tDV", P.cnt["tDV"]), ("qX", c_idb)],
                inc=("tPE", 1))
            c_ev = P.do("vector", lambda e, _dc=dc, _pb=pb, _col=dst_fm_col, _tp=tp_ps:
                        e.tensor_copy(
                            xlf[:, _col + 256 * _dc:_col + 256 * _dc + 128],
                            _tp[_pb][:, :]),
                        waits=[("tPE", c_tp)], inc=("tDV", 1))
        lnst["xlt_free"] = [("tPE", c_tp)]
        return c_ev

    p0 = ExitStack()
    wk_b = p0.enter_context(nc.sbuf_tensor("wk_b", [128, 8192], BF16))
    wv_b = p0.enter_context(nc.sbuf_tensor("wv_b", [128, 8192], BF16))
    tp_ps_a = [p0.enter_context(nc.psum_tensor(f"tp_ps{i}", [128, 128], BF16)) for i in range(2)]
    kq_ps_a = [p0.enter_context(nc.psum_tensor(f"kq_ps{i}", [128, 256], F32)) for i in range(2)]
    v_ps = [p0.enter_context(nc.psum_tensor(f"v_ps{i}", [128, 1024], F32)) for i in range(2)]

    c_wk = P.do("gpsimd", wcast_load(wk_b, 0, wk, 0), inc=("qW", 16))
    c_wv = P.do("gpsimd", wcast_load(wv_b, 0, wv, 0), inc=("qW", 16))
    c_vones = P.do("gpsimd", lambda e: e.memset(v_all[:, :], 1.0), inc=("tGP", 1))

    kq_free = [[], []]
    v_free = [[], []]
    for kb in range(NCTX // 256):
        c_ev = 0
        for t2 in range(2):
            c_ev = ln_tile(x_ctx, 256 * kb + 128 * t2, 128 * t2, tp_ps_a,
                           dump=q_all[:, 0:1024])
            if stop_after == 'ln0' and kb == 0 and t2 == 0:
                P.do("sync", lambda e: e.dma_start(out_d[0:128, 0:24], stats[:, :]),
                     waits=[("tSC", P.cnt["tSC"]), ("tDV", P.cnt["tDV"])], inc=("qX", 16))
                P.do("sync", lambda e: e.dma_start(
                    bass.AP(av_d, 0, [[2048, 128], [1, 1024]]), xlt[:, :]), inc=("qX", 16))
                P.do("sync", lambda e: e.dma_start(
                    bass.AP(av_d, 128 * 2048, [[2048, 128], [1, 2048]]), xlf[:, :]), inc=("qX", 16))
                P.flush()
                return nc
        for hp in range(HP):
            pb = hp % 2
            w0 = [("tDV", c_ev), ("qW", c_wk)] + kq_free[pb]
            c_mm = None
            for dc in range(8):
                c_mm = P.do("tensor", lambda e, _hp=hp, _dc=dc, _pb=pb: e.matmul(
                    kq_ps_a[_pb][:, :],
                    wk_b[:, 1024 * _dc + 128 * _hp:1024 * _dc + 128 * _hp + 128],
                    xlf[:, 256 * _dc:256 * _dc + 256],
                    start=(_dc == 0), stop=(_dc == 7), skip_group_check=True),
                    waits=w0 if dc == 0 else [], inc=("tPE", 1))
            c_kev = P.do("scalar", lambda e, _hp=hp, _kb=kb, _pb=pb: e.activation(
                k_all[:, NCTX * _hp + 256 * _kb:NCTX * _hp + 256 * _kb + 256],
                kq_ps_a[_pb][:, :], AF.Copy), waits=[("tPE", c_mm)], inc=("tSC", 1))
            kq_free[pb] = [("tSC", c_kev)]
        for t2 in range(2):
            j = 2 * kb + t2
            pb = j % 2
            w0 = [("tDV", c_ev), ("qW", c_wv)] + v_free[pb]
            c_mm = None
            for dc in range(8):
                for nh in range(2):
                    c_mm = P.do("tensor", lambda e, _t2=t2, _dc=dc, _nh=nh, _pb=pb: e.matmul(
                        v_ps[_pb][:, 512 * _nh:512 * _nh + 512],
                        xlf[:, 256 * _dc + 128 * _t2:256 * _dc + 128 * _t2 + 128],
                        wv_b[:, 1024 * _dc + 512 * _nh:1024 * _dc + 512 * _nh + 512],
                        start=(_dc == 0), stop=(_dc == 7), skip_group_check=True),
                        waits=w0 if (dc == 0 and nh == 0) else [], inc=("tPE", 1))
            c_vev = P.do("scalar", lambda e, _j=j, _pb=pb: e.activation(
                bass.AP(v_all, 1040 * _j, [[NKVT * 1040, 128], [130, 8], [65, 2], [1, 64]]),
                bass.AP(v_ps[_pb], 0, [[1024, 128], [128, 8], [64, 2], [1, 64]]),
                AF.Copy),
                waits=[("tPE", c_mm), ("tGP", c_vones)], inc=("tSC", 1))
            v_free[pb] = [("tSC", c_vev)]
    c_p0a_pe = P.cnt["tPE"]

    P.flush()
    if stop_after == 'p0a':
        return nc
    p0.close()

    p0b = ExitStack()
    wq_b = p0b.enter_context(nc.sbuf_tensor("wq_b", [128, 8192], BF16))
    tp_ps_b = [p0b.enter_context(nc.psum_tensor(f"tpb_ps{i}", [128, 128], BF16)) for i in range(2)]
    kq_ps_b = [p0b.enter_context(nc.psum_tensor(f"kqb_ps{i}", [128, 256], F32)) for i in range(2)]

    c_wq = P.do("gpsimd", wcast_load(wq_b, 0, wq, 0),
                waits=[("tPE", c_p0a_pe)], inc=("qW", 16))
    kq_free = [[], []]
    for qb in range(NQ // 256):
        c_ev = 0
        for t2 in range(2):
            c_ev = ln_tile(x_q, 256 * qb + 128 * t2, 128 * t2, tp_ps_b)
        for hp in range(HP):
            pb = hp % 2
            w0 = [("tDV", c_ev), ("qW", c_wq)] + kq_free[pb]
            c_mm = None
            for dc in range(8):
                c_mm = P.do("tensor", lambda e, _hp=hp, _dc=dc, _pb=pb: e.matmul(
                    kq_ps_b[_pb][:, :],
                    wq_b[:, 1024 * _dc + 128 * _hp:1024 * _dc + 128 * _hp + 128],
                    xlf[:, 256 * _dc:256 * _dc + 256],
                    start=(_dc == 0), stop=(_dc == 7), skip_group_check=True),
                    waits=w0 if dc == 0 else [], inc=("tPE", 1))
            c_qev = P.do("scalar", lambda e, _hp=hp, _qb=qb, _pb=pb: e.activation(
                q_all[:, NQ * _hp + 256 * _qb:NQ * _hp + 256 * _qb + 256],
                kq_ps_b[_pb][:, :], AF.Copy), waits=[("tPE", c_mm)], inc=("tSC", 1))
            kq_free[pb] = [("tSC", c_qev)]
    c_p0b_pe = P.cnt["tPE"]
    c_p0b_sc = P.cnt["tSC"]

    if stop_after == 'p0b':
        P.do("sync", lambda e: e.dma_start(
            bass.AP(av_d, 0, [[2048, 128], [1, 2048]]),
            k_all[:, 0:2048]), waits=[("tSC", P.cnt["tSC"])], inc=("qX", 16))
        P.do("sync", lambda e: e.dma_start(
            bass.AP(av_d, 128 * 2048, [[2048, 128], [1, 2048]]),
            k_all[:, 7 * 4096:7 * 4096 + 2048]), inc=("qX", 16))
        P.do("sync", lambda e: e.dma_start(
            bass.AP(av_d, 2 * 128 * 2048, [[2048, 128], [1, 2048]]),
            q_all[:, 7 * 2048:7 * 2048 + 2048]), inc=("qX", 16))
        P.do("sync", lambda e: e.dma_start(
            bass.AP(av_d, 3 * 128 * 2048, [[2048, 128], [1, 2048]]),
            v_all[:, 0:2048]), inc=("qX", 16))
        P.flush()
        return nc
    P.flush()
    p0b.close()

    aex = ExitStack()
    pt = aex.enter_context(nc.sbuf_tensor("pt", [128, 4096], BF16))
    ksq = pt
    s64r = aex.enter_context(nc.sbuf_tensor("s64r", [65, 2048], F32))
    av_spl = aex.enter_context(nc.sbuf_tensor("av_spl", [64, 4096], BF16))
    km = aex.enter_context(nc.sbuf_tensor("km", [1, 16], F32))
    bsc = aex.enter_context(nc.sbuf_tensor("bsc", [1, 4], F32))
    negvb = aex.enter_context(nc.sbuf_tensor("negvb", [1, 1], BF16))
    av_ps = [aex.enter_context(nc.psum_tensor(f"av_ps{i}", [65, 1024], F32)) for i in range(2)]
    d_psA = aex.enter_context(nc.psum_tensor("d_psA", [128, 1024], F32))
    d_psB = aex.enter_context(nc.psum_tensor("d_psB", [128, 1024], F32))

    for hp in range(HP):
        c_k2 = P.do("vector", lambda e, _hp=hp: e.tensor_tensor(
            ksq[:, :], k_all[:, NCTX * _hp:NCTX * _hp + 4096],
            k_all[:, NCTX * _hp:NCTX * _hp + 4096], ALU.mult),
            waits=[("tPE", c_p0b_pe), ("tSC", c_p0b_sc)], inc=("tDV", 1))
        c_rm = 0
        for c in range(8):
            pb = c % 2
            c_n2 = P.do("tensor", lambda e, _c=c, _pb=pb: e.matmul(
                d_psA[0:1, 512 * _pb:512 * _pb + 512], ones1[:],
                ksq[:, 512 * _c:512 * _c + 512],
                start=True, stop=True, skip_group_check=True),
                waits=[("tDV", c_k2), ("tGP", c_ones)] if c < 2 else [("tDV", c_rm)],
                inc=("tPE", 1))
            c_rm = P.do("vector", lambda e, _c=c, _pb=pb: e.reduce_max(
                km[0:1, _c:_c + 1], d_psA[0:1, 512 * _pb:512 * _pb + 512], axis=AX.X),
                waits=[("tPE", c_n2)], inc=("tDV", 1))
        c_km = P.do("vector", lambda e: e.reduce_max(
            bsc[0:1, 0:1], km[0:1, 0:8], axis=AX.X),
            waits=[("tDV", c_rm)], inc=("tDV", 1))
        c_q2 = P.do("vector", lambda e, _hp=hp: e.tensor_tensor(
            ksq[:, 0:2048], q_all[:, NQ * _hp:NQ * _hp + 2048],
            q_all[:, NQ * _hp:NQ * _hp + 2048], ALU.mult),
            waits=[("tPE", P.cnt["tPE"])], inc=("tDV", 1))
        for c in range(4):
            pb = c % 2
            c_n2 = P.do("tensor", lambda e, _c=c, _pb=pb: e.matmul(
                d_psA[0:1, 512 * _pb:512 * _pb + 512], ones1[:],
                ksq[:, 512 * _c:512 * _c + 512],
                start=True, stop=True, skip_group_check=True),
                waits=[("tDV", c_q2)] if c < 2 else [("tDV", c_rm)], inc=("tPE", 1))
            c_rm = P.do("vector", lambda e, _c=c, _pb=pb: e.reduce_max(
                km[0:1, 8 + _c:9 + _c], d_psA[0:1, 512 * _pb:512 * _pb + 512], axis=AX.X),
                waits=[("tPE", c_n2)], inc=("tDV", 1))
        c_qm = P.do("vector", lambda e: e.reduce_max(
            bsc[0:1, 1:2], km[0:1, 8:12], axis=AX.X),
            waits=[("tDV", c_rm)], inc=("tDV", 1))
        c_pr = P.do("vector", lambda e: e.tensor_tensor(
            bsc[0:1, 2:3], bsc[0:1, 0:1], bsc[0:1, 1:2], ALU.mult),
            waits=[("tDV", c_qm)], inc=("tDV", 1))
        c_sq2 = P.do("scalar", lambda e: e.activation(
            bsc[0:1, 3:4], bsc[0:1, 2:3], AF.Sqrt), waits=[("tDV", c_pr)], inc=("tSC", 1))
        c_nv = P.do("vector", lambda e: e.tensor_scalar(
            negvb[0:1, 0:1], bsc[0:1, 3:4], -SCALE, None, ALU.mult, ALU.bypass),
            waits=[("tSC", c_sq2)], inc=("tDV", 1))
        c_bc = P.do("tensor", lambda e: e.matmul(
            d_psA[:, 0:1], ones2[0:1, :], negvb[0:1, 0:1], start=True, stop=True,
            skip_group_check=True), waits=[("tDV", c_nv)], inc=("tPE", 1))
        P.do("vector", lambda e, _hp=hp: e.tensor_copy(
            negb8[:, _hp:_hp + 1], d_psA[:, 0:1]), waits=[("tPE", c_bc)],
            inc=("tDV", 1))
    c_bounds_dv = P.cnt["tDV"]
    c_bounds_sc = P.cnt["tSC"]

    slot_list = []
    for hp in range(HP):
        for h in range(2):
            jmax = 32 if h == 0 else 16
            for j in range(jmax):
                w = min(1024, 64 * (32 - j)) if h == 0 else 64 * (16 - j)
                masked = (h == 0 and j >= 16) or h == 1
                jlast0 = 31 if h == 0 else 15
                jlast1 = (31 - 8) if h == 0 else (15 - 8)
                slot_list.append((hp, h, j, w, masked,
                                  j == 0, (j == jlast0, j == jlast1),
                                  j == jmax - 1))
    exp_done = {}
    av_done = {}
    c_spill = {}
    c_sums_dma = {}
    dps_guard = [("tDV", c_bounds_dv)]

    def emit_av(s):
        hp, h, j, w, masked, st, sps, last = slot_list[s]
        p = s % 2
        vcol = 1040 * j + 130 * hp
        r = {}
        for hd in range(2):
            ptc = 2048 * hd + 1024 * p
            if s in mask_done:
                wa = [("tDV", mask_done[s][hd])]
            else:
                wa = [("tSC", exp_done[s][hd]), ("tGP", c_vones)]
            first = True
            for c0 in range(0, w, 512):
                cw = min(512, w - c0)
                sp = sps[c0 // 512]
                r[hd] = P.do("tensor", lambda e, _hd=hd, _v=vcol, _c0=c0, _cw=cw,
                             _st=st, _sp=sp, _pc=ptc: e.matmul(
                                 av_ps[_hd][0:65, _c0:_c0 + _cw],
                                 v_all[:, _v + 65 * _hd:_v + 65 * _hd + 65],
                                 pt[:, _pc + _c0:_pc + _c0 + _cw], start=_st, stop=_sp,
                                 skip_group_check=True),
                             waits=wa if first else [], inc=("tPE", 1))
                first = False
        av_done[s] = r

    mask_done = {}
    ns = len(slot_list)
    for s in range(ns):
        hp, h, j, w, masked, st, sps, last = slot_list[s]
        p = s % 2
        kcol = NCTX * hp + 128 * j
        qcol = NQ * hp + 1024 * h
        wtA = list(dps_guard)
        wtB = []
        dps_guard = []
        if s >= 1:
            wtA.append(("tSC", exp_done[s - 1][0]))
            wtB.append(("tSC", exp_done[s - 1][1]))
        dots_done = {}
        for c0 in range(0, w, 512):
            cw = min(512, w - c0)
            dots_done[0] = P.do("tensor", lambda e, _k=kcol, _q=qcol + c0, _c0=c0,
                                _cw=cw: e.matmul(
                d_psA[:, _c0:_c0 + _cw], k_all[0:64, _k:_k + 128],
                q_all[0:64, _q:_q + _cw], start=True, stop=True,
                tile_position=(0, 0), skip_group_check=True),
                waits=wtA if c0 == 0 else [], inc=("tPE", 1))
        for c0 in range(0, w, 512):
            cw = min(512, w - c0)
            dots_done[1] = P.do("tensor", lambda e, _k=kcol, _q=qcol + c0, _c0=c0,
                                _cw=cw: e.matmul(
                d_psB[:, _c0:_c0 + _cw], k_all[64:128, _k:_k + 128],
                q_all[64:128, _q:_q + _cw], start=True, stop=True,
                tile_position=(64, 0), skip_group_check=True),
                waits=wtB if c0 == 0 else [], inc=("tPE", 1))
        if s >= 1:
            emit_av(s - 1)
        eA = {}
        for hd in range(2):
            ws = [("tPE", dots_done[hd]), ("tDV", c_bounds_dv)]
            if s >= 2:
                ws.append(("tPE", av_done[s - 2][hd]))
            dps = d_psA if hd == 0 else d_psB
            eA[hd] = P.do("scalar", lambda e, _w=w, _hp=hp, _pc=2048 * hd + 1024 * p,
                          _d=dps: e.activation(
                pt[:, _pc:_pc + _w], _d[:, 0:_w], AF.Exp,
                bias=negb8[:, _hp:_hp + 1], scale=SCALE), waits=ws, inc=("tSC", 1))
        exp_done[s] = eA
        if masked:
            moff = w - 64
            mk = {}
            for hd in range(2):
                mk[hd] = P.do("vector", lambda e, _m=moff, _pc=2048 * hd + 1024 * p:
                              e.tensor_tensor(
                    pt[:, _pc + _m:_pc + _m + 64], pt[:, _pc + _m:_pc + _m + 64],
                    mask_s[:], ALU.mult),
                    waits=[("tSC", exp_done[s][hd]), ("qX", c_msk)], inc=("tDV", 1))
            mask_done[s] = mk
        if last:
            emit_av(s)
            c_av = av_done[s][1]
            wsp = [("qX", c_spill[hp - 1])] if (hp - 1 in c_spill and h == 0) else []
            c_sumsdma = c_sums_dma.get((hp, h - 1)) or c_sums_dma.get((hp - 1, 1))
            c_evs = []
            for head in range(2):
                c_evs.append(P.do("scalar", lambda e, _hd=head, _h=h: e.activation(
                    av_spl[0:64, 2048 * _hd + 1024 * _h:2048 * _hd + 1024 * _h + 1024],
                    av_ps[_hd][0:64, :], AF.Copy),
                    waits=[("tPE", c_av)] + wsp, inc=("tSC", 1)))
                c_evs.append(P.do("scalar", lambda e, _hd=head: e.activation(
                    s64r[64:65, 1024 * _hd:1024 * _hd + 1024],
                    av_ps[_hd][64:65, :], AF.Copy),
                    waits=([("qX", c_sumsdma)] if c_sumsdma else []), inc=("tSC", 1)))
            c_ep = c_evs[-1]
            dps_guard = [("tSC", c_ep)]
            c_sums_dma[(hp, h)] = P.do("sync", lambda e, _hp=hp, _h=h: e.dma_start(
                bass.AP(sums_d, _hp * 2 * NQ + 1024 * _h, [[2 * NQ, 1], [NQ, 2], [1, 1024]]),
                bass.AP(s64r, 64 * 2048, [[2048, 1], [1024, 2], [1, 1024]])),
                waits=[("tSC", c_ep)], inc=("qX", 16))
            if h == 1:
                for head in range(2):
                    c_spill[hp] = P.do("sync", lambda e, _hp=hp, _hd=head: e.dma_start(
                        bass.AP(av_d, (_hp * 128 + 64 * _hd) * NQ, [[NQ, 64], [1, NQ]]),
                        av_spl[0:64, 2048 * _hd:2048 * _hd + 2048]),
                        waits=[("tSC", c_ep)], inc=("qX", 16))
    c_spill_last = c_spill[HP - 1]

    P.flush()
    if stop_after == 'attn':
        return nc
    aex.close()
    kvq.close()

    P.barrier()
    om = ExitStack()
    u_fm = om.enter_context(nc.sbuf_tensor("u_fm", [128, 8 * NQ], BF16))
    facc = om.enter_context(nc.sbuf_tensor("facc", [128, NQT * 1024], F32))
    g_all = om.enter_context(nc.sbuf_tensor("g_all", [128, NQT * 16], F32))
    g2t = om.enter_context(nc.sbuf_tensor("g2t", [16, NQ], BF16))
    br_b = om.enter_context(nc.sbuf_tensor("br_b", [16, 1024], BF16))

    oex = ExitStack()
    av_all = oex.enter_context(nc.sbuf_tensor("av_all", [128, HP * NQ], BF16))
    wo_b = oex.enter_context(nc.sbuf_tensor("wo_b", [128, 8192], BF16))
    ident_f = oex.enter_context(nc.sbuf_tensor("ident_f_s", [128, 128], F32))
    sel2 = oex.enter_context(nc.sbuf_tensor("sel2_s", [16, 1024], BF16))
    rec2 = oex.enter_context(nc.sbuf_tensor("rec2", [16, 2048], F32))
    rec2h = oex.enter_context(nc.sbuf_tensor("rec2h", [16, 2048], BF16))
    rec2l = oex.enter_context(nc.sbuf_tensor("rec2l", [16, 2048], BF16))
    xq_t = oex.enter_context(nc.sbuf_tensor("xq_t", [128, 1024], F32))
    res1_t = oex.enter_context(nc.sbuf_tensor("res1", [128, 2048], F32))
    u_t2 = oex.enter_context(nc.sbuf_tensor("u_t", [128, 2048], F32))
    u_bf = oex.enter_context(nc.sbuf_tensor("u_bf", [128, 2048], BF16))
    cent_b = oex.enter_context(nc.sbuf_tensor("cent_b", [128, 128], BF16))
    sc_em = oex.enter_context(nc.sbuf_tensor("sc_em", [16, NQ], F32))
    eb_s = oex.enter_context(nc.sbuf_tensor("eb_s", [16, 1], F32))
    bs_b = oex.enter_context(nc.sbuf_tensor("bs_b", [2, 1024], BF16))
    rt2 = oex.enter_context(nc.sbuf_tensor("rt", [128, 160], F32))
    gg_t = oex.enter_context(nc.sbuf_tensor("gg_t", [128, 16], BF16))
    o_ps = [oex.enter_context(nc.psum_tensor(f"o_ps{i}", [128, 1024], F32)) for i in range(2)]
    tp2_ps = [oex.enter_context(nc.psum_tensor("tp2_ps0", [128, 128], BF16))]
    tp2_ps.append(tp2_ps[0])
    sc_ps = oex.enter_context(nc.psum_tensor("sc_ps", [16, 512], F32))
    rt_ps = [oex.enter_context(nc.psum_tensor(f"rt_ps{i}", [128, 128], F32)) for i in range(2)]
    misc_ps = rt_ps[0]

    c_av_ld = P.do("sync", lambda e: e.dma_start(
        bass.AP(av_all, 0, [[HP * NQ, 128], [NQ, HP], [1, NQ]]),
        bass.AP(av_d, 0, [[NQ, 128], [128 * NQ, HP], [1, NQ]])),
        waits=[("qX", c_spill_last)], inc=("qX", 16))
    c_sums_ld = P.do("sync", lambda e: e.dma_start(
        bass.AP(rec2, 0, [[2048, 16], [1, 2048]]),
        bass.AP(sums_d, 0, [[2048, 16], [1, 2048]])), inc=("qX", 16))
    c_sel2 = P.do("sync", lambda e: e.dma_start(sel2[:], sel2_in[:, :]), inc=("qX", 16))
    c_rc2 = P.do("vector", lambda e: e.reciprocal(rec2[:, :], rec2[:, :]),
                 waits=[("qX", c_sums_ld)], inc=("tDV", 1))
    c_rh = P.do("vector", lambda e: e.tensor_copy(rec2h[:, :], rec2[:, :]),
                waits=[("tDV", c_rc2)], inc=("tDV", 1))
    c_rl = P.do("vector", lambda e: e.tensor_tensor(
        rec2l[:, :], rec2[:, :], rec2h[:, :], ALU.subtract),
        waits=[("tDV", c_rh)], inc=("tDV", 1))

    nrm_done = {}
    for hp in range(HP):
        for qt in range(NQT):
            i = hp * NQT + qt
            rp = i % 2
            wn = [("tDV", c_rl), ("qX", c_av_ld), ("qX", c_sel2)]
            if i >= 2:
                wn.append(("tDV", nrm_done[i - 2]))
            P.do("tensor", lambda e, _hp=hp, _qt=qt, _rp=rp: e.matmul(
                rt_ps[_rp][:, :], sel2[:, 128 * _hp:128 * _hp + 128],
                rec2h[:, 128 * _qt:128 * _qt + 128],
                start=True, stop=False, skip_group_check=True),
                waits=wn, inc=("tPE", 1))
            c_nm = P.do("tensor", lambda e, _hp=hp, _qt=qt, _rp=rp: e.matmul(
                rt_ps[_rp][:, :], sel2[:, 128 * _hp:128 * _hp + 128],
                rec2l[:, 128 * _qt:128 * _qt + 128],
                start=False, stop=True, skip_group_check=True), inc=("tPE", 1))
            nrm_done[i] = P.do("vector", lambda e, _hp=hp, _qt=qt, _rp=rp: e.tensor_tensor(
                av_all[:, NQ * _hp + 128 * _qt:NQ * _hp + 128 * _qt + 128],
                av_all[:, NQ * _hp + 128 * _qt:NQ * _hp + 128 * _qt + 128],
                rt_ps[_rp][:, :], ALU.mult), waits=[("tPE", c_nm)], inc=("tDV", 1))
    c_nrm_all = P.cnt["tDV"]
    c_idf = P.do("sync", lambda e: e.dma_start(ident_f[:], ident_f_in[:, :]), inc=("qX", 16))
    c_wo = P.do("gpsimd", wcast_load(wo_b, 0, wo, 0), inc=("qW", 16))
    c_cent = P.do("gpsimd", lambda e: e.dma_start(
        bass.AP(cent_b, 0, [[128, 128], [16, 8], [1, 16]]),
        bass.AP(cent_t, 0, [[16, 128], [2048, 8], [1, 16]])), inc=("qW", 16))
    c_eb = P.do("gpsimd", lambda e: e.dma_start(eb_s[:], eb[:, :]), inc=("qW", 16))
    c_br = P.do("gpsimd", lambda e: e.dma_start(br_b[0:16, :], br[:, :]), inc=("qW", 16))
    c_bs = P.do("gpsimd", lambda e: e.dma_start(bs_b[:], bs[:, :]), inc=("qW", 16))

    xq_free = [[], []]
    ops_free = [[], []]
    ubf_free = [[], []]
    res1_free = [[], []]
    c_sq_g = {0: 0, 1: 0}
    c_r_g = {0: 0, 1: 0}
    for qt in range(NQT):
        b = qt % 2
        res1 = res1_t[:, 1024 * b:1024 * b + 1024]
        u_t = u_t2[:, 1024 * b:1024 * b + 1024]
        rt = rt2[:, 80 * b:80 * b + 80]
        c_xq = P.do("sync", lambda e, _qt=qt: e.dma_start(
            xq_t[:, :], x_q[128 * _qt:128 * _qt + 128, :]),
            waits=list(xq_free[0]), inc=("qX", 16))
        wmm = [("tDV", c_nrm_all), ("qW", c_wo)] + ops_free[b]
        c_mm = None
        for ic in range(8):
            for nh in range(2):
                c_mm = P.do("tensor", lambda e, _qt=qt, _ic=ic, _nh=nh, _b=b: e.matmul(
                    o_ps[_b][:, 512 * _nh:512 * _nh + 512],
                    av_all[:, NQ * _ic + 128 * _qt:NQ * _ic + 128 * _qt + 128],
                    wo_b[:, 1024 * _ic + 512 * _nh:1024 * _ic + 512 * _nh + 512],
                    start=(_ic == 0), stop=(_ic == 7), skip_group_check=True),
                    waits=wmm if (ic == 0 and nh == 0) else [], inc=("tPE", 1))
        c_r1 = P.do("vector", lambda e, _b=b, _r=res1: e.tensor_tensor(
            _r, o_ps[_b][:, :], xq_t[:, :], ALU.add),
            waits=[("tPE", c_mm), ("qX", c_xq)] + res1_free[b], inc=("tDV", 1))
        ops_free[b] = [("tDV", c_r1)]
        xq_free[0] = [("tDV", c_r1)]
        c_rs1 = P.do("vector", lambda e, _r=res1, _rt=rt: e.reduce_sum(
            _rt[:, 0:1], _r, axis=AX.X),
            waits=[("tSC", c_r_g[b])], inc=("tDV", 1))
        wub = list(ubf_free[b])
        P.do("scalar", lambda e, _b=b, _r=res1, _rt=rt: e.activation(
            u_bf[:, 1024 * _b:1024 * _b + 1024], _r, AF.Square,
            accum_out=_rt[:, 1:2]), waits=[("tDV", c_r1)] + wub)
        c_sq = P.do("scalar", lambda e, _rt=rt: e.copy(_rt[:, 14:15], _rt[:, 1:2]),
                    inc=("tSC", 1))
        c_sq_g[b] = c_sq
        d1 = P.do("vector", lambda e, _rt=rt: e.tensor_scalar(
            _rt[:, 2:3], _rt[:, 0:1], 1.0 / D, None, ALU.mult, ALU.bypass),
            waits=[("tDV", c_rs1)], inc=("tDV", 1))
        d2 = P.do("vector", lambda e, _rt=rt: e.tensor_tensor(
            _rt[:, 3:4], _rt[:, 2:3], _rt[:, 2:3], ALU.mult),
            waits=[("tDV", d1)], inc=("tDV", 1))
        d3 = P.do("vector", lambda e, _rt=rt: e.tensor_scalar(
            _rt[:, 4:5], _rt[:, 14:15], 1.0 / D, EPS, ALU.mult, ALU.add),
            waits=[("tSC", c_sq)], inc=("tDV", 1))
        d4 = P.do("vector", lambda e, _rt=rt: e.tensor_tensor(
            _rt[:, 5:6], _rt[:, 4:5], _rt[:, 3:4], ALU.subtract),
            waits=[("tDV", d3)], inc=("tDV", 1))
        c_vr = P.do("vector", lambda e, _rt=rt: e.reciprocal(_rt[:, 6:7], _rt[:, 5:6]),
                    waits=[("tDV", d4)], inc=("tDV", 1))
        c_r = P.do("scalar", lambda e, _rt=rt: e.activation(_rt[:, 7:8], _rt[:, 6:7], AF.Sqrt),
                   waits=[("tDV", c_vr)], inc=("tSC", 1))
        c_r_g[b] = c_r
        d5 = P.do("vector", lambda e, _rt=rt: e.tensor_scalar(
            _rt[:, 8:9], _rt[:, 2:3], _rt[:, 7:8], -1.0, ALU.mult, ALU.mult),
            waits=[("tSC", c_r)], inc=("tDV", 1))
        d6 = P.do("vector", lambda e, _r=res1, _u=u_t, _rt=rt: e.tensor_scalar(
            _u, _r, _rt[:, 7:8], _rt[:, 8:9], ALU.mult, ALU.add),
            waits=[("tDV", d5)], inc=("tDV", 1))
        c_fc = P.do("vector", lambda e, _qt=qt, _r=res1, _u=u_t: e.tensor_tensor(
            facc[:, 1024 * _qt:1024 * _qt + 1024], _r, _u, ALU.add),
            waits=[("tDV", d6)], inc=("tDV", 1))
        c_ub = P.do("vector", lambda e, _b=b, _u=u_t: e.tensor_copy(
            u_bf[:, 1024 * _b:1024 * _b + 1024], _u), inc=("tDV", 1))
        res1_free[b] = [("tSC", c_sq), ("tDV", c_fc)]
        c_ev = None
        for dc in range(8):
            pb = dc % 2
            c_tp = P.do("tensor", lambda e, _b=b, _dc=dc, _pb=pb: e.transpose(
                tp2_ps[_pb][:, :], u_bf[:, 1024 * _b + 128 * _dc:1024 * _b + 128 * _dc + 128],
                ident_b[:]), waits=[("tDV", c_ub), ("tSC", P.cnt["tSC"])],
                inc=("tPE", 1))
            c_ev = P.do("scalar", lambda e, _dc=dc, _pb=pb, _qt=qt: e.activation(
                u_fm[:, NQ * _dc + 128 * _qt:NQ * _dc + 128 * _qt + 128],
                tp2_ps[_pb][:, :], AF.Copy), waits=[("tPE", c_tp)], inc=("tSC", 1))
        ubf_free[b] = [("tSC", c_ev)]
    c_o1_sc = P.cnt["tSC"]

    c_scev = {}
    for c in range(4):
        c_mm = None
        for dc in range(8):
            w0 = [("qW", c_cent), ("tSC", P.cnt["tSC"])]
            if c > 0:
                w0.append(("tDV", c_scev[c - 1]))
            c_mm = P.do("tensor", lambda e, _c=c, _dc=dc: e.matmul(
                sc_ps[:, :], cent_b[:, 16 * _dc:16 * _dc + 16],
                u_fm[:, NQ * _dc + 512 * _c:NQ * _dc + 512 * _c + 512],
                start=(_dc == 0), stop=(_dc == 7), skip_group_check=True),
                waits=w0 if dc == 0 else [], inc=("tPE", 1))
        c_scev[c] = P.do("vector", lambda e, _c=c: e.tensor_scalar(
            sc_em[:, 512 * _c:512 * _c + 512], sc_ps[:, :], eb_s[:, 0:1], None,
            ALU.add, ALU.bypass), waits=[("tPE", c_mm), ("qW", c_eb)], inc=("tDV", 1))
    c_es_prev = {0: 0, 1: 0}
    c_m1_prev = {0: 0, 1: 0}
    c_gt_prev = 0
    c_g_prev = {0: 0, 1: 0}
    for qt in range(NQT):
        b = qt % 2
        rt = rt2[:, 80 * b:80 * b + 80]
        mps = rt_ps[b]
        c_tr = P.do("tensor", lambda e, _qt=qt, _m=mps: e.transpose(
            _m[:, 0:16], sc_em[0:16, 128 * _qt:128 * _qt + 128], ident_f[0:16, 0:16]),
            waits=[("tDV", c_scev[qt // 4]), ("qX", c_idf),
                   ("tSC", c_es_prev[b]), ("tDV", c_m1_prev[b])], inc=("tPE", 1))
        c_m1 = P.do("vector", lambda e, _rt=rt, _m=mps: e.reduce_max(
            _rt[:, 9:10], _m[:, 0:16], axis=AX.X, negate=True),
            waits=[("tPE", c_tr), ("tSC", c_r_g[b])], inc=("tDV", 1))
        c_m1_prev[b] = c_m1
        c_es = P.do("scalar", lambda e, _rt=rt, _m=mps: e.activation(
            _rt[:, 32:48], _m[:, 0:16], AF.Exp, bias=_rt[:, 9:10], scale=1.0),
            waits=[("tDV", c_m1)], inc=("tSC", 1))
        c_es_prev[b] = c_es
        c_rsm = P.do("vector", lambda e, _rt=rt: e.reduce_sum(
            _rt[:, 10:11], _rt[:, 32:48], axis=AX.X), waits=[("tSC", c_es)],
            inc=("tDV", 1))
        e1 = P.do("vector", lambda e, _rt=rt: e.reciprocal(_rt[:, 11:12], _rt[:, 10:11]),
             waits=[("tDV", c_rsm)], inc=("tDV", 1))
        e2 = P.do("vector", lambda e, _rt=rt: e.tensor_scalar(
            _rt[:, 48:64], _rt[:, 32:48], _rt[:, 11:12], None, ALU.mult, ALU.bypass),
            waits=[("tDV", e1)], inc=("tDV", 1))
        c_rm1 = P.do("vector", lambda e, _rt=rt: e.reduce_max(
            _rt[:, 12:13], _rt[:, 48:64], axis=AX.X),
            waits=[("tDV", e2)], inc=("tDV", 1))
        e3 = P.do("vector", lambda e, _rt=rt: e.tensor_scalar(
            _rt[:, 64:80], _rt[:, 48:64], _rt[:, 12:13], None, ALU.is_ge, ALU.bypass),
            waits=[("tDV", c_rm1)], inc=("tDV", 1))
        e4 = P.do("vector", lambda e, _rt=rt: e.scalar_tensor_tensor(
            _rt[:, 16:32], _rt[:, 64:80], -1e4, _rt[:, 48:64], ALU.mult, ALU.add),
            waits=[("tDV", e3)], inc=("tDV", 1))
        c_rm2 = P.do("vector", lambda e, _rt=rt: e.reduce_max(
            _rt[:, 13:14], _rt[:, 16:32], axis=AX.X),
            waits=[("tDV", e4)], inc=("tDV", 1))
        e5 = P.do("vector", lambda e, _rt=rt: e.tensor_scalar(
            _rt[:, 64:80], _rt[:, 48:64], _rt[:, 13:14], None, ALU.is_ge, ALU.bypass),
            waits=[("tDV", c_rm2)], inc=("tDV", 1))
        c_g = P.do("vector", lambda e, _qt=qt, _rt=rt: e.tensor_tensor(
            g_all[:, 16 * _qt:16 * _qt + 16], _rt[:, 48:64], _rt[:, 64:80], ALU.mult),
            waits=[("tDV", e5)], inc=("tDV", 1))
        c_g_prev[b] = c_g
        c_gg = P.do("vector", lambda e, _qt=qt: e.tensor_scalar(
            gg_t[:, :], g_all[:, 16 * _qt:16 * _qt + 16], 2.0, None,
            ALU.mult, ALU.bypass), waits=[("tDV", c_g), ("tPE", c_gt_prev)],
            inc=("tDV", 1))
        c_gt = P.do("tensor", lambda e: e.transpose(
            tp2_ps[0][0:16, :], gg_t[:, :], ident_b[:]),
            waits=[("tDV", c_gg), ("tSC", P.cnt["tSC"])], inc=("tPE", 1))
        c_gt_prev = c_gt
        P.do("scalar", lambda e, _qt=qt: e.activation(
            g2t[0:16, 128 * _qt:128 * _qt + 128], tp2_ps[0][0:16, :], AF.Copy),
            waits=[("tPE", c_gt)], inc=("tSC", 1))
    c_o_done_sc = P.cnt["tSC"]
    c_o_done_dv = P.cnt["tDV"]

    if stop_after == 'o':
        P.do("sync", lambda e: e.dma_start(
            bass.AP(out_d, 0, [[1024, 128], [131072, 16], [1, 1024]]),
            bass.AP(facc, 0, [[16384, 128], [1024, 16], [1, 1024]])),
            waits=[("tDV", P.cnt["tDV"]), ("tSC", P.cnt["tSC"])], inc=("qX", 16))
        P.do("gpsimd", lambda e: e.dma_start(
            bass.AP(av_d, 0, [[256, 128], [1, 256]]),
            bass.AP(g_all, 0, [[256, 128], [1, 256]])),
            waits=[("tDV", P.cnt["tDV"])], inc=("qW", 16))
        P.do("sync", lambda e: e.dma_start(
            bass.AP(av_d, 128 * 256, [[2048, 16], [1, 2048]]),
            bass.AP(g2t, 0, [[2048, 16], [1, 2048]])), inc=("qX", 16))
        P.do("sync", lambda e: e.dma_start(
            bass.AP(av_d, 2 * 128 * 256, [[2048, 128], [1, 2048]]),
            u_fm[:, 0:2048]), inc=("qX", 16))
        P.flush()
        return nc
    P.flush()
    oex.close()

    P.barrier()
    mex = ExitStack()
    wr_b = mex.enter_context(nc.sbuf_tensor("wr_b", [128, 2 * 8192], BF16))
    ws_tmp = mex.enter_context(nc.sbuf_tensor("ws_tmp", [128, 8192], BF16))
    e_ps = [mex.enter_context(nc.psum_tensor(f"e_ps{i}", [128, 1024], F32)) for i in range(3)]

    c_ws0 = P.do("gpsimd", wcast_load(wr_b, 0, wst, 0), inc=("qW", 16))
    c_ws1 = P.do("gpsimd", wcast_load(ws_tmp, 0, wst, D * D), inc=("qW", 16))
    c_wsum = P.do("vector", lambda e: e.tensor_tensor(
        wr_b[:, 0:8192], wr_b[:, 0:8192], ws_tmp[:, 0:8192], ALU.add),
        waits=[("qW", c_ws1)], inc=("tDV", 1))
    w_ready = {0: ("tDV", c_wsum)}
    wbuf_last_mm = {0: 0, 1: 0}
    mm_done = {}
    stt_done = {}
    psum_guard = {0: None, 1: None, 2: None}
    for t in range(NE + 1):
        if t + 1 <= NE:
            ee = t
            half = (t + 1) % 2
            w_ready[t + 1] = ("qW", P.do(
                "gpsimd", wcast_load(wr_b, 8192 * half, wrt, ee * D * D),
                waits=[("tPE", wbuf_last_mm[half])] if wbuf_last_mm[half] else [],
                inc=("qW", 16)))
        half = t % 2
        for qt in range(NQT):
            r = (t * NQT + qt) % 3
            w0 = [w_ready[t]]
            if psum_guard[r] is not None:
                w0.append(psum_guard[r])
            c_mm = None
            for dc in range(8):
                for nh in range(2):
                    c_mm = P.do("tensor", lambda e, _t=t, _qt=qt, _dc=dc, _nh=nh,
                                _r=r, _hf=half: e.matmul(
                        e_ps[_r][:, 512 * _nh:512 * _nh + 512],
                        u_fm[:, NQ * _dc + 128 * _qt:NQ * _dc + 128 * _qt + 128],
                        wr_b[:, 8192 * _hf + 1024 * _dc + 512 * _nh:
                             8192 * _hf + 1024 * _dc + 512 * _nh + 512],
                        start=(_dc == 0), stop=(_dc == 7 and _t > 0),
                        skip_group_check=True),
                        waits=w0 if (dc == 0 and nh == 0) else [], inc=("tPE", 1))
            if t == 0:
                for nh in range(2):
                    P.do("tensor", lambda e, _qt=qt, _nh=nh, _r=r: e.matmul(
                        e_ps[_r][:, 512 * _nh:512 * _nh + 512],
                        g2t[0:16, 128 * _qt:128 * _qt + 128],
                        br_b[0:16, 512 * _nh:512 * _nh + 512],
                        start=False, stop=False, skip_group_check=True),
                        waits=[("tSC", c_o_done_sc), ("qW", c_br), ("qW", c_bs)] if (qt == 0 and nh == 0) else [],
                        inc=("tPE", 1))
                    c_mm = P.do("tensor", lambda e, _nh=nh, _r=r: e.matmul(
                        e_ps[_r][:, 512 * _nh:512 * _nh + 512],
                        ones2[:, 0:128],
                        bs_b[:, 512 * _nh:512 * _nh + 512],
                        start=False, stop=True, skip_group_check=True),
                        inc=("tPE", 1))
            mm_done[(t, qt)] = c_mm
            wbuf_last_mm[half] = c_mm
            c_stt = None
            w_prev_stt = [("tDV", stt_done[(t - 1, qt)])] if t > 0 else [("tDV", c_o_done_dv)]
            for nh in range(2):
                if t == 0:
                    c_stt = P.do("vector", lambda e, _qt=qt, _nh=nh, _r=r:
                                 e.scalar_tensor_tensor(
                                     facc[:, 1024 * _qt + 512 * _nh:1024 * _qt + 512 * _nh + 512],
                                     e_ps[_r][:, 512 * _nh:512 * _nh + 512], 0.5,
                                     facc[:, 1024 * _qt + 512 * _nh:1024 * _qt + 512 * _nh + 512],
                                     ALU.mult, ALU.add),
                                 waits=([("tPE", c_mm)] + w_prev_stt) if nh == 0 else [],
                                 inc=("tDV", 1))
                else:
                    c_stt = P.do("vector", lambda e, _t=t, _qt=qt, _nh=nh, _r=r:
                                 e.scalar_tensor_tensor(
                                     facc[:, 1024 * _qt + 512 * _nh:1024 * _qt + 512 * _nh + 512],
                                     e_ps[_r][:, 512 * _nh:512 * _nh + 512],
                                     g_all[:, 16 * _qt + (_t - 1):16 * _qt + _t],
                                     facc[:, 1024 * _qt + 512 * _nh:1024 * _qt + 512 * _nh + 512],
                                     ALU.mult, ALU.add),
                                 waits=([("tPE", c_mm)] + w_prev_stt) if nh == 0 else [],
                                 inc=("tDV", 1))
            stt_done[(t, qt)] = c_stt
            psum_guard[r] = ("tDV", c_stt)

    c_store = None
    for qt in range(NQT):
        c_store = P.do("sync", lambda e, _qt=qt: e.dma_start(
            out_d[128 * _qt:128 * _qt + 128, :], facc[:, 1024 * _qt:1024 * _qt + 1024]),
            waits=[("tDV", stt_done[(NE, qt)])], inc=("qX", 16))
    P.do("sync", lambda e: (e.wait_ge(sems["qX"], c_store), None)[1])
    P.flush()
    mex.close()
    om.close()
    es.close()
    return nc


_NC_CACHE = None


def _get_nc():
    global _NC_CACHE
    if _NC_CACHE is None:
        _NC_CACHE = build_nc()
    return _NC_CACHE


def kernel(L0_attn_x, L0_attn_Wq, L0_attn_Wk, L0_attn_Wv, L0_attn_Wo,
           L0_ffn_centroids, L0_ffn_expert_biases, L0_ffn_Wr, L0_ffn_br,
           L0_ffn_Ws, L0_ffn_bs):
    x = np.ascontiguousarray(np.asarray(L0_attn_x, dtype=np.float32))
    wq = np.ascontiguousarray(np.asarray(L0_attn_Wq, dtype=np.float32))
    wk = np.ascontiguousarray(np.asarray(L0_attn_Wk, dtype=np.float32))
    wv = np.ascontiguousarray(np.asarray(L0_attn_Wv, dtype=np.float32))
    wo = np.ascontiguousarray(np.asarray(L0_attn_Wo, dtype=np.float32))
    wrt = np.ascontiguousarray(np.asarray(L0_ffn_Wr, dtype=np.float32).transpose(0, 2, 1))
    wst = np.ascontiguousarray(np.asarray(L0_ffn_Ws, dtype=np.float32).transpose(0, 2, 1))
    cent_t = np.ascontiguousarray(np.asarray(L0_ffn_centroids, dtype=np.float32).T)
    ebv = np.ascontiguousarray(np.asarray(L0_ffn_expert_biases, dtype=np.float32).reshape(16, 1))
    brv = np.ascontiguousarray(np.asarray(L0_ffn_br, dtype=np.float32))
    bsv = np.ascontiguousarray(np.asarray(L0_ffn_bs, dtype=np.float32))
    ident_b = np.eye(128, dtype=BF)
    ident_f = np.eye(128, dtype=np.float32)
    sel16 = np.zeros((16, 1024), dtype=BF)
    for hp_ in range(8):
        sel16[2 * hp_, 128 * hp_:128 * hp_ + 64] = 1.0
        sel16[2 * hp_ + 1, 128 * hp_ + 64:128 * hp_ + 128] = 1.0

    perms = []
    for par in range(2):
        idx = np.concatenate([
            np.arange(128 * (31 - k) + 64 * par, 128 * (31 - k) + 64 * par + 64)
            for k in range(32)])
        perms.append(idx)
    masks = []
    for par in range(2):
        jj = np.arange(128)[:, None]
        ii = np.arange(64)[None, :]
        masks.append((jj <= 64 * par + ii).astype(BF))

    in_maps = []
    for c in range(8):
        b, par = c // 2, c % 2
        in_maps.append({
            "x_ctx": x[b],
            "x_q": np.ascontiguousarray(x[b][perms[par]]),
            "wq": wq, "wk": wk, "wv": wv, "wo": wo,
            "wrt": wrt, "wst": wst, "cent_t": cent_t, "eb": ebv,
            "br": brv, "bs": bsv, "mask": masks[par],
            "ident_b": ident_b, "ident_f": ident_f, "sel2": sel16,
        })

    nc = _get_nc()
    res = run_bass_kernel_spmd(nc, in_maps, core_ids=list(range(8)))
    out = np.empty((4, 4096, 1024), dtype=np.float32)
    for c in range(8):
        b, par = c // 2, c % 2
        out[b][perms[par]] = res.results[c]["out"]
    return out


# version 09
# speedup vs baseline: 1.7007x; 1.0472x over previous
import numpy as np
import ml_dtypes
from contextlib import ExitStack

import concourse.bass as bass
import concourse.mybir as mybir
from concourse.bass_utils import run_bass_kernel_spmd

F32 = mybir.dt.float32
BF16 = mybir.dt.bfloat16
AF = mybir.ActivationFunctionType
ALU = mybir.AluOpType
AX = mybir.AxisListType
BF = ml_dtypes.bfloat16

D = 1024
NCTX = 4096
NQ = 2048
NKVT = 32
NQT = 16
HP = 8
SCALE = 0.125
EPS = 1e-5
NE = 16

ENGS = ("tensor", "vector", "scalar", "gpsimd", "sync")


class Prog:

    def __init__(self, nc, sems):
        self.nc = nc
        self.sems = sems
        self.cnt = {k: 0 for k in sems}
        self.q = {e: [] for e in ENGS}
        self.lastw = {e: {} for e in ENGS}

    def do(self, eng, fn, waits=(), inc=None):
        real_waits = []
        lw = self.lastw[eng]
        for s, v in waits:
            if s is None or v <= 0:
                continue
            if lw.get(s, -1) < v:
                real_waits.append((s, v))
                lw[s] = v
        if inc is not None:
            self.cnt[inc[0]] += inc[1]
        self.q[eng].append((fn, tuple(real_waits), inc))
        return self.cnt[inc[0]] if inc is not None else None

    def barrier(self):
        self.cnt["bar"] += 5
        tgt = self.cnt["bar"]
        for e in ENGS:
            def fn(eng, _t=tgt):
                eng.sem_inc(self.sems["bar"], 1)
                eng.wait_ge(self.sems["bar"], _t)
                return None
            self.q[e].append((fn, (), None))
            self.lastw[e]["bar"] = tgt

    def flush(self):
        with self.nc.Block() as blk:
            for e in ENGS:
                ops = self.q[e]
                self.q[e] = []
                if not ops:
                    continue

                def body(eng, _ops=tuple(ops), _sems=self.sems):
                    for fn, waits, inc in _ops:
                        for s, v in waits:
                            eng.wait_ge(_sems[s], v)
                        r = fn(eng)
                        if inc is not None and r is not None:
                            r.then_inc(_sems[inc[0]], inc[1])
                getattr(blk, e)(body)


def build_nc(stop_after='all'):
    nc = bass.Bass("TRN2", target_bir_lowering=False, detect_race_conditions=False)

    x_ctx = nc.dram_tensor("x_ctx", [NCTX, D], F32, kind="ExternalInput")
    x_q = nc.dram_tensor("x_q", [NQ, D], F32, kind="ExternalInput")
    wq = nc.dram_tensor("wq", [D, D], F32, kind="ExternalInput")
    wk = nc.dram_tensor("wk", [D, D], F32, kind="ExternalInput")
    wv = nc.dram_tensor("wv", [D, D], F32, kind="ExternalInput")
    wo = nc.dram_tensor("wo", [D, D], F32, kind="ExternalInput")
    wrt = nc.dram_tensor("wrt", [NE, D, D], F32, kind="ExternalInput")
    wst = nc.dram_tensor("wst", [2, D, D], F32, kind="ExternalInput")
    cent_t = nc.dram_tensor("cent_t", [D, 16], F32, kind="ExternalInput")
    eb = nc.dram_tensor("eb", [16, 1], F32, kind="ExternalInput")
    br = nc.dram_tensor("br", [16, D], F32, kind="ExternalInput")
    bs = nc.dram_tensor("bs", [2, D], F32, kind="ExternalInput")
    mask_in = nc.dram_tensor("mask", [128, 64], BF16, kind="ExternalInput")
    ident_b_in = nc.dram_tensor("ident_b", [128, 128], BF16, kind="ExternalInput")
    ident_f_in = nc.dram_tensor("ident_f", [128, 128], F32, kind="ExternalInput")
    sel2_in = nc.dram_tensor("sel2", [16, 1024], BF16, kind="ExternalInput")
    out_d = nc.dram_tensor("out", [NQ, D], F32, kind="ExternalOutput")
    av_d = nc.dram_tensor("av_spill", [HP, 128, NQ], BF16, kind="ExternalOutput")
    sums_d = nc.dram_tensor("sums_spill", [HP, 2, NQ], F32)

    es = ExitStack()
    sems = {}
    for s in ("qW", "qX", "tPE", "tSC", "tDV", "tGP", "bar"):
        sems[s] = es.enter_context(nc.semaphore(s))
    P = Prog(nc, sems)

    ident_b = es.enter_context(nc.sbuf_tensor("ident_b_s", [128, 128], BF16))
    mask_s = es.enter_context(nc.sbuf_tensor("mask_s", [128, 64], BF16))
    ones1 = es.enter_context(nc.sbuf_tensor("ones1", [128, 1], BF16))
    ones2 = es.enter_context(nc.sbuf_tensor("ones2", [2, 128], BF16))
    negb8 = es.enter_context(nc.sbuf_tensor("negb8", [128, 8], F32))
    stats = es.enter_context(nc.sbuf_tensor("stats", [128, 24], F32))

    c_idb = P.do("sync", lambda e: e.dma_start(ident_b[:], ident_b_in[:, :]), inc=("qX", 16))
    c_msk = P.do("sync", lambda e: e.dma_start(mask_s[:], mask_in[:, :]), inc=("qX", 16))
    P.do("gpsimd", lambda e: e.memset(ones1[:], 1.0))
    c_ones = P.do("gpsimd", lambda e: e.memset(ones2[:], 1.0), inc=("tGP", 1))

    def wcast_load(dst, dst_col, src, e_off):
        dst_stride = dst.shape[1]

        def fn(eng):
            return eng.dma_start(
                bass.AP(dst, dst_col, [[dst_stride, 128], [1024, 8], [1, 1024]]),
                bass.AP(src, e_off, [[1024, 128], [131072, 8], [1, 1024]]),
            )
        return fn

    kvq = ExitStack()
    k_all = kvq.enter_context(nc.sbuf_tensor("k_all", [128, HP * NCTX], BF16))
    v_all = kvq.enter_context(nc.sbuf_tensor("v_all", [128, NKVT * 1040], BF16))
    q_all = kvq.enter_context(nc.sbuf_tensor("q_all", [128, HP * NQ], BF16))
    xt = kvq.enter_context(nc.sbuf_tensor("xt", [128, 2 * 1024], F32))
    xlt = kvq.enter_context(nc.sbuf_tensor("xlt", [128, 1024], BF16))
    xlf = kvq.enter_context(nc.sbuf_tensor("xlf", [128, 8 * 256], BF16))

    lnst = {
        "xt_free": [[], []],
        "xlt_free": [],
        "cr_prev": [0, 0],
        "tp_guard": [],
        "g": 0,
    }

    def ln_stats(src_dram, row0, dump=None):
        g = lnst["g"]
        lnst["g"] += 1
        b = g % 2
        sb = b * 12
        c_x = P.do("sync", lambda e, _b=b, _r=row0, _s=src_dram: e.dma_start(
            xt[:, 1024 * _b:1024 * _b + 1024], _s[_r:_r + 128, :]),
            waits=list(lnst["xt_free"][b]), inc=("qX", 16))
        xv = xt[:, 1024 * b:1024 * b + 1024]
        c_rs = P.do("vector", lambda e, _x=xv, _c=sb: e.reduce_sum(
            stats[:, _c:_c + 1], _x, axis=AX.X),
            waits=[("qX", c_x), ("tSC", lnst["cr_prev"][b])], inc=("tDV", 1))
        dmp = dump if dump is not None else xlt[:, :]
        wsq = [("qX", c_x), ("tSC", lnst["cr_prev"][b])]
        if dump is None:
            wsq += list(lnst["xlt_free"])
        P.do("scalar", lambda e, _x=xv, _c=sb, _d=dmp: e.activation(
            _d, _x, AF.Square, accum_out=stats[:, _c + 1:_c + 2]), waits=wsq)
        c_sq = P.do("scalar", lambda e, _c=sb: e.copy(
            stats[:, _c + 9:_c + 10], stats[:, _c + 1:_c + 2]), inc=("tSC", 1))
        c1 = P.do("vector", lambda e, _c=sb: e.tensor_scalar(
            stats[:, _c + 2:_c + 3], stats[:, _c:_c + 1], 1.0 / D, None,
            ALU.mult, ALU.bypass), waits=[("tDV", c_rs)], inc=("tDV", 1))
        c2 = P.do("vector", lambda e, _c=sb: e.tensor_tensor(
            stats[:, _c + 3:_c + 4], stats[:, _c + 2:_c + 3], stats[:, _c + 2:_c + 3],
            ALU.mult), waits=[("tDV", c1)], inc=("tDV", 1))
        c3 = P.do("vector", lambda e, _c=sb: e.tensor_scalar(
            stats[:, _c + 4:_c + 5], stats[:, _c + 9:_c + 10], 1.0 / D, EPS,
            ALU.mult, ALU.add), waits=[("tSC", c_sq)], inc=("tDV", 1))
        c4 = P.do("vector", lambda e, _c=sb: e.tensor_tensor(
            stats[:, _c + 5:_c + 6], stats[:, _c + 4:_c + 5], stats[:, _c + 3:_c + 4],
            ALU.subtract), waits=[("tDV", c3)], inc=("tDV", 1))
        c_vr = P.do("vector", lambda e, _c=sb: e.reciprocal(
            stats[:, _c + 6:_c + 7], stats[:, _c + 5:_c + 6]),
            waits=[("tDV", c4)], inc=("tDV", 1))
        c_r = P.do("scalar", lambda e, _c=sb: e.activation(
            stats[:, _c + 7:_c + 8], stats[:, _c + 6:_c + 7], AF.Sqrt),
            waits=[("tDV", c_vr)], inc=("tSC", 1))
        lnst["cr_prev"][b] = c_r
        c5 = P.do("vector", lambda e, _c=sb: e.tensor_scalar(
            stats[:, _c + 8:_c + 9], stats[:, _c + 2:_c + 3], stats[:, _c + 7:_c + 8],
            -1.0, ALU.mult, ALU.mult), waits=[("tSC", c_r)], inc=("tDV", 1))
        return (c5, b, sb, xv, c_sq)

    def ln_transpose(cap, dst_fm_col, tp_ps):
        c5, b, sb, xv, c_sq = cap
        c_ap = P.do("vector", lambda e, _x=xv, _c=sb: e.tensor_scalar(
            xlt[:, :], _x, stats[:, _c + 7:_c + 8],
            stats[:, _c + 8:_c + 9], ALU.mult, ALU.add),
            waits=[("tDV", c5)] + list(lnst["xlt_free"]), inc=("tDV", 1))
        lnst["xt_free"][b] = [("tDV", c_ap), ("tSC", c_sq)]
        c_ev = None
        c_tp = None
        evs = {}
        for dc in range(8):
            pb = dc % 2
            wtp = [("tDV", c_ap), ("qX", c_idb)]
            if dc >= 2:
                wtp.append(("tDV", evs[dc - 2]))
            elif lnst["tp_guard"]:
                wtp += lnst["tp_guard"]
            c_tp = P.do("tensor", lambda e, _dc=dc, _pb=pb, _tp=tp_ps: e.transpose(
                _tp[_pb][:, :], xlt[:, 128 * _dc:128 * _dc + 128],
                ident_b[:]), waits=wtp, inc=("tPE", 1))
            c_ev = P.do("vector", lambda e, _dc=dc, _pb=pb, _col=dst_fm_col, _tp=tp_ps:
                        e.tensor_copy(
                            xlf[:, _col + 256 * _dc:_col + 256 * _dc + 128],
                            _tp[_pb][:, :]),
                        waits=[("tPE", c_tp)], inc=("tDV", 1))
            evs[dc] = c_ev
        lnst["xlt_free"] = [("tPE", c_tp)]
        lnst["tp_guard"] = [("tDV", c_ev)]
        return c_ev

    p0 = ExitStack()
    wk_b = p0.enter_context(nc.sbuf_tensor("wk_b", [128, 8192], BF16))
    wv_b = p0.enter_context(nc.sbuf_tensor("wv_b", [128, 8192], BF16))
    tp_ps_a = [p0.enter_context(nc.psum_tensor(f"tp_ps{i}", [128, 128], BF16)) for i in range(2)]
    kq_ps_a = [p0.enter_context(nc.psum_tensor(f"kq_ps{i}", [128, 256], F32)) for i in range(2)]
    v_ps = [p0.enter_context(nc.psum_tensor(f"v_ps{i}", [128, 1024], F32)) for i in range(2)]

    c_wk = P.do("gpsimd", wcast_load(wk_b, 0, wk, 0), inc=("qW", 16))
    c_wv = P.do("gpsimd", wcast_load(wv_b, 0, wv, 0), inc=("qW", 16))
    c_vones = P.do("gpsimd", lambda e: e.memset(v_all[:, :], 1.0), inc=("tGP", 1))

    kq_free = [[], []]
    v_free = [[], []]
    for kb in range(NCTX // 256):
        c_ev = 0
        caps = [ln_stats(x_ctx, 256 * kb + 128 * t2, dump=q_all[:, 0:1024])
                for t2 in range(2)]
        for t2 in range(2):
            c_ev = ln_transpose(caps[t2], 128 * t2, tp_ps_a)
            if stop_after == 'ln0' and kb == 0 and t2 == 0:
                P.do("sync", lambda e: e.dma_start(out_d[0:128, 0:24], stats[:, :]),
                     waits=[("tSC", P.cnt["tSC"]), ("tDV", P.cnt["tDV"])], inc=("qX", 16))
                P.do("sync", lambda e: e.dma_start(
                    bass.AP(av_d, 0, [[2048, 128], [1, 1024]]), xlt[:, :]), inc=("qX", 16))
                P.do("sync", lambda e: e.dma_start(
                    bass.AP(av_d, 128 * 2048, [[2048, 128], [1, 2048]]), xlf[:, :]), inc=("qX", 16))
                P.flush()
                return nc
        for hp in range(HP):
            pb = hp % 2
            w0 = [("tDV", c_ev), ("qW", c_wk)] + kq_free[pb]
            c_mm = None
            for dc in range(8):
                c_mm = P.do("tensor", lambda e, _hp=hp, _dc=dc, _pb=pb: e.matmul(
                    kq_ps_a[_pb][:, :],
                    wk_b[:, 1024 * _dc + 128 * _hp:1024 * _dc + 128 * _hp + 128],
                    xlf[:, 256 * _dc:256 * _dc + 256],
                    start=(_dc == 0), stop=(_dc == 7), skip_group_check=True),
                    waits=w0 if dc == 0 else [], inc=("tPE", 1))
            c_kev = P.do("scalar", lambda e, _hp=hp, _kb=kb, _pb=pb: e.activation(
                k_all[:, NCTX * _hp + 256 * _kb:NCTX * _hp + 256 * _kb + 256],
                kq_ps_a[_pb][:, :], AF.Copy), waits=[("tPE", c_mm)], inc=("tSC", 1))
            kq_free[pb] = [("tSC", c_kev)]
        for t2 in range(2):
            j = 2 * kb + t2
            pb = j % 2
            w0 = [("tDV", c_ev), ("qW", c_wv)] + v_free[pb]
            c_mm = None
            for dc in range(8):
                for nh in range(2):
                    c_mm = P.do("tensor", lambda e, _t2=t2, _dc=dc, _nh=nh, _pb=pb: e.matmul(
                        v_ps[_pb][:, 512 * _nh:512 * _nh + 512],
                        xlf[:, 256 * _dc + 128 * _t2:256 * _dc + 128 * _t2 + 128],
                        wv_b[:, 1024 * _dc + 512 * _nh:1024 * _dc + 512 * _nh + 512],
                        start=(_dc == 0), stop=(_dc == 7), skip_group_check=True),
                        waits=w0 if (dc == 0 and nh == 0) else [], inc=("tPE", 1))
            c_vev = P.do("scalar", lambda e, _j=j, _pb=pb: e.activation(
                bass.AP(v_all, 1040 * _j, [[NKVT * 1040, 128], [130, 8], [65, 2], [1, 64]]),
                bass.AP(v_ps[_pb], 0, [[1024, 128], [128, 8], [64, 2], [1, 64]]),
                AF.Copy),
                waits=[("tPE", c_mm), ("tGP", c_vones)], inc=("tSC", 1))
            v_free[pb] = [("tSC", c_vev)]
    c_p0a_pe = P.cnt["tPE"]

    P.flush()
    if stop_after == 'p0a':
        return nc
    p0.close()

    p0b = ExitStack()
    wq_b = p0b.enter_context(nc.sbuf_tensor("wq_b", [128, 8192], BF16))
    tp_ps_b = [p0b.enter_context(nc.psum_tensor(f"tpb_ps{i}", [128, 128], BF16)) for i in range(2)]
    kq_ps_b = [p0b.enter_context(nc.psum_tensor(f"kqb_ps{i}", [128, 256], F32)) for i in range(2)]

    c_wq = P.do("gpsimd", wcast_load(wq_b, 0, wq, 0),
                waits=[("tPE", c_p0a_pe)], inc=("qW", 16))
    kq_free = [[], []]
    for qb in range(NQ // 256):
        c_ev = 0
        caps = [ln_stats(x_q, 256 * qb + 128 * t2) for t2 in range(2)]
        for t2 in range(2):
            c_ev = ln_transpose(caps[t2], 128 * t2, tp_ps_b)
        for hp in range(HP):
            pb = hp % 2
            w0 = [("tDV", c_ev), ("qW", c_wq)] + kq_free[pb]
            c_mm = None
            for dc in range(8):
                c_mm = P.do("tensor", lambda e, _hp=hp, _dc=dc, _pb=pb: e.matmul(
                    kq_ps_b[_pb][:, :],
                    wq_b[:, 1024 * _dc + 128 * _hp:1024 * _dc + 128 * _hp + 128],
                    xlf[:, 256 * _dc:256 * _dc + 256],
                    start=(_dc == 0), stop=(_dc == 7), skip_group_check=True),
                    waits=w0 if dc == 0 else [], inc=("tPE", 1))
            c_qev = P.do("scalar", lambda e, _hp=hp, _qb=qb, _pb=pb: e.activation(
                q_all[:, NQ * _hp + 256 * _qb:NQ * _hp + 256 * _qb + 256],
                kq_ps_b[_pb][:, :], AF.Copy), waits=[("tPE", c_mm)], inc=("tSC", 1))
            kq_free[pb] = [("tSC", c_qev)]
    c_p0b_pe = P.cnt["tPE"]
    c_p0b_sc = P.cnt["tSC"]

    if stop_after == 'p0b':
        P.do("sync", lambda e: e.dma_start(
            bass.AP(av_d, 0, [[2048, 128], [1, 2048]]),
            k_all[:, 0:2048]), waits=[("tSC", P.cnt["tSC"])], inc=("qX", 16))
        P.do("sync", lambda e: e.dma_start(
            bass.AP(av_d, 128 * 2048, [[2048, 128], [1, 2048]]),
            k_all[:, 7 * 4096:7 * 4096 + 2048]), inc=("qX", 16))
        P.do("sync", lambda e: e.dma_start(
            bass.AP(av_d, 2 * 128 * 2048, [[2048, 128], [1, 2048]]),
            q_all[:, 7 * 2048:7 * 2048 + 2048]), inc=("qX", 16))
        P.do("sync", lambda e: e.dma_start(
            bass.AP(av_d, 3 * 128 * 2048, [[2048, 128], [1, 2048]]),
            v_all[:, 0:2048]), inc=("qX", 16))
        P.flush()
        return nc
    P.flush()
    p0b.close()

    aex = ExitStack()
    pt = aex.enter_context(nc.sbuf_tensor("pt", [128, 4096], BF16))
    ksq = pt
    s64r = aex.enter_context(nc.sbuf_tensor("s64r", [65, 2048], F32))
    av_spl = aex.enter_context(nc.sbuf_tensor("av_spl", [64, 4096], BF16))
    km = aex.enter_context(nc.sbuf_tensor("km", [1, 16], F32))
    bsc = aex.enter_context(nc.sbuf_tensor("bsc", [1, 4], F32))
    negvb = aex.enter_context(nc.sbuf_tensor("negvb", [1, 1], BF16))
    av_ps = [aex.enter_context(nc.psum_tensor(f"av_ps{i}", [65, 1024], F32)) for i in range(2)]
    d_psA = aex.enter_context(nc.psum_tensor("d_psA", [128, 1024], F32))
    d_psB = aex.enter_context(nc.psum_tensor("d_psB", [128, 1024], F32))

    for hp in range(HP):
        c_k2 = P.do("vector", lambda e, _hp=hp: e.tensor_tensor(
            ksq[:, :], k_all[:, NCTX * _hp:NCTX * _hp + 4096],
            k_all[:, NCTX * _hp:NCTX * _hp + 4096], ALU.mult),
            waits=[("tPE", c_p0b_pe), ("tSC", c_p0b_sc)], inc=("tDV", 1))
        c_rm = 0
        for c in range(8):
            pb = c % 2
            c_n2 = P.do("tensor", lambda e, _c=c, _pb=pb: e.matmul(
                d_psA[0:1, 512 * _pb:512 * _pb + 512], ones1[:],
                ksq[:, 512 * _c:512 * _c + 512],
                start=True, stop=True, skip_group_check=True),
                waits=[("tDV", c_k2), ("tGP", c_ones)] if c < 2 else [("tDV", c_rm)],
                inc=("tPE", 1))
            c_rm = P.do("vector", lambda e, _c=c, _pb=pb: e.reduce_max(
                km[0:1, _c:_c + 1], d_psA[0:1, 512 * _pb:512 * _pb + 512], axis=AX.X),
                waits=[("tPE", c_n2)], inc=("tDV", 1))
        c_km = P.do("vector", lambda e: e.reduce_max(
            bsc[0:1, 0:1], km[0:1, 0:8], axis=AX.X),
            waits=[("tDV", c_rm)], inc=("tDV", 1))
        c_q2 = P.do("vector", lambda e, _hp=hp: e.tensor_tensor(
            ksq[:, 0:2048], q_all[:, NQ * _hp:NQ * _hp + 2048],
            q_all[:, NQ * _hp:NQ * _hp + 2048], ALU.mult),
            waits=[("tPE", P.cnt["tPE"])], inc=("tDV", 1))
        for c in range(4):
            pb = c % 2
            c_n2 = P.do("tensor", lambda e, _c=c, _pb=pb: e.matmul(
                d_psA[0:1, 512 * _pb:512 * _pb + 512], ones1[:],
                ksq[:, 512 * _c:512 * _c + 512],
                start=True, stop=True, skip_group_check=True),
                waits=[("tDV", c_q2)] if c < 2 else [("tDV", c_rm)], inc=("tPE", 1))
            c_rm = P.do("vector", lambda e, _c=c, _pb=pb: e.reduce_max(
                km[0:1, 8 + _c:9 + _c], d_psA[0:1, 512 * _pb:512 * _pb + 512], axis=AX.X),
                waits=[("tPE", c_n2)], inc=("tDV", 1))
        c_qm = P.do("vector", lambda e: e.reduce_max(
            bsc[0:1, 1:2], km[0:1, 8:12], axis=AX.X),
            waits=[("tDV", c_rm)], inc=("tDV", 1))
        c_pr = P.do("vector", lambda e: e.tensor_tensor(
            bsc[0:1, 2:3], bsc[0:1, 0:1], bsc[0:1, 1:2], ALU.mult),
            waits=[("tDV", c_qm)], inc=("tDV", 1))
        c_sq2 = P.do("scalar", lambda e: e.activation(
            bsc[0:1, 3:4], bsc[0:1, 2:3], AF.Sqrt), waits=[("tDV", c_pr)], inc=("tSC", 1))
        c_nv = P.do("vector", lambda e: e.tensor_scalar(
            negvb[0:1, 0:1], bsc[0:1, 3:4], -SCALE, None, ALU.mult, ALU.bypass),
            waits=[("tSC", c_sq2)], inc=("tDV", 1))
        c_bc = P.do("tensor", lambda e: e.matmul(
            d_psA[:, 0:1], ones2[0:1, :], negvb[0:1, 0:1], start=True, stop=True,
            skip_group_check=True), waits=[("tDV", c_nv)], inc=("tPE", 1))
        P.do("vector", lambda e, _hp=hp: e.tensor_copy(
            negb8[:, _hp:_hp + 1], d_psA[:, 0:1]), waits=[("tPE", c_bc)],
            inc=("tDV", 1))
    c_bounds_dv = P.cnt["tDV"]
    c_bounds_sc = P.cnt["tSC"]

    slot_list = []
    for hp in range(HP):
        for h in range(2):
            jmax = 32 if h == 0 else 16
            for j in range(jmax):
                w = min(1024, 64 * (32 - j)) if h == 0 else 64 * (16 - j)
                masked = (h == 0 and j >= 16) or h == 1
                jlast0 = 31 if h == 0 else 15
                jlast1 = (31 - 8) if h == 0 else (15 - 8)
                slot_list.append((hp, h, j, w, masked,
                                  j == 0, (j == jlast0, j == jlast1),
                                  j == jmax - 1))
    exp_done = {}
    av_done = {}
    c_spill = {}
    c_sums_dma = {}
    dps_guard = [("tDV", c_bounds_dv)]

    def emit_av(s):
        hp, h, j, w, masked, st, sps, last = slot_list[s]
        p = s % 2
        vcol = 1040 * j + 130 * hp
        r = {}
        for hd in range(2):
            ptc = 2048 * hd + 1024 * p
            if s in mask_done:
                wa = [("tDV", mask_done[s][hd])]
            else:
                wa = [("tSC", exp_done[s][hd]), ("tGP", c_vones)]
            first = True
            for c0 in range(0, w, 512):
                cw = min(512, w - c0)
                sp = sps[c0 // 512]
                r[hd] = P.do("tensor", lambda e, _hd=hd, _v=vcol, _c0=c0, _cw=cw,
                             _st=st, _sp=sp, _pc=ptc: e.matmul(
                                 av_ps[_hd][0:65, _c0:_c0 + _cw],
                                 v_all[:, _v + 65 * _hd:_v + 65 * _hd + 65],
                                 pt[:, _pc + _c0:_pc + _c0 + _cw], start=_st, stop=_sp,
                                 skip_group_check=True),
                             waits=wa if first else [], inc=("tPE", 1))
                first = False
        av_done[s] = r

    mask_done = {}
    ns = len(slot_list)
    for s in range(ns):
        hp, h, j, w, masked, st, sps, last = slot_list[s]
        p = s % 2
        kcol = NCTX * hp + 128 * j
        qcol = NQ * hp + 1024 * h
        wtA = list(dps_guard)
        wtB = []
        dps_guard = []
        if s >= 1:
            wtA.append(("tSC", exp_done[s - 1][0]))
            wtB.append(("tSC", exp_done[s - 1][1]))
        dots_done = {}
        for c0 in range(0, w, 512):
            cw = min(512, w - c0)
            dots_done[0] = P.do("tensor", lambda e, _k=kcol, _q=qcol + c0, _c0=c0,
                                _cw=cw: e.matmul(
                d_psA[:, _c0:_c0 + _cw], k_all[0:64, _k:_k + 128],
                q_all[0:64, _q:_q + _cw], start=True, stop=True,
                tile_position=(0, 0), skip_group_check=True),
                waits=wtA if c0 == 0 else [], inc=("tPE", 1))
        for c0 in range(0, w, 512):
            cw = min(512, w - c0)
            dots_done[1] = P.do("tensor", lambda e, _k=kcol, _q=qcol + c0, _c0=c0,
                                _cw=cw: e.matmul(
                d_psB[:, _c0:_c0 + _cw], k_all[64:128, _k:_k + 128],
                q_all[64:128, _q:_q + _cw], start=True, stop=True,
                tile_position=(64, 0), skip_group_check=True),
                waits=wtB if c0 == 0 else [], inc=("tPE", 1))
        if s >= 1:
            emit_av(s - 1)
        eA = {}
        for hd in range(2):
            ws = [("tPE", dots_done[hd]), ("tDV", c_bounds_dv)]
            if s >= 2:
                ws.append(("tPE", av_done[s - 2][hd]))
            dps = d_psA if hd == 0 else d_psB
            eA[hd] = P.do("scalar", lambda e, _w=w, _hp=hp, _pc=2048 * hd + 1024 * p,
                          _d=dps: e.activation(
                pt[:, _pc:_pc + _w], _d[:, 0:_w], AF.Exp,
                bias=negb8[:, _hp:_hp + 1], scale=SCALE), waits=ws, inc=("tSC", 1))
        exp_done[s] = eA
        if masked:
            moff = w - 64
            mk = {}
            for hd in range(2):
                mk[hd] = P.do("vector", lambda e, _m=moff, _pc=2048 * hd + 1024 * p:
                              e.tensor_tensor(
                    pt[:, _pc + _m:_pc + _m + 64], pt[:, _pc + _m:_pc + _m + 64],
                    mask_s[:], ALU.mult),
                    waits=[("tSC", exp_done[s][hd]), ("qX", c_msk)], inc=("tDV", 1))
            mask_done[s] = mk
        if last:
            emit_av(s)
            c_av = av_done[s][1]
            wsp = [("qX", c_spill[hp - 1])] if (hp - 1 in c_spill and h == 0) else []
            c_sumsdma = c_sums_dma.get((hp, h - 1)) or c_sums_dma.get((hp - 1, 1))
            c_evs = []
            for head in range(2):
                c_evs.append(P.do("scalar", lambda e, _hd=head, _h=h: e.activation(
                    av_spl[0:64, 2048 * _hd + 1024 * _h:2048 * _hd + 1024 * _h + 1024],
                    av_ps[_hd][0:64, :], AF.Copy),
                    waits=[("tPE", c_av)] + wsp, inc=("tSC", 1)))
                c_evs.append(P.do("scalar", lambda e, _hd=head: e.activation(
                    s64r[64:65, 1024 * _hd:1024 * _hd + 1024],
                    av_ps[_hd][64:65, :], AF.Copy),
                    waits=([("qX", c_sumsdma)] if c_sumsdma else []), inc=("tSC", 1)))
            c_ep = c_evs[-1]
            dps_guard = [("tSC", c_ep)]
            c_sums_dma[(hp, h)] = P.do("sync", lambda e, _hp=hp, _h=h: e.dma_start(
                bass.AP(sums_d, _hp * 2 * NQ + 1024 * _h, [[2 * NQ, 1], [NQ, 2], [1, 1024]]),
                bass.AP(s64r, 64 * 2048, [[2048, 1], [1024, 2], [1, 1024]])),
                waits=[("tSC", c_ep)], inc=("qX", 16))
            if h == 1:
                for head in range(2):
                    c_spill[hp] = P.do("sync", lambda e, _hp=hp, _hd=head: e.dma_start(
                        bass.AP(av_d, (_hp * 128 + 64 * _hd) * NQ, [[NQ, 64], [1, NQ]]),
                        av_spl[0:64, 2048 * _hd:2048 * _hd + 2048]),
                        waits=[("tSC", c_ep)], inc=("qX", 16))
    c_spill_last = c_spill[HP - 1]

    P.flush()
    if stop_after == 'attn':
        return nc
    aex.close()
    kvq.close()

    P.barrier()
    om = ExitStack()
    u_fm = om.enter_context(nc.sbuf_tensor("u_fm", [128, 8 * NQ], BF16))
    facc = om.enter_context(nc.sbuf_tensor("facc", [128, NQT * 1024], F32))
    g_all = om.enter_context(nc.sbuf_tensor("g_all", [128, NQT * 16], F32))
    g2t = om.enter_context(nc.sbuf_tensor("g2t", [16, NQ], BF16))
    br_b = om.enter_context(nc.sbuf_tensor("br_b", [16, 1024], BF16))

    oex = ExitStack()
    av_all = oex.enter_context(nc.sbuf_tensor("av_all", [128, HP * NQ], BF16))
    wo_b = oex.enter_context(nc.sbuf_tensor("wo_b", [128, 8192], BF16))
    ident_f = oex.enter_context(nc.sbuf_tensor("ident_f_s", [128, 128], F32))
    sel2 = oex.enter_context(nc.sbuf_tensor("sel2_s", [16, 1024], BF16))
    rec2 = oex.enter_context(nc.sbuf_tensor("rec2", [16, 2048], F32))
    rec2h = oex.enter_context(nc.sbuf_tensor("rec2h", [16, 2048], BF16))
    rec2l = oex.enter_context(nc.sbuf_tensor("rec2l", [16, 2048], BF16))
    xq_t = oex.enter_context(nc.sbuf_tensor("xq_t", [128, 1024], F32))
    res1_t = oex.enter_context(nc.sbuf_tensor("res1", [128, 2048], F32))
    u_t2 = oex.enter_context(nc.sbuf_tensor("u_t", [128, 2048], F32))
    u_bf = oex.enter_context(nc.sbuf_tensor("u_bf", [128, 2048], BF16))
    cent_b = oex.enter_context(nc.sbuf_tensor("cent_b", [128, 128], BF16))
    sc_em = oex.enter_context(nc.sbuf_tensor("sc_em", [16, NQ], F32))
    eb_s = oex.enter_context(nc.sbuf_tensor("eb_s", [16, 1], F32))
    bs_b = oex.enter_context(nc.sbuf_tensor("bs_b", [2, 1024], BF16))
    rt2 = oex.enter_context(nc.sbuf_tensor("rt", [128, 160], F32))
    gg_t = oex.enter_context(nc.sbuf_tensor("gg_t", [128, 16], BF16))
    o_ps = [oex.enter_context(nc.psum_tensor(f"o_ps{i}", [128, 1024], F32)) for i in range(2)]
    tp2_ps = [oex.enter_context(nc.psum_tensor("tp2_ps0", [128, 128], BF16))]
    tp2_ps.append(tp2_ps[0])
    sc_ps = oex.enter_context(nc.psum_tensor("sc_ps", [16, 512], F32))
    rt_ps = [oex.enter_context(nc.psum_tensor(f"rt_ps{i}", [128, 128], F32)) for i in range(2)]
    misc_ps = rt_ps[0]

    c_av_ld = P.do("sync", lambda e: e.dma_start(
        bass.AP(av_all, 0, [[HP * NQ, 128], [NQ, HP], [1, NQ]]),
        bass.AP(av_d, 0, [[NQ, 128], [128 * NQ, HP], [1, NQ]])),
        waits=[("qX", c_spill_last)], inc=("qX", 16))
    c_sums_ld = P.do("sync", lambda e: e.dma_start(
        bass.AP(rec2, 0, [[2048, 16], [1, 2048]]),
        bass.AP(sums_d, 0, [[2048, 16], [1, 2048]])), inc=("qX", 16))
    c_sel2 = P.do("sync", lambda e: e.dma_start(sel2[:], sel2_in[:, :]), inc=("qX", 16))
    c_rc2 = P.do("vector", lambda e: e.reciprocal(rec2[:, :], rec2[:, :]),
                 waits=[("qX", c_sums_ld)], inc=("tDV", 1))
    c_rh = P.do("vector", lambda e: e.tensor_copy(rec2h[:, :], rec2[:, :]),
                waits=[("tDV", c_rc2)], inc=("tDV", 1))
    c_rl = P.do("vector", lambda e: e.tensor_tensor(
        rec2l[:, :], rec2[:, :], rec2h[:, :], ALU.subtract),
        waits=[("tDV", c_rh)], inc=("tDV", 1))

    nrm_done = {}
    for hp in range(HP):
        for qt in range(NQT):
            i = hp * NQT + qt
            rp = i % 2
            wn = [("tDV", c_rl), ("qX", c_av_ld), ("qX", c_sel2)]
            if i >= 2:
                wn.append(("tDV", nrm_done[i - 2]))
            P.do("tensor", lambda e, _hp=hp, _qt=qt, _rp=rp: e.matmul(
                rt_ps[_rp][:, :], sel2[:, 128 * _hp:128 * _hp + 128],
                rec2h[:, 128 * _qt:128 * _qt + 128],
                start=True, stop=False, skip_group_check=True),
                waits=wn, inc=("tPE", 1))
            c_nm = P.do("tensor", lambda e, _hp=hp, _qt=qt, _rp=rp: e.matmul(
                rt_ps[_rp][:, :], sel2[:, 128 * _hp:128 * _hp + 128],
                rec2l[:, 128 * _qt:128 * _qt + 128],
                start=False, stop=True, skip_group_check=True), inc=("tPE", 1))
            nrm_done[i] = P.do("vector", lambda e, _hp=hp, _qt=qt, _rp=rp: e.tensor_tensor(
                av_all[:, NQ * _hp + 128 * _qt:NQ * _hp + 128 * _qt + 128],
                av_all[:, NQ * _hp + 128 * _qt:NQ * _hp + 128 * _qt + 128],
                rt_ps[_rp][:, :], ALU.mult), waits=[("tPE", c_nm)], inc=("tDV", 1))
    c_nrm_all = P.cnt["tDV"]
    c_idf = P.do("sync", lambda e: e.dma_start(ident_f[:], ident_f_in[:, :]), inc=("qX", 16))
    c_wo = P.do("gpsimd", wcast_load(wo_b, 0, wo, 0), inc=("qW", 16))
    c_cent = P.do("gpsimd", lambda e: e.dma_start(
        bass.AP(cent_b, 0, [[128, 128], [16, 8], [1, 16]]),
        bass.AP(cent_t, 0, [[16, 128], [2048, 8], [1, 16]])), inc=("qW", 16))
    c_eb = P.do("gpsimd", lambda e: e.dma_start(eb_s[:], eb[:, :]), inc=("qW", 16))
    c_br = P.do("gpsimd", lambda e: e.dma_start(br_b[0:16, :], br[:, :]), inc=("qW", 16))
    c_bs = P.do("gpsimd", lambda e: e.dma_start(bs_b[:], bs[:, :]), inc=("qW", 16))

    xq_free = [[], []]
    ops_free = [[], []]
    ubf_free = [[], []]
    res1_free = [[], []]
    c_sq_g = {0: 0, 1: 0}
    c_r_g = {0: 0, 1: 0}
    for qt in range(NQT):
        b = qt % 2
        res1 = res1_t[:, 1024 * b:1024 * b + 1024]
        u_t = u_t2[:, 1024 * b:1024 * b + 1024]
        rt = rt2[:, 80 * b:80 * b + 80]
        c_xq = P.do("sync", lambda e, _qt=qt: e.dma_start(
            xq_t[:, :], x_q[128 * _qt:128 * _qt + 128, :]),
            waits=list(xq_free[0]), inc=("qX", 16))
        wmm = [("tDV", c_nrm_all), ("qW", c_wo)] + ops_free[b]
        c_mm = None
        for ic in range(8):
            for nh in range(2):
                c_mm = P.do("tensor", lambda e, _qt=qt, _ic=ic, _nh=nh, _b=b: e.matmul(
                    o_ps[_b][:, 512 * _nh:512 * _nh + 512],
                    av_all[:, NQ * _ic + 128 * _qt:NQ * _ic + 128 * _qt + 128],
                    wo_b[:, 1024 * _ic + 512 * _nh:1024 * _ic + 512 * _nh + 512],
                    start=(_ic == 0), stop=(_ic == 7), skip_group_check=True),
                    waits=wmm if (ic == 0 and nh == 0) else [], inc=("tPE", 1))
        c_r1 = P.do("vector", lambda e, _b=b, _r=res1: e.tensor_tensor(
            _r, o_ps[_b][:, :], xq_t[:, :], ALU.add),
            waits=[("tPE", c_mm), ("qX", c_xq)] + res1_free[b], inc=("tDV", 1))
        ops_free[b] = [("tDV", c_r1)]
        xq_free[0] = [("tDV", c_r1)]
        c_rs1 = P.do("vector", lambda e, _r=res1, _rt=rt: e.reduce_sum(
            _rt[:, 0:1], _r, axis=AX.X),
            waits=[("tSC", c_r_g[b])], inc=("tDV", 1))
        wub = list(ubf_free[b])
        P.do("scalar", lambda e, _b=b, _r=res1, _rt=rt: e.activation(
            u_bf[:, 1024 * _b:1024 * _b + 1024], _r, AF.Square,
            accum_out=_rt[:, 1:2]), waits=[("tDV", c_r1)] + wub)
        c_sq = P.do("scalar", lambda e, _rt=rt: e.copy(_rt[:, 14:15], _rt[:, 1:2]),
                    inc=("tSC", 1))
        c_sq_g[b] = c_sq
        d1 = P.do("vector", lambda e, _rt=rt: e.tensor_scalar(
            _rt[:, 2:3], _rt[:, 0:1], 1.0 / D, None, ALU.mult, ALU.bypass),
            waits=[("tDV", c_rs1)], inc=("tDV", 1))
        d2 = P.do("vector", lambda e, _rt=rt: e.tensor_tensor(
            _rt[:, 3:4], _rt[:, 2:3], _rt[:, 2:3], ALU.mult),
            waits=[("tDV", d1)], inc=("tDV", 1))
        d3 = P.do("vector", lambda e, _rt=rt: e.tensor_scalar(
            _rt[:, 4:5], _rt[:, 14:15], 1.0 / D, EPS, ALU.mult, ALU.add),
            waits=[("tSC", c_sq)], inc=("tDV", 1))
        d4 = P.do("vector", lambda e, _rt=rt: e.tensor_tensor(
            _rt[:, 5:6], _rt[:, 4:5], _rt[:, 3:4], ALU.subtract),
            waits=[("tDV", d3)], inc=("tDV", 1))
        c_vr = P.do("vector", lambda e, _rt=rt: e.reciprocal(_rt[:, 6:7], _rt[:, 5:6]),
                    waits=[("tDV", d4)], inc=("tDV", 1))
        c_r = P.do("scalar", lambda e, _rt=rt: e.activation(_rt[:, 7:8], _rt[:, 6:7], AF.Sqrt),
                   waits=[("tDV", c_vr)], inc=("tSC", 1))
        c_r_g[b] = c_r
        d5 = P.do("vector", lambda e, _rt=rt: e.tensor_scalar(
            _rt[:, 8:9], _rt[:, 2:3], _rt[:, 7:8], -1.0, ALU.mult, ALU.mult),
            waits=[("tSC", c_r)], inc=("tDV", 1))
        d6 = P.do("vector", lambda e, _r=res1, _u=u_t, _rt=rt: e.tensor_scalar(
            _u, _r, _rt[:, 7:8], _rt[:, 8:9], ALU.mult, ALU.add),
            waits=[("tDV", d5)], inc=("tDV", 1))
        c_fc = P.do("vector", lambda e, _qt=qt, _r=res1, _u=u_t: e.tensor_tensor(
            facc[:, 1024 * _qt:1024 * _qt + 1024], _r, _u, ALU.add),
            waits=[("tDV", d6)], inc=("tDV", 1))
        c_ub = P.do("vector", lambda e, _b=b, _u=u_t: e.tensor_copy(
            u_bf[:, 1024 * _b:1024 * _b + 1024], _u), inc=("tDV", 1))
        res1_free[b] = [("tSC", c_sq), ("tDV", c_fc)]
        c_ev = None
        for dc in range(8):
            pb = dc % 2
            c_tp = P.do("tensor", lambda e, _b=b, _dc=dc, _pb=pb: e.transpose(
                tp2_ps[_pb][:, :], u_bf[:, 1024 * _b + 128 * _dc:1024 * _b + 128 * _dc + 128],
                ident_b[:]), waits=[("tDV", c_ub), ("tSC", P.cnt["tSC"])],
                inc=("tPE", 1))
            c_ev = P.do("scalar", lambda e, _dc=dc, _pb=pb, _qt=qt: e.activation(
                u_fm[:, NQ * _dc + 128 * _qt:NQ * _dc + 128 * _qt + 128],
                tp2_ps[_pb][:, :], AF.Copy), waits=[("tPE", c_tp)], inc=("tSC", 1))
        ubf_free[b] = [("tSC", c_ev)]
    c_o1_sc = P.cnt["tSC"]

    c_scev = {}
    for c in range(4):
        c_mm = None
        for dc in range(8):
            w0 = [("qW", c_cent), ("tSC", P.cnt["tSC"])]
            if c > 0:
                w0.append(("tDV", c_scev[c - 1]))
            c_mm = P.do("tensor", lambda e, _c=c, _dc=dc: e.matmul(
                sc_ps[:, :], cent_b[:, 16 * _dc:16 * _dc + 16],
                u_fm[:, NQ * _dc + 512 * _c:NQ * _dc + 512 * _c + 512],
                start=(_dc == 0), stop=(_dc == 7), skip_group_check=True),
                waits=w0 if dc == 0 else [], inc=("tPE", 1))
        c_scev[c] = P.do("vector", lambda e, _c=c: e.tensor_scalar(
            sc_em[:, 512 * _c:512 * _c + 512], sc_ps[:, :], eb_s[:, 0:1], None,
            ALU.add, ALU.bypass), waits=[("tPE", c_mm), ("qW", c_eb)], inc=("tDV", 1))
    c_es_prev = {0: 0, 1: 0}
    c_m1_prev = {0: 0, 1: 0}
    c_gt_prev = 0
    c_g_prev = {0: 0, 1: 0}
    for qt in range(NQT):
        b = qt % 2
        rt = rt2[:, 80 * b:80 * b + 80]
        mps = rt_ps[b]
        c_tr = P.do("tensor", lambda e, _qt=qt, _m=mps: e.transpose(
            _m[:, 0:16], sc_em[0:16, 128 * _qt:128 * _qt + 128], ident_f[0:16, 0:16]),
            waits=[("tDV", c_scev[qt // 4]), ("qX", c_idf),
                   ("tSC", c_es_prev[b]), ("tDV", c_m1_prev[b])], inc=("tPE", 1))
        c_m1 = P.do("vector", lambda e, _rt=rt, _m=mps: e.reduce_max(
            _rt[:, 9:10], _m[:, 0:16], axis=AX.X, negate=True),
            waits=[("tPE", c_tr), ("tSC", c_r_g[b])], inc=("tDV", 1))
        c_m1_prev[b] = c_m1
        c_es = P.do("scalar", lambda e, _rt=rt, _m=mps: e.activation(
            _rt[:, 32:48], _m[:, 0:16], AF.Exp, bias=_rt[:, 9:10], scale=1.0),
            waits=[("tDV", c_m1)], inc=("tSC", 1))
        c_es_prev[b] = c_es
        c_rsm = P.do("vector", lambda e, _rt=rt: e.reduce_sum(
            _rt[:, 10:11], _rt[:, 32:48], axis=AX.X), waits=[("tSC", c_es)],
            inc=("tDV", 1))
        e1 = P.do("vector", lambda e, _rt=rt: e.reciprocal(_rt[:, 11:12], _rt[:, 10:11]),
             waits=[("tDV", c_rsm)], inc=("tDV", 1))
        e2 = P.do("vector", lambda e, _rt=rt: e.tensor_scalar(
            _rt[:, 48:64], _rt[:, 32:48], _rt[:, 11:12], None, ALU.mult, ALU.bypass),
            waits=[("tDV", e1)], inc=("tDV", 1))
        c_rm1 = P.do("vector", lambda e, _rt=rt: e.reduce_max(
            _rt[:, 12:13], _rt[:, 48:64], axis=AX.X),
            waits=[("tDV", e2)], inc=("tDV", 1))
        e3 = P.do("vector", lambda e, _rt=rt: e.tensor_scalar(
            _rt[:, 64:80], _rt[:, 48:64], _rt[:, 12:13], None, ALU.is_ge, ALU.bypass),
            waits=[("tDV", c_rm1)], inc=("tDV", 1))
        e4 = P.do("vector", lambda e, _rt=rt: e.scalar_tensor_tensor(
            _rt[:, 16:32], _rt[:, 64:80], -1e4, _rt[:, 48:64], ALU.mult, ALU.add),
            waits=[("tDV", e3)], inc=("tDV", 1))
        c_rm2 = P.do("vector", lambda e, _rt=rt: e.reduce_max(
            _rt[:, 13:14], _rt[:, 16:32], axis=AX.X),
            waits=[("tDV", e4)], inc=("tDV", 1))
        e5 = P.do("vector", lambda e, _rt=rt: e.tensor_scalar(
            _rt[:, 64:80], _rt[:, 48:64], _rt[:, 13:14], None, ALU.is_ge, ALU.bypass),
            waits=[("tDV", c_rm2)], inc=("tDV", 1))
        c_g = P.do("vector", lambda e, _qt=qt, _rt=rt: e.tensor_tensor(
            g_all[:, 16 * _qt:16 * _qt + 16], _rt[:, 48:64], _rt[:, 64:80], ALU.mult),
            waits=[("tDV", e5)], inc=("tDV", 1))
        c_g_prev[b] = c_g
        c_gg = P.do("vector", lambda e, _qt=qt: e.tensor_scalar(
            gg_t[:, :], g_all[:, 16 * _qt:16 * _qt + 16], 2.0, None,
            ALU.mult, ALU.bypass), waits=[("tDV", c_g), ("tPE", c_gt_prev)],
            inc=("tDV", 1))
        c_gt = P.do("tensor", lambda e: e.transpose(
            tp2_ps[0][0:16, :], gg_t[:, :], ident_b[:]),
            waits=[("tDV", c_gg), ("tSC", P.cnt["tSC"])], inc=("tPE", 1))
        c_gt_prev = c_gt
        P.do("scalar", lambda e, _qt=qt: e.activation(
            g2t[0:16, 128 * _qt:128 * _qt + 128], tp2_ps[0][0:16, :], AF.Copy),
            waits=[("tPE", c_gt)], inc=("tSC", 1))
    c_o_done_sc = P.cnt["tSC"]
    c_o_done_dv = P.cnt["tDV"]

    if stop_after == 'o':
        P.do("sync", lambda e: e.dma_start(
            bass.AP(out_d, 0, [[1024, 128], [131072, 16], [1, 1024]]),
            bass.AP(facc, 0, [[16384, 128], [1024, 16], [1, 1024]])),
            waits=[("tDV", P.cnt["tDV"]), ("tSC", P.cnt["tSC"])], inc=("qX", 16))
        P.do("gpsimd", lambda e: e.dma_start(
            bass.AP(av_d, 0, [[256, 128], [1, 256]]),
            bass.AP(g_all, 0, [[256, 128], [1, 256]])),
            waits=[("tDV", P.cnt["tDV"])], inc=("qW", 16))
        P.do("sync", lambda e: e.dma_start(
            bass.AP(av_d, 128 * 256, [[2048, 16], [1, 2048]]),
            bass.AP(g2t, 0, [[2048, 16], [1, 2048]])), inc=("qX", 16))
        P.do("sync", lambda e: e.dma_start(
            bass.AP(av_d, 2 * 128 * 256, [[2048, 128], [1, 2048]]),
            u_fm[:, 0:2048]), inc=("qX", 16))
        P.flush()
        return nc
    P.flush()
    oex.close()

    P.barrier()
    mex = ExitStack()
    wr_b = mex.enter_context(nc.sbuf_tensor("wr_b", [128, 2 * 8192], BF16))
    ws_tmp = mex.enter_context(nc.sbuf_tensor("ws_tmp", [128, 8192], BF16))
    e_ps = [mex.enter_context(nc.psum_tensor(f"e_ps{i}", [128, 1024], F32)) for i in range(3)]

    c_ws0 = P.do("gpsimd", wcast_load(wr_b, 0, wst, 0), inc=("qW", 16))
    c_ws1 = P.do("gpsimd", wcast_load(ws_tmp, 0, wst, D * D), inc=("qW", 16))
    c_wsum = P.do("vector", lambda e: e.tensor_tensor(
        wr_b[:, 0:8192], wr_b[:, 0:8192], ws_tmp[:, 0:8192], ALU.add),
        waits=[("qW", c_ws1)], inc=("tDV", 1))
    w_ready = {0: ("tDV", c_wsum)}
    wbuf_last_mm = {0: 0, 1: 0}
    mm_done = {}
    stt_done = {}
    psum_guard = {0: None, 1: None, 2: None}
    for t in range(NE + 1):
        if t + 1 <= NE:
            ee = t
            half = (t + 1) % 2
            w_ready[t + 1] = ("qW", P.do(
                "gpsimd", wcast_load(wr_b, 8192 * half, wrt, ee * D * D),
                waits=[("tPE", wbuf_last_mm[half])] if wbuf_last_mm[half] else [],
                inc=("qW", 16)))
        half = t % 2
        for qt in range(NQT):
            r = (t * NQT + qt) % 3
            w0 = [w_ready[t]]
            if psum_guard[r] is not None:
                w0.append(psum_guard[r])
            c_mm = None
            for dc in range(8):
                for nh in range(2):
                    c_mm = P.do("tensor", lambda e, _t=t, _qt=qt, _dc=dc, _nh=nh,
                                _r=r, _hf=half: e.matmul(
                        e_ps[_r][:, 512 * _nh:512 * _nh + 512],
                        u_fm[:, NQ * _dc + 128 * _qt:NQ * _dc + 128 * _qt + 128],
                        wr_b[:, 8192 * _hf + 1024 * _dc + 512 * _nh:
                             8192 * _hf + 1024 * _dc + 512 * _nh + 512],
                        start=(_dc == 0), stop=(_dc == 7 and _t > 0),
                        skip_group_check=True),
                        waits=w0 if (dc == 0 and nh == 0) else [], inc=("tPE", 1))
            if t == 0:
                for nh in range(2):
                    P.do("tensor", lambda e, _qt=qt, _nh=nh, _r=r: e.matmul(
                        e_ps[_r][:, 512 * _nh:512 * _nh + 512],
                        g2t[0:16, 128 * _qt:128 * _qt + 128],
                        br_b[0:16, 512 * _nh:512 * _nh + 512],
                        start=False, stop=False, skip_group_check=True),
                        waits=[("tSC", c_o_done_sc), ("qW", c_br), ("qW", c_bs)] if (qt == 0 and nh == 0) else [],
                        inc=("tPE", 1))
                    c_mm = P.do("tensor", lambda e, _nh=nh, _r=r: e.matmul(
                        e_ps[_r][:, 512 * _nh:512 * _nh + 512],
                        ones2[:, 0:128],
                        bs_b[:, 512 * _nh:512 * _nh + 512],
                        start=False, stop=True, skip_group_check=True),
                        inc=("tPE", 1))
            mm_done[(t, qt)] = c_mm
            wbuf_last_mm[half] = c_mm
            c_stt = None
            w_prev_stt = [("tDV", stt_done[(t - 1, qt)])] if t > 0 else [("tDV", c_o_done_dv)]
            for nh in range(2):
                if t == 0:
                    c_stt = P.do("vector", lambda e, _qt=qt, _nh=nh, _r=r:
                                 e.scalar_tensor_tensor(
                                     facc[:, 1024 * _qt + 512 * _nh:1024 * _qt + 512 * _nh + 512],
                                     e_ps[_r][:, 512 * _nh:512 * _nh + 512], 0.5,
                                     facc[:, 1024 * _qt + 512 * _nh:1024 * _qt + 512 * _nh + 512],
                                     ALU.mult, ALU.add),
                                 waits=([("tPE", c_mm)] + w_prev_stt) if nh == 0 else [],
                                 inc=("tDV", 1))
                else:
                    c_stt = P.do("vector", lambda e, _t=t, _qt=qt, _nh=nh, _r=r:
                                 e.scalar_tensor_tensor(
                                     facc[:, 1024 * _qt + 512 * _nh:1024 * _qt + 512 * _nh + 512],
                                     e_ps[_r][:, 512 * _nh:512 * _nh + 512],
                                     g_all[:, 16 * _qt + (_t - 1):16 * _qt + _t],
                                     facc[:, 1024 * _qt + 512 * _nh:1024 * _qt + 512 * _nh + 512],
                                     ALU.mult, ALU.add),
                                 waits=([("tPE", c_mm)] + w_prev_stt) if nh == 0 else [],
                                 inc=("tDV", 1))
            stt_done[(t, qt)] = c_stt
            psum_guard[r] = ("tDV", c_stt)

    c_store = None
    for qt in range(NQT):
        c_store = P.do("sync", lambda e, _qt=qt: e.dma_start(
            out_d[128 * _qt:128 * _qt + 128, :], facc[:, 1024 * _qt:1024 * _qt + 1024]),
            waits=[("tDV", stt_done[(NE, qt)])], inc=("qX", 16))
    P.do("sync", lambda e: (e.wait_ge(sems["qX"], c_store), None)[1])
    P.flush()
    mex.close()
    om.close()
    es.close()
    return nc


_NC_CACHE = None


def _get_nc():
    global _NC_CACHE
    if _NC_CACHE is None:
        _NC_CACHE = build_nc()
    return _NC_CACHE


def kernel(L0_attn_x, L0_attn_Wq, L0_attn_Wk, L0_attn_Wv, L0_attn_Wo,
           L0_ffn_centroids, L0_ffn_expert_biases, L0_ffn_Wr, L0_ffn_br,
           L0_ffn_Ws, L0_ffn_bs):
    x = np.ascontiguousarray(np.asarray(L0_attn_x, dtype=np.float32))
    wq = np.ascontiguousarray(np.asarray(L0_attn_Wq, dtype=np.float32))
    wk = np.ascontiguousarray(np.asarray(L0_attn_Wk, dtype=np.float32))
    wv = np.ascontiguousarray(np.asarray(L0_attn_Wv, dtype=np.float32))
    wo = np.ascontiguousarray(np.asarray(L0_attn_Wo, dtype=np.float32))
    wrt = np.ascontiguousarray(np.asarray(L0_ffn_Wr, dtype=np.float32).transpose(0, 2, 1))
    wst = np.ascontiguousarray(np.asarray(L0_ffn_Ws, dtype=np.float32).transpose(0, 2, 1))
    cent_t = np.ascontiguousarray(np.asarray(L0_ffn_centroids, dtype=np.float32).T)
    ebv = np.ascontiguousarray(np.asarray(L0_ffn_expert_biases, dtype=np.float32).reshape(16, 1))
    brv = np.ascontiguousarray(np.asarray(L0_ffn_br, dtype=np.float32))
    bsv = np.ascontiguousarray(np.asarray(L0_ffn_bs, dtype=np.float32))
    ident_b = np.eye(128, dtype=BF)
    ident_f = np.eye(128, dtype=np.float32)
    sel16 = np.zeros((16, 1024), dtype=BF)
    for hp_ in range(8):
        sel16[2 * hp_, 128 * hp_:128 * hp_ + 64] = 1.0
        sel16[2 * hp_ + 1, 128 * hp_ + 64:128 * hp_ + 128] = 1.0

    perms = []
    for par in range(2):
        idx = np.concatenate([
            np.arange(128 * (31 - k) + 64 * par, 128 * (31 - k) + 64 * par + 64)
            for k in range(32)])
        perms.append(idx)
    masks = []
    for par in range(2):
        jj = np.arange(128)[:, None]
        ii = np.arange(64)[None, :]
        masks.append((jj <= 64 * par + ii).astype(BF))

    in_maps = []
    for c in range(8):
        b, par = c // 2, c % 2
        in_maps.append({
            "x_ctx": x[b],
            "x_q": np.ascontiguousarray(x[b][perms[par]]),
            "wq": wq, "wk": wk, "wv": wv, "wo": wo,
            "wrt": wrt, "wst": wst, "cent_t": cent_t, "eb": ebv,
            "br": brv, "bs": bsv, "mask": masks[par],
            "ident_b": ident_b, "ident_f": ident_f, "sel2": sel16,
        })

    nc = _get_nc()
    res = run_bass_kernel_spmd(nc, in_maps, core_ids=list(range(8)))
    out = np.empty((4, 4096, 1024), dtype=np.float32)
    for c in range(8):
        b, par = c // 2, c % 2
        out[b][perms[par]] = res.results[c]["out"]
    return out


# version 10
# speedup vs baseline: 1.7481x; 1.0279x over previous
import numpy as np
import ml_dtypes
from contextlib import ExitStack

import concourse.bass as bass
import concourse.mybir as mybir
from concourse.bass_utils import run_bass_kernel_spmd

F32 = mybir.dt.float32
BF16 = mybir.dt.bfloat16
AF = mybir.ActivationFunctionType
ALU = mybir.AluOpType
AX = mybir.AxisListType
BF = ml_dtypes.bfloat16

D = 1024
NCTX = 4096
NQ = 2048
NKVT = 32
NQT = 16
HP = 8
SCALE = 0.125
EPS = 1e-5
NE = 16

ENGS = ("tensor", "vector", "scalar", "gpsimd", "sync")


class Prog:

    def __init__(self, nc, sems):
        self.nc = nc
        self.sems = sems
        self.cnt = {k: 0 for k in sems}
        self.q = {e: [] for e in ENGS}
        self.lastw = {e: {} for e in ENGS}

    def do(self, eng, fn, waits=(), inc=None):
        real_waits = []
        lw = self.lastw[eng]
        for s, v in waits:
            if s is None or v <= 0:
                continue
            if lw.get(s, -1) < v:
                real_waits.append((s, v))
                lw[s] = v
        if inc is not None:
            self.cnt[inc[0]] += inc[1]
        self.q[eng].append((fn, tuple(real_waits), inc))
        return self.cnt[inc[0]] if inc is not None else None

    def barrier(self):
        self.cnt["bar"] += 5
        tgt = self.cnt["bar"]
        for e in ENGS:
            def fn(eng, _t=tgt):
                eng.sem_inc(self.sems["bar"], 1)
                eng.wait_ge(self.sems["bar"], _t)
                return None
            self.q[e].append((fn, (), None))
            self.lastw[e]["bar"] = tgt

    def flush(self):
        with self.nc.Block() as blk:
            for e in ENGS:
                ops = self.q[e]
                self.q[e] = []
                if not ops:
                    continue

                def body(eng, _ops=tuple(ops), _sems=self.sems):
                    for fn, waits, inc in _ops:
                        for s, v in waits:
                            eng.wait_ge(_sems[s], v)
                        r = fn(eng)
                        if inc is not None and r is not None:
                            r.then_inc(_sems[inc[0]], inc[1])
                getattr(blk, e)(body)


def build_nc(stop_after='all'):
    nc = bass.Bass("TRN2", target_bir_lowering=False, detect_race_conditions=False)

    x_ctx = nc.dram_tensor("x_ctx", [NCTX, D], F32, kind="ExternalInput")
    x_q = nc.dram_tensor("x_q", [NQ, D], F32, kind="ExternalInput")
    wq = nc.dram_tensor("wq", [D, D], F32, kind="ExternalInput")
    wk = nc.dram_tensor("wk", [D, D], F32, kind="ExternalInput")
    wv = nc.dram_tensor("wv", [D, D], F32, kind="ExternalInput")
    wo = nc.dram_tensor("wo", [D, D], F32, kind="ExternalInput")
    wrt = nc.dram_tensor("wrt", [NE, D, D], F32, kind="ExternalInput")
    wst = nc.dram_tensor("wst", [2, D, D], F32, kind="ExternalInput")
    cent_t = nc.dram_tensor("cent_t", [D, 16], F32, kind="ExternalInput")
    eb = nc.dram_tensor("eb", [16, 1], F32, kind="ExternalInput")
    br = nc.dram_tensor("br", [16, D], F32, kind="ExternalInput")
    bs = nc.dram_tensor("bs", [2, D], F32, kind="ExternalInput")
    mask_in = nc.dram_tensor("mask", [128, 64], BF16, kind="ExternalInput")
    ident_b_in = nc.dram_tensor("ident_b", [128, 128], BF16, kind="ExternalInput")
    ident_f_in = nc.dram_tensor("ident_f", [128, 128], F32, kind="ExternalInput")
    sel2_in = nc.dram_tensor("sel2", [16, 1024], BF16, kind="ExternalInput")
    out_d = nc.dram_tensor("out", [NQ, D], F32, kind="ExternalOutput")
    av_d = nc.dram_tensor("av_spill", [HP, 128, NQ], BF16, kind="ExternalOutput")
    sums_d = nc.dram_tensor("sums_spill", [HP, 2, NQ], F32)

    es = ExitStack()
    sems = {}
    for s in ("qW", "qX", "tPE", "tSC", "tDV", "tGP", "bar"):
        sems[s] = es.enter_context(nc.semaphore(s))
    P = Prog(nc, sems)

    ident_b = es.enter_context(nc.sbuf_tensor("ident_b_s", [128, 128], BF16))
    mask_s = es.enter_context(nc.sbuf_tensor("mask_s", [128, 64], BF16))
    ones1 = es.enter_context(nc.sbuf_tensor("ones1", [128, 1], BF16))
    ones2 = es.enter_context(nc.sbuf_tensor("ones2", [2, 128], BF16))
    negb8 = es.enter_context(nc.sbuf_tensor("negb8", [128, 8], F32))
    stats = es.enter_context(nc.sbuf_tensor("stats", [128, 24], F32))

    c_idb = P.do("sync", lambda e: e.dma_start(ident_b[:], ident_b_in[:, :]), inc=("qX", 16))
    c_msk = P.do("sync", lambda e: e.dma_start(mask_s[:], mask_in[:, :]), inc=("qX", 16))
    P.do("gpsimd", lambda e: e.memset(ones1[:], 1.0))
    c_ones = P.do("gpsimd", lambda e: e.memset(ones2[:], 1.0), inc=("tGP", 1))

    def wcast_load(dst, dst_col, src, e_off):
        dst_stride = dst.shape[1]

        def fn(eng):
            return eng.dma_start(
                bass.AP(dst, dst_col, [[dst_stride, 128], [1024, 8], [1, 1024]]),
                bass.AP(src, e_off, [[1024, 128], [131072, 8], [1, 1024]]),
            )
        return fn

    kvq = ExitStack()
    k_all = kvq.enter_context(nc.sbuf_tensor("k_all", [128, HP * NCTX], BF16))
    v_all = kvq.enter_context(nc.sbuf_tensor("v_all", [128, NKVT * 1040], BF16))
    q_all = kvq.enter_context(nc.sbuf_tensor("q_all", [128, HP * NQ], BF16))
    xt = kvq.enter_context(nc.sbuf_tensor("xt", [128, 2 * 1024], F32))
    xlt = kvq.enter_context(nc.sbuf_tensor("xlt", [128, 1024], BF16))
    xlf = kvq.enter_context(nc.sbuf_tensor("xlf", [128, 8 * 256], BF16))

    lnst = {
        "xt_free": [[], []],
        "xlt_free": [],
        "cr_prev": [0, 0],
        "tp_guard": [],
        "g": 0,
    }

    def ln_stats(src_dram, row0, dump=None):
        g = lnst["g"]
        lnst["g"] += 1
        b = g % 2
        sb = b * 12
        c_x = P.do("sync", lambda e, _b=b, _r=row0, _s=src_dram: e.dma_start(
            xt[:, 1024 * _b:1024 * _b + 1024], _s[_r:_r + 128, :]),
            waits=list(lnst["xt_free"][b]), inc=("qX", 16))
        xv = xt[:, 1024 * b:1024 * b + 1024]
        c_rs = P.do("vector", lambda e, _x=xv, _c=sb: e.reduce_sum(
            stats[:, _c:_c + 1], _x, axis=AX.X),
            waits=[("qX", c_x), ("tSC", lnst["cr_prev"][b])], inc=("tDV", 1))
        dmp = dump if dump is not None else xlt[:, :]
        wsq = [("qX", c_x), ("tSC", lnst["cr_prev"][b])]
        if dump is None:
            wsq += list(lnst["xlt_free"])
        P.do("scalar", lambda e, _x=xv, _c=sb, _d=dmp: e.activation(
            _d, _x, AF.Square, accum_out=stats[:, _c + 1:_c + 2]), waits=wsq)
        c_sq = P.do("scalar", lambda e, _c=sb: e.copy(
            stats[:, _c + 9:_c + 10], stats[:, _c + 1:_c + 2]), inc=("tSC", 1))
        c1 = P.do("vector", lambda e, _c=sb: e.tensor_scalar(
            stats[:, _c + 2:_c + 3], stats[:, _c:_c + 1], 1.0 / D, None,
            ALU.mult, ALU.bypass), waits=[("tDV", c_rs)], inc=("tDV", 1))
        c2 = P.do("vector", lambda e, _c=sb: e.tensor_tensor(
            stats[:, _c + 3:_c + 4], stats[:, _c + 2:_c + 3], stats[:, _c + 2:_c + 3],
            ALU.mult), waits=[("tDV", c1)], inc=("tDV", 1))
        c3 = P.do("vector", lambda e, _c=sb: e.tensor_scalar(
            stats[:, _c + 4:_c + 5], stats[:, _c + 9:_c + 10], 1.0 / D, EPS,
            ALU.mult, ALU.add), waits=[("tSC", c_sq)], inc=("tDV", 1))
        c4 = P.do("vector", lambda e, _c=sb: e.tensor_tensor(
            stats[:, _c + 5:_c + 6], stats[:, _c + 4:_c + 5], stats[:, _c + 3:_c + 4],
            ALU.subtract), waits=[("tDV", c3)], inc=("tDV", 1))
        c_vr = P.do("vector", lambda e, _c=sb: e.reciprocal(
            stats[:, _c + 6:_c + 7], stats[:, _c + 5:_c + 6]),
            waits=[("tDV", c4)], inc=("tDV", 1))
        c_r = P.do("scalar", lambda e, _c=sb: e.activation(
            stats[:, _c + 7:_c + 8], stats[:, _c + 6:_c + 7], AF.Sqrt),
            waits=[("tDV", c_vr)], inc=("tSC", 1))
        lnst["cr_prev"][b] = c_r
        c5 = P.do("vector", lambda e, _c=sb: e.tensor_scalar(
            stats[:, _c + 8:_c + 9], stats[:, _c + 2:_c + 3], stats[:, _c + 7:_c + 8],
            -1.0, ALU.mult, ALU.mult), waits=[("tSC", c_r)], inc=("tDV", 1))
        return (c5, b, sb, xv, c_sq)

    def ln_transpose(cap, dst_fm_col, tp_ps):
        c5, b, sb, xv, c_sq = cap
        c_ap = P.do("vector", lambda e, _x=xv, _c=sb: e.tensor_scalar(
            xlt[:, :], _x, stats[:, _c + 7:_c + 8],
            stats[:, _c + 8:_c + 9], ALU.mult, ALU.add),
            waits=[("tDV", c5)] + list(lnst["xlt_free"]), inc=("tDV", 1))
        lnst["xt_free"][b] = [("tDV", c_ap), ("tSC", c_sq)]
        c_ev = None
        c_tp = None
        evs = {}
        for dc in range(8):
            pb = dc % 2
            wtp = [("tDV", c_ap), ("qX", c_idb)]
            if dc >= 2:
                wtp.append(("tDV", evs[dc - 2]))
            elif lnst["tp_guard"]:
                wtp += lnst["tp_guard"]
            c_tp = P.do("tensor", lambda e, _dc=dc, _pb=pb, _tp=tp_ps: e.transpose(
                _tp[_pb][:, :], xlt[:, 128 * _dc:128 * _dc + 128],
                ident_b[:]), waits=wtp, inc=("tPE", 1))
            c_ev = P.do("vector", lambda e, _dc=dc, _pb=pb, _col=dst_fm_col, _tp=tp_ps:
                        e.tensor_copy(
                            xlf[:, _col + 256 * _dc:_col + 256 * _dc + 128],
                            _tp[_pb][:, :]),
                        waits=[("tPE", c_tp)], inc=("tDV", 1))
            evs[dc] = c_ev
        lnst["xlt_free"] = [("tPE", c_tp)]
        lnst["tp_guard"] = [("tDV", c_ev)]
        return c_ev

    p0 = ExitStack()
    wk_b = p0.enter_context(nc.sbuf_tensor("wk_b", [128, 8192], BF16))
    wv_b = p0.enter_context(nc.sbuf_tensor("wv_b", [128, 8192], BF16))
    tp_ps_a = [p0.enter_context(nc.psum_tensor(f"tp_ps{i}", [128, 128], BF16)) for i in range(2)]
    kq_ps_a = [p0.enter_context(nc.psum_tensor(f"kq_ps{i}", [128, 256], F32)) for i in range(2)]
    v_ps = [p0.enter_context(nc.psum_tensor(f"v_ps{i}", [128, 1024], F32)) for i in range(2)]

    c_wk = P.do("gpsimd", wcast_load(wk_b, 0, wk, 0), inc=("qW", 16))
    c_wv = P.do("gpsimd", wcast_load(wv_b, 0, wv, 0), inc=("qW", 16))
    c_vones = P.do("gpsimd", lambda e: e.memset(v_all[:, :], 1.0), inc=("tGP", 1))

    kq_free = [[], []]
    v_free = [[], []]
    caps_next = [ln_stats(x_ctx, 128 * t2, dump=q_all[:, 0:1024]) for t2 in range(2)]
    for kb in range(NCTX // 256):
        c_ev = 0
        caps = caps_next
        for t2 in range(2):
            c_ev = ln_transpose(caps[t2], 128 * t2, tp_ps_a)
            if stop_after == 'ln0' and kb == 0 and t2 == 0:
                P.do("sync", lambda e: e.dma_start(out_d[0:128, 0:24], stats[:, :]),
                     waits=[("tSC", P.cnt["tSC"]), ("tDV", P.cnt["tDV"])], inc=("qX", 16))
                P.do("sync", lambda e: e.dma_start(
                    bass.AP(av_d, 0, [[2048, 128], [1, 1024]]), xlt[:, :]), inc=("qX", 16))
                P.do("sync", lambda e: e.dma_start(
                    bass.AP(av_d, 128 * 2048, [[2048, 128], [1, 2048]]), xlf[:, :]), inc=("qX", 16))
                P.flush()
                return nc
        if kb + 1 < NCTX // 256:
            caps_next = [ln_stats(x_ctx, 256 * (kb + 1) + 128 * t2,
                                  dump=q_all[:, 0:1024]) for t2 in range(2)]
        for hp in range(HP):
            pb = hp % 2
            w0 = [("tDV", c_ev), ("qW", c_wk)] + kq_free[pb]
            c_mm = None
            for dc in range(8):
                c_mm = P.do("tensor", lambda e, _hp=hp, _dc=dc, _pb=pb: e.matmul(
                    kq_ps_a[_pb][:, :],
                    wk_b[:, 1024 * _dc + 128 * _hp:1024 * _dc + 128 * _hp + 128],
                    xlf[:, 256 * _dc:256 * _dc + 256],
                    start=(_dc == 0), stop=(_dc == 7), skip_group_check=True),
                    waits=w0 if dc == 0 else [], inc=("tPE", 1))
            c_kev = P.do("scalar", lambda e, _hp=hp, _kb=kb, _pb=pb: e.activation(
                k_all[:, NCTX * _hp + 256 * _kb:NCTX * _hp + 256 * _kb + 256],
                kq_ps_a[_pb][:, :], AF.Copy), waits=[("tPE", c_mm)], inc=("tSC", 1))
            kq_free[pb] = [("tSC", c_kev)]
        for t2 in range(2):
            j = 2 * kb + t2
            pb = j % 2
            w0 = [("tDV", c_ev), ("qW", c_wv)] + v_free[pb]
            c_mm = None
            for dc in range(8):
                for nh in range(2):
                    c_mm = P.do("tensor", lambda e, _t2=t2, _dc=dc, _nh=nh, _pb=pb: e.matmul(
                        v_ps[_pb][:, 512 * _nh:512 * _nh + 512],
                        xlf[:, 256 * _dc + 128 * _t2:256 * _dc + 128 * _t2 + 128],
                        wv_b[:, 1024 * _dc + 512 * _nh:1024 * _dc + 512 * _nh + 512],
                        start=(_dc == 0), stop=(_dc == 7), skip_group_check=True),
                        waits=w0 if (dc == 0 and nh == 0) else [], inc=("tPE", 1))
            c_vev = P.do("scalar", lambda e, _j=j, _pb=pb: e.activation(
                bass.AP(v_all, 1040 * _j, [[NKVT * 1040, 128], [130, 8], [65, 2], [1, 64]]),
                bass.AP(v_ps[_pb], 0, [[1024, 128], [128, 8], [64, 2], [1, 64]]),
                AF.Copy),
                waits=[("tPE", c_mm), ("tGP", c_vones)], inc=("tSC", 1))
            v_free[pb] = [("tSC", c_vev)]
    c_p0a_pe = P.cnt["tPE"]

    P.flush()
    if stop_after == 'p0a':
        return nc
    p0.close()

    p0b = ExitStack()
    wq_b = p0b.enter_context(nc.sbuf_tensor("wq_b", [128, 8192], BF16))
    tp_ps_b = [p0b.enter_context(nc.psum_tensor(f"tpb_ps{i}", [128, 128], BF16)) for i in range(2)]
    kq_ps_b = [p0b.enter_context(nc.psum_tensor(f"kqb_ps{i}", [128, 256], F32)) for i in range(2)]

    c_wq = P.do("gpsimd", wcast_load(wq_b, 0, wq, 0),
                waits=[("tPE", c_p0a_pe)], inc=("qW", 16))
    kq_free = [[], []]
    caps_next = [ln_stats(x_q, 128 * t2) for t2 in range(2)]
    for qb in range(NQ // 256):
        c_ev = 0
        caps = caps_next
        for t2 in range(2):
            c_ev = ln_transpose(caps[t2], 128 * t2, tp_ps_b)
        if qb + 1 < NQ // 256:
            caps_next = [ln_stats(x_q, 256 * (qb + 1) + 128 * t2) for t2 in range(2)]
        for hp in range(HP):
            pb = hp % 2
            w0 = [("tDV", c_ev), ("qW", c_wq)] + kq_free[pb]
            c_mm = None
            for dc in range(8):
                c_mm = P.do("tensor", lambda e, _hp=hp, _dc=dc, _pb=pb: e.matmul(
                    kq_ps_b[_pb][:, :],
                    wq_b[:, 1024 * _dc + 128 * _hp:1024 * _dc + 128 * _hp + 128],
                    xlf[:, 256 * _dc:256 * _dc + 256],
                    start=(_dc == 0), stop=(_dc == 7), skip_group_check=True),
                    waits=w0 if dc == 0 else [], inc=("tPE", 1))
            c_qev = P.do("scalar", lambda e, _hp=hp, _qb=qb, _pb=pb: e.activation(
                q_all[:, NQ * _hp + 256 * _qb:NQ * _hp + 256 * _qb + 256],
                kq_ps_b[_pb][:, :], AF.Copy), waits=[("tPE", c_mm)], inc=("tSC", 1))
            kq_free[pb] = [("tSC", c_qev)]
    c_p0b_pe = P.cnt["tPE"]
    c_p0b_sc = P.cnt["tSC"]

    if stop_after == 'p0b':
        P.do("sync", lambda e: e.dma_start(
            bass.AP(av_d, 0, [[2048, 128], [1, 2048]]),
            k_all[:, 0:2048]), waits=[("tSC", P.cnt["tSC"])], inc=("qX", 16))
        P.do("sync", lambda e: e.dma_start(
            bass.AP(av_d, 128 * 2048, [[2048, 128], [1, 2048]]),
            k_all[:, 7 * 4096:7 * 4096 + 2048]), inc=("qX", 16))
        P.do("sync", lambda e: e.dma_start(
            bass.AP(av_d, 2 * 128 * 2048, [[2048, 128], [1, 2048]]),
            q_all[:, 7 * 2048:7 * 2048 + 2048]), inc=("qX", 16))
        P.do("sync", lambda e: e.dma_start(
            bass.AP(av_d, 3 * 128 * 2048, [[2048, 128], [1, 2048]]),
            v_all[:, 0:2048]), inc=("qX", 16))
        P.flush()
        return nc
    P.flush()
    p0b.close()

    aex = ExitStack()
    pt = aex.enter_context(nc.sbuf_tensor("pt", [128, 4096], BF16))
    ksq = pt
    s64r = aex.enter_context(nc.sbuf_tensor("s64r", [65, 2048], F32))
    av_spl = aex.enter_context(nc.sbuf_tensor("av_spl", [64, 4096], BF16))
    km = aex.enter_context(nc.sbuf_tensor("km", [1, 16], F32))
    bsc = aex.enter_context(nc.sbuf_tensor("bsc", [1, 4], F32))
    negvb = aex.enter_context(nc.sbuf_tensor("negvb", [1, 1], BF16))
    av_ps = [aex.enter_context(nc.psum_tensor(f"av_ps{i}", [65, 1024], F32)) for i in range(2)]
    d_psA = aex.enter_context(nc.psum_tensor("d_psA", [128, 1024], F32))
    d_psB = aex.enter_context(nc.psum_tensor("d_psB", [128, 1024], F32))

    for hp in range(HP):
        c_k2 = P.do("vector", lambda e, _hp=hp: e.tensor_tensor(
            ksq[:, :], k_all[:, NCTX * _hp:NCTX * _hp + 4096],
            k_all[:, NCTX * _hp:NCTX * _hp + 4096], ALU.mult),
            waits=[("tPE", c_p0b_pe), ("tSC", c_p0b_sc)], inc=("tDV", 1))
        c_rm = 0
        for c in range(8):
            pb = c % 2
            c_n2 = P.do("tensor", lambda e, _c=c, _pb=pb: e.matmul(
                d_psA[0:1, 512 * _pb:512 * _pb + 512], ones1[:],
                ksq[:, 512 * _c:512 * _c + 512],
                start=True, stop=True, skip_group_check=True),
                waits=[("tDV", c_k2), ("tGP", c_ones)] if c < 2 else [("tDV", c_rm)],
                inc=("tPE", 1))
            c_rm = P.do("vector", lambda e, _c=c, _pb=pb: e.reduce_max(
                km[0:1, _c:_c + 1], d_psA[0:1, 512 * _pb:512 * _pb + 512], axis=AX.X),
                waits=[("tPE", c_n2)], inc=("tDV", 1))
        c_km = P.do("vector", lambda e: e.reduce_max(
            bsc[0:1, 0:1], km[0:1, 0:8], axis=AX.X),
            waits=[("tDV", c_rm)], inc=("tDV", 1))
        c_q2 = P.do("vector", lambda e, _hp=hp: e.tensor_tensor(
            ksq[:, 0:2048], q_all[:, NQ * _hp:NQ * _hp + 2048],
            q_all[:, NQ * _hp:NQ * _hp + 2048], ALU.mult),
            waits=[("tPE", P.cnt["tPE"])], inc=("tDV", 1))
        for c in range(4):
            pb = c % 2
            c_n2 = P.do("tensor", lambda e, _c=c, _pb=pb: e.matmul(
                d_psA[0:1, 512 * _pb:512 * _pb + 512], ones1[:],
                ksq[:, 512 * _c:512 * _c + 512],
                start=True, stop=True, skip_group_check=True),
                waits=[("tDV", c_q2)] if c < 2 else [("tDV", c_rm)], inc=("tPE", 1))
            c_rm = P.do("vector", lambda e, _c=c, _pb=pb: e.reduce_max(
                km[0:1, 8 + _c:9 + _c], d_psA[0:1, 512 * _pb:512 * _pb + 512], axis=AX.X),
                waits=[("tPE", c_n2)], inc=("tDV", 1))
        c_qm = P.do("vector", lambda e: e.reduce_max(
            bsc[0:1, 1:2], km[0:1, 8:12], axis=AX.X),
            waits=[("tDV", c_rm)], inc=("tDV", 1))
        c_pr = P.do("vector", lambda e: e.tensor_tensor(
            bsc[0:1, 2:3], bsc[0:1, 0:1], bsc[0:1, 1:2], ALU.mult),
            waits=[("tDV", c_qm)], inc=("tDV", 1))
        c_sq2 = P.do("scalar", lambda e: e.activation(
            bsc[0:1, 3:4], bsc[0:1, 2:3], AF.Sqrt), waits=[("tDV", c_pr)], inc=("tSC", 1))
        c_nv = P.do("vector", lambda e: e.tensor_scalar(
            negvb[0:1, 0:1], bsc[0:1, 3:4], -SCALE, None, ALU.mult, ALU.bypass),
            waits=[("tSC", c_sq2)], inc=("tDV", 1))
        c_bc = P.do("tensor", lambda e: e.matmul(
            d_psA[:, 0:1], ones2[0:1, :], negvb[0:1, 0:1], start=True, stop=True,
            skip_group_check=True), waits=[("tDV", c_nv)], inc=("tPE", 1))
        P.do("vector", lambda e, _hp=hp: e.tensor_copy(
            negb8[:, _hp:_hp + 1], d_psA[:, 0:1]), waits=[("tPE", c_bc)],
            inc=("tDV", 1))
    c_bounds_dv = P.cnt["tDV"]
    c_bounds_sc = P.cnt["tSC"]

    slot_list = []
    for hp in range(HP):
        for h in range(2):
            jmax = 32 if h == 0 else 16
            for j in range(jmax):
                w = min(1024, 64 * (32 - j)) if h == 0 else 64 * (16 - j)
                masked = (h == 0 and j >= 16) or h == 1
                jlast0 = 31 if h == 0 else 15
                jlast1 = (31 - 8) if h == 0 else (15 - 8)
                slot_list.append((hp, h, j, w, masked,
                                  j == 0, (j == jlast0, j == jlast1),
                                  j == jmax - 1))
    exp_done = {}
    av_done = {}
    c_spill = {}
    c_sums_dma = {}
    dps_guard = [("tDV", c_bounds_dv)]

    def emit_av(s):
        hp, h, j, w, masked, st, sps, last = slot_list[s]
        p = s % 2
        vcol = 1040 * j + 130 * hp
        r = {}
        for hd in range(2):
            ptc = 2048 * hd + 1024 * p
            if s in mask_done:
                wa = [("tDV", mask_done[s][hd])]
            else:
                wa = [("tSC", exp_done[s][hd]), ("tGP", c_vones)]
            first = True
            for c0 in range(0, w, 512):
                cw = min(512, w - c0)
                sp = sps[c0 // 512]
                r[hd] = P.do("tensor", lambda e, _hd=hd, _v=vcol, _c0=c0, _cw=cw,
                             _st=st, _sp=sp, _pc=ptc: e.matmul(
                                 av_ps[_hd][0:65, _c0:_c0 + _cw],
                                 v_all[:, _v + 65 * _hd:_v + 65 * _hd + 65],
                                 pt[:, _pc + _c0:_pc + _c0 + _cw], start=_st, stop=_sp,
                                 skip_group_check=True),
                             waits=wa if first else [], inc=("tPE", 1))
                first = False
        av_done[s] = r

    mask_done = {}
    ns = len(slot_list)
    for s in range(ns):
        hp, h, j, w, masked, st, sps, last = slot_list[s]
        p = s % 2
        kcol = NCTX * hp + 128 * j
        qcol = NQ * hp + 1024 * h
        wtA = list(dps_guard)
        wtB = []
        dps_guard = []
        if s >= 1:
            wtA.append(("tSC", exp_done[s - 1][0]))
            wtB.append(("tSC", exp_done[s - 1][1]))
        dots_done = {}
        for c0 in range(0, w, 512):
            cw = min(512, w - c0)
            dots_done[0] = P.do("tensor", lambda e, _k=kcol, _q=qcol + c0, _c0=c0,
                                _cw=cw: e.matmul(
                d_psA[:, _c0:_c0 + _cw], k_all[0:64, _k:_k + 128],
                q_all[0:64, _q:_q + _cw], start=True, stop=True,
                tile_position=(0, 0), skip_group_check=True),
                waits=wtA if c0 == 0 else [], inc=("tPE", 1))
        for c0 in range(0, w, 512):
            cw = min(512, w - c0)
            dots_done[1] = P.do("tensor", lambda e, _k=kcol, _q=qcol + c0, _c0=c0,
                                _cw=cw: e.matmul(
                d_psB[:, _c0:_c0 + _cw], k_all[64:128, _k:_k + 128],
                q_all[64:128, _q:_q + _cw], start=True, stop=True,
                tile_position=(64, 0), skip_group_check=True),
                waits=wtB if c0 == 0 else [], inc=("tPE", 1))
        if s >= 1:
            emit_av(s - 1)
        eA = {}
        for hd in range(2):
            ws = [("tPE", dots_done[hd]), ("tDV", c_bounds_dv)]
            if s >= 2:
                ws.append(("tPE", av_done[s - 2][hd]))
            dps = d_psA if hd == 0 else d_psB
            eA[hd] = P.do("scalar", lambda e, _w=w, _hp=hp, _pc=2048 * hd + 1024 * p,
                          _d=dps: e.activation(
                pt[:, _pc:_pc + _w], _d[:, 0:_w], AF.Exp,
                bias=negb8[:, _hp:_hp + 1], scale=SCALE), waits=ws, inc=("tSC", 1))
        exp_done[s] = eA
        if masked:
            moff = w - 64
            mk = {}
            for hd in range(2):
                mk[hd] = P.do("vector", lambda e, _m=moff, _pc=2048 * hd + 1024 * p:
                              e.tensor_tensor(
                    pt[:, _pc + _m:_pc + _m + 64], pt[:, _pc + _m:_pc + _m + 64],
                    mask_s[:], ALU.mult),
                    waits=[("tSC", exp_done[s][hd]), ("qX", c_msk)], inc=("tDV", 1))
            mask_done[s] = mk
        if last:
            emit_av(s)
            c_av = av_done[s][1]
            wsp = [("qX", c_spill[hp - 1])] if (hp - 1 in c_spill and h == 0) else []
            c_sumsdma = c_sums_dma.get((hp, h - 1)) or c_sums_dma.get((hp - 1, 1))
            c_evs = []
            for head in range(2):
                c_evs.append(P.do("scalar", lambda e, _hd=head, _h=h: e.activation(
                    av_spl[0:64, 2048 * _hd + 1024 * _h:2048 * _hd + 1024 * _h + 1024],
                    av_ps[_hd][0:64, :], AF.Copy),
                    waits=[("tPE", c_av)] + wsp, inc=("tSC", 1)))
                c_evs.append(P.do("scalar", lambda e, _hd=head: e.activation(
                    s64r[64:65, 1024 * _hd:1024 * _hd + 1024],
                    av_ps[_hd][64:65, :], AF.Copy),
                    waits=([("qX", c_sumsdma)] if c_sumsdma else []), inc=("tSC", 1)))
            c_ep = c_evs[-1]
            dps_guard = [("tSC", c_ep)]
            c_sums_dma[(hp, h)] = P.do("sync", lambda e, _hp=hp, _h=h: e.dma_start(
                bass.AP(sums_d, _hp * 2 * NQ + 1024 * _h, [[2 * NQ, 1], [NQ, 2], [1, 1024]]),
                bass.AP(s64r, 64 * 2048, [[2048, 1], [1024, 2], [1, 1024]])),
                waits=[("tSC", c_ep)], inc=("qX", 16))
            if h == 1:
                for head in range(2):
                    c_spill[hp] = P.do("sync", lambda e, _hp=hp, _hd=head: e.dma_start(
                        bass.AP(av_d, (_hp * 128 + 64 * _hd) * NQ, [[NQ, 64], [1, NQ]]),
                        av_spl[0:64, 2048 * _hd:2048 * _hd + 2048]),
                        waits=[("tSC", c_ep)], inc=("qX", 16))
    c_spill_last = c_spill[HP - 1]

    P.flush()
    if stop_after == 'attn':
        return nc
    aex.close()
    kvq.close()

    P.barrier()
    om = ExitStack()
    u_fm = om.enter_context(nc.sbuf_tensor("u_fm", [128, 8 * NQ], BF16))
    facc = om.enter_context(nc.sbuf_tensor("facc", [128, NQT * 1024], F32))
    g_all = om.enter_context(nc.sbuf_tensor("g_all", [128, NQT * 16], F32))
    g2t = om.enter_context(nc.sbuf_tensor("g2t", [16, NQ], BF16))
    br_b = om.enter_context(nc.sbuf_tensor("br_b", [16, 1024], BF16))

    oex = ExitStack()
    av_all = oex.enter_context(nc.sbuf_tensor("av_all", [128, HP * NQ], BF16))
    wo_b = oex.enter_context(nc.sbuf_tensor("wo_b", [128, 8192], BF16))
    ident_f = oex.enter_context(nc.sbuf_tensor("ident_f_s", [128, 128], F32))
    sel2 = oex.enter_context(nc.sbuf_tensor("sel2_s", [16, 1024], BF16))
    rec2 = oex.enter_context(nc.sbuf_tensor("rec2", [16, 2048], F32))
    rec2h = oex.enter_context(nc.sbuf_tensor("rec2h", [16, 2048], BF16))
    rec2l = oex.enter_context(nc.sbuf_tensor("rec2l", [16, 2048], BF16))
    xq_t = oex.enter_context(nc.sbuf_tensor("xq_t", [128, 1024], F32))
    res1_t = oex.enter_context(nc.sbuf_tensor("res1", [128, 2048], F32))
    u_t2 = oex.enter_context(nc.sbuf_tensor("u_t", [128, 2048], F32))
    u_bf = oex.enter_context(nc.sbuf_tensor("u_bf", [128, 2048], BF16))
    cent_b = oex.enter_context(nc.sbuf_tensor("cent_b", [128, 128], BF16))
    sc_em = oex.enter_context(nc.sbuf_tensor("sc_em", [16, NQ], F32))
    eb_s = oex.enter_context(nc.sbuf_tensor("eb_s", [16, 1], F32))
    bs_b = oex.enter_context(nc.sbuf_tensor("bs_b", [2, 1024], BF16))
    rt2 = oex.enter_context(nc.sbuf_tensor("rt", [128, 160], F32))
    gg_t = oex.enter_context(nc.sbuf_tensor("gg_t", [128, 16], BF16))
    o_ps = [oex.enter_context(nc.psum_tensor(f"o_ps{i}", [128, 1024], F32)) for i in range(2)]
    tp2_ps = [oex.enter_context(nc.psum_tensor("tp2_ps0", [128, 128], BF16))]
    tp2_ps.append(tp2_ps[0])
    sc_ps = oex.enter_context(nc.psum_tensor("sc_ps", [16, 512], F32))
    rt_ps = [oex.enter_context(nc.psum_tensor(f"rt_ps{i}", [128, 128], F32)) for i in range(2)]
    misc_ps = rt_ps[0]

    c_av_ld = P.do("sync", lambda e: e.dma_start(
        bass.AP(av_all, 0, [[HP * NQ, 128], [NQ, HP], [1, NQ]]),
        bass.AP(av_d, 0, [[NQ, 128], [128 * NQ, HP], [1, NQ]])),
        waits=[("qX", c_spill_last)], inc=("qX", 16))
    c_sums_ld = P.do("sync", lambda e: e.dma_start(
        bass.AP(rec2, 0, [[2048, 16], [1, 2048]]),
        bass.AP(sums_d, 0, [[2048, 16], [1, 2048]])), inc=("qX", 16))
    c_sel2 = P.do("sync", lambda e: e.dma_start(sel2[:], sel2_in[:, :]), inc=("qX", 16))
    c_rc2 = P.do("vector", lambda e: e.reciprocal(rec2[:, :], rec2[:, :]),
                 waits=[("qX", c_sums_ld)], inc=("tDV", 1))
    c_rh = P.do("vector", lambda e: e.tensor_copy(rec2h[:, :], rec2[:, :]),
                waits=[("tDV", c_rc2)], inc=("tDV", 1))
    c_rl = P.do("vector", lambda e: e.tensor_tensor(
        rec2l[:, :], rec2[:, :], rec2h[:, :], ALU.subtract),
        waits=[("tDV", c_rh)], inc=("tDV", 1))

    nrm_done = {}
    for hp in range(HP):
        for qt in range(NQT):
            i = hp * NQT + qt
            rp = i % 2
            wn = [("tDV", c_rl), ("qX", c_av_ld), ("qX", c_sel2)]
            if i >= 2:
                wn.append(("tDV", nrm_done[i - 2]))
            P.do("tensor", lambda e, _hp=hp, _qt=qt, _rp=rp: e.matmul(
                rt_ps[_rp][:, :], sel2[:, 128 * _hp:128 * _hp + 128],
                rec2h[:, 128 * _qt:128 * _qt + 128],
                start=True, stop=False, skip_group_check=True),
                waits=wn, inc=("tPE", 1))
            c_nm = P.do("tensor", lambda e, _hp=hp, _qt=qt, _rp=rp: e.matmul(
                rt_ps[_rp][:, :], sel2[:, 128 * _hp:128 * _hp + 128],
                rec2l[:, 128 * _qt:128 * _qt + 128],
                start=False, stop=True, skip_group_check=True), inc=("tPE", 1))
            nrm_done[i] = P.do("vector", lambda e, _hp=hp, _qt=qt, _rp=rp: e.tensor_tensor(
                av_all[:, NQ * _hp + 128 * _qt:NQ * _hp + 128 * _qt + 128],
                av_all[:, NQ * _hp + 128 * _qt:NQ * _hp + 128 * _qt + 128],
                rt_ps[_rp][:, :], ALU.mult), waits=[("tPE", c_nm)], inc=("tDV", 1))
    c_nrm_all = P.cnt["tDV"]
    c_idf = P.do("sync", lambda e: e.dma_start(ident_f[:], ident_f_in[:, :]), inc=("qX", 16))
    c_wo = P.do("gpsimd", wcast_load(wo_b, 0, wo, 0), inc=("qW", 16))
    c_cent = P.do("gpsimd", lambda e: e.dma_start(
        bass.AP(cent_b, 0, [[128, 128], [16, 8], [1, 16]]),
        bass.AP(cent_t, 0, [[16, 128], [2048, 8], [1, 16]])), inc=("qW", 16))
    c_eb = P.do("gpsimd", lambda e: e.dma_start(eb_s[:], eb[:, :]), inc=("qW", 16))
    c_br = P.do("gpsimd", lambda e: e.dma_start(br_b[0:16, :], br[:, :]), inc=("qW", 16))
    c_bs = P.do("gpsimd", lambda e: e.dma_start(bs_b[:], bs[:, :]), inc=("qW", 16))

    xq_free = [[], []]
    ops_free = [[], []]
    ubf_free = [[], []]
    res1_free = [[], []]
    c_sq_g = {0: 0, 1: 0}
    c_r_g = {0: 0, 1: 0}
    for qt in range(NQT):
        b = qt % 2
        res1 = res1_t[:, 1024 * b:1024 * b + 1024]
        u_t = u_t2[:, 1024 * b:1024 * b + 1024]
        rt = rt2[:, 80 * b:80 * b + 80]
        c_xq = P.do("sync", lambda e, _qt=qt: e.dma_start(
            xq_t[:, :], x_q[128 * _qt:128 * _qt + 128, :]),
            waits=list(xq_free[0]), inc=("qX", 16))
        wmm = [("tDV", c_nrm_all), ("qW", c_wo)] + ops_free[b]
        c_mm = None
        for ic in range(8):
            for nh in range(2):
                c_mm = P.do("tensor", lambda e, _qt=qt, _ic=ic, _nh=nh, _b=b: e.matmul(
                    o_ps[_b][:, 512 * _nh:512 * _nh + 512],
                    av_all[:, NQ * _ic + 128 * _qt:NQ * _ic + 128 * _qt + 128],
                    wo_b[:, 1024 * _ic + 512 * _nh:1024 * _ic + 512 * _nh + 512],
                    start=(_ic == 0), stop=(_ic == 7), skip_group_check=True),
                    waits=wmm if (ic == 0 and nh == 0) else [], inc=("tPE", 1))
        c_r1 = P.do("vector", lambda e, _b=b, _r=res1: e.tensor_tensor(
            _r, o_ps[_b][:, :], xq_t[:, :], ALU.add),
            waits=[("tPE", c_mm), ("qX", c_xq)] + res1_free[b], inc=("tDV", 1))
        ops_free[b] = [("tDV", c_r1)]
        xq_free[0] = [("tDV", c_r1)]
        c_rs1 = P.do("vector", lambda e, _r=res1, _rt=rt: e.reduce_sum(
            _rt[:, 0:1], _r, axis=AX.X),
            waits=[("tSC", c_r_g[b])], inc=("tDV", 1))
        wub = list(ubf_free[b])
        P.do("scalar", lambda e, _b=b, _r=res1, _rt=rt: e.activation(
            u_bf[:, 1024 * _b:1024 * _b + 1024], _r, AF.Square,
            accum_out=_rt[:, 1:2]), waits=[("tDV", c_r1)] + wub)
        c_sq = P.do("scalar", lambda e, _rt=rt: e.copy(_rt[:, 14:15], _rt[:, 1:2]),
                    inc=("tSC", 1))
        c_sq_g[b] = c_sq
        d1 = P.do("vector", lambda e, _rt=rt: e.tensor_scalar(
            _rt[:, 2:3], _rt[:, 0:1], 1.0 / D, None, ALU.mult, ALU.bypass),
            waits=[("tDV", c_rs1)], inc=("tDV", 1))
        d2 = P.do("vector", lambda e, _rt=rt: e.tensor_tensor(
            _rt[:, 3:4], _rt[:, 2:3], _rt[:, 2:3], ALU.mult),
            waits=[("tDV", d1)], inc=("tDV", 1))
        d3 = P.do("vector", lambda e, _rt=rt: e.tensor_scalar(
            _rt[:, 4:5], _rt[:, 14:15], 1.0 / D, EPS, ALU.mult, ALU.add),
            waits=[("tSC", c_sq)], inc=("tDV", 1))
        d4 = P.do("vector", lambda e, _rt=rt: e.tensor_tensor(
            _rt[:, 5:6], _rt[:, 4:5], _rt[:, 3:4], ALU.subtract),
            waits=[("tDV", d3)], inc=("tDV", 1))
        c_vr = P.do("vector", lambda e, _rt=rt: e.reciprocal(_rt[:, 6:7], _rt[:, 5:6]),
                    waits=[("tDV", d4)], inc=("tDV", 1))
        c_r = P.do("scalar", lambda e, _rt=rt: e.activation(_rt[:, 7:8], _rt[:, 6:7], AF.Sqrt),
                   waits=[("tDV", c_vr)], inc=("tSC", 1))
        c_r_g[b] = c_r
        d5 = P.do("vector", lambda e, _rt=rt: e.tensor_scalar(
            _rt[:, 8:9], _rt[:, 2:3], _rt[:, 7:8], -1.0, ALU.mult, ALU.mult),
            waits=[("tSC", c_r)], inc=("tDV", 1))
        d6 = P.do("vector", lambda e, _r=res1, _u=u_t, _rt=rt: e.tensor_scalar(
            _u, _r, _rt[:, 7:8], _rt[:, 8:9], ALU.mult, ALU.add),
            waits=[("tDV", d5)], inc=("tDV", 1))
        c_fc = P.do("vector", lambda e, _qt=qt, _r=res1, _u=u_t: e.tensor_tensor(
            facc[:, 1024 * _qt:1024 * _qt + 1024], _r, _u, ALU.add),
            waits=[("tDV", d6)], inc=("tDV", 1))
        c_ub = P.do("vector", lambda e, _b=b, _u=u_t: e.tensor_copy(
            u_bf[:, 1024 * _b:1024 * _b + 1024], _u), inc=("tDV", 1))
        res1_free[b] = [("tSC", c_sq), ("tDV", c_fc)]
        c_ev = None
        for dc in range(8):
            pb = dc % 2
            c_tp = P.do("tensor", lambda e, _b=b, _dc=dc, _pb=pb: e.transpose(
                tp2_ps[_pb][:, :], u_bf[:, 1024 * _b + 128 * _dc:1024 * _b + 128 * _dc + 128],
                ident_b[:]), waits=[("tDV", c_ub), ("tSC", P.cnt["tSC"])],
                inc=("tPE", 1))
            c_ev = P.do("scalar", lambda e, _dc=dc, _pb=pb, _qt=qt: e.activation(
                u_fm[:, NQ * _dc + 128 * _qt:NQ * _dc + 128 * _qt + 128],
                tp2_ps[_pb][:, :], AF.Copy), waits=[("tPE", c_tp)], inc=("tSC", 1))
        ubf_free[b] = [("tSC", c_ev)]
    c_o1_sc = P.cnt["tSC"]

    c_scev = {}
    for c in range(4):
        c_mm = None
        for dc in range(8):
            w0 = [("qW", c_cent), ("tSC", P.cnt["tSC"])]
            if c > 0:
                w0.append(("tDV", c_scev[c - 1]))
            c_mm = P.do("tensor", lambda e, _c=c, _dc=dc: e.matmul(
                sc_ps[:, :], cent_b[:, 16 * _dc:16 * _dc + 16],
                u_fm[:, NQ * _dc + 512 * _c:NQ * _dc + 512 * _c + 512],
                start=(_dc == 0), stop=(_dc == 7), skip_group_check=True),
                waits=w0 if dc == 0 else [], inc=("tPE", 1))
        c_scev[c] = P.do("vector", lambda e, _c=c: e.tensor_scalar(
            sc_em[:, 512 * _c:512 * _c + 512], sc_ps[:, :], eb_s[:, 0:1], None,
            ALU.add, ALU.bypass), waits=[("tPE", c_mm), ("qW", c_eb)], inc=("tDV", 1))
    c_es_prev = {0: 0, 1: 0}
    c_m1_prev = {0: 0, 1: 0}
    c_gt_prev = 0
    c_g_prev = {0: 0, 1: 0}
    for qt in range(NQT):
        b = qt % 2
        rt = rt2[:, 80 * b:80 * b + 80]
        mps = rt_ps[b]
        c_tr = P.do("tensor", lambda e, _qt=qt, _m=mps: e.transpose(
            _m[:, 0:16], sc_em[0:16, 128 * _qt:128 * _qt + 128], ident_f[0:16, 0:16]),
            waits=[("tDV", c_scev[qt // 4]), ("qX", c_idf),
                   ("tSC", c_es_prev[b]), ("tDV", c_m1_prev[b])], inc=("tPE", 1))
        c_m1 = P.do("vector", lambda e, _rt=rt, _m=mps: e.reduce_max(
            _rt[:, 9:10], _m[:, 0:16], axis=AX.X, negate=True),
            waits=[("tPE", c_tr), ("tSC", c_r_g[b])], inc=("tDV", 1))
        c_m1_prev[b] = c_m1
        c_es = P.do("scalar", lambda e, _rt=rt, _m=mps: e.activation(
            _rt[:, 32:48], _m[:, 0:16], AF.Exp, bias=_rt[:, 9:10], scale=1.0),
            waits=[("tDV", c_m1)], inc=("tSC", 1))
        c_es_prev[b] = c_es
        c_rsm = P.do("vector", lambda e, _rt=rt: e.reduce_sum(
            _rt[:, 10:11], _rt[:, 32:48], axis=AX.X), waits=[("tSC", c_es)],
            inc=("tDV", 1))
        e1 = P.do("vector", lambda e, _rt=rt: e.reciprocal(_rt[:, 11:12], _rt[:, 10:11]),
             waits=[("tDV", c_rsm)], inc=("tDV", 1))
        e2 = P.do("vector", lambda e, _rt=rt: e.tensor_scalar(
            _rt[:, 48:64], _rt[:, 32:48], _rt[:, 11:12], None, ALU.mult, ALU.bypass),
            waits=[("tDV", e1)], inc=("tDV", 1))
        c_rm1 = P.do("vector", lambda e, _rt=rt: e.reduce_max(
            _rt[:, 12:13], _rt[:, 48:64], axis=AX.X),
            waits=[("tDV", e2)], inc=("tDV", 1))
        e3 = P.do("vector", lambda e, _rt=rt: e.tensor_scalar(
            _rt[:, 64:80], _rt[:, 48:64], _rt[:, 12:13], None, ALU.is_ge, ALU.bypass),
            waits=[("tDV", c_rm1)], inc=("tDV", 1))
        e4 = P.do("vector", lambda e, _rt=rt: e.scalar_tensor_tensor(
            _rt[:, 16:32], _rt[:, 64:80], -1e4, _rt[:, 48:64], ALU.mult, ALU.add),
            waits=[("tDV", e3)], inc=("tDV", 1))
        c_rm2 = P.do("vector", lambda e, _rt=rt: e.reduce_max(
            _rt[:, 13:14], _rt[:, 16:32], axis=AX.X),
            waits=[("tDV", e4)], inc=("tDV", 1))
        e5 = P.do("vector", lambda e, _rt=rt: e.tensor_scalar(
            _rt[:, 64:80], _rt[:, 48:64], _rt[:, 13:14], None, ALU.is_ge, ALU.bypass),
            waits=[("tDV", c_rm2)], inc=("tDV", 1))
        c_g = P.do("vector", lambda e, _qt=qt, _rt=rt: e.tensor_tensor(
            g_all[:, 16 * _qt:16 * _qt + 16], _rt[:, 48:64], _rt[:, 64:80], ALU.mult),
            waits=[("tDV", e5)], inc=("tDV", 1))
        c_g_prev[b] = c_g
        c_gg = P.do("vector", lambda e, _qt=qt: e.tensor_scalar(
            gg_t[:, :], g_all[:, 16 * _qt:16 * _qt + 16], 2.0, None,
            ALU.mult, ALU.bypass), waits=[("tDV", c_g), ("tPE", c_gt_prev)],
            inc=("tDV", 1))
        c_gt = P.do("tensor", lambda e: e.transpose(
            tp2_ps[0][0:16, :], gg_t[:, :], ident_b[:]),
            waits=[("tDV", c_gg), ("tSC", P.cnt["tSC"])], inc=("tPE", 1))
        c_gt_prev = c_gt
        P.do("scalar", lambda e, _qt=qt: e.activation(
            g2t[0:16, 128 * _qt:128 * _qt + 128], tp2_ps[0][0:16, :], AF.Copy),
            waits=[("tPE", c_gt)], inc=("tSC", 1))
    c_o_done_sc = P.cnt["tSC"]
    c_o_done_dv = P.cnt["tDV"]

    if stop_after == 'o':
        P.do("sync", lambda e: e.dma_start(
            bass.AP(out_d, 0, [[1024, 128], [131072, 16], [1, 1024]]),
            bass.AP(facc, 0, [[16384, 128], [1024, 16], [1, 1024]])),
            waits=[("tDV", P.cnt["tDV"]), ("tSC", P.cnt["tSC"])], inc=("qX", 16))
        P.do("gpsimd", lambda e: e.dma_start(
            bass.AP(av_d, 0, [[256, 128], [1, 256]]),
            bass.AP(g_all, 0, [[256, 128], [1, 256]])),
            waits=[("tDV", P.cnt["tDV"])], inc=("qW", 16))
        P.do("sync", lambda e: e.dma_start(
            bass.AP(av_d, 128 * 256, [[2048, 16], [1, 2048]]),
            bass.AP(g2t, 0, [[2048, 16], [1, 2048]])), inc=("qX", 16))
        P.do("sync", lambda e: e.dma_start(
            bass.AP(av_d, 2 * 128 * 256, [[2048, 128], [1, 2048]]),
            u_fm[:, 0:2048]), inc=("qX", 16))
        P.flush()
        return nc
    P.flush()
    oex.close()

    P.barrier()
    mex = ExitStack()
    wr_b = mex.enter_context(nc.sbuf_tensor("wr_b", [128, 2 * 8192], BF16))
    ws_tmp = mex.enter_context(nc.sbuf_tensor("ws_tmp", [128, 8192], BF16))
    e_ps = [mex.enter_context(nc.psum_tensor(f"e_ps{i}", [128, 1024], F32)) for i in range(3)]

    c_ws0 = P.do("gpsimd", wcast_load(wr_b, 0, wst, 0), inc=("qW", 16))
    c_ws1 = P.do("gpsimd", wcast_load(ws_tmp, 0, wst, D * D), inc=("qW", 16))
    c_wsum = P.do("vector", lambda e: e.tensor_tensor(
        wr_b[:, 0:8192], wr_b[:, 0:8192], ws_tmp[:, 0:8192], ALU.add),
        waits=[("qW", c_ws1)], inc=("tDV", 1))
    w_ready = {0: ("tDV", c_wsum)}
    wbuf_last_mm = {0: 0, 1: 0}
    mm_done = {}
    stt_done = {}
    psum_guard = {0: None, 1: None, 2: None}
    for t in range(NE + 1):
        if t + 1 <= NE:
            ee = t
            half = (t + 1) % 2
            w_ready[t + 1] = ("qW", P.do(
                "gpsimd", wcast_load(wr_b, 8192 * half, wrt, ee * D * D),
                waits=[("tPE", wbuf_last_mm[half])] if wbuf_last_mm[half] else [],
                inc=("qW", 16)))
        half = t % 2
        for qt in range(NQT):
            r = (t * NQT + qt) % 3
            w0 = [w_ready[t]]
            if psum_guard[r] is not None:
                w0.append(psum_guard[r])
            c_mm = None
            for dc in range(8):
                for nh in range(2):
                    c_mm = P.do("tensor", lambda e, _t=t, _qt=qt, _dc=dc, _nh=nh,
                                _r=r, _hf=half: e.matmul(
                        e_ps[_r][:, 512 * _nh:512 * _nh + 512],
                        u_fm[:, NQ * _dc + 128 * _qt:NQ * _dc + 128 * _qt + 128],
                        wr_b[:, 8192 * _hf + 1024 * _dc + 512 * _nh:
                             8192 * _hf + 1024 * _dc + 512 * _nh + 512],
                        start=(_dc == 0), stop=(_dc == 7 and _t > 0),
                        skip_group_check=True),
                        waits=w0 if (dc == 0 and nh == 0) else [], inc=("tPE", 1))
            if t == 0:
                for nh in range(2):
                    P.do("tensor", lambda e, _qt=qt, _nh=nh, _r=r: e.matmul(
                        e_ps[_r][:, 512 * _nh:512 * _nh + 512],
                        g2t[0:16, 128 * _qt:128 * _qt + 128],
                        br_b[0:16, 512 * _nh:512 * _nh + 512],
                        start=False, stop=False, skip_group_check=True),
                        waits=[("tSC", c_o_done_sc), ("qW", c_br), ("qW", c_bs)] if (qt == 0 and nh == 0) else [],
                        inc=("tPE", 1))
                    c_mm = P.do("tensor", lambda e, _nh=nh, _r=r: e.matmul(
                        e_ps[_r][:, 512 * _nh:512 * _nh + 512],
                        ones2[:, 0:128],
                        bs_b[:, 512 * _nh:512 * _nh + 512],
                        start=False, stop=True, skip_group_check=True),
                        inc=("tPE", 1))
            mm_done[(t, qt)] = c_mm
            wbuf_last_mm[half] = c_mm
            c_stt = None
            w_prev_stt = [("tDV", stt_done[(t - 1, qt)])] if t > 0 else [("tDV", c_o_done_dv)]
            for nh in range(2):
                if t == 0:
                    c_stt = P.do("vector", lambda e, _qt=qt, _nh=nh, _r=r:
                                 e.scalar_tensor_tensor(
                                     facc[:, 1024 * _qt + 512 * _nh:1024 * _qt + 512 * _nh + 512],
                                     e_ps[_r][:, 512 * _nh:512 * _nh + 512], 0.5,
                                     facc[:, 1024 * _qt + 512 * _nh:1024 * _qt + 512 * _nh + 512],
                                     ALU.mult, ALU.add),
                                 waits=([("tPE", c_mm)] + w_prev_stt) if nh == 0 else [],
                                 inc=("tDV", 1))
                else:
                    c_stt = P.do("vector", lambda e, _t=t, _qt=qt, _nh=nh, _r=r:
                                 e.scalar_tensor_tensor(
                                     facc[:, 1024 * _qt + 512 * _nh:1024 * _qt + 512 * _nh + 512],
                                     e_ps[_r][:, 512 * _nh:512 * _nh + 512],
                                     g_all[:, 16 * _qt + (_t - 1):16 * _qt + _t],
                                     facc[:, 1024 * _qt + 512 * _nh:1024 * _qt + 512 * _nh + 512],
                                     ALU.mult, ALU.add),
                                 waits=([("tPE", c_mm)] + w_prev_stt) if nh == 0 else [],
                                 inc=("tDV", 1))
            stt_done[(t, qt)] = c_stt
            psum_guard[r] = ("tDV", c_stt)

    c_store = None
    for qt in range(NQT):
        c_store = P.do("sync", lambda e, _qt=qt: e.dma_start(
            out_d[128 * _qt:128 * _qt + 128, :], facc[:, 1024 * _qt:1024 * _qt + 1024]),
            waits=[("tDV", stt_done[(NE, qt)])], inc=("qX", 16))
    P.do("sync", lambda e: (e.wait_ge(sems["qX"], c_store), None)[1])
    P.flush()
    mex.close()
    om.close()
    es.close()
    return nc


_NC_CACHE = None


def _get_nc():
    global _NC_CACHE
    if _NC_CACHE is None:
        _NC_CACHE = build_nc()
    return _NC_CACHE


def kernel(L0_attn_x, L0_attn_Wq, L0_attn_Wk, L0_attn_Wv, L0_attn_Wo,
           L0_ffn_centroids, L0_ffn_expert_biases, L0_ffn_Wr, L0_ffn_br,
           L0_ffn_Ws, L0_ffn_bs):
    x = np.ascontiguousarray(np.asarray(L0_attn_x, dtype=np.float32))
    wq = np.ascontiguousarray(np.asarray(L0_attn_Wq, dtype=np.float32))
    wk = np.ascontiguousarray(np.asarray(L0_attn_Wk, dtype=np.float32))
    wv = np.ascontiguousarray(np.asarray(L0_attn_Wv, dtype=np.float32))
    wo = np.ascontiguousarray(np.asarray(L0_attn_Wo, dtype=np.float32))
    wrt = np.ascontiguousarray(np.asarray(L0_ffn_Wr, dtype=np.float32).transpose(0, 2, 1))
    wst = np.ascontiguousarray(np.asarray(L0_ffn_Ws, dtype=np.float32).transpose(0, 2, 1))
    cent_t = np.ascontiguousarray(np.asarray(L0_ffn_centroids, dtype=np.float32).T)
    ebv = np.ascontiguousarray(np.asarray(L0_ffn_expert_biases, dtype=np.float32).reshape(16, 1))
    brv = np.ascontiguousarray(np.asarray(L0_ffn_br, dtype=np.float32))
    bsv = np.ascontiguousarray(np.asarray(L0_ffn_bs, dtype=np.float32))
    ident_b = np.eye(128, dtype=BF)
    ident_f = np.eye(128, dtype=np.float32)
    sel16 = np.zeros((16, 1024), dtype=BF)
    for hp_ in range(8):
        sel16[2 * hp_, 128 * hp_:128 * hp_ + 64] = 1.0
        sel16[2 * hp_ + 1, 128 * hp_ + 64:128 * hp_ + 128] = 1.0

    perms = []
    for par in range(2):
        idx = np.concatenate([
            np.arange(128 * (31 - k) + 64 * par, 128 * (31 - k) + 64 * par + 64)
            for k in range(32)])
        perms.append(idx)
    masks = []
    for par in range(2):
        jj = np.arange(128)[:, None]
        ii = np.arange(64)[None, :]
        masks.append((jj <= 64 * par + ii).astype(BF))

    in_maps = []
    for c in range(8):
        b, par = c // 2, c % 2
        in_maps.append({
            "x_ctx": x[b],
            "x_q": np.ascontiguousarray(x[b][perms[par]]),
            "wq": wq, "wk": wk, "wv": wv, "wo": wo,
            "wrt": wrt, "wst": wst, "cent_t": cent_t, "eb": ebv,
            "br": brv, "bs": bsv, "mask": masks[par],
            "ident_b": ident_b, "ident_f": ident_f, "sel2": sel16,
        })

    nc = _get_nc()
    res = run_bass_kernel_spmd(nc, in_maps, core_ids=list(range(8)))
    out = np.empty((4, 4096, 1024), dtype=np.float32)
    for c in range(8):
        b, par = c // 2, c % 2
        out[b][perms[par]] = res.results[c]["out"]
    return out
